# version 18
# baseline (speedup 1.0000x reference)
import numpy as np
import concourse.bass as bass
import concourse.mybir as mybir
from concourse.bass_utils import run_bass_kernel_spmd
from contextlib import ExitStack

F32 = mybir.dt.float32
BF16 = mybir.dt.bfloat16
I32 = mybir.dt.int32
AF = mybir.ActivationFunctionType
ALU = mybir.AluOpType
AX = mybir.AxisListType

DEBUG = False
SIM1 = False
PHASES = 99
N_CORES = 8
R = 2048
TG = 512
NG = 4
CH = 128
NCH = 16
H = 16; DH = 64; DM = 1024; DMT = 8; HID = 1024; HT = 8
FF = 4096; FFT = 32
LN_EPS = 1e-5
RMS_EPS = float(np.finfo(np.float32).eps)
Q_EPS = 1e-5
NEG = -1.0e30
LN127 = float(np.log(127.0))


class EW:
    OPS = {"tensor_copy", "tensor_tensor", "tensor_add", "tensor_sub", "tensor_mul",
           "tensor_max", "tensor_scalar", "tensor_single_scalar", "tensor_scalar_add",
           "tensor_scalar_sub", "tensor_scalar_mul", "tensor_scalar_max",
           "tensor_scalar_min", "tensor_relu", "tensor_reduce", "reduce_max",
           "reduce_sum", "scalar_tensor_tensor", "tensor_tensor_scan", "memset",
           "memzero", "select", "copy_predicated", "reciprocal", "transpose",
           "activation", "sign", "square", "sqrt", "mul", "copy", "add",
           "bn_stats", "bn_aggr", "max", "pool", "pool_avg", "pool_max",
           "stream_shuffle", "iota"}

    def __init__(self, eng, sem):
        self.e = eng
        self.s = sem

    def __getattr__(self, k):
        f = getattr(self.e, k)
        if k not in EW.OPS:
            return f
        def g(*a, **kw):
            if self.s.n > 0:
                self.e.wait_ge(self.s.h, self.s.n)
            ins = f(*a, **kw)
            return self.s.inc(ins)
        return g


class Sem:
    def __init__(self, nc, es, name):
        self.h = es.enter_context(nc.semaphore(name))
        self.n = 0

    def inc(self, instr, k=1):
        instr.then_inc(self.h, k)
        self.n += k
        return self.n

    def dma(self, instr):
        return self.inc(instr, 16)

    def wait(self, eng, n=None):
        e = eng.e if isinstance(eng, EW) else eng
        e.wait_ge(self.h, self.n if n is None else n)


class MK:

    def __init__(self):
        self.nc = bass.Bass("TRN2", target_bir_lowering=False)
        self.es = ExitStack()

    def sb(self, name, shape, dt):
        return self.es.enter_context(self.nc.sbuf_tensor(name, shape, dt))

    def ps(self, name, shape, dt=F32):
        return self.es.enter_context(self.nc.psum_tensor(name, shape, dt))

    def inp(self, name, shape, dt=F32):
        return self.nc.dram_tensor(name, shape, dt, kind="ExternalInput").ap()

    def out(self, name, shape, dt=F32):
        return self.nc.dram_tensor(name, shape, dt, kind="ExternalOutput").ap()

    def dram(self, name, shape, dt):
        return self.nc.dram_tensor(name, shape, dt, kind="Internal").ap()

    def qdma(self, queue, sem, out, in_, **kw):
        sem.wait(queue)
        return sem.dma(queue.dma_start(out=out, in_=in_, **kw))

    def build(self):
        nc = self.nc
        self.xT = self.inp("xT", [DM, R])
        self.WqT = self.inp("WqT", [DM, HID]); self.WkT = self.inp("WkT", [DM, HID])
        self.WvT = self.inp("WvT", [DM, HID]); self.WgT = self.inp("WgT", [DM, HID])
        self.WogT = self.inp("WogT", [DM, HID]); self.WoT = self.inp("WoT", [HID, DM])
        self.WdTp = self.inp("WdTp", [128, DMT * H])
        self.bd = self.inp("bd", [1, H])
        self.normw = self.inp("normw", [128, HT])
        self.WTg = self.inp("WTg", [DM, FF]); self.WTu = self.inp("WTu", [DM, FF])
        self.WTd = self.inp("WTd", [FF, DM])
        self.c_id_bf = self.inp("c_id_bf", [128, 128], BF16)
        self.c_id_f = self.inp("c_id_f", [128, 128], F32)
        self.c_negUT = self.inp("c_negUT", [128, 128], F32)
        self.c_onescb = self.inp("c_onescb", [128, 1], BF16)
        self.c_onescf = self.inp("c_onescf", [128, 1], F32)
        self.c_onesrb = self.inp("c_onesrb", [1, 128], BF16)
        self.c_onesrf = self.inp("c_onesrf", [1, 128], F32)
        self.c_i16f = self.inp("c_i16f", [16, 16], F32)
        self.c_i16b = self.inp("c_i16b", [16, 16], BF16)
        self.c_ones16f = self.inp("c_ones16f", [16, 128], F32)
        self.c_ones16b = self.inp("c_ones16b", [16, 128], BF16)
        self.c_actc = self.inp("c_actc", [128, 5], F32)
        self.c_negdiag = self.inp("c_negdiag", [128, 128], BF16)
        self.c_slrep = self.inp("c_slrep", [128, CH], BF16)
        self.c_wind = self.inp("c_wind", [8, 8], F32)
        self.c_alog = self.inp("c_alog", [128, 8], F32)
        self.outT = self.out("outT", [DM, R], F32)
        self.qT_d = self.dram("qT_d", [HT * NCH * 128, CH], BF16)
        self.kgT_d = self.dram("kgT_d", [HT * NCH * 128, CH], BF16)
        self.vN_d = self.dram("vN_d", [NCH * 128, HID], BF16)
        self.x2_d = self.dram("x2_d", [DM, R], F32)
        self.h_d = self.dram("h_d", [FFT * 128, R], BF16)
        self.mx_d = self.dram("mx_d", [2 * NG * 4, 128], F32)
        self.cc_in = self.dram("cc_in", [129, 512], F32)
        self.cc_out = self.dram("cc_out", [8 * 129, 512], F32)
        self.std_d = self.dram("std_d", [NCH * 128, 512], BF16)

        self.XN = self.sb("XN", [128, DMT * R], BF16)
        self.ARE = self.sb("ARE", [128, 32768], BF16)
        self.WBUF = self.sb("WBUF", [128, 2 * 8192], BF16)
        self.XG = self.sb("XG", [128, DMT * TG], F32)
        self.SCR = self.sb("SCR", [128, 1024], F32)
        self.SCB = self.sb("SCB", [128, 2048], BF16)
        self.OT = self.sb("OT", [128, HT * TG], BF16)
        self.C_all = self.sb("C_all", [128, NCH * H], F32)
        self.negC_all = self.sb("negC_all", [128, NCH * H], F32)
        self.A2S = self.sb("A2S", [128, 5120], BF16)
        self.CHIA = self.sb("CHIA", [16, 2048], F32)
        self.STB2 = self.sb("STB2", [128, 1024], BF16)
        self.LST = self.sb("LST", [128, 1024], BF16)
        self.st16 = self.sb("st16", [16, 256], F32)
        self.rows = self.sb("rows", [1, 4 * TG], F32)
        self.cCcol = self.sb("cCcol", [16, NCH], F32)
        self.Ballc = self.sb("Ballc", [16, NCH + 1], F32)
        self.S_master = self.sb("S_master", [128, 512], F32)
        self.smalls = self.sb("smalls", [128, 256], F32)
        self.smallsb = self.sb("smallsb", [128, 256], BF16)
        self.wd_sb = self.sb("wd_sb", [128, DMT * H], F32)
        self.wdbar = self.sb("wdbar", [128, H], F32)
        self.bdb = self.sb("bdb", [128, H], F32)
        self.normw_sb = self.sb("normw_sb", [128, HT], F32)
        self.consts = self.sb("consts", [128, 400], F32)
        self.constsb = self.sb("constsb", [128, 264], BF16)
        self.rowsb = self.A2S[0:1, 0:1024]
        self.i16f = self.sb("i16f", [16, 16], F32)
        self.i16b = self.sb("i16b", [16, 16], BF16)
        self.ones16f = self.sb("ones16f", [16, 128], F32)
        self.ones16b = self.sb("ones16b", [16, 128], BF16)
        self.negdiag = self.sb("negdiag", [128, 128], BF16)
        self.slrep = self.sb("slrep", [128, CH], BF16)
        self.wind = self.sb("wind", [8, 8], F32)
        self.alog = self.sb("alog", [128, 8], F32)
        self.p_a = self.ps("p_a", [128, 512]); self.p_b = self.ps("p_b", [128, 512])
        self.p_c = self.ps("p_c", [128, 512]); self.p_d = self.ps("p_d", [128, 512])
        self.p_e = self.ps("p_e", [128, 512])
        self.p_o = self.ps("p_o", [128, 1024])
        self.p_t = self.ps("p_t", [128, 1024], BF16)

        es = self.es
        self.s_wl = Sem(nc, es, "s_wl")
        self.s_x = Sem(nc, es, "s_x")
        self.s_sp = Sem(nc, es, "s_sp")
        self.s_te = Sem(nc, es, "s_te")
        self.s_ve = Sem(nc, es, "s_ve")
        self.s_se = Sem(nc, es, "s_se")
        self.s_po = Sem(nc, es, "s_po")
        self.s_wq = Sem(nc, es, "s_wq")
        self.s_sc = Sem(nc, es, "s_sc")
        self.s_cc = Sem(nc, es, "s_cc")
        self.s_out = Sem(nc, es, "s_out")

        with self.es:
            with nc.Block() as block:
                @block.sync
                def _(sync):
                    self.emit()
        return nc

    def emit(self):
        nc = self.nc
        self.sy = nc.sync
        self.te = nc.tensor
        self.ve = EW(nc.vector, self.s_ve)
        self.se = EW(nc.scalar, self.s_se)
        self.po = EW(nc.gpsimd, self.s_po)
        self.phase0()
        self.phase1a()
        if PHASES >= 2:
            self.phase1b()
        if PHASES >= 3:
            self.phase2a()
        if PHASES >= 4:
            self.phase2b()
        if PHASES >= 5:
            self.phase3_f0()
            self.phase3_f1()
            self.phase3_f2()
        if DEBUG:
            self.emit_debug()
        self.finish()

    def preload_set(self, name):
        from concourse.hw_specs import get_activation_tables
        import concourse.mybir as mb
        tabs = get_activation_tables(self.nc.m.arch)
        idx = list(tabs.keys()).index(name)
        return self.se.e.add_instruction(mb.InstLoadActFuncSet(
            name=self.nc.get_next_instruction_name(),
            act_func_set_id=idx, ins=[], outs=[]))

    def tem(self, *a, **kw):
        ins = self.te.matmul(*a, **kw)
        return self.s_te.inc(ins)

    def tet(self, *a, **kw):
        ins = self.te.transpose(*a, **kw)
        return self.s_te.inc(ins)

    def phase0(self):
        sy, te, ve, se, po = self.sy, self.te, self.ve, self.se, self.po
        cs, cb = self.consts, self.constsb
        self.ID_F = cs[:, 0:128]
        self.NEGUT = cs[:, 128:256]
        self.ONES_COL_F = cs[:, 256:257]
        self.ONES_ROW_F = cs[0:1, 257:385]
        self.B_LNEPS = cs[:, 385:386]
        self.B_ONE = cs[:, 386:387]
        self.B_RMSEPS = cs[:, 387:388]
        self.B_LN127 = cs[:, 388:389]
        self.B_LNQEPS = cs[:, 389:390]
        self.ID_B = cb[:, 0:128]
        self.ONES_COL_B = cb[:, 128:129]
        self.ONES_ROW_B = cb[0:1, 129:257]
        for dst, src in [
            (self.ID_F, self.c_id_f), (self.NEGUT, self.c_negUT),
            (self.ONES_COL_F, self.c_onescf), (self.ONES_ROW_F, self.c_onesrf),
            (self.ID_B, self.c_id_bf), (self.ONES_COL_B, self.c_onescb),
            (self.ONES_ROW_B, self.c_onesrb),             (self.normw_sb[:], self.normw),
            (self.i16f[:], self.c_i16f), (self.i16b[:], self.c_i16b),
            (self.ones16f[:], self.c_ones16f), (self.ones16b[:], self.c_ones16b),
            (cs[:, 385:390], self.c_actc), (self.wd_sb[:], self.WdTp),
            (self.st16[0:1, 240:256], self.bd),
            (self.negdiag[:], self.c_negdiag), (self.slrep[:], self.c_slrep),
            (self.wind[:], self.c_wind), (self.alog[:], self.c_alog),
        ]:
            self.qdma(sy, self.s_x, dst, src)
        self.s_x.wait(te)
        self.tem(self.p_a[:, 0:H], self.ONES_ROW_F, self.st16[0:1, 240:256], start=True, stop=True)
        for a in range(DMT):
            ins = te.matmul(self.p_b[0:1, 0:H], self.ONES_COL_F, self.wd_sb[:, a * H:(a + 1) * H],
                            start=(a == 0), stop=(a == DMT - 1))
        t0 = self.s_te.inc(ins)
        self.s_te.wait(ve, t0)
        ve.tensor_copy(self.bdb[:], self.p_a[:, 0:H])
        v0 = ve.tensor_copy(self.st16[0:1, 224:240], self.p_b[0:1, 0:H])
        self.s_ve.wait(te, v0)
        t1 = self.tem(self.p_a[:, H:2 * H], self.ONES_ROW_F, self.st16[0:1, 224:240],
                      start=True, stop=True)
        self.s_te.wait(ve, t1)
        ve.tensor_copy(self.wdbar[:], self.p_a[:, H:2 * H])
        ve.memset(self.S_master[:], 0.0)
        ve.memset(self.Ballc[:], 0.0)

    def phase1a(self):
        sy, te, ve, se, po = self.sy, self.te, self.ve, self.se, self.po
        self.preload_set("natural_log_exp_and_others")
        ROW = self.rows
        AREF = self.ARE[:].bitcast(F32)
        XGB1 = {0: self.XG[:], 1: AREF[:, 12288:16384]}
        ldx = {}
        lastz = {}
        self._p1ve = {}

        def p1_load(tg):
            if tg >= 2:
                self.s_te.wait(sy, lastz[tg - 2])
                self.s_ve.wait(sy, self._p1ve[tg - 2])
            ldx[tg] = self.qdma(sy, self.s_x,
                                XGB1[tg % 2].rearrange("p (a f) -> p a f", f=TG),
                                self.xT.rearrange("(a p) t -> p a t", p=128)[:, :, tg * TG:(tg + 1) * TG])

        self.s_te.wait(sy)
        p1_load(0)
        p1_load(1)
        for tg in range(NG):
            XGg = XGB1[tg % 2]
            ld = ldx[tg]
            self.s_x.wait(ve, ld)
            self.s_x.wait(se, ld)
            if tg > 0:
                self.s_te.wait(ve)
                self.s_te.wait(se)
            xb = self.ARE[:, 16384:16384 + DMT * TG]
            sqt = self.ARE[:, 16384 + DMT * TG:16384 + 2 * DMT * TG]
            for a in range(DMT):
                vml = ve.tensor_copy(xb[:, a * TG:(a + 1) * TG], XGg[:, a * TG:(a + 1) * TG])
            for a in range(DMT):
                sml = se.activation(sqt[:, a * TG:(a + 1) * TG], XGg[:, a * TG:(a + 1) * TG], AF.Square)
            self.s_ve.wait(te, vml)
            self.s_se.wait(te, sml)
            for a in range(DMT):
                ins = te.matmul(self.p_a[0:1, 0:TG], self.ONES_COL_B, xb[:, a * TG:(a + 1) * TG],
                                start=(a == 0), stop=(a == DMT - 1))
            self.s_te.inc(ins)
            for a in range(DMT):
                ins = te.matmul(self.p_b[0:1, 0:TG], self.ONES_COL_B, sqt[:, a * TG:(a + 1) * TG],
                                start=(a == 0), stop=(a == DMT - 1))
            tml = self.s_te.inc(ins)
            self.s_te.wait(ve, tml)
            if DEBUG and tg == 3:
                ve.tensor_copy(self.SCR[0:1, 512:1024], self.p_b[0:1, 0:TG])
                ve.tensor_copy(self.SCR[0:1, 0:512], self.p_a[0:1, 0:TG])
            rm = ROW[0:1, 0:TG]; rvar = ROW[0:1, TG:2 * TG]
            rr = ROW[0:1, 2 * TG:3 * TG]; rmr = ROW[0:1, 3 * TG:4 * TG]
            ve.tensor_scalar(rm, self.p_a[0:1, 0:TG], 1.0 / DM, None, ALU.mult)
            ve.tensor_scalar(rvar, self.p_b[0:1, 0:TG], 1.0 / DM, None, ALU.mult)
            ve.tensor_mul(rr, rm, rm)
            vst = ve.tensor_sub(rvar, rvar, rr)
            if DEBUG and tg == 3:
                vst = ve.tensor_copy(self.rowsb[0:1, 0:512], rvar)
            self.s_ve.wait(se, vst)
            se.activation(rvar, rvar, AF.Ln, bias=self.B_LNEPS[0:1, :])
            sst = se.activation(rr, rvar, AF.Exp, scale=-0.5)
            if DEBUG and tg == 3:
                sst = se.activation(self.rowsb[0:1, 512:1024], rvar, AF.Copy)
            self.s_se.wait(ve, sst)
            vrb = ve.tensor_mul(rmr, rm, rr)
            self.s_ve.wait(te, vrb)
            self.tem(self.p_c[:, 0:TG], self.ONES_ROW_F, rr, start=True, stop=True)
            tbc = self.tem(self.p_d[:, 0:TG], self.ONES_ROW_F, rmr, start=True, stop=True)
            self.s_te.wait(ve, tbc)
            for a in range(DMT):
                t1 = self.SCR[:, 0:TG]
                ve.tensor_mul(t1, XGg[:, a * TG:(a + 1) * TG], self.p_c[:, 0:TG])
                ve.tensor_sub(self.XN[:, a * R + tg * TG: a * R + (tg + 1) * TG], t1, self.p_d[:, 0:TG])

            for cc in range(NG):
                c = tg * NG + cc
                t0 = cc * CH
                self.s_ve.wait(te)
                te.transpose(self.p_e[0:CH, 0:1], rr[0:1, t0:t0 + CH], self.ID_F[0:1, 0:1])
                ins = te.transpose(self.p_e[0:CH, 1:2], rmr[0:1, t0:t0 + CH], self.ID_F[0:1, 0:1])
                self.s_te.inc(ins)
                for a in range(DMT):
                    ins = te.matmul(self.p_o[:, 0:H], XGg[:, a * TG + t0:a * TG + t0 + CH],
                                    self.wd_sb[:, a * H:(a + 1) * H], start=(a == 0), stop=(a == DMT - 1))
                tz = self.s_te.inc(ins)
                self.s_te.wait(ve, tz)
                rcol = self.smalls[:, 0:1]; rmcol = self.smalls[:, 1:2]
                ve.tensor_copy(rcol, self.p_e[:, 0:1])
                ve.tensor_copy(rmcol, self.p_e[:, 1:2])
                z1 = self.smalls[:, 16:32]; z2 = self.smalls[:, 32:48]; zz = self.smalls[:, 48:64]
                ve.tensor_scalar(z1, self.p_o[:, 0:H], rcol, None, ALU.mult)
                ve.tensor_scalar(z2, self.wdbar[:], rmcol, None, ALU.mult)
                ve.tensor_sub(z1, z1, z2)
                vz = ve.tensor_add(zz, z1, self.bdb[:])
                self.s_ve.wait(se, vz)
                az = self.smalls[:, 64:80]
                se.activation(az, zz, AF.Abs)
                se.activation(az, az, AF.Exp, scale=-1.0)
                sz = se.activation(az, az, AF.Ln, bias=self.B_ONE)
                self.s_se.wait(ve, sz)
                sp = self.smalls[:, 80:96]
                ve.tensor_scalar(sp, zz, 0.0, None, ALU.max)
                vsp = ve.tensor_add(sp, sp, az)
                self.s_ve.wait(te, vsp)
                tc = self.tem(self.p_o[:, 512:512 + H], self.NEGUT, sp, start=True, stop=True)
                self.s_te.wait(ve, tc)
                Cc = self.C_all[:, c * H:(c + 1) * H]
                nCc = self.negC_all[:, c * H:(c + 1) * H]
                ve.tensor_copy(Cc, self.p_o[:, 512:512 + H])
                vch = ve.tensor_scalar(nCc, self.p_o[:, 512:512 + H], -1.0, None, ALU.mult)
                self.s_ve.wait(te, vch)
                tt = self.tet(self.p_o[0:H, 640:640 + CH], Cc, self.ID_F)
                self.s_te.wait(ve, tt)
                tot = self.Ballc[:, NCH:NCH + 1]
                ve.tensor_copy(self.Ballc[:, c:c + 1], tot)
                ve.tensor_copy(self.cCcol[:, c:c + 1], self.p_o[0:H, 640 + CH - 1:640 + CH])
                vbk = ve.tensor_add(tot, tot, self.cCcol[:, c:c + 1])
            lastz[tg] = tz
            self._p1ve[tg] = vbk
            if tg + 2 < NG:
                p1_load(tg + 2)

    def load_weight(self, w_dram, slot, cols=HID, after_te=None):
        po = self.po
        if after_te is not None:
            self.s_te.wait(po, after_te)
        self.s_wl.wait(po)
        dst = self.WBUF[:, slot * 8192: slot * 8192 + DMT * cols]
        ins = po.dma_start(
            out=dst.rearrange("p (a f) -> p a f", f=cols),
            in_=w_dram.rearrange("(a p) f -> p a f", p=128))
        return self.s_wl.dma(ins)

    def wtile(self, slot, a, m, w=128, cols=HID):
        base = slot * 8192 + a * cols + m * w
        return self.WBUF[:, base: base + w]

    def spill4(self, dram, m, tg, src):
        dst = dram.rearrange("(a c p) t -> p a c t", p=128, c=NCH)[
            :, m:m + 1, tg * NG:(tg + 1) * NG, :]
        s4 = src.rearrange("p (x c t) -> p x c t", x=1, t=CH)
        self.s_sp.wait(self.sy)
        return self.s_sp.dma(self.sy.dma_start(out=dst, in_=s4))

    def phase1b(self):
        sy, te, ve, se, po = self.sy, self.te, self.ve, self.se, self.po
        se.e.preload_act_table(AF.Sigmoid)
        gS = self.ARE[:, 16384:32768]
        ogS = self.ARE[:, 0:16384]
        ld_g = self.load_weight(self.WgT, 0)
        ld_k = self.load_weight(self.WkT, 1)
        psums = [self.p_a, self.p_b, self.p_c, self.p_d]
        self.s_ve.wait(te)
        self.s_se.wait(te)

        marks = {}
        for i in range(HT * NG):
            m, tg = divmod(i, NG)
            p = psums[i % 4]
            if i >= 4:
                self.s_se.wait(te, marks[i - 4])
            if i == 0:
                self.s_wl.wait(te, ld_g)
            for a in range(DMT):
                ins = te.matmul(p[:, 0:TG], self.wtile(0, a, m),
                                self.XN[:, a * R + tg * TG:a * R + (tg + 1) * TG],
                                start=(a == 0), stop=(a == DMT - 1))
            tm = self.s_te.inc(ins)
            self.s_te.wait(se, tm)
            marks[i] = se.activation(gS[:, m * R + tg * TG: m * R + (tg + 1) * TG], p[:, 0:TG], AF.Sigmoid)
        te_g_done = self.s_te.n

        ld_q = self.load_weight(self.WqT, 0, after_te=te_g_done)
        kge = {}
        spills = {}
        for i in range(HT * NG):
            m, tg = divmod(i, NG)
            p = psums[i % 4]
            if i >= 4:
                self.s_ve.wait(te, kge[i - 4])
            elif i == 0:
                self.s_se.wait(te, marks[HT * NG - 1])
            if i == 0:
                self.s_wl.wait(te, ld_k)
            for a in range(DMT):
                ins = te.matmul(p[:, 0:TG], self.wtile(1, a, m),
                                self.XN[:, a * R + tg * TG:a * R + (tg + 1) * TG],
                                start=(a == 0), stop=(a == DMT - 1))
            tm = self.s_te.inc(ins)
            self.s_te.wait(ve, tm)
            if i >= 4:
                self.s_sp.wait(ve, spills[i - 4])
            dst = self.SCB[:, (i % 4) * TG:(i % 4) * TG + TG]
            kge[i] = ve.tensor_mul(dst, p[:, 0:TG], gS[:, m * R + tg * TG: m * R + (tg + 1) * TG])
            self.s_ve.wait(sy, kge[i])
            spills[i] = self.spill4(self.kgT_d, m, tg, dst)
        te_k_done = self.s_te.n

        ld_v = self.load_weight(self.WvT, 1, after_te=te_k_done)
        qe = {}
        for i in range(HT * NG):
            m, tg = divmod(i, NG)
            p = psums[i % 4]
            if i >= 4:
                self.s_ve.wait(te, qe[i - 4])
            elif i == 0:
                self.s_ve.wait(te, kge[HT * NG - 1])
            if i == 0:
                self.s_wl.wait(te, ld_q)
            for a in range(DMT):
                ins = te.matmul(p[:, 0:TG], self.wtile(0, a, m),
                                self.XN[:, a * R + tg * TG:a * R + (tg + 1) * TG],
                                start=(a == 0), stop=(a == DMT - 1))
            tm = self.s_te.inc(ins)
            self.s_te.wait(ve, tm)
            self.s_sp.wait(ve, spills[i - 4] if i >= 4 else spills[HT * NG - 4 + i])
            dst = self.SCB[:, (i % 4) * TG:(i % 4) * TG + TG]
            qe[i] = ve.tensor_copy(dst, p[:, 0:TG])
            self.s_ve.wait(sy, qe[i])
            spills[i] = self.spill4(self.qT_d, m, tg, dst)
        te_q_done = self.s_te.n

        ld_og = self.load_weight(self.WogT, 0, after_te=te_q_done)
        vev = {}
        for i in range(NCH * 2):
            c, vb = divmod(i, 2)
            p = psums[i % 4]
            if i >= 4:
                self.s_ve.wait(te, vev[i - 4])
            elif i == 0:
                self.s_ve.wait(te, qe[HT * NG - 1])
            if i == 0:
                self.s_wl.wait(te, ld_v)
            for a in range(DMT):
                ins = te.matmul(p[:, 0:TG],
                                self.XN[:, a * R + c * CH: a * R + (c + 1) * CH],
                                self.WBUF[:, 8192 + a * HID + vb * TG: 8192 + a * HID + (vb + 1) * TG],
                                start=(a == 0), stop=(a == DMT - 1))
            tm = self.s_te.inc(ins)
            self.s_te.wait(ve, tm)
            self.s_sp.wait(ve, spills[i - 4] if i >= 4 else spills[HT * NG - 4 + i])
            dst = self.SCB[:, (i % 4) * TG:(i % 4) * TG + TG]
            vev[i] = ve.tensor_copy(dst, p[:, 0:TG])
            self.s_ve.wait(sy, vev[i])
            self.s_sp.wait(sy)
            spills[i] = self.s_sp.dma(sy.dma_start(
                out=self.vN_d[c * 128:(c + 1) * 128, vb * TG:(vb + 1) * TG], in_=dst))
        self.te_v_done = self.s_te.n

        oge = {}
        for i in range(HT * NG):
            m, tg = divmod(i, NG)
            p = psums[i % 4]
            if i >= 4:
                self.s_se.wait(te, oge[i - 4])
            elif i == 0:
                self.s_ve.wait(te, vev[NCH * 2 - 1])
            if i == 0:
                self.s_wl.wait(te, ld_og)
            for a in range(DMT):
                ins = te.matmul(p[:, 0:TG], self.wtile(0, a, m),
                                self.XN[:, a * R + tg * TG:a * R + (tg + 1) * TG],
                                start=(a == 0), stop=(a == DMT - 1))
            tm = self.s_te.inc(ins)
            self.s_te.wait(se, tm)
            oge[i] = se.activation(ogS[:, m * R + tg * TG: m * R + (tg + 1) * TG], p[:, 0:TG], AF.Sigmoid)


    def phase2a(self):
        sy, te, ve, se, po = self.sy, self.te, self.ve, self.se, self.po
        self.preload_set("natural_log_exp_and_others")
        self._aspl = {}
        KGC = {0: self.SCB[:, 0:1024], 1: self.OT[:, 1024:2048]}
        VNC = {0: self.SCB[:, 1024:2048], 1: self.OT[:, 2048:3072]}
        KHN = self.OT[:, 0:1024]

        SC16A = self.smalls[0:16, 0:256]
        ve.tensor_tensor(SC16A.rearrange("k (c h) -> k c h", h=H),
                         self.i16f[:].rearrange("k (x h) -> k x h", x=1).broadcast_to([16, NCH, H]),
                         self.cCcol[:].rearrange("k (c x) -> k c x", x=1).broadcast_to([16, NCH, H]),
                         ALU.mult)
        vsc = self.s_ve.n
        self.s_ve.wait(te, vsc)
        tca = self.tem(self.p_e[:, 0:256], self.ones16f[:], SC16A, start=True, stop=True)
        self.s_te.wait(ve, tca)
        DIFA = self.SCR[:, 512:768]
        EXPD = self.SCR[:, 768:1024]
        vda = ve.tensor_tensor(DIFA, self.p_e[:, 0:256], self.negC_all[:], ALU.add)
        self.s_ve.wait(se, vda)
        sea = se.activation(EXPD, DIFA, AF.Exp)
        e16a = self.smalls[0:16, 488:504] if False else self.smalls[0:16, 100:116]
        sca = se.activation(e16a, self.cCcol[:], AF.Exp)
        self.s_se.wait(ve, sca)
        E16BA = self.smallsb[0:16, 0:256]
        veb = ve.tensor_tensor(
            E16BA.rearrange("k (c h) -> k c h", h=H),
            self.i16b[:].rearrange("k (x h) -> k x h", x=1).broadcast_to([16, NCH, H]),
            e16a.rearrange("k (c x) -> k c x", x=1).broadcast_to([16, NCH, H]), ALU.mult)

        lda = {}
        tcoh = {}

        def a1_loads(c):
            p = c % 2
            if c >= 2:
                self.s_te.wait(sy, tcoh[c - 2])
            l1 = self.qdma(sy, self.s_x,
                           KGC[p].rearrange("p (a x t) -> p a x t", x=1, t=CH),
                           self.kgT_d.rearrange("(a c p) t -> p a c t", p=128, c=NCH)[:, :, c:c + 1, :])
            l2 = self.qdma(sy, self.s_x, VNC[p], self.vN_d[c * 128:(c + 1) * 128, :])
            lda[c] = (l1, l2)

        self.s_te.wait(sy)
        self.s_sp.wait(sy)
        a1_loads(0)
        a1_loads(1)

        scm = 0
        for c in range(NCH):
            p = c % 2
            sl = E16BA[0:16, c * H:(c + 1) * H]
            rhs_e = sl.rearrange("k (a b) -> k a b", b=2)[:, :, 0:1].broadcast_to([16, 8, 64])
            rhs_o = sl.rearrange("k (a b) -> k a b", b=2)[:, :, 1:2].broadcast_to([16, 8, 64])
            self.s_ve.wait(te, max(veb, scm))
            self.tem(self.p_d[0:64, 0:512], self.ones16b[:, 0:64], rhs_e, start=True, stop=True)
            teb = self.tem(self.p_d[64:128, 0:512], self.ones16b[:, 64:128], rhs_o, start=True, stop=True,
                           tile_position=(0, 64))
            if c >= 2:
                self.s_sp.wait(ve, self._aspl[c - 2])
            vsv = ve.tensor_copy(self.STB2[:, (c % 2) * 512:(c % 2) * 512 + 512],
                                 self.S_master[:])
            self.s_ve.wait(sy, vsv)
            self._aspl[c] = self.qdma(sy, self.s_sp, self.std_d[c * 128:(c + 1) * 128, :],
                                      self.STB2[:, (c % 2) * 512:(c % 2) * 512 + 512])
            if c + 1 < NCH:
                a1_loads(c + 1)
            self.s_x.wait(te, lda[c][0])
            self.s_ve.wait(te, vsv)
            for i in range(8):
                ins = te.transpose(self.p_t[:, i * 128:(i + 1) * 128],
                                   KGC[p][:, i * 128:(i + 1) * 128], self.ID_B)
            ttr = self.s_te.inc(ins)
            self.s_te.wait(ve, ttr)
            self.s_se.wait(ve, sea)
            vkh = ve.tensor_tensor(
                KHN.rearrange("p (h t) -> p h t", t=64),
                self.p_t[:, 0:1024].rearrange("p (h t) -> p h t", t=64),
                EXPD[:, c * H:(c + 1) * H].rearrange("p (h x) -> p h x", x=1).broadcast_to([128, H, 64]),
                ALU.mult)
            self.s_ve.wait(te, vkh)
            self.s_x.wait(te, lda[c][1])
            for h in range(H):
                pb = self.p_b if h % 2 == 0 else self.p_c
                ins = te.matmul(
                    pb[(h % 2) * 64:(h % 2) * 64 + 64,
                       (h // 2) * 64:(h // 2) * 64 + 64],
                    KHN[:, h * 64:(h + 1) * 64], VNC[p][:, h * 64:(h + 1) * 64],
                    start=(h < 2), stop=(h >= H - 2),
                    tile_position=(0, (h % 2) * 64))
            tcoh[c] = tco = self.s_te.inc(ins)
            self.s_te.wait(ve, max(tco, teb))
            scm = ve.tensor_tensor(self.SCR[:, 0:512], self.S_master[:], self.p_d[:, 0:512], ALU.mult)
            ve.tensor_tensor(self.S_master[0:64, :], self.SCR[0:64, 0:512], self.p_b[0:64, 0:512], ALU.add)
            ve.tensor_tensor(self.S_master[64:128, :], self.SCR[64:128, 0:512], self.p_c[64:128, 0:512], ALU.add)
        vbt = ve.tensor_copy(self.smalls[0:16, 224:225], self.Ballc[:, NCH:NCH + 1])
        self.s_ve.wait(te, vbt)
        tbt = self.tet(self.p_e[0:1, 0:16], self.smalls[0:16, 224:225], self.ID_F[0:16, 0:16])
        self.s_te.wait(ve, tbt)
        ve.memset(self.rows[0:1, 0:512], 0.0)
        vrow = ve.tensor_copy(self.rows[0:1, 0:16], self.p_e[0:1, 0:16])
        self.s_ve.wait(sy, vrow)
        self.qdma(sy, self.s_sp, self.cc_in[0:128, :], self.S_master[:])
        self.qdma(sy, self.s_sp, self.cc_in[128:129, :], self.rows[0:1, 0:512])
        self.s_sp.wait(po)
        if SIM1:
            for b in range(8):
                self.s_cc.inc(po.dma_start(out=self.cc_out[b * 129:(b + 1) * 129, :],
                                           in_=self.cc_in[:, :]), 16)
        else:
            ins = po.collective_compute(
                "AllGather", ALU.bypass,
                ins=[self.cc_in.opt()], outs=[self.cc_out.opt()],
                replica_groups=[list(range(N_CORES))])
            self.s_cc.inc(ins, 1)


    def phase2b(self):
        sy, te, ve, se, po = self.sy, self.te, self.ve, self.se, self.po
        A2S = self.A2S
        KG = {0: A2S[:, 0:1024], 1: A2S[:, 2048:3072]}
        QC = {0: A2S[:, 1024:2048], 1: A2S[:, 3072:4096]}
        SEF = {0: A2S[:, 4096:4608], 1: A2S[:, 4608:5120]}
        VN = {0: self.SCB[:, 0:1024], 1: self.SCB[:, 1024:2048]}
        QT = {0: self.ARE[:, 16384:16896], 1: self.ARE[:, 16896:17408]}
        PB = {0: self.ARE[:, 17408:17920], 1: self.ARE[:, 17920:18432]}
        ABUF = self.ARE[:, 18432:18944]
        EXPC = self.ARE[:, 18944:19456]
        ogS = self.ARE[:, 0:16384]
        SQA = self.ARE[:, 16384:16384 + 4096]
        YB = self.ARE[:, 16384 + 4096:16384 + 8192]
        XB2 = self.ARE[:, 16384 + 8192:16384 + 12288]
        SQ2 = self.ARE[:, 16384 + 12288:16384 + 16384]
        Dbuf = self.SCR[:, 512:1024]
        ld_o = self.load_weight(self.WoT, 0, after_te=self.s_te.n)
        self.f0_init()

        ld = {}
        self._tgo = {}; self._vql = {}; self._ogo = {}; self._evac = {}
        self._post_c_ve = 0
        self._post_c_te = 0
        self._spx = 0

        def emit_loads(c):
            p = c % 2
            if c >= 2:
                self.s_te.wait(sy, self._tgo[c - 2])
                self.s_ve.wait(sy, self._vql[c - 2])
            l1 = self.qdma(sy, self.s_x, KG[p].rearrange("p (a x t) -> p a x t", x=1, t=CH),
                           self.kgT_d.rearrange("(a c p) t -> p a c t", p=128, c=NCH)[:, :, c:c + 1, :])
            l2 = self.qdma(sy, self.s_x, QC[p].rearrange("p (a x t) -> p a x t", x=1, t=CH),
                           self.qT_d.rearrange("(a c p) t -> p a c t", p=128, c=NCH)[:, :, c:c + 1, :])
            l3 = self.qdma(sy, self.s_x, VN[p], self.vN_d[c * 128:(c + 1) * 128, :])
            l4 = self.qdma(sy, self.s_x, self.LST[:, p * 512:p * 512 + 512],
                           self.std_d[c * 128:(c + 1) * 128, :])
            ld[c] = (l1, l2, l3, l4)

        self.s_sp.wait(sy)
        self.s_te.wait(sy)
        self.s_ve.wait(sy)
        emit_loads(0)
        emit_loads(1)
        next_ld = 2

        self.s_ve.wait(te)
        te.transpose(self.p_d[:, 0:128], self.C_all[:, 0:128], self.ID_F)
        tt2 = self.tet(self.p_d[:, 128:256], self.C_all[:, 128:256], self.ID_F)
        self.s_te.wait(ve, tt2)
        CT = self.SCR[:, 0:256]
        vct = ve.tensor_copy(CT, self.p_d[:, 0:256])
        self.s_ve.wait(sy, vct)
        self.s_sp.wait(sy)
        self.qdma(sy, self.s_sp, self.CHIA.rearrange("c (h t) -> c h t", t=128)[0:8],
                  CT[:, 0:128])
        rch = self.qdma(sy, self.s_sp, self.CHIA.rearrange("c (h t) -> c h t", t=128)[8:16],
                        CT[:, 128:256])
        self.s_sp.wait(se, rch)
        eBa = self.smalls[0:16, 144:160]
        sba = se.activation(eBa, self.Ballc[0:16, 0:16], AF.Exp)
        self.s_se.wait(ve, sba)
        E16A = self.smallsb[0:16, 0:256]
        vE16 = ve.tensor_tensor(
            E16A.rearrange("k (c h) -> k c h", h=16),
            self.i16b[:].rearrange("k (x h) -> k x h", x=1).broadcast_to([16, 16, 16]),
            eBa.rearrange("k (c x) -> k c x", x=1).broadcast_to([16, 16, 16]), ALU.mult)

        self.s_cc.wait(sy)
        B8 = self.smalls[0:8, 208:224]
        self.qdma(sy, self.s_x, B8,
                  self.cc_out.rearrange("(b r) f -> b r f", r=129)[:, 128:129, 0:16])
        EBF = self.SCR[:, 0:512]
        for b in range(8):
            slot = self.XG[:, (1 + (b % 2)) * 512:(2 + (b % 2)) * 512]
            lb = self.qdma(sy, self.s_x, slot,
                           self.cc_out[b * 129:(b * 129) + 128, :])
            wb = self.wind[:, b:b + 1].broadcast_to([8, 64])
            be = B8.rearrange("k (a b2) -> k a b2", b2=2)[:, :, 0:1].broadcast_to([8, 8, 64])
            bo = B8.rearrange("k (a b2) -> k a b2", b2=2)[:, :, 1:2].broadcast_to([8, 8, 64])
            if b == 0:
                self.s_x.wait(te)
                self.s_ve.wait(te)
            self.tem(self.p_a[0:64, 0:512], wb, be, start=True, stop=True)
            tsb = self.tem(self.p_a[64:128, 0:512], wb, bo, start=True, stop=True,
                           tile_position=(0, 64))
            self.s_te.wait(se, tsb)
            seb = se.activation(EBF, self.p_a[:, 0:512], AF.Exp, bias=self.alog[:, b:b + 1])
            self.s_x.wait(ve, lb)
            self.s_se.wait(ve, seb)
            if b == 0:
                vup = ve.tensor_tensor(self.S_master[:], EBF, slot, ALU.mult)
            else:
                ve.tensor_tensor(self.SCR[:, 512:1024], EBF, slot, ALU.mult)
                vup = ve.tensor_tensor(self.S_master[:], self.S_master[:],
                                       self.SCR[:, 512:1024], ALU.add)
            self.s_ve.wait(te, vup)
            self.s_ve.wait(sy, vup)
        comb_seb = seb
        self._post_c_ve = self.s_ve.n

        vse_c = {}; scrm = {}

        def emit_sef(c):
            p = c % 2
            sl = E16A[0:16, c * 16:(c + 1) * 16]
            rhs_e = sl.rearrange("k (a b) -> k a b", b=2)[:, :, 0:1].broadcast_to([16, 8, 64])
            rhs_o = sl.rearrange("k (a b) -> k a b", b=2)[:, :, 1:2].broadcast_to([16, 8, 64])
            if c == 0:
                self.s_ve.wait(te, vE16)
            else:
                self.s_ve.wait(te, max(scrm[c - 1], self._post_c_ve))
            self.tem(self.p_d[0:64, 0:512], self.ones16b[:, 0:64], rhs_e, start=True, stop=True)
            tse = self.tem(self.p_d[64:128, 0:512], self.ones16b[:, 64:128], rhs_o,
                           start=True, stop=True, tile_position=(0, 64))
            self.s_te.wait(ve, tse)
            if c >= 2:
                self.s_te.wait(ve, self._tgo[c - 2])
            if c == 0:
                self.s_se.wait(ve, comb_seb)
            scrm[c] = ve.tensor_tensor(self.SCR[:, 0:512], self.S_master[:],
                                       self.p_d[:, 0:512], ALU.mult)
            self.s_x.wait(ve, ld[c][3])
            vse_c[c] = ve.tensor_tensor(SEF[p], self.SCR[:, 0:512],
                                        self.LST[:, p * 512:p * 512 + 512], ALU.add)

        emit_sef(0)

        gh = {}

        def emit_front(c, g):
            ga = c * 4 + g; gp = ga % 2; p = c % 2
            if ga == 0:
                self.s_se.wait(te, comb_seb)
                self.s_sp.wait(te, rch)
                self.s_ve.wait(te, vse_c[0])
            else:
                self.s_ve.wait(te, max(gh[ga - 1]["vd"], self._post_c_ve))
            selc = self.i16f[:, c:c + 1].broadcast_to([16, 128])
            te.matmul(self.p_a[:, 0:512], selc,
                      self.CHIA[0:16, g * 512:(g + 1) * 512], start=True, stop=False)
            tma = self.tem(self.p_a[:, 0:512], self.negdiag[:],
                           self.slrep[:].rearrange("p (x t) -> p x t", x=1).broadcast_to([128, 4, CH]),
                           start=False, stop=True)
            if ga > 0:
                self.s_se.wait(te, gh[ga - 1]["sx"])
            tme = self.tem(self.p_e[:, 0:512], selc,
                           self.CHIA[0:16, g * 512:(g + 1) * 512], start=True, stop=True)
            if g == 0:
                self.s_x.wait(te, ld[c][1])
            for hh in range(4):
                h = 4 * g + hh
                pb = self.p_b if hh % 2 == 0 else self.p_c
                ins = te.matmul(pb[:, gp * 256 + (hh // 2) * 128:gp * 256 + (hh // 2) * 128 + 128],
                                KG[p][(h % 2) * 64:(h % 2) * 64 + 64,
                                      (h // 2) * 128:(h // 2) * 128 + 128],
                                QC[p][(h % 2) * 64:(h % 2) * 64 + 64,
                                      (h // 2) * 128:(h // 2) * 128 + 128],
                                start=True, stop=True,
                                tile_position=((h % 2) * 64, 0))
            gh[ga] = {"tma": tma, "tme": tme, "tsp": self.s_te.inc(ins)}

        def emit_chain(c, g):
            ga = c * 4 + g; gp = ga % 2; p = c % 2
            G = gh[ga]
            negb = self.negC_all[:, c * H + 4 * g:c * H + 4 * g + 4].rearrange(
                "p (h x) -> p h x", x=1).broadcast_to([128, 4, 128])
            self.s_te.wait(ve, G["tma"])
            vd = ve.tensor_tensor(Dbuf.rearrange("p (h t) -> p h t", t=128),
                                  self.p_a[:, 0:512].rearrange("p (h t) -> p h t", t=128),
                                  negb, ALU.add)
            self.s_ve.wait(se, vd)
            sa = se.activation(ABUF, Dbuf, AF.Exp)
            self.s_te.wait(se, G["tme"])
            sx = se.activation(EXPC, self.p_e[:, 0:512], AF.Exp)
            self.s_se.wait(ve, sa)
            self.s_te.wait(ve, max(G["tsp"], self._post_c_te))
            if ga >= 2:
                self.s_te.wait(ve, self._ogo[ga - 2])
            a4 = ABUF.rearrange("p (a b t) -> p a b t", a=2, b=2, t=128)
            o4 = PB[gp].rearrange("p (a b t) -> p a b t", a=2, b=2, t=128)
            ve.tensor_tensor(o4[:, :, 0:1, :], a4[:, :, 0:1, :],
                             self.p_b[:, gp * 256:gp * 256 + 256].rearrange(
                                 "p (a x t) -> p a x t", x=1, t=128), ALU.mult)
            vp = ve.tensor_tensor(o4[:, :, 1:2, :], a4[:, :, 1:2, :],
                                  self.p_c[:, gp * 256:gp * 256 + 256].rearrange(
                                      "p (a x t) -> p a x t", x=1, t=128), ALU.mult)
            self.s_se.wait(ve, sx)
            if g == 0:
                self.s_x.wait(ve, ld[c][1])
            qt4 = QT[gp].rearrange("p (a b t) -> p a b t", a=2, b=2, t=128)
            ex4 = EXPC.rearrange("p (a b t) -> p a b t", a=2, b=2, t=128)
            qsrc = QC[p][:, 2 * g * 128:2 * g * 128 + 256].rearrange(
                "p (a x t) -> p a x t", x=1, t=128)
            ve.tensor_tensor(qt4[0:64, :, 0:1, :], qsrc[0:64], ex4[0:64, :, 0:1, :], ALU.mult)
            vq = ve.tensor_tensor(qt4[64:128, :, 1:2, :], qsrc[64:128], ex4[64:128, :, 1:2, :],
                                  ALU.mult)
            G["vd"] = vd; G["sx"] = sx; G["vp"] = vp; G["vq"] = vq

        def emit_O(c, g):
            ga = c * 4 + g; gp = ga % 2; p = c % 2
            G = gh[ga]
            self.s_ve.wait(te, max(G["vp"], G["vq"], vse_c[c]))
            if g == 0:
                self.s_x.wait(te, ld[c][2])
                if c >= 1:
                    self.s_se.wait(te, self._evac[c - 1])
            for hh in range(4):
                h = 4 * g + hh
                te.matmul(self.p_o[(h % 2) * 64:(h % 2) * 64 + 64,
                                   (h // 2) * 128:(h // 2) * 128 + 128],
                          VN[p][:, h * 64:(h + 1) * 64],
                          PB[gp][:, hh * 128:(hh + 1) * 128],
                          start=True, stop=False,
                          tile_position=(0, (h % 2) * 64))
                ins = te.matmul(self.p_o[(h % 2) * 64:(h % 2) * 64 + 64,
                                         (h // 2) * 128:(h // 2) * 128 + 128],
                                SEF[p][(h % 2) * 64:(h % 2) * 64 + 64,
                                       (h // 2) * 64:(h // 2) * 64 + 64],
                                QT[gp][(h % 2) * 64:(h % 2) * 64 + 64,
                                       hh * 128:(hh + 1) * 128],
                                start=False, stop=True,
                                tile_position=((h % 2) * 64, (h % 2) * 64))
            self._ogo[ga] = self.s_te.inc(ins)
            if g == 3:
                self._tgo[c] = self._ogo[ga]
                self._vql[c] = G["vq"]

        def emit_evac(c):
            cc = c % 4
            self.s_te.wait(se, self._tgo[c])
            if c % 4 == 0 and c > 0:
                self.s_ve.wait(se, self._post_c_ve)
            self._evac[c] = se.activation(
                self.OT[:].rearrange("p (m t) -> p m t", t=512)[:, :, cc * 128:(cc + 1) * 128],
                self.p_o[:, 0:1024].rearrange("p (m t) -> p m t", t=128), AF.Copy)

        pend = None
        for gi in range(NG):
            for cc in range(NG):
                c = gi * NG + cc
                if cc == 0:
                    self.s_ve.wait(sy, self._post_c_ve)
                    if self._spx:
                        self.s_sp.wait(sy, self._spx)
                    self._lx = self.qdma(sy, self.s_x,
                                         self.XG[:].rearrange("p (a f) -> p a f", f=TG),
                                         self.xT.rearrange("(a p) t -> p a t", p=128)[:, :, gi * TG:(gi + 1) * TG])
                for g in range(4):
                    emit_front(c, g)
                    emit_chain(c, g)
                    if pend is not None:
                        pc, pg = pend
                        emit_O(pc, pg)
                        if pg == 3:
                            emit_evac(pc)
                            while next_ld < NCH and next_ld <= pc + 2:
                                emit_loads(next_ld)
                                next_ld += 1
                    pend = (c, g)
                if c + 1 < NCH:
                    emit_sef(c + 1)
                self.f0_step(2)
            pc, pg = pend
            emit_O(pc, pg)
            emit_evac(pc)
            pend = None
            while next_ld < NCH and next_ld <= pc + 2:
                emit_loads(next_ld)
                next_ld += 1
            self._c_group(gi, ld_o, ogS, SQA, YB, XB2, SQ2)
            self._post_c_ve = self.s_ve.n
            self._post_c_te = self.s_te.n

    def _c_group(self, gi, ld_o, ogS, SQA, YB, XB2, SQ2):
        sy, te, ve, se, po = self.sy, self.te, self.ve, self.se, self.po
        self.f0_step(4)
        self.s_ve.wait(se)
        for a in range(DMT):
            sq = se.activation(SQA[:, a * 512:(a + 1) * 512],
                               self.OT[:, a * 512:(a + 1) * 512], AF.Square)
        self.s_se.wait(te, sq)
        self.s_ve.wait(te)
        for a in range(DMT):
            ins = te.matmul(self.p_a[0:1, 0:512], self.ONES_COL_B,
                            SQA[:, a * 512:(a + 1) * 512],
                            start=(a == 0), stop=(a == DMT - 1))
        tst = self.s_te.inc(ins)
        self.s_te.wait(se, tst)
        rrow = self.rows[0:1, 0:512]
        se.activation(rrow, self.p_a[0:1, 0:512], AF.Ln,
                      scale=1.0 / HID, bias=self.B_RMSEPS[0:1, :])
        srs = se.activation(rrow, rrow, AF.Exp, scale=-0.5)
        self.s_se.wait(te, srs)
        trb = self.tem(self.p_c[:, 0:512], self.ONES_ROW_F, rrow, start=True, stop=True)
        self.s_te.wait(ve, trb)
        for a in range(DMT):
            ve.scalar_tensor_tensor(YB[:, a * 512:(a + 1) * 512],
                                    self.OT[:, a * 512:(a + 1) * 512],
                                    self.normw_sb[:, a:a + 1],
                                    self.p_c[:, 0:512], ALU.mult, ALU.mult)
            vy = ve.tensor_tensor(YB[:, a * 512:(a + 1) * 512],
                                  YB[:, a * 512:(a + 1) * 512],
                                  ogS[:, a * R + gi * 512:a * R + gi * 512 + 512], ALU.mult)
        self.s_ve.wait(te, vy)
        self.s_wl.wait(te, ld_o)
        self.s_x.wait(ve, self._lx)
        prev_m = {}
        for m in range(DMT):
            pp = self.p_o[:, 0:512] if m % 2 == 0 else self.p_o[:, 512:1024]
            if m >= 2:
                self.s_ve.wait(te, prev_m[m - 2])
            for k in range(DMT):
                ins = te.matmul(pp, self.wtile(0, k, m), YB[:, k * 512:(k + 1) * 512],
                                start=(k == 0), stop=(k == DMT - 1))
            tm = self.s_te.inc(ins)
            self.s_te.wait(ve, tm)
            prev_m[m] = ve.tensor_tensor(self.XG[:, m * 512:(m + 1) * 512],
                                         self.XG[:, m * 512:(m + 1) * 512], pp, ALU.add)
        self.s_ve.wait(sy, prev_m[DMT - 1])
        spx = self.qdma(sy, self.s_sp,
                        self.x2_d.rearrange("(a p) t -> p a t", p=128)[:, :, gi * TG:(gi + 1) * TG],
                        self.XG[:].rearrange("p (a f) -> p a f", f=TG))
        self._spx = spx
        self.s_x.wait(se, self._lx)
        self.s_ve.wait(se, prev_m[DMT - 1])
        for a in range(DMT):
            se.activation(XB2[:, a * 512:(a + 1) * 512], self.XG[:, a * 512:(a + 1) * 512],
                          AF.Copy)
            sml = se.activation(SQ2[:, a * 512:(a + 1) * 512], self.XG[:, a * 512:(a + 1) * 512],
                                AF.Square)
        self.s_se.wait(te, sml)
        for a in range(DMT):
            ins = te.matmul(self.p_a[0:1, 0:512], self.ONES_COL_B,
                            XB2[:, a * 512:(a + 1) * 512], start=(a == 0), stop=(a == DMT - 1))
        self.s_te.inc(ins)
        for a in range(DMT):
            ins = te.matmul(self.p_b[0:1, 0:512], self.ONES_COL_B,
                            SQ2[:, a * 512:(a + 1) * 512], start=(a == 0), stop=(a == DMT - 1))
        tml = self.s_te.inc(ins)
        self.s_te.wait(ve, tml)
        rm = self.rows[0:1, 0:512]; rvar = self.rows[0:1, 512:1024]
        rr = self.rows[0:1, 1024:1536]; rmr = self.rows[0:1, 1536:2048]
        ve.tensor_scalar(rm, self.p_a[0:1, 0:512], 1.0 / DM, None, ALU.mult)
        ve.tensor_scalar(rvar, self.p_b[0:1, 0:512], 1.0 / DM, None, ALU.mult)
        ve.tensor_mul(rr, rm, rm)
        vst2 = ve.tensor_sub(rvar, rvar, rr)
        self.s_ve.wait(se, vst2)
        se.activation(rvar, rvar, AF.Ln, bias=self.B_LNEPS[0:1, :])
        sst2 = se.activation(rr, rvar, AF.Exp, scale=-0.5)
        self.s_se.wait(ve, sst2)
        vrb2 = ve.tensor_mul(rmr, rm, rr)
        self.s_ve.wait(te, vrb2)
        self.tem(self.p_c[:, 0:512], self.ONES_ROW_F, rr, start=True, stop=True)
        tbc2 = self.tem(self.p_d[:, 0:512], self.ONES_ROW_F, rmr, start=True, stop=True)
        self.s_te.wait(ve, tbc2)
        self.s_sp.wait(ve, spx)
        for a in range(DMT):
            ve.tensor_mul(self.SCR[:, 0:512], self.XG[:, a * 512:(a + 1) * 512], self.p_c[:, 0:512])
            vln2 = ve.tensor_sub(self.XG[:, a * 512:(a + 1) * 512], self.SCR[:, 0:512],
                                 self.p_d[:, 0:512])
        rmx = self.SCR[:, 0:512]; rmn = self.SCR[:, 512:1024]
        ve.tensor_copy(rmx, self.XG[:, 0:512])
        ve.tensor_copy(rmn, self.XG[:, 0:512])
        for a in range(1, DMT):
            ve.tensor_tensor(rmx, rmx, self.XG[:, a * 512:(a + 1) * 512], ALU.max)
            ve.tensor_tensor(rmn, rmn, self.XG[:, a * 512:(a + 1) * 512], ALU.min)
        ve.tensor_scalar(rmn, rmn, -1.0, None, ALU.mult)
        vtr = ve.tensor_tensor(rmx, rmx, rmn, ALU.max)
        self.s_ve.wait(te, vtr)
        for ct in range(4):
            ins = te.transpose(self.p_e[:, ct * 128:(ct + 1) * 128],
                               rmx[:, ct * 128:(ct + 1) * 128], self.ID_F)
        ttd = self.s_te.inc(ins)
        self.s_te.wait(ve, ttd)
        red = self.smalls[:, 0:4]
        ve.tensor_reduce(red, self.p_e[:, 0:512].rearrange("p (a b) -> p a b", b=128),
                         AX.X, ALU.max)
        vrd = ve.tensor_scalar(red, red, Q_EPS, None, ALU.max)
        self.s_ve.wait(te, vrd)
        tred = self.tet(self.p_e[0:4, 256:384], red, self.ID_F)
        self.s_te.wait(ve, tred)
        vev = ve.tensor_copy(self.st16[0:4, 0:128], self.p_e[0:4, 256:384])
        self.s_ve.wait(sy, vev)
        self.qdma(sy, self.s_sp, self.mx_d[gi * 4:(gi + 1) * 4, :], self.st16[0:4, 0:128])
        mq = self.qdma(sy, self.s_sp, self.SCR[0:1, 512:1024], self.st16[0:4, 0:128])
        self.s_sp.wait(se, mq)
        srow = self.rows[0:1, 0:512]
        se.activation(srow, self.SCR[0:1, 512:1024], AF.Ln)
        sq2 = se.activation(srow, srow, AF.Exp, scale=-1.0, bias=self.B_LN127[0:1, :])
        self.s_se.wait(te, sq2)
        tqb = self.tem(self.p_c[:, 0:512], self.ONES_ROW_F, srow, start=True, stop=True)
        self.s_te.wait(ve, tqb)
        i32v = self.SCR[:, 512:1024].bitcast(I32)
        for a in range(DMT):
            ve.tensor_tensor(i32v, self.XG[:, a * 512:(a + 1) * 512],
                             self.p_c[:, 0:512], ALU.mult)
            vq2 = ve.tensor_copy(self.XN[:, a * R + gi * 512:a * R + gi * 512 + 512], i32v)


    def f0_init(self):
        self._f0_steps = []
        for wi, (wd, na, fw) in enumerate([(self.WTg, 8, 4096), (self.WTu, 8, 4096),
                                           (self.WTd, 32, 1024)]):
            for a in range(na):
                if fw > 2048:
                    self._f0_steps.append((wi, wd, a, 0, 2048))
                    self._f0_steps.append((wi, wd, a, 2048, 2048))
                else:
                    self._f0_steps.append((wi, wd, a, 0, fw))
        self._f0_i = 0
        self._f0_l = 0
        self._f0_ld = {}
        self._f0_sab = {}
        self._f0_pm = None
        self.ve.memset(self.smalls[:, 230:233], 0.0)
        self.f0_load()

    def f0_load(self):
        sy, se, po = self.sy, self.se, self.po
        slot1f = self.WBUF[:, 8192:16384].bitcast(F32)
        if self._f0_l >= len(self._f0_steps):
            return
        i = self._f0_l
        self._f0_l += 1
        wi, wd, a, co, fw = self._f0_steps[i]
        half = i % 2
        dst = slot1f[:, half * 2048:half * 2048 + fw].rearrange("p (x f) -> p x f", x=1)
        src = wd.rearrange("(a p) f -> p a f", p=128)[:, a:a + 1, co:co + fw]
        if i < 2:
            self.s_te.wait(po, self.te_v_done)
        else:
            self.s_se.wait(po, self._f0_sab[i - 2])
        self._f0_ld[i] = self.qdma(po, self.s_wl, dst, src)

    def f0_step(self, n=1):
        sy, te, ve, se, po = self.sy, self.te, self.ve, self.se, self.po
        slot1f = self.WBUF[:, 8192:16384].bitcast(F32)
        red = self.smalls[:, 233:234]
        for _ in range(n):
            if self._f0_i >= len(self._f0_steps):
                return
            self.f0_load()
            i = self._f0_i
            self._f0_i += 1
            wi, wd, a, co, fw = self._f0_steps[i]
            half = i % 2
            sl = slot1f[:, half * 2048:half * 2048 + fw]
            self.s_wl.wait(se, self._f0_ld[i])
            if i > 0:
                self.s_ve.wait(se, self._f0_pm)
            self._f0_sab[i] = se.activation(sl, sl, AF.Abs, accum_out=red)
            self.s_se.wait(ve, self._f0_sab[i])
            self._f0_pm = ve.tensor_tensor(self.smalls[:, 230 + wi:231 + wi],
                                           self.smalls[:, 230 + wi:231 + wi], red, ALU.add)

    def phase3_f0(self):
        sy, te, ve, se, po = self.sy, self.te, self.ve, self.se, self.po
        self.f0_step(len(self._f0_steps))
        acc = self.smalls[:, 230:233]
        pm = self._f0_pm
        self.s_ve.wait(te, pm)
        for wi in range(3):
            self.tem(self.p_e[0:1, 100 + wi:101 + wi], self.ONES_COL_F, acc[:, wi:wi + 1],
                     start=True, stop=True)
        tf0 = self.s_te.n
        self.s_te.wait(ve, tf0)
        mrow = self.smalls[0:1, 236:239]
        ve.tensor_scalar(mrow, self.p_e[0:1, 100:103], 1.0 / (DM * FF), None, ALU.mult)
        vm0 = ve.tensor_scalar(mrow, mrow, Q_EPS, None, ALU.max)
        thr = self.smalls[0:1, 240:243]; nthr = self.smalls[0:1, 243:246]
        ve.tensor_scalar(thr, mrow, 0.5, None, ALU.mult)
        vth = ve.tensor_scalar(nthr, thr, -1.0, None, ALU.mult)
        sgm = self.smalls[0:1, 246:247]; md127 = self.smalls[0:1, 247:248]
        ve.tensor_tensor(sgm, mrow[0:1, 0:1], mrow[0:1, 0:1], ALU.mult)
        ve.tensor_tensor(sgm, sgm, mrow[0:1, 1:2], ALU.mult)
        ve.tensor_scalar(sgm, sgm, 1.0 / (127.0 ** 3), None, ALU.mult)
        vsc = ve.tensor_scalar(md127, mrow[0:1, 2:3], 1.0 / 127.0, None, ALU.mult)
        self.s_ve.wait(te, max(vth, vsc))
        for wi in range(3):
            te.matmul(self.p_e[:, 104 + wi:105 + wi], self.ONES_ROW_F, thr[0:1, wi:wi + 1],
                      start=True, stop=True)
            ins = te.matmul(self.p_e[:, 108 + wi:109 + wi], self.ONES_ROW_F, nthr[0:1, wi:wi + 1],
                            start=True, stop=True)
        tthr = self.s_te.inc(ins)
        self.s_te.wait(ve, tthr)
        self.thrc = self.smalls[:, 240:243]
        self.nthrc = self.smalls[:, 243:246]
        ve.tensor_copy(self.thrc, self.p_e[:, 104:107])
        ve.tensor_copy(self.nthrc, self.p_e[:, 108:111])
        self.sgm = sgm; self.md127 = md127

    def ternarize(self, eng, wsrc_f32, tern_dst, tmp, wi, width):
        eng.tensor_scalar(tmp[:, 0:width], wsrc_f32[:, 0:width], self.thrc[:, wi:wi + 1],
                          None, ALU.is_gt)
        eng.tensor_scalar(tern_dst[:, 0:width], wsrc_f32[:, 0:width], self.nthrc[:, wi:wi + 1],
                          None, ALU.is_lt)
        return eng.tensor_tensor(tern_dst[:, 0:width], tmp[:, 0:width], tern_dst[:, 0:width],
                                 ALU.subtract)

    def ternarize4(self, eng, wsrc, dst, tmp, wi):
        eng.tensor_scalar(tmp, wsrc, self.thrc[:, wi:wi + 1], None, ALU.is_gt)
        eng.tensor_scalar(dst, wsrc, self.nthrc[:, wi:wi + 1], None, ALU.is_lt)
        return eng.tensor_tensor(dst, tmp, dst, ALU.subtract)

    def fd_step(self, ternd, tstg, ttmp):
        sy, ve = self.sy, self.ve
        k, half = divmod(self._fd_i, 2)
        if k >= FFT:
            return
        self._fd_i += 1
        if self._fd_t is not None:
            self.s_ve.wait(sy, self._fd_t)
        lw = self.qdma(sy, self.s_x, tstg.rearrange("p (x y f) -> p x y f", x=1, y=1),
                       self.WTd.rearrange("(a p) (h f) -> p a h f", p=128, h=2)[:, k:k + 1, half:half + 1, :])
        self.s_x.wait(ve, lw)
        base = k * 1024 + half * 512
        self._fd_t = self.ternarize(ve, tstg, ternd[:, base:base + 512], ttmp, 2, 512)

    def phase3_f1(self):
        sy, te, ve, se, po = self.sy, self.te, self.ve, self.se, self.po
        slot1f = self.WBUF[:, 8192:16384].bitcast(F32)
        slot0f = self.WBUF[:, 0:8192].bitcast(F32)
        terng = self.A2S[:, 0:4096]
        ternu = self.XG[:].bitcast(BF16)[:, 4096:8192]
        hst = self.SCB[:, 1536:2048]
        ttmp = self.OT[:, 0:4096]
        ternd = self.ARE[:, 0:32768]
        tstg = self.SCB[:, 0:1024].bitcast(F32)
        RMX = self.XG[:, 0:2048]
        self.s_te.wait(ve)
        self.s_se.wait(ve)
        vmn = ve.memset(RMX, 0.0)
        self._fd_i = 0
        self._fd_t = None
        prev_h = None
        prev_ev = {0: None, 1: None}
        def ld_gu(ig):
            lg = []
            lu = []
            for it in range(4):
                lg.append(self.qdma(
                    sy, self.s_x,
                    slot1f[:, it * 1024:(it + 1) * 1024].rearrange("p (a f) -> p a f", f=128),
                    self.WTg.rearrange("(a p) q -> p a q", p=128)[:, :, ig * 512 + it * 128:
                                                                ig * 512 + (it + 1) * 128]))
                lu.append(self.qdma(
                    sy, self.s_x,
                    slot0f[:, it * 1024:(it + 1) * 1024].rearrange("p (a f) -> p a f", f=128),
                    self.WTu.rearrange("(a p) q -> p a q", p=128)[:, :, ig * 512 + it * 128:
                                                                ig * 512 + (it + 1) * 128]))
            return lg, lu

        self.s_po.wait(sy)
        self.s_ve.wait(sy)
        self.s_te.wait(sy)
        self.s_se.wait(te)
        self.s_ve.wait(se)
        lds = {0: ld_gu(0)}
        prev_se = {0: None, 1: None}
        prev_vh = None

        def sl4(buf, it):
            return buf.rearrange("p (a f) -> p a f", f=512)[:, :, it * 128:(it + 1) * 128]

        for ig in range(DMT):
            lg, lu = lds[ig]
            vt = {}
            for tg in range(NG):
                for it in range(4):
                    if tg == 0:
                        self.s_x.wait(ve, lg[it])
                        self.ternarize(ve, slot1f[:, it * 1024:(it + 1) * 1024],
                                       terng[:, it * 1024:(it + 1) * 1024], ttmp, 0, 1024)
                        self.s_x.wait(ve, lu[it])
                        vt[it] = self.ternarize(ve, slot0f[:, it * 1024:(it + 1) * 1024],
                                                ternu[:, it * 1024:(it + 1) * 1024], ttmp, 1, 1024)
                        if it == 3 and ig + 1 < DMT:
                            self.s_ve.wait(sy, vt[3])
                            lds[ig + 1] = ld_gu(ig + 1)
                    self.fd_step(ternd, tstg, ttmp)
                    par = it % 2
                    pg = self.p_a if par == 0 else self.p_c
                    pu = self.p_b if par == 0 else self.p_d
                    if tg == 0:
                        self.s_ve.wait(te, vt[it])
                    if prev_ev[par] is not None:
                        self.s_ve.wait(te, prev_ev[par])
                        self.s_se.wait(te, prev_se[par])
                    for a2 in range(DMT):
                        te.matmul(pg[:, 0:512], terng[:, it * 1024 + a2 * 128:it * 1024 + a2 * 128 + 128],
                                  self.XN[:, a2 * R + tg * TG:a2 * R + (tg + 1) * TG],
                                  start=(a2 == 0), stop=(a2 == DMT - 1))
                    for a2 in range(DMT):
                        ins = te.matmul(pu[:, 0:512], ternu[:, it * 1024 + a2 * 128:it * 1024 + a2 * 128 + 128],
                                        self.XN[:, a2 * R + tg * TG:a2 * R + (tg + 1) * TG],
                                        start=(a2 == 0), stop=(a2 == DMT - 1))
                    tgu = self.s_te.inc(ins)
                    self.s_te.wait(ve, tgu)
                    self.s_te.wait(se, tgu)
                    if prev_h is not None:
                        self.s_sc.wait(ve, prev_h)
                    if prev_vh is not None:
                        self.s_ve.wait(se, prev_vh)
                    sr = se.activation(self.SCR[:, 0:512], pg[:, 0:512], AF.Relu)
                    sqm = se.activation(self.SCR[:, 512:1024], self.SCR[:, 0:512], AF.Square)
                    self.s_se.wait(ve, sqm)
                    vh = ve.tensor_tensor(hst, self.SCR[:, 512:1024], pu[:, 0:512], ALU.mult)
                    self.s_ve.wait(se, vh)
                    sab = se.activation(self.SCR[:, 0:512], hst, AF.Abs)
                    self.s_se.wait(ve, sab)
                    vhm = ve.tensor_tensor(RMX[:, tg * 512:(tg + 1) * 512], RMX[:, tg * 512:(tg + 1) * 512],
                                           self.SCR[:, 0:512], ALU.max)
                    prev_ev[par] = vh
                    prev_se[par] = sr
                    prev_vh = vhm
                    self.s_ve.wait(self.se.e, vh)
                    prev_h = self.qdma(self.se.e, self.s_sc,
                                       self.h_d[(ig * 4 + it) * 128:(ig * 4 + it + 1) * 128,
                                                tg * TG:(tg + 1) * TG], hst)
        self.s_ve.wait(sy)
        self.s_sc.wait(sy)
        self._f2lh0 = self.qdma(sy, self.s_x,
                                self.WBUF[:, 0:16384].rearrange("p (k t) -> p k t", t=512),
                                self.h_d.rearrange("(k p) t -> p k t", p=128)[:, :, 0:TG])
        for tg in range(NG):
            rx = RMX[:, tg * 512:(tg + 1) * 512]
            self.s_ve.wait(te)
            for ct in range(4):
                ins = te.transpose(self.p_e[:, ct * 128:(ct + 1) * 128],
                                   rx[:, ct * 128:(ct + 1) * 128], self.ID_F)
            ttd = self.s_te.inc(ins)
            self.s_te.wait(ve, ttd)
            red = self.smalls[:, 0:4]
            vrd = ve.tensor_reduce(red, self.p_e[:, 0:512].rearrange("p (a b) -> p a b", b=128),
                                   AX.X, ALU.max)
            self.s_ve.wait(te, vrd)
            tred = self.tet(self.p_e[0:4, 256:384], red, self.ID_F)
            self.s_te.wait(ve, tred)
            self.s_sp.wait(ve)
            vev = ve.tensor_copy(self.st16[0:4, 0:128], self.p_e[0:4, 256:384])
            self.s_ve.wait(sy, vev)
            self.qdma(sy, self.s_sp, self.mx_d[16 + tg * 4:16 + (tg + 1) * 4, :],
                      self.st16[0:4, 0:128])

    def phase3_f2(self):
        sy, te, ve, se, po = self.sy, self.te, self.ve, self.se, self.po
        ternd = self.ARE[:, 0:32768]
        HB = {0: self.WBUF[:, 0:16384], 1: self.XN[:]}
        OTF = self.OT[:].bitcast(F32)
        SCBF = self.SCB[:].bitcast(F32)
        S1B = {0: SCBF[:, 0:512], 1: SCBF[:, 512:1024]}
        DSC = {0: self.SCR[:, 0:512], 1: self.SCR[:, 512:1024]}
        i32v = OTF[:, 512:1024].bitcast(I32)
        OSTG = OTF[:, 0:512]
        accs1 = [self.p_a[:, 0:512], self.p_b[:, 0:512], self.p_c[:, 0:512],
                 self.p_e[:, 0:512], self.p_o[:, 0:512], self.p_o[:, 512:1024]]
        self.s_sc.wait(sy)
        self.s_te.wait(sy)
        self.s_ve.wait(sy)
        ldh = {}; ldx = {}; rmc = {}; smath = {}; vmath = {}
        self._f2t8 = {}; self._f2vq = {}; self._f2vo = {}

        def emit_lh(tg):
            p = tg % 2
            if tg >= 2:
                self.s_te.wait(sy, self._f2t8[tg - 2])
                self.s_ve.wait(sy, self._f2vq[tg - 2])
            ldh[tg] = self.qdma(sy, self.s_x, HB[p].rearrange("p (k t) -> p k t", t=512),
                                self.h_d.rearrange("(k p) t -> p k t", p=128)[:, :, tg * TG:(tg + 1) * TG])

        def emit_lx2(tg):
            if tg >= 1:
                self.s_ve.wait(sy, self._f2vo[tg - 1])
            ldx[tg] = self.qdma(sy, self.s_x, self.XG[:].rearrange("p (a f) -> p a f", f=TG),
                                self.x2_d.rearrange("(a p) t -> p a t", p=128)[:, :, tg * TG:(tg + 1) * TG])

        def emit_rows(tg):
            p = tg % 2
            self.s_sp.wait(sy)
            if tg >= 1:
                self.s_ve.wait(sy, vmath[tg - 1])
                self.s_se.wait(sy, smath[tg - 1])
            lr1 = self.qdma(sy, self.s_x, self.st16[0:4, 0:128], self.mx_d[tg * 4:(tg + 1) * 4, :])
            lr2 = self.qdma(sy, self.s_x, self.st16[0:4, 128:256], self.mx_d[16 + tg * 4:16 + (tg + 1) * 4, :])
            self.s_x.wait(sy, lr2)
            r1 = self.qdma(sy, self.s_sp, self.rows[0:1, 0:512], self.st16[0:4, 0:128])
            r2 = self.qdma(sy, self.s_sp, self.rows[0:1, 512:1024], self.st16[0:4, 128:256])
            self.s_sp.wait(ve, r2)
            mxcr = self.rows[0:1, 0:512]; M1r = self.rows[0:1, 512:1024]
            Fr = self.rows[0:1, 1024:1536]; Dr = self.rows[0:1, 1536:2048]
            ve.tensor_tensor(Fr, mxcr, mxcr, ALU.mult)
            ve.tensor_tensor(Fr, Fr, mxcr, ALU.mult)
            vF = ve.tensor_scalar(Fr, Fr, self.sgm, None, ALU.mult)
            self.s_ve.wait(se, vF)
            sF = se.activation(Dr, Fr, AF.Ln)
            self.s_se.wait(ve, sF)
            veF = ve.tensor_scalar(Dr, Dr, -1.0, None, ALU.mult)
            self.s_ve.wait(se, veF)
            sE = se.activation(Dr, Dr, AF.Exp, bias=self.B_LNQEPS[0:1, :])
            self.s_se.wait(ve, sE)
            vD = ve.tensor_tensor(Dr, M1r, Dr, ALU.max)
            self.s_ve.wait(se, vD)
            s1 = se.activation(mxcr, Dr, AF.Ln)
            smath[tg] = se.activation(mxcr, mxcr, AF.Exp, scale=-1.0,
                                      bias=self.B_LN127[0:1, :])
            self.s_se.wait(ve, smath[tg])
            ve.tensor_tensor(M1r, Fr, Dr, ALU.mult)
            vdsc = ve.tensor_scalar(M1r, M1r, self.md127, None, ALU.mult)
            self.s_ve.wait(te, vdsc)
            if tg >= 1:
                self.s_ve.wait(te, rmc[tg - 1])
            tb1 = self.tem(self.p_d[:, 0:512], self.ONES_ROW_F, mxcr, start=True, stop=True)
            self.s_te.wait(ve, tb1)
            vc1 = ve.tensor_copy(S1B[p], self.p_d[:, 0:512])
            self.s_ve.wait(te, vc1)
            tb2 = self.tem(self.p_d[:, 0:512], self.ONES_ROW_F, M1r, start=True, stop=True)
            self.s_te.wait(ve, tb2)
            rmc[tg] = ve.tensor_copy(DSC[p], self.p_d[:, 0:512])
            vmath[tg] = rmc[tg]

        ldh[0] = self._f2lh0
        emit_lx2(0)
        emit_rows(0)
        outr = self.outT.rearrange("(a p) t -> p a t", p=128)
        for tg in range(NG):
            p = tg % 2
            self.s_x.wait(ve, ldh[tg])
            vqs = {}
            for k in range(FFT):
                ve.tensor_tensor(i32v, HB[p][:, k * 512:(k + 1) * 512], S1B[p], ALU.mult)
                vqs[k] = ve.tensor_copy(HB[p][:, k * 512:(k + 1) * 512], i32v)
            self._f2vq[tg] = vqs[FFT - 1]
            if tg + 1 < NG:
                emit_lh(tg + 1)
            for k in range(FFT):
                self.s_ve.wait(te, vqs[k])
                for m2 in range(6):
                    ins = te.matmul(accs1[m2],
                                    ternd[:, k * 1024 + m2 * 128:k * 1024 + m2 * 128 + 128],
                                    HB[p][:, k * 512:(k + 1) * 512],
                                    start=(k == 0), stop=(k == FFT - 1))
            t6 = self.s_te.inc(ins)
            if tg + 1 < NG:
                emit_rows(tg + 1)
            self.s_x.wait(ve, ldx[tg])
            vos = {}

            def evac(m2, pp):
                self.s_out.wait(ve)
                ve.tensor_tensor(OSTG, pp, DSC[p], ALU.mult)
                vo = ve.tensor_tensor(OSTG, OSTG, self.XG[:, m2 * 512:(m2 + 1) * 512], ALU.add)
                self.s_ve.wait(sy, vo)
                self.s_out.wait(sy)
                self.s_out.dma(sy.dma_start(
                    out=outr[:, m2:m2 + 1, tg * TG:(tg + 1) * TG],
                    in_=OSTG.rearrange("p (x t) -> p x t", x=1)))
                return vo

            self.s_te.wait(ve, t6)
            vos[0] = evac(0, accs1[0])
            vos[1] = evac(1, accs1[1])
            self.s_ve.wait(te, vos[1])
            for k in range(FFT):
                for m2 in (6, 7):
                    ins = te.matmul(accs1[m2 - 6],
                                    ternd[:, k * 1024 + m2 * 128:k * 1024 + m2 * 128 + 128],
                                    HB[p][:, k * 512:(k + 1) * 512],
                                    start=(k == 0), stop=(k == FFT - 1))
            t8 = self.s_te.inc(ins)
            self._f2t8[tg] = t8
            for m2 in range(2, 6):
                vos[m2] = evac(m2, accs1[m2])
            self.s_te.wait(ve, t8)
            vos[6] = evac(6, accs1[0])
            vos[7] = evac(7, accs1[1])
            self._f2vo[tg] = vos[7]
            if tg + 1 < NG:
                emit_lx2(tg + 1)

    def emit_debug(self):
        sy = self.sy
        self.s_se.wait(sy)
        self.s_ve.wait(sy)
        self.s_te.wait(sy)
        self.s_sp.wait(sy)
        self.s_wl.wait(sy)
        def dbg(name, src, shape, dt):
            o = self.out(name, shape, dt)
            self.s_out.wait(sy)
            self.s_out.dma(sy.dma_start(out=o, in_=src))
        dbg("d_xn", self.XN[:], [128, DMT * R], BF16)
        dbg("d_rows", self.rows[:], [1, 4 * TG], F32)
        dbg("d_xg", self.XG[:], [128, DMT * TG], F32)
        dbg("d_sq", self.ARE[:, 16384 + DMT * TG:16384 + DMT * TG + 512], [128, 512], BF16)
        dbg("d_scrow", self.SCR[0:1, :], [1, 1024], F32)
        dbg("d_rowsb", self.rowsb, [1, 2 * TG], BF16)
        dbg("d_Call", self.C_all[:], [128, NCH * H], F32)
        if PHASES >= 2:
            dbg("d_gS", self.ARE[:, 16384:32768], [128, DMT * R], BF16)
            dbg("d_ogS", self.ARE[:, 0:16384], [128, DMT * R], BF16)
        dbg("d_Ball", self.Ballc[:], [16, NCH + 1], F32)
        dbg("d_cC", self.cCcol[:], [16, NCH], F32)
        dbg("d_states", self.std_d, [NCH * 128, 512], BF16)
        if PHASES >= 4:
            dbg("d_xq", self.XN[:], [128, DMT * R], BF16)
            dbg("d_mx", self.mx_d, [2 * NG * 4, 128], F32)
            dbg("d_x2", self.x2_d, [DM, R], F32)
        if PHASES >= 5:
            dbg("d_hd", self.h_d, [FFT * 128, R], BF16)
            dbg("d_smalls", self.smalls[0:1, 236:239], [1, 3], F32)
        dbg("d_Smaster", self.S_master[:], [128, 512], F32)
        if PHASES >= 2:
            dbg("d_kg", self.kgT_d, [HT * NCH * 128, CH], BF16)
            dbg("d_q", self.qT_d, [HT * NCH * 128, CH], BF16)
            dbg("d_v", self.vN_d, [NCH * 128, HID], BF16)

    def finish(self):
        sy = self.sy
        self.s_se.wait(sy)
        self.s_ve.wait(sy)
        self.s_te.wait(sy)
        self.s_sp.wait(sy)
        self.s_wl.wait(sy)
        self.s_po.wait(sy)
        self.s_sp.wait(sy)
        self.s_x.wait(sy)
        self.s_out.wait(sy)


def _consts(rank):
    import ml_dtypes
    j = np.arange(128)
    negUT = -np.triu(np.ones((128, 128), dtype=np.float32))
    wind = np.zeros((8, 8), dtype=np.float32)
    for b in range(8):
        for k in range(8):
            if b + 1 <= k <= rank - 1:
                wind[k, b] = 1.0
    alog = np.array([0.0 if b < rank else NEG for b in range(8)], dtype=np.float32)
    return {
        "c_id_bf": np.eye(128, dtype=ml_dtypes.bfloat16),
        "c_id_f": np.eye(128, dtype=np.float32),
        "c_negUT": negUT,
        "c_onescb": np.ones((128, 1), dtype=ml_dtypes.bfloat16),
        "c_onescf": np.ones((128, 1), dtype=np.float32),
        "c_onesrb": np.ones((1, 128), dtype=ml_dtypes.bfloat16),
        "c_onesrf": np.ones((1, 128), dtype=np.float32),
                "c_i16f": np.eye(16, dtype=np.float32),
        "c_i16b": np.eye(16, dtype=ml_dtypes.bfloat16),
        "c_ones16f": np.ones((16, 128), dtype=np.float32),
        "c_ones16b": np.ones((16, 128), dtype=ml_dtypes.bfloat16),
        "c_actc": np.tile(np.array([[LN_EPS, 1.0, RMS_EPS, LN127, float(np.log(Q_EPS))]], dtype=np.float32), (128, 1)),
        "c_negdiag": (np.eye(128) * NEG).astype(ml_dtypes.bfloat16),
        "c_slrep": np.where(j[:, None] > j[None, :], 1.0, 0.0).astype(ml_dtypes.bfloat16),
        "c_wind": wind,
        "c_alog": np.tile(alog[None, :], (128, 1)).astype(np.float32),
    }


def kernel(L0_attn_x, L0_attn_Wq, L0_attn_Wk, L0_attn_Wv, L0_attn_Wo, L0_attn_Wg,
           L0_attn_Wog, L0_attn_Wd, L0_attn_bd, L0_attn_norm_w,
           L0_ffn_w_gate, L0_ffn_w_up, L0_ffn_w_down):
    x = np.asarray(L0_attn_x)[0]
    T = lambda w: np.ascontiguousarray(np.asarray(w).T.astype(np.float32))
    WqT, WkT, WvT, WgT, WogT = T(L0_attn_Wq), T(L0_attn_Wk), T(L0_attn_Wv), T(L0_attn_Wg), T(L0_attn_Wog)
    WoT = T(L0_attn_Wo)
    WdT = T(L0_attn_Wd)
    WdTp = np.ascontiguousarray(WdT.reshape(DMT, 128, H).transpose(1, 0, 2).reshape(128, DMT * H))
    WTg, WTu = T(L0_ffn_w_gate), T(L0_ffn_w_up)
    WTd = T(L0_ffn_w_down)
    bd = np.asarray(L0_attn_bd, dtype=np.float32).reshape(1, H)
    normw = np.ascontiguousarray(np.asarray(L0_attn_norm_w, dtype=np.float32).reshape(HT, 128).T)

    mk = MK()
    nc = mk.build()
    in_maps = []
    for c in range(N_CORES):
        xT_c = np.ascontiguousarray(x[c * R:(c + 1) * R, :].T)
        m = {"xT": xT_c, "WqT": WqT, "WkT": WkT, "WvT": WvT, "WgT": WgT, "WogT": WogT,
             "WoT": WoT, "WdTp": WdTp, "bd": bd, "normw": normw,
             "WTg": WTg, "WTu": WTu, "WTd": WTd}
        m.update(_consts(c))
        in_maps.append(m)
    res = run_bass_kernel_spmd(nc, in_maps, core_ids=list(range(N_CORES)))
    outs = [np.asarray(r["outT"]).T for r in res.results]
    out = np.concatenate(outs, axis=0)[None]
    if DEBUG:
        kernel._results = res
    return out.astype(np.float32)


# version 19
# speedup vs baseline: 1.1112x; 1.1112x over previous
import numpy as np
import concourse.bass as bass
import concourse.mybir as mybir
from concourse.bass_utils import run_bass_kernel_spmd
from contextlib import ExitStack

F32 = mybir.dt.float32
BF16 = mybir.dt.bfloat16
I32 = mybir.dt.int32
AF = mybir.ActivationFunctionType
ALU = mybir.AluOpType
AX = mybir.AxisListType

DEBUG = False
SIM1 = False
PHASES = 99
N_CORES = 8
R = 2048
TG = 512
NG = 4
CH = 128
NCH = 16
H = 16; DH = 64; DM = 1024; DMT = 8; HID = 1024; HT = 8
FF = 4096; FFT = 32
LN_EPS = 1e-5
RMS_EPS = float(np.finfo(np.float32).eps)
Q_EPS = 1e-5
NEG = -1.0e30
LN127 = float(np.log(127.0))


class EW:
    OPS = {"tensor_copy", "tensor_tensor", "tensor_add", "tensor_sub", "tensor_mul",
           "tensor_max", "tensor_scalar", "tensor_single_scalar", "tensor_scalar_add",
           "tensor_scalar_sub", "tensor_scalar_mul", "tensor_scalar_max",
           "tensor_scalar_min", "tensor_relu", "tensor_reduce", "reduce_max",
           "reduce_sum", "scalar_tensor_tensor", "tensor_tensor_scan", "memset",
           "memzero", "select", "copy_predicated", "reciprocal", "transpose",
           "activation", "sign", "square", "sqrt", "mul", "copy", "add",
           "bn_stats", "bn_aggr", "max", "pool", "pool_avg", "pool_max",
           "stream_shuffle", "iota"}

    def __init__(self, eng, sem):
        self.e = eng
        self.s = sem

    def __getattr__(self, k):
        f = getattr(self.e, k)
        if k not in EW.OPS:
            return f
        def g(*a, **kw):
            if self.s.n > 0:
                self.e.wait_ge(self.s.h, self.s.n)
            ins = f(*a, **kw)
            return self.s.inc(ins)
        return g


class Sem:
    def __init__(self, nc, es, name):
        self.h = es.enter_context(nc.semaphore(name))
        self.n = 0

    def inc(self, instr, k=1):
        instr.then_inc(self.h, k)
        self.n += k
        return self.n

    def dma(self, instr):
        return self.inc(instr, 16)

    def wait(self, eng, n=None):
        e = eng.e if isinstance(eng, EW) else eng
        e.wait_ge(self.h, self.n if n is None else n)


class MK:

    def __init__(self):
        self.nc = bass.Bass("TRN2", target_bir_lowering=False)
        self.es = ExitStack()

    def sb(self, name, shape, dt):
        return self.es.enter_context(self.nc.sbuf_tensor(name, shape, dt))

    def ps(self, name, shape, dt=F32):
        return self.es.enter_context(self.nc.psum_tensor(name, shape, dt))

    def inp(self, name, shape, dt=F32):
        return self.nc.dram_tensor(name, shape, dt, kind="ExternalInput").ap()

    def out(self, name, shape, dt=F32):
        return self.nc.dram_tensor(name, shape, dt, kind="ExternalOutput").ap()

    def dram(self, name, shape, dt):
        return self.nc.dram_tensor(name, shape, dt, kind="Internal").ap()

    def qdma(self, queue, sem, out, in_, **kw):
        sem.wait(queue)
        return sem.dma(queue.dma_start(out=out, in_=in_, **kw))

    def build(self):
        nc = self.nc
        self.xT = self.inp("xT", [DM, R])
        self.WqT = self.inp("WqT", [DM, HID]); self.WkT = self.inp("WkT", [DM, HID])
        self.WvT = self.inp("WvT", [DM, HID]); self.WgT = self.inp("WgT", [DM, HID])
        self.WogT = self.inp("WogT", [DM, HID]); self.WoT = self.inp("WoT", [HID, DM])
        self.WdTp = self.inp("WdTp", [128, DMT * H])
        self.bd = self.inp("bd", [1, H])
        self.normw = self.inp("normw", [128, HT])
        self.WTg = self.inp("WTg", [DM, FF]); self.WTu = self.inp("WTu", [DM, FF])
        self.WTd = self.inp("WTd", [FF, DM])
        self.c_id_bf = self.inp("c_id_bf", [128, 128], BF16)
        self.c_id_f = self.inp("c_id_f", [128, 128], F32)
        self.c_negUT = self.inp("c_negUT", [128, 128], F32)
        self.c_onescb = self.inp("c_onescb", [128, 1], BF16)
        self.c_onescf = self.inp("c_onescf", [128, 1], F32)
        self.c_onesrb = self.inp("c_onesrb", [1, 128], BF16)
        self.c_onesrf = self.inp("c_onesrf", [1, 128], F32)
        self.c_i16f = self.inp("c_i16f", [16, 16], F32)
        self.c_i16b = self.inp("c_i16b", [16, 16], BF16)
        self.c_ones16f = self.inp("c_ones16f", [16, 128], F32)
        self.c_ones16b = self.inp("c_ones16b", [16, 128], BF16)
        self.c_actc = self.inp("c_actc", [128, 5], F32)
        self.c_negdiag = self.inp("c_negdiag", [128, 128], BF16)
        self.c_slrep = self.inp("c_slrep", [128, CH], BF16)
        self.c_wind = self.inp("c_wind", [8, 8], F32)
        self.c_alog = self.inp("c_alog", [128, 8], F32)
        self.outT = self.out("outT", [DM, R], F32)
        self.qT_d = self.dram("qT_d", [HT * NCH * 128, CH], BF16)
        self.kgT_d = self.dram("kgT_d", [HT * NCH * 128, CH], BF16)
        self.vN_d = self.dram("vN_d", [NCH * 128, HID], BF16)
        self.x2_d = self.dram("x2_d", [DM, R], F32)
        self.h_d = self.dram("h_d", [FFT * 128, R], BF16)
        self.mx_d = self.dram("mx_d", [2 * NG * 4, 128], F32)
        self.cc_in = self.dram("cc_in", [129, 512], F32)
        self.cc_out = self.dram("cc_out", [8 * 129, 512], F32)
        self.std_d = self.dram("std_d", [NCH * 128, 512], BF16)

        self.XN = self.sb("XN", [128, DMT * R], BF16)
        self.ARE = self.sb("ARE", [128, 32768], BF16)
        self.WBUF = self.sb("WBUF", [128, 2 * 8192], BF16)
        self.XG = self.sb("XG", [128, DMT * TG], F32)
        self.SCR = self.sb("SCR", [128, 1024], F32)
        self.SCB = self.sb("SCB", [128, 2048], BF16)
        self.OT = self.sb("OT", [128, HT * TG], BF16)
        self.C_all = self.sb("C_all", [128, NCH * H], F32)
        self.negC_all = self.sb("negC_all", [128, NCH * H], F32)
        self.A2S = self.sb("A2S", [128, 5120], BF16)
        self.CHIA = self.sb("CHIA", [16, 2048], F32)
        self.STB2 = self.sb("STB2", [128, 1024], BF16)
        self.LST = self.sb("LST", [128, 1024], BF16)
        self.st16 = self.sb("st16", [16, 256], F32)
        self.rows = self.sb("rows", [1, 4 * TG], F32)
        self.cCcol = self.sb("cCcol", [16, NCH], F32)
        self.Ballc = self.sb("Ballc", [16, NCH + 1], F32)
        self.S_master = self.sb("S_master", [128, 512], F32)
        self.smalls = self.sb("smalls", [128, 256], F32)
        self.smallsb = self.sb("smallsb", [128, 256], BF16)
        self.wd_sb = self.sb("wd_sb", [128, DMT * H], F32)
        self.wdbar = self.sb("wdbar", [128, H], F32)
        self.bdb = self.sb("bdb", [128, H], F32)
        self.normw_sb = self.sb("normw_sb", [128, HT], F32)
        self.consts = self.sb("consts", [128, 400], F32)
        self.constsb = self.sb("constsb", [128, 264], BF16)
        self.rowsb = self.A2S[0:1, 0:1024]
        self.i16f = self.sb("i16f", [16, 16], F32)
        self.i16b = self.sb("i16b", [16, 16], BF16)
        self.ones16f = self.sb("ones16f", [16, 128], F32)
        self.ones16b = self.sb("ones16b", [16, 128], BF16)
        self.negdiag = self.sb("negdiag", [128, 128], BF16)
        self.slrep = self.sb("slrep", [128, CH], BF16)
        self.wind = self.sb("wind", [8, 8], F32)
        self.alog = self.sb("alog", [128, 8], F32)
        self.p_a = self.ps("p_a", [128, 512]); self.p_b = self.ps("p_b", [128, 512])
        self.p_c = self.ps("p_c", [128, 512]); self.p_d = self.ps("p_d", [128, 512])
        self.p_e = self.ps("p_e", [128, 512])
        self.p_o = self.ps("p_o", [128, 1024])
        self.p_t = self.ps("p_t", [128, 1024], BF16)

        es = self.es
        self.s_wl = Sem(nc, es, "s_wl")
        self.s_x = Sem(nc, es, "s_x")
        self.s_sp = Sem(nc, es, "s_sp")
        self.s_te = Sem(nc, es, "s_te")
        self.s_ve = Sem(nc, es, "s_ve")
        self.s_se = Sem(nc, es, "s_se")
        self.s_po = Sem(nc, es, "s_po")
        self.s_wq = Sem(nc, es, "s_wq")
        self.s_sc = Sem(nc, es, "s_sc")
        self.s_cc = Sem(nc, es, "s_cc")
        self.s_out = Sem(nc, es, "s_out")

        with self.es:
            with nc.Block() as block:
                @block.sync
                def _(sync):
                    self.emit()
        return nc

    def emit(self):
        nc = self.nc
        self.sy = nc.sync
        self.te = nc.tensor
        self.ve = EW(nc.vector, self.s_ve)
        self.se = EW(nc.scalar, self.s_se)
        self.po = EW(nc.gpsimd, self.s_po)
        self.phase0()
        self.phase1a()
        if PHASES >= 2:
            self.phase1b()
        if PHASES >= 3:
            self.phase2a()
        if PHASES >= 4:
            self.phase2b()
        if PHASES >= 5:
            self.phase3_f0()
            self.phase3_f1()
            self.phase3_f2()
        if DEBUG:
            self.emit_debug()
        self.finish()

    def preload_set(self, name):
        from concourse.hw_specs import get_activation_tables
        import concourse.mybir as mb
        tabs = get_activation_tables(self.nc.m.arch)
        idx = list(tabs.keys()).index(name)
        return self.se.e.add_instruction(mb.InstLoadActFuncSet(
            name=self.nc.get_next_instruction_name(),
            act_func_set_id=idx, ins=[], outs=[]))

    def tem(self, *a, **kw):
        ins = self.te.matmul(*a, **kw)
        return self.s_te.inc(ins)

    def tet(self, *a, **kw):
        ins = self.te.transpose(*a, **kw)
        return self.s_te.inc(ins)

    def phase0(self):
        sy, te, ve, se, po = self.sy, self.te, self.ve, self.se, self.po
        cs, cb = self.consts, self.constsb
        self.ID_F = cs[:, 0:128]
        self.NEGUT = cs[:, 128:256]
        self.ONES_COL_F = cs[:, 256:257]
        self.ONES_ROW_F = cs[0:1, 257:385]
        self.B_LNEPS = cs[:, 385:386]
        self.B_ONE = cs[:, 386:387]
        self.B_RMSEPS = cs[:, 387:388]
        self.B_LN127 = cs[:, 388:389]
        self.B_LNQEPS = cs[:, 389:390]
        self.ID_B = cb[:, 0:128]
        self.ONES_COL_B = cb[:, 128:129]
        self.ONES_ROW_B = cb[0:1, 129:257]
        for dst, src in [
            (self.ID_F, self.c_id_f), (self.NEGUT, self.c_negUT),
            (self.ONES_COL_F, self.c_onescf), (self.ONES_ROW_F, self.c_onesrf),
            (self.ID_B, self.c_id_bf), (self.ONES_COL_B, self.c_onescb),
            (self.ONES_ROW_B, self.c_onesrb),             (self.normw_sb[:], self.normw),
            (self.i16f[:], self.c_i16f), (self.i16b[:], self.c_i16b),
            (self.ones16f[:], self.c_ones16f), (self.ones16b[:], self.c_ones16b),
            (cs[:, 385:390], self.c_actc), (self.wd_sb[:], self.WdTp),
            (self.st16[0:1, 240:256], self.bd),
            (self.negdiag[:], self.c_negdiag), (self.slrep[:], self.c_slrep),
            (self.wind[:], self.c_wind), (self.alog[:], self.c_alog),
        ]:
            self.qdma(sy, self.s_x, dst, src)
        self.s_x.wait(te)
        self.tem(self.p_a[:, 0:H], self.ONES_ROW_F, self.st16[0:1, 240:256], start=True, stop=True)
        for a in range(DMT):
            ins = te.matmul(self.p_b[0:1, 0:H], self.ONES_COL_F, self.wd_sb[:, a * H:(a + 1) * H],
                            start=(a == 0), stop=(a == DMT - 1))
        t0 = self.s_te.inc(ins)
        self.s_te.wait(ve, t0)
        ve.tensor_copy(self.bdb[:], self.p_a[:, 0:H])
        v0 = ve.tensor_copy(self.st16[0:1, 224:240], self.p_b[0:1, 0:H])
        self.s_ve.wait(te, v0)
        t1 = self.tem(self.p_a[:, H:2 * H], self.ONES_ROW_F, self.st16[0:1, 224:240],
                      start=True, stop=True)
        self.s_te.wait(ve, t1)
        ve.tensor_copy(self.wdbar[:], self.p_a[:, H:2 * H])
        ve.memset(self.S_master[:], 0.0)
        ve.memset(self.Ballc[:], 0.0)

    def phase1a(self):
        sy, te, ve, se, po = self.sy, self.te, self.ve, self.se, self.po
        self.preload_set("natural_log_exp_and_others")
        ROW = self.rows
        AREF = self.ARE[:].bitcast(F32)
        XGB1 = {0: self.XG[:], 1: AREF[:, 12288:16384]}
        ldx = {}
        lastz = {}
        self._p1ve = {}

        def p1_load(tg):
            if tg >= 2:
                self.s_te.wait(sy, lastz[tg - 2])
                self.s_ve.wait(sy, self._p1ve[tg - 2])
            ldx[tg] = self.qdma(sy, self.s_x,
                                XGB1[tg % 2].rearrange("p (a f) -> p a f", f=TG),
                                self.xT.rearrange("(a p) t -> p a t", p=128)[:, :, tg * TG:(tg + 1) * TG])

        self.s_te.wait(sy)
        p1_load(0)
        p1_load(1)
        for tg in range(NG):
            XGg = XGB1[tg % 2]
            ld = ldx[tg]
            self.s_x.wait(ve, ld)
            self.s_x.wait(se, ld)
            if tg > 0:
                self.s_te.wait(ve)
                self.s_te.wait(se)
            xb = self.ARE[:, 16384:16384 + DMT * TG]
            sqt = self.ARE[:, 16384 + DMT * TG:16384 + 2 * DMT * TG]
            for a in range(DMT):
                vml = ve.tensor_copy(xb[:, a * TG:(a + 1) * TG], XGg[:, a * TG:(a + 1) * TG])
            for a in range(DMT):
                sml = se.activation(sqt[:, a * TG:(a + 1) * TG], XGg[:, a * TG:(a + 1) * TG], AF.Square)
            self.s_ve.wait(te, vml)
            self.s_se.wait(te, sml)
            for a in range(DMT):
                ins = te.matmul(self.p_a[0:1, 0:TG], self.ONES_COL_B, xb[:, a * TG:(a + 1) * TG],
                                start=(a == 0), stop=(a == DMT - 1))
            self.s_te.inc(ins)
            for a in range(DMT):
                ins = te.matmul(self.p_b[0:1, 0:TG], self.ONES_COL_B, sqt[:, a * TG:(a + 1) * TG],
                                start=(a == 0), stop=(a == DMT - 1))
            tml = self.s_te.inc(ins)
            self.s_te.wait(ve, tml)
            if DEBUG and tg == 3:
                ve.tensor_copy(self.SCR[0:1, 512:1024], self.p_b[0:1, 0:TG])
                ve.tensor_copy(self.SCR[0:1, 0:512], self.p_a[0:1, 0:TG])
            rm = ROW[0:1, 0:TG]; rvar = ROW[0:1, TG:2 * TG]
            rr = ROW[0:1, 2 * TG:3 * TG]; rmr = ROW[0:1, 3 * TG:4 * TG]
            ve.tensor_scalar(rm, self.p_a[0:1, 0:TG], 1.0 / DM, None, ALU.mult)
            ve.tensor_scalar(rvar, self.p_b[0:1, 0:TG], 1.0 / DM, None, ALU.mult)
            ve.tensor_mul(rr, rm, rm)
            vst = ve.tensor_sub(rvar, rvar, rr)
            if DEBUG and tg == 3:
                vst = ve.tensor_copy(self.rowsb[0:1, 0:512], rvar)
            self.s_ve.wait(se, vst)
            se.activation(rvar, rvar, AF.Ln, bias=self.B_LNEPS[0:1, :])
            sst = se.activation(rr, rvar, AF.Exp, scale=-0.5)
            if DEBUG and tg == 3:
                sst = se.activation(self.rowsb[0:1, 512:1024], rvar, AF.Copy)
            self.s_se.wait(ve, sst)
            vrb = ve.tensor_mul(rmr, rm, rr)
            self.s_ve.wait(te, vrb)
            self.tem(self.p_c[:, 0:TG], self.ONES_ROW_F, rr, start=True, stop=True)
            tbc = self.tem(self.p_d[:, 0:TG], self.ONES_ROW_F, rmr, start=True, stop=True)
            self.s_te.wait(ve, tbc)
            for a in range(DMT):
                t1 = self.SCR[:, 0:TG]
                ve.tensor_mul(t1, XGg[:, a * TG:(a + 1) * TG], self.p_c[:, 0:TG])
                ve.tensor_sub(self.XN[:, a * R + tg * TG: a * R + (tg + 1) * TG], t1, self.p_d[:, 0:TG])

            self.s_ve.wait(te)
            for cc in range(NG):
                t0 = cc * CH
                te.transpose(self.p_e[0:CH, cc:cc + 1], rr[0:1, t0:t0 + CH], self.ID_F[0:1, 0:1])
                ins = te.transpose(self.p_e[0:CH, 4 + cc:5 + cc], rmr[0:1, t0:t0 + CH],
                                   self.ID_F[0:1, 0:1])
            self.s_te.inc(ins)
            for cc in range(NG):
                t0 = cc * CH
                for a in range(DMT):
                    ins = te.matmul(self.p_o[:, cc * H:(cc + 1) * H],
                                    XGg[:, a * TG + t0:a * TG + t0 + CH],
                                    self.wd_sb[:, a * H:(a + 1) * H],
                                    start=(a == 0), stop=(a == DMT - 1))
            tz = self.s_te.inc(ins)
            self.s_te.wait(ve, tz)
            rc8 = self.smalls[:, 0:8]
            ve.tensor_copy(rc8, self.p_e[:, 0:8])
            z1 = self.smalls[:, 16:80]; zz = self.smalls[:, 96:160]
            ve.tensor_tensor(z1.rearrange("p (c h) -> p c h", h=H),
                             self.p_o[:, 0:64].rearrange("p (c h) -> p c h", h=H),
                             rc8[:, 0:4].rearrange("p (c x) -> p c x", x=1).broadcast_to([128, 4, H]),
                             ALU.mult)
            ve.tensor_tensor(zz.rearrange("p (c h) -> p c h", h=H),
                             self.wdbar[:].rearrange("p (x h) -> p x h", x=1).broadcast_to([128, 4, H]),
                             rc8[:, 4:8].rearrange("p (c x) -> p c x", x=1).broadcast_to([128, 4, H]),
                             ALU.mult)
            ve.tensor_sub(z1, z1, zz)
            vz = ve.tensor_tensor(zz.rearrange("p (c h) -> p c h", h=H),
                                  z1.rearrange("p (c h) -> p c h", h=H),
                                  self.bdb[:].rearrange("p (x h) -> p x h", x=1).broadcast_to([128, 4, H]),
                                  ALU.add)
            self.s_ve.wait(se, vz)
            az = self.smalls[:, 160:224]
            se.activation(az, zz, AF.Abs)
            se.activation(az, az, AF.Exp, scale=-1.0)
            sz = se.activation(az, az, AF.Ln, bias=self.B_ONE)
            self.s_se.wait(ve, sz)
            sp = z1
            ve.tensor_scalar(sp, zz, 0.0, None, ALU.max)
            vsp = ve.tensor_add(sp, sp, az)
            self.s_ve.wait(te, vsp)
            for cc in range(NG):
                ins = self.tem(self.p_o[:, 512 + cc * H:512 + (cc + 1) * H], self.NEGUT,
                               sp[:, cc * H:(cc + 1) * H], start=True, stop=True)
            tc = ins
            self.s_te.wait(ve, tc)
            Cg = self.C_all[:, tg * 64:(tg + 1) * 64]
            ve.tensor_copy(Cg, self.p_o[:, 512:576])
            vch = ve.tensor_scalar(self.negC_all[:, tg * 64:(tg + 1) * 64],
                                   self.p_o[:, 512:576], -1.0, None, ALU.mult)
            self.s_ve.wait(te, vch)
            for cc in range(NG):
                ins = self.tet(self.p_e[0:H, cc * CH:(cc + 1) * CH],
                               Cg[:, cc * H:(cc + 1) * H], self.ID_F)
            tt = ins
            self.s_te.wait(ve, tt)
            tot = self.Ballc[:, NCH:NCH + 1]
            for cc in range(NG):
                c = tg * NG + cc
                ve.tensor_copy(self.Ballc[:, c:c + 1], tot)
                ve.tensor_copy(self.cCcol[:, c:c + 1],
                               self.p_e[0:H, cc * CH + CH - 1:cc * CH + CH])
                vbk = ve.tensor_add(tot, tot, self.cCcol[:, c:c + 1])
            lastz[tg] = tz
            self._p1ve[tg] = vbk
            if tg + 2 < NG:
                p1_load(tg + 2)

    def load_weight(self, w_dram, slot, cols=HID, after_te=None):
        po = self.po
        if after_te is not None:
            self.s_te.wait(po, after_te)
        self.s_wl.wait(po)
        dst = self.WBUF[:, slot * 8192: slot * 8192 + DMT * cols]
        ins = po.dma_start(
            out=dst.rearrange("p (a f) -> p a f", f=cols),
            in_=w_dram.rearrange("(a p) f -> p a f", p=128))
        return self.s_wl.dma(ins)

    def wtile(self, slot, a, m, w=128, cols=HID):
        base = slot * 8192 + a * cols + m * w
        return self.WBUF[:, base: base + w]

    def spill4(self, dram, m, tg, src):
        dst = dram.rearrange("(a c p) t -> p a c t", p=128, c=NCH)[
            :, m:m + 1, tg * NG:(tg + 1) * NG, :]
        s4 = src.rearrange("p (x c t) -> p x c t", x=1, t=CH)
        self.s_sp.wait(self.sy)
        return self.s_sp.dma(self.sy.dma_start(out=dst, in_=s4))

    def phase1b(self):
        sy, te, ve, se, po = self.sy, self.te, self.ve, self.se, self.po
        se.e.preload_act_table(AF.Sigmoid)
        gS = self.ARE[:, 16384:32768]
        ogS = self.ARE[:, 0:16384]
        ld_g = self.load_weight(self.WgT, 0)
        ld_k = self.load_weight(self.WkT, 1)
        psums = [self.p_a, self.p_b, self.p_c, self.p_d]
        self.s_ve.wait(te)
        self.s_se.wait(te)

        marks = {}
        for i in range(HT * NG):
            m, tg = divmod(i, NG)
            p = psums[i % 4]
            if i >= 4:
                self.s_se.wait(te, marks[i - 4])
            if i == 0:
                self.s_wl.wait(te, ld_g)
            for a in range(DMT):
                ins = te.matmul(p[:, 0:TG], self.wtile(0, a, m),
                                self.XN[:, a * R + tg * TG:a * R + (tg + 1) * TG],
                                start=(a == 0), stop=(a == DMT - 1))
            tm = self.s_te.inc(ins)
            self.s_te.wait(se, tm)
            marks[i] = se.activation(gS[:, m * R + tg * TG: m * R + (tg + 1) * TG], p[:, 0:TG], AF.Sigmoid)
        te_g_done = self.s_te.n

        ld_q = self.load_weight(self.WqT, 0, after_te=te_g_done)
        kge = {}
        spills = {}
        for i in range(HT * NG):
            m, tg = divmod(i, NG)
            p = psums[i % 4]
            if i >= 4:
                self.s_ve.wait(te, kge[i - 4])
            elif i == 0:
                self.s_se.wait(te, marks[HT * NG - 1])
            if i == 0:
                self.s_wl.wait(te, ld_k)
            for a in range(DMT):
                ins = te.matmul(p[:, 0:TG], self.wtile(1, a, m),
                                self.XN[:, a * R + tg * TG:a * R + (tg + 1) * TG],
                                start=(a == 0), stop=(a == DMT - 1))
            tm = self.s_te.inc(ins)
            self.s_te.wait(ve, tm)
            if i >= 4:
                self.s_sp.wait(ve, spills[i - 4])
            dst = self.SCB[:, (i % 4) * TG:(i % 4) * TG + TG]
            kge[i] = ve.tensor_mul(dst, p[:, 0:TG], gS[:, m * R + tg * TG: m * R + (tg + 1) * TG])
            self.s_ve.wait(sy, kge[i])
            spills[i] = self.spill4(self.kgT_d, m, tg, dst)
        te_k_done = self.s_te.n

        ld_v = self.load_weight(self.WvT, 1, after_te=te_k_done)
        qe = {}
        for i in range(HT * NG):
            m, tg = divmod(i, NG)
            p = psums[i % 4]
            if i >= 4:
                self.s_ve.wait(te, qe[i - 4])
            elif i == 0:
                self.s_ve.wait(te, kge[HT * NG - 1])
            if i == 0:
                self.s_wl.wait(te, ld_q)
            for a in range(DMT):
                ins = te.matmul(p[:, 0:TG], self.wtile(0, a, m),
                                self.XN[:, a * R + tg * TG:a * R + (tg + 1) * TG],
                                start=(a == 0), stop=(a == DMT - 1))
            tm = self.s_te.inc(ins)
            self.s_te.wait(ve, tm)
            self.s_sp.wait(ve, spills[i - 4] if i >= 4 else spills[HT * NG - 4 + i])
            dst = self.SCB[:, (i % 4) * TG:(i % 4) * TG + TG]
            qe[i] = ve.tensor_copy(dst, p[:, 0:TG])
            self.s_ve.wait(sy, qe[i])
            spills[i] = self.spill4(self.qT_d, m, tg, dst)
        te_q_done = self.s_te.n

        ld_og = self.load_weight(self.WogT, 0, after_te=te_q_done)
        vev = {}
        for i in range(NCH * 2):
            c, vb = divmod(i, 2)
            p = psums[i % 4]
            if i >= 4:
                self.s_ve.wait(te, vev[i - 4])
            elif i == 0:
                self.s_ve.wait(te, qe[HT * NG - 1])
            if i == 0:
                self.s_wl.wait(te, ld_v)
            for a in range(DMT):
                ins = te.matmul(p[:, 0:TG],
                                self.XN[:, a * R + c * CH: a * R + (c + 1) * CH],
                                self.WBUF[:, 8192 + a * HID + vb * TG: 8192 + a * HID + (vb + 1) * TG],
                                start=(a == 0), stop=(a == DMT - 1))
            tm = self.s_te.inc(ins)
            self.s_te.wait(ve, tm)
            self.s_sp.wait(ve, spills[i - 4] if i >= 4 else spills[HT * NG - 4 + i])
            dst = self.SCB[:, (i % 4) * TG:(i % 4) * TG + TG]
            vev[i] = ve.tensor_copy(dst, p[:, 0:TG])
            self.s_ve.wait(sy, vev[i])
            self.s_sp.wait(sy)
            spills[i] = self.s_sp.dma(sy.dma_start(
                out=self.vN_d[c * 128:(c + 1) * 128, vb * TG:(vb + 1) * TG], in_=dst))
        self.te_v_done = self.s_te.n

        oge = {}
        for i in range(HT * NG):
            m, tg = divmod(i, NG)
            p = psums[i % 4]
            if i >= 4:
                self.s_se.wait(te, oge[i - 4])
            elif i == 0:
                self.s_ve.wait(te, vev[NCH * 2 - 1])
            if i == 0:
                self.s_wl.wait(te, ld_og)
            for a in range(DMT):
                ins = te.matmul(p[:, 0:TG], self.wtile(0, a, m),
                                self.XN[:, a * R + tg * TG:a * R + (tg + 1) * TG],
                                start=(a == 0), stop=(a == DMT - 1))
            tm = self.s_te.inc(ins)
            self.s_te.wait(se, tm)
            oge[i] = se.activation(ogS[:, m * R + tg * TG: m * R + (tg + 1) * TG], p[:, 0:TG], AF.Sigmoid)


    def phase2a(self):
        sy, te, ve, se, po = self.sy, self.te, self.ve, self.se, self.po
        self.preload_set("natural_log_exp_and_others")
        self._aspl = {}
        KGC = {0: self.SCB[:, 0:1024], 1: self.OT[:, 1024:2048]}
        VNC = {0: self.SCB[:, 1024:2048], 1: self.OT[:, 2048:3072]}
        KHN = self.OT[:, 0:1024]

        SC16A = self.smalls[0:16, 0:256]
        ve.tensor_tensor(SC16A.rearrange("k (c h) -> k c h", h=H),
                         self.i16f[:].rearrange("k (x h) -> k x h", x=1).broadcast_to([16, NCH, H]),
                         self.cCcol[:].rearrange("k (c x) -> k c x", x=1).broadcast_to([16, NCH, H]),
                         ALU.mult)
        vsc = self.s_ve.n
        self.s_ve.wait(te, vsc)
        tca = self.tem(self.p_e[:, 0:256], self.ones16f[:], SC16A, start=True, stop=True)
        self.s_te.wait(ve, tca)
        DIFA = self.SCR[:, 512:768]
        EXPD = self.SCR[:, 768:1024]
        vda = ve.tensor_tensor(DIFA, self.p_e[:, 0:256], self.negC_all[:], ALU.add)
        self.s_ve.wait(se, vda)
        sea = se.activation(EXPD, DIFA, AF.Exp)
        e16a = self.smalls[0:16, 488:504] if False else self.smalls[0:16, 100:116]
        sca = se.activation(e16a, self.cCcol[:], AF.Exp)
        self.s_se.wait(ve, sca)
        E16BA = self.smallsb[0:16, 0:256]
        veb = ve.tensor_tensor(
            E16BA.rearrange("k (c h) -> k c h", h=H),
            self.i16b[:].rearrange("k (x h) -> k x h", x=1).broadcast_to([16, NCH, H]),
            e16a.rearrange("k (c x) -> k c x", x=1).broadcast_to([16, NCH, H]), ALU.mult)

        lda = {}
        tcoh = {}

        def a1_loads(c):
            p = c % 2
            if c >= 2:
                self.s_te.wait(sy, tcoh[c - 2])
            l1 = self.qdma(sy, self.s_x,
                           KGC[p].rearrange("p (a x t) -> p a x t", x=1, t=CH),
                           self.kgT_d.rearrange("(a c p) t -> p a c t", p=128, c=NCH)[:, :, c:c + 1, :])
            l2 = self.qdma(sy, self.s_x, VNC[p], self.vN_d[c * 128:(c + 1) * 128, :])
            lda[c] = (l1, l2)

        self.s_te.wait(sy)
        self.s_sp.wait(sy)
        a1_loads(0)
        a1_loads(1)

        scm = 0
        for c in range(NCH):
            p = c % 2
            sl = E16BA[0:16, c * H:(c + 1) * H]
            rhs_e = sl.rearrange("k (a b) -> k a b", b=2)[:, :, 0:1].broadcast_to([16, 8, 64])
            rhs_o = sl.rearrange("k (a b) -> k a b", b=2)[:, :, 1:2].broadcast_to([16, 8, 64])
            self.s_ve.wait(te, max(veb, scm))
            self.tem(self.p_d[0:64, 0:512], self.ones16b[:, 0:64], rhs_e, start=True, stop=True)
            teb = self.tem(self.p_d[64:128, 0:512], self.ones16b[:, 64:128], rhs_o, start=True, stop=True,
                           tile_position=(0, 64))
            if c >= 2:
                self.s_sp.wait(ve, self._aspl[c - 2])
            vsv = ve.tensor_copy(self.STB2[:, (c % 2) * 512:(c % 2) * 512 + 512],
                                 self.S_master[:])
            self.s_ve.wait(sy, vsv)
            self._aspl[c] = self.qdma(sy, self.s_sp, self.std_d[c * 128:(c + 1) * 128, :],
                                      self.STB2[:, (c % 2) * 512:(c % 2) * 512 + 512])
            if c + 1 < NCH:
                a1_loads(c + 1)
            self.s_x.wait(te, lda[c][0])
            self.s_ve.wait(te, vsv)
            for i in range(8):
                ins = te.transpose(self.p_t[:, i * 128:(i + 1) * 128],
                                   KGC[p][:, i * 128:(i + 1) * 128], self.ID_B)
            ttr = self.s_te.inc(ins)
            self.s_te.wait(ve, ttr)
            self.s_se.wait(ve, sea)
            vkh = ve.tensor_tensor(
                KHN.rearrange("p (h t) -> p h t", t=64),
                self.p_t[:, 0:1024].rearrange("p (h t) -> p h t", t=64),
                EXPD[:, c * H:(c + 1) * H].rearrange("p (h x) -> p h x", x=1).broadcast_to([128, H, 64]),
                ALU.mult)
            self.s_ve.wait(te, vkh)
            self.s_x.wait(te, lda[c][1])
            for h in range(H):
                pb = self.p_b if h % 2 == 0 else self.p_c
                ins = te.matmul(
                    pb[(h % 2) * 64:(h % 2) * 64 + 64,
                       (h // 2) * 64:(h // 2) * 64 + 64],
                    KHN[:, h * 64:(h + 1) * 64], VNC[p][:, h * 64:(h + 1) * 64],
                    start=(h < 2), stop=(h >= H - 2),
                    tile_position=(0, (h % 2) * 64))
            tcoh[c] = tco = self.s_te.inc(ins)
            self.s_te.wait(ve, max(tco, teb))
            scm = ve.tensor_tensor(self.SCR[:, 0:512], self.S_master[:], self.p_d[:, 0:512], ALU.mult)
            ve.tensor_tensor(self.S_master[0:64, :], self.SCR[0:64, 0:512], self.p_b[0:64, 0:512], ALU.add)
            ve.tensor_tensor(self.S_master[64:128, :], self.SCR[64:128, 0:512], self.p_c[64:128, 0:512], ALU.add)
        vbt = ve.tensor_copy(self.smalls[0:16, 224:225], self.Ballc[:, NCH:NCH + 1])
        self.s_ve.wait(te, vbt)
        tbt = self.tet(self.p_e[0:1, 0:16], self.smalls[0:16, 224:225], self.ID_F[0:16, 0:16])
        self.s_te.wait(ve, tbt)
        ve.memset(self.rows[0:1, 0:512], 0.0)
        vrow = ve.tensor_copy(self.rows[0:1, 0:16], self.p_e[0:1, 0:16])
        self.s_ve.wait(sy, vrow)
        self.qdma(sy, self.s_sp, self.cc_in[0:128, :], self.S_master[:])
        self.qdma(sy, self.s_sp, self.cc_in[128:129, :], self.rows[0:1, 0:512])
        self.s_sp.wait(po)
        if SIM1:
            for b in range(8):
                self.s_cc.inc(po.dma_start(out=self.cc_out[b * 129:(b + 1) * 129, :],
                                           in_=self.cc_in[:, :]), 16)
        else:
            ins = po.collective_compute(
                "AllGather", ALU.bypass,
                ins=[self.cc_in.opt()], outs=[self.cc_out.opt()],
                replica_groups=[list(range(N_CORES))])
            self.s_cc.inc(ins, 1)


    def phase2b(self):
        sy, te, ve, se, po = self.sy, self.te, self.ve, self.se, self.po
        A2S = self.A2S
        KG = {0: A2S[:, 0:1024], 1: A2S[:, 2048:3072]}
        QC = {0: A2S[:, 1024:2048], 1: A2S[:, 3072:4096]}
        SEF = {0: A2S[:, 4096:4608], 1: A2S[:, 4608:5120]}
        VN = {0: self.SCB[:, 0:1024], 1: self.SCB[:, 1024:2048]}
        QT = {0: self.ARE[:, 16384:16896], 1: self.ARE[:, 16896:17408]}
        PB = {0: self.ARE[:, 17408:17920], 1: self.ARE[:, 17920:18432]}
        ABUF = self.ARE[:, 18432:18944]
        EXPC = self.ARE[:, 18944:19456]
        ogS = self.ARE[:, 0:16384]
        SQA = self.ARE[:, 16384:16384 + 4096]
        YB = self.ARE[:, 16384 + 4096:16384 + 8192]
        XB2 = self.ARE[:, 16384 + 8192:16384 + 12288]
        SQ2 = self.ARE[:, 16384 + 12288:16384 + 16384]
        Dbuf = self.SCR[:, 512:1024]
        ld_o = self.load_weight(self.WoT, 0, after_te=self.s_te.n)
        self.f0_init()

        ld = {}
        self._tgo = {}; self._vql = {}; self._ogo = {}; self._evac = {}
        self._post_c_ve = 0
        self._post_c_te = 0
        self._spx = 0

        def emit_loads(c):
            p = c % 2
            if c >= 2:
                self.s_te.wait(sy, self._tgo[c - 2])
                self.s_ve.wait(sy, self._vql[c - 2])
            l1 = self.qdma(sy, self.s_x, KG[p].rearrange("p (a x t) -> p a x t", x=1, t=CH),
                           self.kgT_d.rearrange("(a c p) t -> p a c t", p=128, c=NCH)[:, :, c:c + 1, :])
            l2 = self.qdma(sy, self.s_x, QC[p].rearrange("p (a x t) -> p a x t", x=1, t=CH),
                           self.qT_d.rearrange("(a c p) t -> p a c t", p=128, c=NCH)[:, :, c:c + 1, :])
            l3 = self.qdma(sy, self.s_x, VN[p], self.vN_d[c * 128:(c + 1) * 128, :])
            l4 = self.qdma(sy, self.s_x, self.LST[:, p * 512:p * 512 + 512],
                           self.std_d[c * 128:(c + 1) * 128, :])
            ld[c] = (l1, l2, l3, l4)

        self.s_sp.wait(sy)
        self.s_te.wait(sy)
        self.s_ve.wait(sy)
        emit_loads(0)
        emit_loads(1)
        next_ld = 2

        self.s_ve.wait(te)
        te.transpose(self.p_d[:, 0:128], self.C_all[:, 0:128], self.ID_F)
        tt2 = self.tet(self.p_d[:, 128:256], self.C_all[:, 128:256], self.ID_F)
        self.s_te.wait(ve, tt2)
        CT = self.SCR[:, 0:256]
        vct = ve.tensor_copy(CT, self.p_d[:, 0:256])
        self.s_ve.wait(sy, vct)
        self.s_sp.wait(sy)
        self.qdma(sy, self.s_sp, self.CHIA.rearrange("c (h t) -> c h t", t=128)[0:8],
                  CT[:, 0:128])
        rch = self.qdma(sy, self.s_sp, self.CHIA.rearrange("c (h t) -> c h t", t=128)[8:16],
                        CT[:, 128:256])
        self.s_sp.wait(se, rch)
        eBa = self.smalls[0:16, 144:160]
        sba = se.activation(eBa, self.Ballc[0:16, 0:16], AF.Exp)
        self.s_se.wait(ve, sba)
        E16A = self.smallsb[0:16, 0:256]
        vE16 = ve.tensor_tensor(
            E16A.rearrange("k (c h) -> k c h", h=16),
            self.i16b[:].rearrange("k (x h) -> k x h", x=1).broadcast_to([16, 16, 16]),
            eBa.rearrange("k (c x) -> k c x", x=1).broadcast_to([16, 16, 16]), ALU.mult)

        self.s_cc.wait(sy)
        B8 = self.smalls[0:8, 208:224]
        self.qdma(sy, self.s_x, B8,
                  self.cc_out.rearrange("(b r) f -> b r f", r=129)[:, 128:129, 0:16])
        EBF = self.SCR[:, 0:512]
        for b in range(8):
            slot = self.XG[:, (1 + (b % 2)) * 512:(2 + (b % 2)) * 512]
            lb = self.qdma(sy, self.s_x, slot,
                           self.cc_out[b * 129:(b * 129) + 128, :])
            wb = self.wind[:, b:b + 1].broadcast_to([8, 64])
            be = B8.rearrange("k (a b2) -> k a b2", b2=2)[:, :, 0:1].broadcast_to([8, 8, 64])
            bo = B8.rearrange("k (a b2) -> k a b2", b2=2)[:, :, 1:2].broadcast_to([8, 8, 64])
            if b == 0:
                self.s_x.wait(te)
                self.s_ve.wait(te)
            self.tem(self.p_a[0:64, 0:512], wb, be, start=True, stop=True)
            tsb = self.tem(self.p_a[64:128, 0:512], wb, bo, start=True, stop=True,
                           tile_position=(0, 64))
            self.s_te.wait(se, tsb)
            seb = se.activation(EBF, self.p_a[:, 0:512], AF.Exp, bias=self.alog[:, b:b + 1])
            self.s_x.wait(ve, lb)
            self.s_se.wait(ve, seb)
            if b == 0:
                vup = ve.tensor_tensor(self.S_master[:], EBF, slot, ALU.mult)
            else:
                ve.tensor_tensor(self.SCR[:, 512:1024], EBF, slot, ALU.mult)
                vup = ve.tensor_tensor(self.S_master[:], self.S_master[:],
                                       self.SCR[:, 512:1024], ALU.add)
            self.s_ve.wait(te, vup)
            self.s_ve.wait(sy, vup)
        comb_seb = seb
        self._post_c_ve = self.s_ve.n

        vse_c = {}; scrm = {}

        def emit_sef(c):
            p = c % 2
            sl = E16A[0:16, c * 16:(c + 1) * 16]
            rhs_e = sl.rearrange("k (a b) -> k a b", b=2)[:, :, 0:1].broadcast_to([16, 8, 64])
            rhs_o = sl.rearrange("k (a b) -> k a b", b=2)[:, :, 1:2].broadcast_to([16, 8, 64])
            if c == 0:
                self.s_ve.wait(te, vE16)
            else:
                self.s_ve.wait(te, max(scrm[c - 1], self._post_c_ve))
            self.tem(self.p_d[0:64, 0:512], self.ones16b[:, 0:64], rhs_e, start=True, stop=True)
            tse = self.tem(self.p_d[64:128, 0:512], self.ones16b[:, 64:128], rhs_o,
                           start=True, stop=True, tile_position=(0, 64))
            self.s_te.wait(ve, tse)
            if c >= 2:
                self.s_te.wait(ve, self._tgo[c - 2])
            if c == 0:
                self.s_se.wait(ve, comb_seb)
            scrm[c] = ve.tensor_tensor(self.SCR[:, 0:512], self.S_master[:],
                                       self.p_d[:, 0:512], ALU.mult)
            self.s_x.wait(ve, ld[c][3])
            vse_c[c] = ve.tensor_tensor(SEF[p], self.SCR[:, 0:512],
                                        self.LST[:, p * 512:p * 512 + 512], ALU.add)

        emit_sef(0)

        gh = {}

        def emit_front(c, g):
            ga = c * 4 + g; gp = ga % 2; p = c % 2
            if ga == 0:
                self.s_se.wait(te, comb_seb)
                self.s_sp.wait(te, rch)
                self.s_ve.wait(te, vse_c[0])
            else:
                self.s_ve.wait(te, max(gh[ga - 1]["vd"], self._post_c_ve))
            selc = self.i16f[:, c:c + 1].broadcast_to([16, 128])
            te.matmul(self.p_a[:, 0:512], selc,
                      self.CHIA[0:16, g * 512:(g + 1) * 512], start=True, stop=False)
            tma = self.tem(self.p_a[:, 0:512], self.negdiag[:],
                           self.slrep[:].rearrange("p (x t) -> p x t", x=1).broadcast_to([128, 4, CH]),
                           start=False, stop=True)
            if ga > 0:
                self.s_se.wait(te, gh[ga - 1]["sx"])
            tme = self.tem(self.p_e[:, 0:512], selc,
                           self.CHIA[0:16, g * 512:(g + 1) * 512], start=True, stop=True)
            if g == 0:
                self.s_x.wait(te, ld[c][1])
            for hh in range(4):
                h = 4 * g + hh
                pb = self.p_b if hh % 2 == 0 else self.p_c
                ins = te.matmul(pb[:, gp * 256 + (hh // 2) * 128:gp * 256 + (hh // 2) * 128 + 128],
                                KG[p][(h % 2) * 64:(h % 2) * 64 + 64,
                                      (h // 2) * 128:(h // 2) * 128 + 128],
                                QC[p][(h % 2) * 64:(h % 2) * 64 + 64,
                                      (h // 2) * 128:(h // 2) * 128 + 128],
                                start=True, stop=True,
                                tile_position=((h % 2) * 64, 0))
            gh[ga] = {"tma": tma, "tme": tme, "tsp": self.s_te.inc(ins)}

        def emit_chain(c, g):
            ga = c * 4 + g; gp = ga % 2; p = c % 2
            G = gh[ga]
            negb = self.negC_all[:, c * H + 4 * g:c * H + 4 * g + 4].rearrange(
                "p (h x) -> p h x", x=1).broadcast_to([128, 4, 128])
            self.s_te.wait(ve, G["tma"])
            vd = ve.tensor_tensor(Dbuf.rearrange("p (h t) -> p h t", t=128),
                                  self.p_a[:, 0:512].rearrange("p (h t) -> p h t", t=128),
                                  negb, ALU.add)
            self.s_ve.wait(se, vd)
            sa = se.activation(ABUF, Dbuf, AF.Exp)
            self.s_te.wait(se, G["tme"])
            sx = se.activation(EXPC, self.p_e[:, 0:512], AF.Exp)
            self.s_se.wait(ve, sa)
            self.s_te.wait(ve, max(G["tsp"], self._post_c_te))
            if ga >= 2:
                self.s_te.wait(ve, self._ogo[ga - 2])
            a4 = ABUF.rearrange("p (a b t) -> p a b t", a=2, b=2, t=128)
            o4 = PB[gp].rearrange("p (a b t) -> p a b t", a=2, b=2, t=128)
            ve.tensor_tensor(o4[:, :, 0:1, :], a4[:, :, 0:1, :],
                             self.p_b[:, gp * 256:gp * 256 + 256].rearrange(
                                 "p (a x t) -> p a x t", x=1, t=128), ALU.mult)
            vp = ve.tensor_tensor(o4[:, :, 1:2, :], a4[:, :, 1:2, :],
                                  self.p_c[:, gp * 256:gp * 256 + 256].rearrange(
                                      "p (a x t) -> p a x t", x=1, t=128), ALU.mult)
            self.s_se.wait(ve, sx)
            if g == 0:
                self.s_x.wait(ve, ld[c][1])
            qt4 = QT[gp].rearrange("p (a b t) -> p a b t", a=2, b=2, t=128)
            ex4 = EXPC.rearrange("p (a b t) -> p a b t", a=2, b=2, t=128)
            qsrc = QC[p][:, 2 * g * 128:2 * g * 128 + 256].rearrange(
                "p (a x t) -> p a x t", x=1, t=128)
            ve.tensor_tensor(qt4[0:64, :, 0:1, :], qsrc[0:64], ex4[0:64, :, 0:1, :], ALU.mult)
            vq = ve.tensor_tensor(qt4[64:128, :, 1:2, :], qsrc[64:128], ex4[64:128, :, 1:2, :],
                                  ALU.mult)
            G["vd"] = vd; G["sx"] = sx; G["vp"] = vp; G["vq"] = vq

        def emit_O(c, g):
            ga = c * 4 + g; gp = ga % 2; p = c % 2
            G = gh[ga]
            self.s_ve.wait(te, max(G["vp"], G["vq"], vse_c[c]))
            if g == 0:
                self.s_x.wait(te, ld[c][2])
                if c >= 1:
                    self.s_se.wait(te, self._evac[c - 1])
            for hh in range(4):
                h = 4 * g + hh
                te.matmul(self.p_o[(h % 2) * 64:(h % 2) * 64 + 64,
                                   (h // 2) * 128:(h // 2) * 128 + 128],
                          VN[p][:, h * 64:(h + 1) * 64],
                          PB[gp][:, hh * 128:(hh + 1) * 128],
                          start=True, stop=False,
                          tile_position=(0, (h % 2) * 64))
                ins = te.matmul(self.p_o[(h % 2) * 64:(h % 2) * 64 + 64,
                                         (h // 2) * 128:(h // 2) * 128 + 128],
                                SEF[p][(h % 2) * 64:(h % 2) * 64 + 64,
                                       (h // 2) * 64:(h // 2) * 64 + 64],
                                QT[gp][(h % 2) * 64:(h % 2) * 64 + 64,
                                       hh * 128:(hh + 1) * 128],
                                start=False, stop=True,
                                tile_position=((h % 2) * 64, (h % 2) * 64))
            self._ogo[ga] = self.s_te.inc(ins)
            if g == 3:
                self._tgo[c] = self._ogo[ga]
                self._vql[c] = G["vq"]

        def emit_evac(c):
            cc = c % 4
            self.s_te.wait(se, self._tgo[c])
            if c % 4 == 0 and c > 0:
                self.s_ve.wait(se, self._post_c_ve)
            self._evac[c] = se.activation(
                self.OT[:].rearrange("p (m t) -> p m t", t=512)[:, :, cc * 128:(cc + 1) * 128],
                self.p_o[:, 0:1024].rearrange("p (m t) -> p m t", t=128), AF.Copy)

        pend = None
        for gi in range(NG):
            for cc in range(NG):
                c = gi * NG + cc
                if cc == 0:
                    self.s_ve.wait(sy, self._post_c_ve)
                    if self._spx:
                        self.s_sp.wait(sy, self._spx)
                    self._lx = self.qdma(sy, self.s_x,
                                         self.XG[:].rearrange("p (a f) -> p a f", f=TG),
                                         self.xT.rearrange("(a p) t -> p a t", p=128)[:, :, gi * TG:(gi + 1) * TG])
                for g in range(4):
                    emit_front(c, g)
                    emit_chain(c, g)
                    if pend is not None:
                        pc, pg = pend
                        emit_O(pc, pg)
                        if pg == 3:
                            emit_evac(pc)
                            while next_ld < NCH and next_ld <= pc + 2:
                                emit_loads(next_ld)
                                next_ld += 1
                    pend = (c, g)
                if c + 1 < NCH:
                    emit_sef(c + 1)
                self.f0_step(2)
            pc, pg = pend
            emit_O(pc, pg)
            emit_evac(pc)
            pend = None
            while next_ld < NCH and next_ld <= pc + 2:
                emit_loads(next_ld)
                next_ld += 1
            self._c_group(gi, ld_o, ogS, SQA, YB, XB2, SQ2)
            self._post_c_ve = self.s_ve.n
            self._post_c_te = self.s_te.n

    def _c_group(self, gi, ld_o, ogS, SQA, YB, XB2, SQ2):
        sy, te, ve, se, po = self.sy, self.te, self.ve, self.se, self.po
        self.f0_step(4)
        self.s_ve.wait(se)
        for a in range(DMT):
            sq = se.activation(SQA[:, a * 512:(a + 1) * 512],
                               self.OT[:, a * 512:(a + 1) * 512], AF.Square)
        self.s_se.wait(te, sq)
        self.s_ve.wait(te)
        for a in range(DMT):
            ins = te.matmul(self.p_a[0:1, 0:512], self.ONES_COL_B,
                            SQA[:, a * 512:(a + 1) * 512],
                            start=(a == 0), stop=(a == DMT - 1))
        tst = self.s_te.inc(ins)
        self.s_te.wait(se, tst)
        rrow = self.rows[0:1, 0:512]
        se.activation(rrow, self.p_a[0:1, 0:512], AF.Ln,
                      scale=1.0 / HID, bias=self.B_RMSEPS[0:1, :])
        srs = se.activation(rrow, rrow, AF.Exp, scale=-0.5)
        self.s_se.wait(te, srs)
        trb = self.tem(self.p_c[:, 0:512], self.ONES_ROW_F, rrow, start=True, stop=True)
        self.s_te.wait(ve, trb)
        for a in range(DMT):
            ve.scalar_tensor_tensor(YB[:, a * 512:(a + 1) * 512],
                                    self.OT[:, a * 512:(a + 1) * 512],
                                    self.normw_sb[:, a:a + 1],
                                    self.p_c[:, 0:512], ALU.mult, ALU.mult)
            vy = ve.tensor_tensor(YB[:, a * 512:(a + 1) * 512],
                                  YB[:, a * 512:(a + 1) * 512],
                                  ogS[:, a * R + gi * 512:a * R + gi * 512 + 512], ALU.mult)
        self.s_ve.wait(te, vy)
        self.s_wl.wait(te, ld_o)
        self.s_x.wait(ve, self._lx)
        prev_m = {}
        for m in range(DMT):
            pp = self.p_o[:, 0:512] if m % 2 == 0 else self.p_o[:, 512:1024]
            if m >= 2:
                self.s_ve.wait(te, prev_m[m - 2])
            for k in range(DMT):
                ins = te.matmul(pp, self.wtile(0, k, m), YB[:, k * 512:(k + 1) * 512],
                                start=(k == 0), stop=(k == DMT - 1))
            tm = self.s_te.inc(ins)
            self.s_te.wait(ve, tm)
            prev_m[m] = ve.tensor_tensor(self.XG[:, m * 512:(m + 1) * 512],
                                         self.XG[:, m * 512:(m + 1) * 512], pp, ALU.add)
        self.s_ve.wait(sy, prev_m[DMT - 1])
        spx = self.qdma(sy, self.s_sp,
                        self.x2_d.rearrange("(a p) t -> p a t", p=128)[:, :, gi * TG:(gi + 1) * TG],
                        self.XG[:].rearrange("p (a f) -> p a f", f=TG))
        self._spx = spx
        self.s_x.wait(se, self._lx)
        self.s_ve.wait(se, prev_m[DMT - 1])
        for a in range(DMT):
            se.activation(XB2[:, a * 512:(a + 1) * 512], self.XG[:, a * 512:(a + 1) * 512],
                          AF.Copy)
            sml = se.activation(SQ2[:, a * 512:(a + 1) * 512], self.XG[:, a * 512:(a + 1) * 512],
                                AF.Square)
        self.s_se.wait(te, sml)
        for a in range(DMT):
            ins = te.matmul(self.p_a[0:1, 0:512], self.ONES_COL_B,
                            XB2[:, a * 512:(a + 1) * 512], start=(a == 0), stop=(a == DMT - 1))
        self.s_te.inc(ins)
        for a in range(DMT):
            ins = te.matmul(self.p_b[0:1, 0:512], self.ONES_COL_B,
                            SQ2[:, a * 512:(a + 1) * 512], start=(a == 0), stop=(a == DMT - 1))
        tml = self.s_te.inc(ins)
        self.s_te.wait(ve, tml)
        rm = self.rows[0:1, 0:512]; rvar = self.rows[0:1, 512:1024]
        rr = self.rows[0:1, 1024:1536]; rmr = self.rows[0:1, 1536:2048]
        ve.tensor_scalar(rm, self.p_a[0:1, 0:512], 1.0 / DM, None, ALU.mult)
        ve.tensor_scalar(rvar, self.p_b[0:1, 0:512], 1.0 / DM, None, ALU.mult)
        ve.tensor_mul(rr, rm, rm)
        vst2 = ve.tensor_sub(rvar, rvar, rr)
        self.s_ve.wait(se, vst2)
        se.activation(rvar, rvar, AF.Ln, bias=self.B_LNEPS[0:1, :])
        sst2 = se.activation(rr, rvar, AF.Exp, scale=-0.5)
        self.s_se.wait(ve, sst2)
        vrb2 = ve.tensor_mul(rmr, rm, rr)
        self.s_ve.wait(te, vrb2)
        self.tem(self.p_c[:, 0:512], self.ONES_ROW_F, rr, start=True, stop=True)
        tbc2 = self.tem(self.p_d[:, 0:512], self.ONES_ROW_F, rmr, start=True, stop=True)
        self.s_te.wait(ve, tbc2)
        self.s_sp.wait(ve, spx)
        for a in range(DMT):
            ve.tensor_mul(self.SCR[:, 0:512], self.XG[:, a * 512:(a + 1) * 512], self.p_c[:, 0:512])
            vln2 = ve.tensor_sub(self.XG[:, a * 512:(a + 1) * 512], self.SCR[:, 0:512],
                                 self.p_d[:, 0:512])
        rmx = self.SCR[:, 0:512]; rmn = self.SCR[:, 512:1024]
        ve.tensor_copy(rmx, self.XG[:, 0:512])
        ve.tensor_copy(rmn, self.XG[:, 0:512])
        for a in range(1, DMT):
            ve.tensor_tensor(rmx, rmx, self.XG[:, a * 512:(a + 1) * 512], ALU.max)
            ve.tensor_tensor(rmn, rmn, self.XG[:, a * 512:(a + 1) * 512], ALU.min)
        ve.tensor_scalar(rmn, rmn, -1.0, None, ALU.mult)
        vtr = ve.tensor_tensor(rmx, rmx, rmn, ALU.max)
        self.s_ve.wait(te, vtr)
        for ct in range(4):
            ins = te.transpose(self.p_e[:, ct * 128:(ct + 1) * 128],
                               rmx[:, ct * 128:(ct + 1) * 128], self.ID_F)
        ttd = self.s_te.inc(ins)
        self.s_te.wait(ve, ttd)
        red = self.smalls[:, 0:4]
        ve.tensor_reduce(red, self.p_e[:, 0:512].rearrange("p (a b) -> p a b", b=128),
                         AX.X, ALU.max)
        vrd = ve.tensor_scalar(red, red, Q_EPS, None, ALU.max)
        self.s_ve.wait(te, vrd)
        tred = self.tet(self.p_e[0:4, 256:384], red, self.ID_F)
        self.s_te.wait(ve, tred)
        vev = ve.tensor_copy(self.st16[0:4, 0:128], self.p_e[0:4, 256:384])
        self.s_ve.wait(sy, vev)
        self.qdma(sy, self.s_sp, self.mx_d[gi * 4:(gi + 1) * 4, :], self.st16[0:4, 0:128])
        mq = self.qdma(sy, self.s_sp, self.SCR[0:1, 512:1024], self.st16[0:4, 0:128])
        self.s_sp.wait(se, mq)
        srow = self.rows[0:1, 0:512]
        se.activation(srow, self.SCR[0:1, 512:1024], AF.Ln)
        sq2 = se.activation(srow, srow, AF.Exp, scale=-1.0, bias=self.B_LN127[0:1, :])
        self.s_se.wait(te, sq2)
        tqb = self.tem(self.p_c[:, 0:512], self.ONES_ROW_F, srow, start=True, stop=True)
        self.s_te.wait(ve, tqb)
        i32v = self.SCR[:, 512:1024].bitcast(I32)
        for a in range(DMT):
            ve.tensor_tensor(i32v, self.XG[:, a * 512:(a + 1) * 512],
                             self.p_c[:, 0:512], ALU.mult)
            vq2 = ve.tensor_copy(self.XN[:, a * R + gi * 512:a * R + gi * 512 + 512], i32v)


    def f0_init(self):
        self._f0_steps = []
        for wi, (wd, na, fw) in enumerate([(self.WTg, 8, 4096), (self.WTu, 8, 4096),
                                           (self.WTd, 32, 1024)]):
            for a in range(na):
                if fw > 2048:
                    self._f0_steps.append((wi, wd, a, 0, 2048))
                    self._f0_steps.append((wi, wd, a, 2048, 2048))
                else:
                    self._f0_steps.append((wi, wd, a, 0, fw))
        self._f0_i = 0
        self._f0_l = 0
        self._f0_ld = {}
        self._f0_sab = {}
        self._f0_pm = None
        self.ve.memset(self.smalls[:, 230:233], 0.0)
        self.f0_load()

    def f0_load(self):
        sy, se, po = self.sy, self.se, self.po
        slot1f = self.WBUF[:, 8192:16384].bitcast(F32)
        if self._f0_l >= len(self._f0_steps):
            return
        i = self._f0_l
        self._f0_l += 1
        wi, wd, a, co, fw = self._f0_steps[i]
        half = i % 2
        dst = slot1f[:, half * 2048:half * 2048 + fw].rearrange("p (x f) -> p x f", x=1)
        src = wd.rearrange("(a p) f -> p a f", p=128)[:, a:a + 1, co:co + fw]
        if i < 2:
            self.s_te.wait(po, self.te_v_done)
        else:
            self.s_se.wait(po, self._f0_sab[i - 2])
        self._f0_ld[i] = self.qdma(po, self.s_wl, dst, src)

    def f0_step(self, n=1):
        sy, te, ve, se, po = self.sy, self.te, self.ve, self.se, self.po
        slot1f = self.WBUF[:, 8192:16384].bitcast(F32)
        red = self.smalls[:, 233:234]
        for _ in range(n):
            if self._f0_i >= len(self._f0_steps):
                return
            self.f0_load()
            i = self._f0_i
            self._f0_i += 1
            wi, wd, a, co, fw = self._f0_steps[i]
            half = i % 2
            sl = slot1f[:, half * 2048:half * 2048 + fw]
            self.s_wl.wait(se, self._f0_ld[i])
            if i > 0:
                self.s_ve.wait(se, self._f0_pm)
            self._f0_sab[i] = se.activation(sl, sl, AF.Abs, accum_out=red)
            self.s_se.wait(ve, self._f0_sab[i])
            self._f0_pm = ve.tensor_tensor(self.smalls[:, 230 + wi:231 + wi],
                                           self.smalls[:, 230 + wi:231 + wi], red, ALU.add)

    def phase3_f0(self):
        sy, te, ve, se, po = self.sy, self.te, self.ve, self.se, self.po
        self.f0_step(len(self._f0_steps))
        acc = self.smalls[:, 230:233]
        pm = self._f0_pm
        self.s_ve.wait(te, pm)
        for wi in range(3):
            self.tem(self.p_e[0:1, 100 + wi:101 + wi], self.ONES_COL_F, acc[:, wi:wi + 1],
                     start=True, stop=True)
        tf0 = self.s_te.n
        self.s_te.wait(ve, tf0)
        mrow = self.smalls[0:1, 236:239]
        ve.tensor_scalar(mrow, self.p_e[0:1, 100:103], 1.0 / (DM * FF), None, ALU.mult)
        vm0 = ve.tensor_scalar(mrow, mrow, Q_EPS, None, ALU.max)
        thr = self.smalls[0:1, 240:243]; nthr = self.smalls[0:1, 243:246]
        ve.tensor_scalar(thr, mrow, 0.5, None, ALU.mult)
        vth = ve.tensor_scalar(nthr, thr, -1.0, None, ALU.mult)
        sgm = self.smalls[0:1, 246:247]; md127 = self.smalls[0:1, 247:248]
        ve.tensor_tensor(sgm, mrow[0:1, 0:1], mrow[0:1, 0:1], ALU.mult)
        ve.tensor_tensor(sgm, sgm, mrow[0:1, 1:2], ALU.mult)
        ve.tensor_scalar(sgm, sgm, 1.0 / (127.0 ** 3), None, ALU.mult)
        vsc = ve.tensor_scalar(md127, mrow[0:1, 2:3], 1.0 / 127.0, None, ALU.mult)
        self.s_ve.wait(te, max(vth, vsc))
        for wi in range(3):
            te.matmul(self.p_e[:, 104 + wi:105 + wi], self.ONES_ROW_F, thr[0:1, wi:wi + 1],
                      start=True, stop=True)
            ins = te.matmul(self.p_e[:, 108 + wi:109 + wi], self.ONES_ROW_F, nthr[0:1, wi:wi + 1],
                            start=True, stop=True)
        tthr = self.s_te.inc(ins)
        self.s_te.wait(ve, tthr)
        self.thrc = self.smalls[:, 240:243]
        self.nthrc = self.smalls[:, 243:246]
        ve.tensor_copy(self.thrc, self.p_e[:, 104:107])
        ve.tensor_copy(self.nthrc, self.p_e[:, 108:111])
        self.sgm = sgm; self.md127 = md127

    def ternarize(self, eng, wsrc_f32, tern_dst, tmp, wi, width):
        eng.tensor_scalar(tmp[:, 0:width], wsrc_f32[:, 0:width], self.thrc[:, wi:wi + 1],
                          None, ALU.is_gt)
        eng.tensor_scalar(tern_dst[:, 0:width], wsrc_f32[:, 0:width], self.nthrc[:, wi:wi + 1],
                          None, ALU.is_lt)
        return eng.tensor_tensor(tern_dst[:, 0:width], tmp[:, 0:width], tern_dst[:, 0:width],
                                 ALU.subtract)

    def ternarize4(self, eng, wsrc, dst, tmp, wi):
        eng.tensor_scalar(tmp, wsrc, self.thrc[:, wi:wi + 1], None, ALU.is_gt)
        eng.tensor_scalar(dst, wsrc, self.nthrc[:, wi:wi + 1], None, ALU.is_lt)
        return eng.tensor_tensor(dst, tmp, dst, ALU.subtract)

    def fd_step(self, ternd, tstg, ttmp):
        sy, ve = self.sy, self.ve
        k, half = divmod(self._fd_i, 2)
        if k >= FFT:
            return
        self._fd_i += 1
        if self._fd_t is not None:
            self.s_ve.wait(sy, self._fd_t)
        lw = self.qdma(sy, self.s_x, tstg.rearrange("p (x y f) -> p x y f", x=1, y=1),
                       self.WTd.rearrange("(a p) (h f) -> p a h f", p=128, h=2)[:, k:k + 1, half:half + 1, :])
        self.s_x.wait(ve, lw)
        base = k * 1024 + half * 512
        self._fd_t = self.ternarize(ve, tstg, ternd[:, base:base + 512], ttmp, 2, 512)

    def phase3_f1(self):
        sy, te, ve, se, po = self.sy, self.te, self.ve, self.se, self.po
        slot1f = self.WBUF[:, 8192:16384].bitcast(F32)
        slot0f = self.WBUF[:, 0:8192].bitcast(F32)
        terng = self.A2S[:, 0:4096]
        ternu = self.XG[:].bitcast(BF16)[:, 4096:8192]
        hst = self.SCB[:, 1536:2048]
        ttmp = self.OT[:, 0:4096]
        ternd = self.ARE[:, 0:32768]
        tstg = self.SCB[:, 0:1024].bitcast(F32)
        RMX = self.XG[:, 0:2048]
        self.s_te.wait(ve)
        self.s_se.wait(ve)
        vmn = ve.memset(RMX, 0.0)
        self._fd_i = 0
        self._fd_t = None
        prev_h = None
        prev_ev = {0: None, 1: None}
        def ld_gu(ig):
            lg = self.qdma(sy, self.s_x, slot1f.rearrange("p (a f) -> p a f", f=512),
                           self.WTg.rearrange("(a p) f -> p a f", p=128)[:, :, ig * 512:(ig + 1) * 512])
            lu = self.qdma(self.se.e, self.s_sc, slot0f.rearrange("p (a f) -> p a f", f=512),
                           self.WTu.rearrange("(a p) f -> p a f", p=128)[:, :, ig * 512:(ig + 1) * 512])
            return lg, lu

        self.s_po.wait(sy)
        self.s_ve.wait(sy)
        self.s_te.wait(sy)
        self.s_se.wait(te)
        self.s_ve.wait(se)
        lds = {0: ld_gu(0)}
        prev_se = {0: None, 1: None}
        prev_vh = None

        def sl4(buf, it):
            return buf.rearrange("p (a f) -> p a f", f=512)[:, :, it * 128:(it + 1) * 128]

        for ig in range(DMT):
            lg, lu = lds[ig]
            self.s_x.wait(ve, lg)
            self.ternarize(ve, slot1f, terng, ttmp, 0, 4096)
            self.s_sc.wait(ve, lu)
            vt = self.ternarize(ve, slot0f, ternu, ttmp, 1, 4096)
            if ig + 1 < DMT:
                self.s_ve.wait(sy, vt)
                self.s_ve.wait(self.se.e, vt)
                lds[ig + 1] = ld_gu(ig + 1)
            self.s_ve.wait(te, vt)
            for tg in range(NG):
                for it in range(4):
                    self.fd_step(ternd, tstg, ttmp)
                    par = it % 2
                    pg = self.p_a if par == 0 else self.p_c
                    pu = self.p_b if par == 0 else self.p_d
                    if prev_ev[par] is not None:
                        self.s_ve.wait(te, prev_ev[par])
                        self.s_se.wait(te, prev_se[par])
                    for a2 in range(DMT):
                        te.matmul(pg[:, 0:512], terng[:, a2 * 512 + it * 128:a2 * 512 + it * 128 + 128],
                                  self.XN[:, a2 * R + tg * TG:a2 * R + (tg + 1) * TG],
                                  start=(a2 == 0), stop=(a2 == DMT - 1))
                    for a2 in range(DMT):
                        ins = te.matmul(pu[:, 0:512], ternu[:, a2 * 512 + it * 128:a2 * 512 + it * 128 + 128],
                                        self.XN[:, a2 * R + tg * TG:a2 * R + (tg + 1) * TG],
                                        start=(a2 == 0), stop=(a2 == DMT - 1))
                    tgu = self.s_te.inc(ins)
                    self.s_te.wait(ve, tgu)
                    self.s_te.wait(se, tgu)
                    if prev_h is not None:
                        self.s_sc.wait(ve, prev_h)
                    if prev_vh is not None:
                        self.s_ve.wait(se, prev_vh)
                    sr = se.activation(self.SCR[:, 0:512], pg[:, 0:512], AF.Relu)
                    sqm = se.activation(self.SCR[:, 512:1024], self.SCR[:, 0:512], AF.Square)
                    self.s_se.wait(ve, sqm)
                    vh = ve.tensor_tensor(hst, self.SCR[:, 512:1024], pu[:, 0:512], ALU.mult)
                    self.s_ve.wait(se, vh)
                    sab = se.activation(self.SCR[:, 0:512], hst, AF.Abs)
                    self.s_se.wait(ve, sab)
                    vhm = ve.tensor_tensor(RMX[:, tg * 512:(tg + 1) * 512], RMX[:, tg * 512:(tg + 1) * 512],
                                           self.SCR[:, 0:512], ALU.max)
                    prev_ev[par] = vh
                    prev_se[par] = sr
                    prev_vh = vhm
                    self.s_ve.wait(self.se.e, vh)
                    prev_h = self.qdma(self.se.e, self.s_sc,
                                       self.h_d[(ig * 4 + it) * 128:(ig * 4 + it + 1) * 128,
                                                tg * TG:(tg + 1) * TG], hst)
        self.s_ve.wait(sy)
        self.s_sc.wait(sy)
        self._f2lh0 = self.qdma(sy, self.s_x,
                                self.WBUF[:, 0:16384].rearrange("p (k t) -> p k t", t=512),
                                self.h_d.rearrange("(k p) t -> p k t", p=128)[:, :, 0:TG])
        for tg in range(NG):
            rx = RMX[:, tg * 512:(tg + 1) * 512]
            self.s_ve.wait(te)
            for ct in range(4):
                ins = te.transpose(self.p_e[:, ct * 128:(ct + 1) * 128],
                                   rx[:, ct * 128:(ct + 1) * 128], self.ID_F)
            ttd = self.s_te.inc(ins)
            self.s_te.wait(ve, ttd)
            red = self.smalls[:, 0:4]
            vrd = ve.tensor_reduce(red, self.p_e[:, 0:512].rearrange("p (a b) -> p a b", b=128),
                                   AX.X, ALU.max)
            self.s_ve.wait(te, vrd)
            tred = self.tet(self.p_e[0:4, 256:384], red, self.ID_F)
            self.s_te.wait(ve, tred)
            self.s_sp.wait(ve)
            vev = ve.tensor_copy(self.st16[0:4, 0:128], self.p_e[0:4, 256:384])
            self.s_ve.wait(sy, vev)
            self.qdma(sy, self.s_sp, self.mx_d[16 + tg * 4:16 + (tg + 1) * 4, :],
                      self.st16[0:4, 0:128])

    def phase3_f2(self):
        sy, te, ve, se, po = self.sy, self.te, self.ve, self.se, self.po
        ternd = self.ARE[:, 0:32768]
        HB = {0: self.WBUF[:, 0:16384], 1: self.XN[:]}
        OTF = self.OT[:].bitcast(F32)
        SCBF = self.SCB[:].bitcast(F32)
        S1B = {0: SCBF[:, 0:512], 1: SCBF[:, 512:1024]}
        DSC = {0: self.SCR[:, 0:512], 1: self.SCR[:, 512:1024]}
        i32v = OTF[:, 512:1024].bitcast(I32)
        OSTG = OTF[:, 0:512]
        accs1 = [self.p_a[:, 0:512], self.p_b[:, 0:512], self.p_c[:, 0:512],
                 self.p_e[:, 0:512], self.p_o[:, 0:512], self.p_o[:, 512:1024]]
        self.s_sc.wait(sy)
        self.s_te.wait(sy)
        self.s_ve.wait(sy)
        ldh = {}; ldx = {}; rmc = {}; smath = {}; vmath = {}
        self._f2t8 = {}; self._f2vq = {}; self._f2vo = {}

        def emit_lh(tg):
            p = tg % 2
            if tg >= 2:
                self.s_te.wait(sy, self._f2t8[tg - 2])
                self.s_ve.wait(sy, self._f2vq[tg - 2])
            ldh[tg] = self.qdma(sy, self.s_x, HB[p].rearrange("p (k t) -> p k t", t=512),
                                self.h_d.rearrange("(k p) t -> p k t", p=128)[:, :, tg * TG:(tg + 1) * TG])

        def emit_lx2(tg):
            if tg >= 1:
                self.s_ve.wait(sy, self._f2vo[tg - 1])
            ldx[tg] = self.qdma(sy, self.s_x, self.XG[:].rearrange("p (a f) -> p a f", f=TG),
                                self.x2_d.rearrange("(a p) t -> p a t", p=128)[:, :, tg * TG:(tg + 1) * TG])

        def emit_rows(tg):
            p = tg % 2
            self.s_sp.wait(sy)
            if tg >= 1:
                self.s_ve.wait(sy, vmath[tg - 1])
                self.s_se.wait(sy, smath[tg - 1])
            lr1 = self.qdma(sy, self.s_x, self.st16[0:4, 0:128], self.mx_d[tg * 4:(tg + 1) * 4, :])
            lr2 = self.qdma(sy, self.s_x, self.st16[0:4, 128:256], self.mx_d[16 + tg * 4:16 + (tg + 1) * 4, :])
            self.s_x.wait(sy, lr2)
            r1 = self.qdma(sy, self.s_sp, self.rows[0:1, 0:512], self.st16[0:4, 0:128])
            r2 = self.qdma(sy, self.s_sp, self.rows[0:1, 512:1024], self.st16[0:4, 128:256])
            self.s_sp.wait(ve, r2)
            mxcr = self.rows[0:1, 0:512]; M1r = self.rows[0:1, 512:1024]
            Fr = self.rows[0:1, 1024:1536]; Dr = self.rows[0:1, 1536:2048]
            ve.tensor_tensor(Fr, mxcr, mxcr, ALU.mult)
            ve.tensor_tensor(Fr, Fr, mxcr, ALU.mult)
            vF = ve.tensor_scalar(Fr, Fr, self.sgm, None, ALU.mult)
            self.s_ve.wait(se, vF)
            sF = se.activation(Dr, Fr, AF.Ln)
            self.s_se.wait(ve, sF)
            veF = ve.tensor_scalar(Dr, Dr, -1.0, None, ALU.mult)
            self.s_ve.wait(se, veF)
            sE = se.activation(Dr, Dr, AF.Exp, bias=self.B_LNQEPS[0:1, :])
            self.s_se.wait(ve, sE)
            vD = ve.tensor_tensor(Dr, M1r, Dr, ALU.max)
            self.s_ve.wait(se, vD)
            s1 = se.activation(mxcr, Dr, AF.Ln)
            smath[tg] = se.activation(mxcr, mxcr, AF.Exp, scale=-1.0,
                                      bias=self.B_LN127[0:1, :])
            self.s_se.wait(ve, smath[tg])
            ve.tensor_tensor(M1r, Fr, Dr, ALU.mult)
            vdsc = ve.tensor_scalar(M1r, M1r, self.md127, None, ALU.mult)
            self.s_ve.wait(te, vdsc)
            if tg >= 1:
                self.s_ve.wait(te, rmc[tg - 1])
            tb1 = self.tem(self.p_d[:, 0:512], self.ONES_ROW_F, mxcr, start=True, stop=True)
            self.s_te.wait(ve, tb1)
            vc1 = ve.tensor_copy(S1B[p], self.p_d[:, 0:512])
            self.s_ve.wait(te, vc1)
            tb2 = self.tem(self.p_d[:, 0:512], self.ONES_ROW_F, M1r, start=True, stop=True)
            self.s_te.wait(ve, tb2)
            rmc[tg] = ve.tensor_copy(DSC[p], self.p_d[:, 0:512])
            vmath[tg] = rmc[tg]

        ldh[0] = self._f2lh0
        emit_lx2(0)
        emit_rows(0)
        outr = self.outT.rearrange("(a p) t -> p a t", p=128)
        for tg in range(NG):
            p = tg % 2
            self.s_x.wait(ve, ldh[tg])
            vqs = {}
            for k in range(FFT):
                ve.tensor_tensor(i32v, HB[p][:, k * 512:(k + 1) * 512], S1B[p], ALU.mult)
                vqs[k] = ve.tensor_copy(HB[p][:, k * 512:(k + 1) * 512], i32v)
            self._f2vq[tg] = vqs[FFT - 1]
            if tg + 1 < NG:
                emit_lh(tg + 1)
            for k in range(FFT):
                self.s_ve.wait(te, vqs[k])
                for m2 in range(6):
                    ins = te.matmul(accs1[m2],
                                    ternd[:, k * 1024 + m2 * 128:k * 1024 + m2 * 128 + 128],
                                    HB[p][:, k * 512:(k + 1) * 512],
                                    start=(k == 0), stop=(k == FFT - 1))
            t6 = self.s_te.inc(ins)
            if tg + 1 < NG:
                emit_rows(tg + 1)
            self.s_x.wait(ve, ldx[tg])
            vos = {}

            def evac(m2, pp):
                self.s_out.wait(ve)
                ve.tensor_tensor(OSTG, pp, DSC[p], ALU.mult)
                vo = ve.tensor_tensor(OSTG, OSTG, self.XG[:, m2 * 512:(m2 + 1) * 512], ALU.add)
                self.s_ve.wait(sy, vo)
                self.s_out.wait(sy)
                self.s_out.dma(sy.dma_start(
                    out=outr[:, m2:m2 + 1, tg * TG:(tg + 1) * TG],
                    in_=OSTG.rearrange("p (x t) -> p x t", x=1)))
                return vo

            self.s_te.wait(ve, t6)
            vos[0] = evac(0, accs1[0])
            vos[1] = evac(1, accs1[1])
            self.s_ve.wait(te, vos[1])
            for k in range(FFT):
                for m2 in (6, 7):
                    ins = te.matmul(accs1[m2 - 6],
                                    ternd[:, k * 1024 + m2 * 128:k * 1024 + m2 * 128 + 128],
                                    HB[p][:, k * 512:(k + 1) * 512],
                                    start=(k == 0), stop=(k == FFT - 1))
            t8 = self.s_te.inc(ins)
            self._f2t8[tg] = t8
            for m2 in range(2, 6):
                vos[m2] = evac(m2, accs1[m2])
            self.s_te.wait(ve, t8)
            vos[6] = evac(6, accs1[0])
            vos[7] = evac(7, accs1[1])
            self._f2vo[tg] = vos[7]
            if tg + 1 < NG:
                emit_lx2(tg + 1)

    def emit_debug(self):
        sy = self.sy
        self.s_se.wait(sy)
        self.s_ve.wait(sy)
        self.s_te.wait(sy)
        self.s_sp.wait(sy)
        self.s_wl.wait(sy)
        def dbg(name, src, shape, dt):
            o = self.out(name, shape, dt)
            self.s_out.wait(sy)
            self.s_out.dma(sy.dma_start(out=o, in_=src))
        dbg("d_xn", self.XN[:], [128, DMT * R], BF16)
        dbg("d_rows", self.rows[:], [1, 4 * TG], F32)
        dbg("d_xg", self.XG[:], [128, DMT * TG], F32)
        dbg("d_sq", self.ARE[:, 16384 + DMT * TG:16384 + DMT * TG + 512], [128, 512], BF16)
        dbg("d_scrow", self.SCR[0:1, :], [1, 1024], F32)
        dbg("d_rowsb", self.rowsb, [1, 2 * TG], BF16)
        dbg("d_Call", self.C_all[:], [128, NCH * H], F32)
        if PHASES >= 2:
            dbg("d_gS", self.ARE[:, 16384:32768], [128, DMT * R], BF16)
            dbg("d_ogS", self.ARE[:, 0:16384], [128, DMT * R], BF16)
        dbg("d_Ball", self.Ballc[:], [16, NCH + 1], F32)
        dbg("d_cC", self.cCcol[:], [16, NCH], F32)
        dbg("d_states", self.std_d, [NCH * 128, 512], BF16)
        if PHASES >= 4:
            dbg("d_xq", self.XN[:], [128, DMT * R], BF16)
            dbg("d_mx", self.mx_d, [2 * NG * 4, 128], F32)
            dbg("d_x2", self.x2_d, [DM, R], F32)
        if PHASES >= 5:
            dbg("d_hd", self.h_d, [FFT * 128, R], BF16)
            dbg("d_smalls", self.smalls[0:1, 236:239], [1, 3], F32)
        dbg("d_Smaster", self.S_master[:], [128, 512], F32)
        if PHASES >= 2:
            dbg("d_kg", self.kgT_d, [HT * NCH * 128, CH], BF16)
            dbg("d_q", self.qT_d, [HT * NCH * 128, CH], BF16)
            dbg("d_v", self.vN_d, [NCH * 128, HID], BF16)

    def finish(self):
        sy = self.sy
        self.s_se.wait(sy)
        self.s_ve.wait(sy)
        self.s_te.wait(sy)
        self.s_sp.wait(sy)
        self.s_wl.wait(sy)
        self.s_po.wait(sy)
        self.s_sp.wait(sy)
        self.s_x.wait(sy)
        self.s_out.wait(sy)


def _consts(rank):
    import ml_dtypes
    j = np.arange(128)
    negUT = -np.triu(np.ones((128, 128), dtype=np.float32))
    wind = np.zeros((8, 8), dtype=np.float32)
    for b in range(8):
        for k in range(8):
            if b + 1 <= k <= rank - 1:
                wind[k, b] = 1.0
    alog = np.array([0.0 if b < rank else NEG for b in range(8)], dtype=np.float32)
    return {
        "c_id_bf": np.eye(128, dtype=ml_dtypes.bfloat16),
        "c_id_f": np.eye(128, dtype=np.float32),
        "c_negUT": negUT,
        "c_onescb": np.ones((128, 1), dtype=ml_dtypes.bfloat16),
        "c_onescf": np.ones((128, 1), dtype=np.float32),
        "c_onesrb": np.ones((1, 128), dtype=ml_dtypes.bfloat16),
        "c_onesrf": np.ones((1, 128), dtype=np.float32),
                "c_i16f": np.eye(16, dtype=np.float32),
        "c_i16b": np.eye(16, dtype=ml_dtypes.bfloat16),
        "c_ones16f": np.ones((16, 128), dtype=np.float32),
        "c_ones16b": np.ones((16, 128), dtype=ml_dtypes.bfloat16),
        "c_actc": np.tile(np.array([[LN_EPS, 1.0, RMS_EPS, LN127, float(np.log(Q_EPS))]], dtype=np.float32), (128, 1)),
        "c_negdiag": (np.eye(128) * NEG).astype(ml_dtypes.bfloat16),
        "c_slrep": np.where(j[:, None] > j[None, :], 1.0, 0.0).astype(ml_dtypes.bfloat16),
        "c_wind": wind,
        "c_alog": np.tile(alog[None, :], (128, 1)).astype(np.float32),
    }


def kernel(L0_attn_x, L0_attn_Wq, L0_attn_Wk, L0_attn_Wv, L0_attn_Wo, L0_attn_Wg,
           L0_attn_Wog, L0_attn_Wd, L0_attn_bd, L0_attn_norm_w,
           L0_ffn_w_gate, L0_ffn_w_up, L0_ffn_w_down):
    x = np.asarray(L0_attn_x)[0]
    T = lambda w: np.ascontiguousarray(np.asarray(w).T.astype(np.float32))
    WqT, WkT, WvT, WgT, WogT = T(L0_attn_Wq), T(L0_attn_Wk), T(L0_attn_Wv), T(L0_attn_Wg), T(L0_attn_Wog)
    WoT = T(L0_attn_Wo)
    WdT = T(L0_attn_Wd)
    WdTp = np.ascontiguousarray(WdT.reshape(DMT, 128, H).transpose(1, 0, 2).reshape(128, DMT * H))
    WTg, WTu = T(L0_ffn_w_gate), T(L0_ffn_w_up)
    WTd = T(L0_ffn_w_down)
    bd = np.asarray(L0_attn_bd, dtype=np.float32).reshape(1, H)
    normw = np.ascontiguousarray(np.asarray(L0_attn_norm_w, dtype=np.float32).reshape(HT, 128).T)

    mk = MK()
    nc = mk.build()
    in_maps = []
    for c in range(N_CORES):
        xT_c = np.ascontiguousarray(x[c * R:(c + 1) * R, :].T)
        m = {"xT": xT_c, "WqT": WqT, "WkT": WkT, "WvT": WvT, "WgT": WgT, "WogT": WogT,
             "WoT": WoT, "WdTp": WdTp, "bd": bd, "normw": normw,
             "WTg": WTg, "WTu": WTu, "WTd": WTd}
        m.update(_consts(c))
        in_maps.append(m)
    res = run_bass_kernel_spmd(nc, in_maps, core_ids=list(range(N_CORES)))
    outs = [np.asarray(r["outT"]).T for r in res.results]
    out = np.concatenate(outs, axis=0)[None]
    if DEBUG:
        kernel._results = res
    return out.astype(np.float32)


# version 20
# speedup vs baseline: 1.1116x; 1.0004x over previous
import numpy as np
import concourse.bass as bass
import concourse.mybir as mybir
from concourse.bass_utils import run_bass_kernel_spmd
from contextlib import ExitStack

F32 = mybir.dt.float32
BF16 = mybir.dt.bfloat16
I32 = mybir.dt.int32
AF = mybir.ActivationFunctionType
ALU = mybir.AluOpType
AX = mybir.AxisListType

DEBUG = False
SIM1 = False
PHASES = 99
N_CORES = 8
R = 2048
TG = 512
NG = 4
CH = 128
NCH = 16
H = 16; DH = 64; DM = 1024; DMT = 8; HID = 1024; HT = 8
FF = 4096; FFT = 32
LN_EPS = 1e-5
RMS_EPS = float(np.finfo(np.float32).eps)
Q_EPS = 1e-5
NEG = -1.0e30
LN127 = float(np.log(127.0))


class EW:
    OPS = {"tensor_copy", "tensor_tensor", "tensor_add", "tensor_sub", "tensor_mul",
           "tensor_max", "tensor_scalar", "tensor_single_scalar", "tensor_scalar_add",
           "tensor_scalar_sub", "tensor_scalar_mul", "tensor_scalar_max",
           "tensor_scalar_min", "tensor_relu", "tensor_reduce", "reduce_max",
           "reduce_sum", "scalar_tensor_tensor", "tensor_tensor_scan", "memset",
           "memzero", "select", "copy_predicated", "reciprocal", "transpose",
           "activation", "sign", "square", "sqrt", "mul", "copy", "add",
           "bn_stats", "bn_aggr", "max", "pool", "pool_avg", "pool_max",
           "stream_shuffle", "iota"}

    def __init__(self, eng, sem):
        self.e = eng
        self.s = sem

    def __getattr__(self, k):
        f = getattr(self.e, k)
        if k not in EW.OPS:
            return f
        def g(*a, **kw):
            if self.s.n > 0:
                self.e.wait_ge(self.s.h, self.s.n)
            ins = f(*a, **kw)
            return self.s.inc(ins)
        return g


class Sem:
    def __init__(self, nc, es, name):
        self.h = es.enter_context(nc.semaphore(name))
        self.n = 0

    def inc(self, instr, k=1):
        instr.then_inc(self.h, k)
        self.n += k
        return self.n

    def dma(self, instr):
        return self.inc(instr, 16)

    def wait(self, eng, n=None):
        e = eng.e if isinstance(eng, EW) else eng
        e.wait_ge(self.h, self.n if n is None else n)


class MK:

    def __init__(self):
        self.nc = bass.Bass("TRN2", target_bir_lowering=False)
        self.es = ExitStack()

    def sb(self, name, shape, dt):
        return self.es.enter_context(self.nc.sbuf_tensor(name, shape, dt))

    def ps(self, name, shape, dt=F32):
        return self.es.enter_context(self.nc.psum_tensor(name, shape, dt))

    def inp(self, name, shape, dt=F32):
        return self.nc.dram_tensor(name, shape, dt, kind="ExternalInput").ap()

    def out(self, name, shape, dt=F32):
        return self.nc.dram_tensor(name, shape, dt, kind="ExternalOutput").ap()

    def dram(self, name, shape, dt):
        return self.nc.dram_tensor(name, shape, dt, kind="Internal").ap()

    def qdma(self, queue, sem, out, in_, **kw):
        sem.wait(queue)
        return sem.dma(queue.dma_start(out=out, in_=in_, **kw))

    def build(self):
        nc = self.nc
        self.xT = self.inp("xT", [DM, R])
        self.WqT = self.inp("WqT", [DM, HID]); self.WkT = self.inp("WkT", [DM, HID])
        self.WvT = self.inp("WvT", [DM, HID]); self.WgT = self.inp("WgT", [DM, HID])
        self.WogT = self.inp("WogT", [DM, HID]); self.WoT = self.inp("WoT", [HID, DM])
        self.WdTp = self.inp("WdTp", [128, DMT * H])
        self.bd = self.inp("bd", [1, H])
        self.normw = self.inp("normw", [128, HT])
        self.WTg = self.inp("WTg", [DM, FF]); self.WTu = self.inp("WTu", [DM, FF])
        self.WTd = self.inp("WTd", [FF, DM])
        self.c_id_bf = self.inp("c_id_bf", [128, 128], BF16)
        self.c_id_f = self.inp("c_id_f", [128, 128], F32)
        self.c_negUT = self.inp("c_negUT", [128, 128], F32)
        self.c_onescb = self.inp("c_onescb", [128, 1], BF16)
        self.c_onescf = self.inp("c_onescf", [128, 1], F32)
        self.c_onesrb = self.inp("c_onesrb", [1, 128], BF16)
        self.c_onesrf = self.inp("c_onesrf", [1, 128], F32)
        self.c_i16f = self.inp("c_i16f", [16, 16], F32)
        self.c_i16b = self.inp("c_i16b", [16, 16], BF16)
        self.c_ones16f = self.inp("c_ones16f", [16, 128], F32)
        self.c_ones16b = self.inp("c_ones16b", [16, 128], BF16)
        self.c_actc = self.inp("c_actc", [128, 5], F32)
        self.c_negdiag = self.inp("c_negdiag", [128, 128], BF16)
        self.c_slrep = self.inp("c_slrep", [128, CH], BF16)
        self.c_wind = self.inp("c_wind", [8, 8], F32)
        self.c_alog = self.inp("c_alog", [128, 8], F32)
        self.outT = self.out("outT", [DM, R], F32)
        self.qT_d = self.dram("qT_d", [HT * NCH * 128, CH], BF16)
        self.kgT_d = self.dram("kgT_d", [HT * NCH * 128, CH], BF16)
        self.vN_d = self.dram("vN_d", [NCH * 128, HID], BF16)
        self.x2_d = self.dram("x2_d", [DM, R], F32)
        self.h_d = self.dram("h_d", [FFT * 128, R], BF16)
        self.mx_d = self.dram("mx_d", [2 * NG * 4, 128], F32)
        self.cc_in = self.dram("cc_in", [129, 512], F32)
        self.cc_out = self.dram("cc_out", [8 * 129, 512], F32)
        self.std_d = self.dram("std_d", [NCH * 128, 512], BF16)

        self.XN = self.sb("XN", [128, DMT * R], BF16)
        self.ARE = self.sb("ARE", [128, 32768], BF16)
        self.WBUF = self.sb("WBUF", [128, 2 * 8192], BF16)
        self.XG = self.sb("XG", [128, DMT * TG], F32)
        self.SCR = self.sb("SCR", [128, 1024], F32)
        self.SCB = self.sb("SCB", [128, 2048], BF16)
        self.OT = self.sb("OT", [128, HT * TG], BF16)
        self.C_all = self.sb("C_all", [128, NCH * H], F32)
        self.negC_all = self.sb("negC_all", [128, NCH * H], F32)
        self.A2S = self.sb("A2S", [128, 5120], BF16)
        self.CHIA = self.sb("CHIA", [16, 2048], F32)
        self.STB2 = self.sb("STB2", [128, 1024], BF16)
        self.LST = self.sb("LST", [128, 1024], BF16)
        self.st16 = self.sb("st16", [16, 256], F32)
        self.rows = self.sb("rows", [1, 4 * TG], F32)
        self.cCcol = self.sb("cCcol", [16, NCH], F32)
        self.Ballc = self.sb("Ballc", [16, NCH + 1], F32)
        self.S_master = self.sb("S_master", [128, 512], F32)
        self.smalls = self.sb("smalls", [128, 256], F32)
        self.smallsb = self.sb("smallsb", [128, 256], BF16)
        self.wd_sb = self.sb("wd_sb", [128, DMT * H], F32)
        self.wdbar = self.sb("wdbar", [128, H], F32)
        self.bdb = self.sb("bdb", [128, H], F32)
        self.normw_sb = self.sb("normw_sb", [128, HT], F32)
        self.consts = self.sb("consts", [128, 400], F32)
        self.constsb = self.sb("constsb", [128, 264], BF16)
        self.rowsb = self.A2S[0:1, 0:1024]
        self.i16f = self.sb("i16f", [16, 16], F32)
        self.i16b = self.sb("i16b", [16, 16], BF16)
        self.ones16f = self.sb("ones16f", [16, 128], F32)
        self.ones16b = self.sb("ones16b", [16, 128], BF16)
        self.negdiag = self.sb("negdiag", [128, 128], BF16)
        self.slrep = self.sb("slrep", [128, CH], BF16)
        self.wind = self.sb("wind", [8, 8], F32)
        self.alog = self.sb("alog", [128, 8], F32)
        self.p_a = self.ps("p_a", [128, 512]); self.p_b = self.ps("p_b", [128, 512])
        self.p_c = self.ps("p_c", [128, 512]); self.p_d = self.ps("p_d", [128, 512])
        self.p_e = self.ps("p_e", [128, 512])
        self.p_o = self.ps("p_o", [128, 1024])
        self.p_t = self.ps("p_t", [128, 1024], BF16)

        es = self.es
        self.s_wl = Sem(nc, es, "s_wl")
        self.s_x = Sem(nc, es, "s_x")
        self.s_sp = Sem(nc, es, "s_sp")
        self.s_te = Sem(nc, es, "s_te")
        self.s_ve = Sem(nc, es, "s_ve")
        self.s_se = Sem(nc, es, "s_se")
        self.s_po = Sem(nc, es, "s_po")
        self.s_wq = Sem(nc, es, "s_wq")
        self.s_sc = Sem(nc, es, "s_sc")
        self.s_cc = Sem(nc, es, "s_cc")
        self.s_out = Sem(nc, es, "s_out")

        with self.es:
            with nc.Block() as block:
                @block.sync
                def _(sync):
                    self.emit()
        return nc

    def emit(self):
        nc = self.nc
        self.sy = nc.sync
        self.te = nc.tensor
        self.ve = EW(nc.vector, self.s_ve)
        self.se = EW(nc.scalar, self.s_se)
        self.po = EW(nc.gpsimd, self.s_po)
        self.phase0()
        self.phase1a()
        if PHASES >= 2:
            self.phase1b()
        if PHASES >= 3:
            self.phase2a()
        if PHASES >= 4:
            self.phase2b()
        if PHASES >= 5:
            self.phase3_f0()
            self.phase3_f1()
            self.phase3_f2()
        if DEBUG:
            self.emit_debug()
        self.finish()

    def preload_set(self, name):
        from concourse.hw_specs import get_activation_tables
        import concourse.mybir as mb
        tabs = get_activation_tables(self.nc.m.arch)
        idx = list(tabs.keys()).index(name)
        return self.se.e.add_instruction(mb.InstLoadActFuncSet(
            name=self.nc.get_next_instruction_name(),
            act_func_set_id=idx, ins=[], outs=[]))

    def tem(self, *a, **kw):
        ins = self.te.matmul(*a, **kw)
        return self.s_te.inc(ins)

    def tet(self, *a, **kw):
        ins = self.te.transpose(*a, **kw)
        return self.s_te.inc(ins)

    def phase0(self):
        sy, te, ve, se, po = self.sy, self.te, self.ve, self.se, self.po
        cs, cb = self.consts, self.constsb
        self.ID_F = cs[:, 0:128]
        self.NEGUT = cs[:, 128:256]
        self.ONES_COL_F = cs[:, 256:257]
        self.ONES_ROW_F = cs[0:1, 257:385]
        self.B_LNEPS = cs[:, 385:386]
        self.B_ONE = cs[:, 386:387]
        self.B_RMSEPS = cs[:, 387:388]
        self.B_LN127 = cs[:, 388:389]
        self.B_LNQEPS = cs[:, 389:390]
        self.ID_B = cb[:, 0:128]
        self.ONES_COL_B = cb[:, 128:129]
        self.ONES_ROW_B = cb[0:1, 129:257]
        for dst, src in [
            (self.ID_F, self.c_id_f), (self.NEGUT, self.c_negUT),
            (self.ONES_COL_F, self.c_onescf), (self.ONES_ROW_F, self.c_onesrf),
            (self.ID_B, self.c_id_bf), (self.ONES_COL_B, self.c_onescb),
            (self.ONES_ROW_B, self.c_onesrb),             (self.normw_sb[:], self.normw),
            (self.i16f[:], self.c_i16f), (self.i16b[:], self.c_i16b),
            (self.ones16f[:], self.c_ones16f), (self.ones16b[:], self.c_ones16b),
            (cs[:, 385:390], self.c_actc), (self.wd_sb[:], self.WdTp),
            (self.st16[0:1, 240:256], self.bd),
            (self.negdiag[:], self.c_negdiag), (self.slrep[:], self.c_slrep),
            (self.wind[:], self.c_wind), (self.alog[:], self.c_alog),
        ]:
            self.qdma(sy, self.s_x, dst, src)
        self.s_x.wait(te)
        self.tem(self.p_a[:, 0:H], self.ONES_ROW_F, self.st16[0:1, 240:256], start=True, stop=True)
        for a in range(DMT):
            ins = te.matmul(self.p_b[0:1, 0:H], self.ONES_COL_F, self.wd_sb[:, a * H:(a + 1) * H],
                            start=(a == 0), stop=(a == DMT - 1))
        t0 = self.s_te.inc(ins)
        self.s_te.wait(ve, t0)
        ve.tensor_copy(self.bdb[:], self.p_a[:, 0:H])
        v0 = ve.tensor_copy(self.st16[0:1, 224:240], self.p_b[0:1, 0:H])
        self.s_ve.wait(te, v0)
        t1 = self.tem(self.p_a[:, H:2 * H], self.ONES_ROW_F, self.st16[0:1, 224:240],
                      start=True, stop=True)
        self.s_te.wait(ve, t1)
        ve.tensor_copy(self.wdbar[:], self.p_a[:, H:2 * H])
        ve.memset(self.S_master[:], 0.0)
        ve.memset(self.Ballc[:], 0.0)

    def phase1a(self):
        sy, te, ve, se, po = self.sy, self.te, self.ve, self.se, self.po
        self.preload_set("natural_log_exp_and_others")
        ROW = self.rows
        AREF = self.ARE[:].bitcast(F32)
        XGB1 = {0: self.XG[:], 1: AREF[:, 12288:16384]}
        ldx = {}
        lastz = {}
        self._p1ve = {}

        def p1_load(tg):
            if tg >= 2:
                self.s_te.wait(sy, lastz[tg - 2])
                self.s_ve.wait(sy, self._p1ve[tg - 2])
            ldx[tg] = self.qdma(sy, self.s_x,
                                XGB1[tg % 2].rearrange("p (a f) -> p a f", f=TG),
                                self.xT.rearrange("(a p) t -> p a t", p=128)[:, :, tg * TG:(tg + 1) * TG])

        self.s_te.wait(sy)
        p1_load(0)
        p1_load(1)
        for tg in range(NG):
            XGg = XGB1[tg % 2]
            ld = ldx[tg]
            self.s_x.wait(ve, ld)
            self.s_x.wait(se, ld)
            if tg > 0:
                self.s_te.wait(ve)
                self.s_te.wait(se)
            xb = self.ARE[:, 16384:16384 + DMT * TG]
            sqt = self.ARE[:, 16384 + DMT * TG:16384 + 2 * DMT * TG]
            for a in range(DMT):
                vml = ve.tensor_copy(xb[:, a * TG:(a + 1) * TG], XGg[:, a * TG:(a + 1) * TG])
            for a in range(DMT):
                sml = se.activation(sqt[:, a * TG:(a + 1) * TG], XGg[:, a * TG:(a + 1) * TG], AF.Square)
            self.s_ve.wait(te, vml)
            self.s_se.wait(te, sml)
            for a in range(DMT):
                ins = te.matmul(self.p_a[0:1, 0:TG], self.ONES_COL_B, xb[:, a * TG:(a + 1) * TG],
                                start=(a == 0), stop=(a == DMT - 1))
            self.s_te.inc(ins)
            for a in range(DMT):
                ins = te.matmul(self.p_b[0:1, 0:TG], self.ONES_COL_B, sqt[:, a * TG:(a + 1) * TG],
                                start=(a == 0), stop=(a == DMT - 1))
            tml = self.s_te.inc(ins)
            self.s_te.wait(ve, tml)
            if DEBUG and tg == 3:
                ve.tensor_copy(self.SCR[0:1, 512:1024], self.p_b[0:1, 0:TG])
                ve.tensor_copy(self.SCR[0:1, 0:512], self.p_a[0:1, 0:TG])
            rm = ROW[0:1, 0:TG]; rvar = ROW[0:1, TG:2 * TG]
            rr = ROW[0:1, 2 * TG:3 * TG]; rmr = ROW[0:1, 3 * TG:4 * TG]
            ve.tensor_scalar(rm, self.p_a[0:1, 0:TG], 1.0 / DM, None, ALU.mult)
            ve.tensor_scalar(rvar, self.p_b[0:1, 0:TG], 1.0 / DM, None, ALU.mult)
            ve.tensor_mul(rr, rm, rm)
            vst = ve.tensor_sub(rvar, rvar, rr)
            if DEBUG and tg == 3:
                vst = ve.tensor_copy(self.rowsb[0:1, 0:512], rvar)
            self.s_ve.wait(se, vst)
            se.activation(rvar, rvar, AF.Ln, bias=self.B_LNEPS[0:1, :])
            sst = se.activation(rr, rvar, AF.Exp, scale=-0.5)
            if DEBUG and tg == 3:
                sst = se.activation(self.rowsb[0:1, 512:1024], rvar, AF.Copy)
            self.s_se.wait(ve, sst)
            vrb = ve.tensor_mul(rmr, rm, rr)
            self.s_ve.wait(te, vrb)
            self.tem(self.p_c[:, 0:TG], self.ONES_ROW_F, rr, start=True, stop=True)
            tbc = self.tem(self.p_d[:, 0:TG], self.ONES_ROW_F, rmr, start=True, stop=True)
            self.s_te.wait(ve, tbc)
            for a in range(DMT):
                t1 = self.SCR[:, 0:TG]
                ve.tensor_mul(t1, XGg[:, a * TG:(a + 1) * TG], self.p_c[:, 0:TG])
                ve.tensor_sub(self.XN[:, a * R + tg * TG: a * R + (tg + 1) * TG], t1, self.p_d[:, 0:TG])

            self.s_ve.wait(te)
            for cc in range(NG):
                t0 = cc * CH
                te.transpose(self.p_e[0:CH, cc:cc + 1], rr[0:1, t0:t0 + CH], self.ID_F[0:1, 0:1])
                ins = te.transpose(self.p_e[0:CH, 4 + cc:5 + cc], rmr[0:1, t0:t0 + CH],
                                   self.ID_F[0:1, 0:1])
            self.s_te.inc(ins)
            for cc in range(NG):
                t0 = cc * CH
                for a in range(DMT):
                    ins = te.matmul(self.p_o[:, cc * H:(cc + 1) * H],
                                    XGg[:, a * TG + t0:a * TG + t0 + CH],
                                    self.wd_sb[:, a * H:(a + 1) * H],
                                    start=(a == 0), stop=(a == DMT - 1))
            tz = self.s_te.inc(ins)
            self.s_te.wait(ve, tz)
            rc8 = self.smalls[:, 0:8]
            ve.tensor_copy(rc8, self.p_e[:, 0:8])
            z1 = self.smalls[:, 16:80]; zz = self.smalls[:, 96:160]
            ve.tensor_tensor(z1.rearrange("p (c h) -> p c h", h=H),
                             self.p_o[:, 0:64].rearrange("p (c h) -> p c h", h=H),
                             rc8[:, 0:4].rearrange("p (c x) -> p c x", x=1).broadcast_to([128, 4, H]),
                             ALU.mult)
            ve.tensor_tensor(zz.rearrange("p (c h) -> p c h", h=H),
                             self.wdbar[:].rearrange("p (x h) -> p x h", x=1).broadcast_to([128, 4, H]),
                             rc8[:, 4:8].rearrange("p (c x) -> p c x", x=1).broadcast_to([128, 4, H]),
                             ALU.mult)
            ve.tensor_sub(z1, z1, zz)
            vz = ve.tensor_tensor(zz.rearrange("p (c h) -> p c h", h=H),
                                  z1.rearrange("p (c h) -> p c h", h=H),
                                  self.bdb[:].rearrange("p (x h) -> p x h", x=1).broadcast_to([128, 4, H]),
                                  ALU.add)
            self.s_ve.wait(se, vz)
            az = self.smalls[:, 160:224]
            se.activation(az, zz, AF.Abs)
            se.activation(az, az, AF.Exp, scale=-1.0)
            sz = se.activation(az, az, AF.Ln, bias=self.B_ONE)
            self.s_se.wait(ve, sz)
            sp = z1
            ve.tensor_scalar(sp, zz, 0.0, None, ALU.max)
            vsp = ve.tensor_add(sp, sp, az)
            self.s_ve.wait(te, vsp)
            for cc in range(NG):
                ins = self.tem(self.p_o[:, 512 + cc * H:512 + (cc + 1) * H], self.NEGUT,
                               sp[:, cc * H:(cc + 1) * H], start=True, stop=True)
            tc = ins
            self.s_te.wait(ve, tc)
            Cg = self.C_all[:, tg * 64:(tg + 1) * 64]
            ve.tensor_copy(Cg, self.p_o[:, 512:576])
            vch = ve.tensor_scalar(self.negC_all[:, tg * 64:(tg + 1) * 64],
                                   self.p_o[:, 512:576], -1.0, None, ALU.mult)
            self.s_ve.wait(te, vch)
            for cc in range(NG):
                ins = self.tet(self.p_e[0:H, cc * CH:(cc + 1) * CH],
                               Cg[:, cc * H:(cc + 1) * H], self.ID_F)
            tt = ins
            self.s_te.wait(ve, tt)
            tot = self.Ballc[:, NCH:NCH + 1]
            for cc in range(NG):
                c = tg * NG + cc
                ve.tensor_copy(self.Ballc[:, c:c + 1], tot)
                ve.tensor_copy(self.cCcol[:, c:c + 1],
                               self.p_e[0:H, cc * CH + CH - 1:cc * CH + CH])
                vbk = ve.tensor_add(tot, tot, self.cCcol[:, c:c + 1])
            lastz[tg] = tz
            self._p1ve[tg] = vbk
            if tg + 2 < NG:
                p1_load(tg + 2)

    def load_weight(self, w_dram, slot, cols=HID, after_te=None):
        po = self.po
        if after_te is not None:
            self.s_te.wait(po, after_te)
        self.s_wl.wait(po)
        dst = self.WBUF[:, slot * 8192: slot * 8192 + DMT * cols]
        ins = po.dma_start(
            out=dst.rearrange("p (a f) -> p a f", f=cols),
            in_=w_dram.rearrange("(a p) f -> p a f", p=128))
        return self.s_wl.dma(ins)

    def wtile(self, slot, a, m, w=128, cols=HID):
        base = slot * 8192 + a * cols + m * w
        return self.WBUF[:, base: base + w]

    def spill4(self, dram, m, tg, src):
        dst = dram.rearrange("(a c p) t -> p a c t", p=128, c=NCH)[
            :, m:m + 1, tg * NG:(tg + 1) * NG, :]
        s4 = src.rearrange("p (x c t) -> p x c t", x=1, t=CH)
        self.s_sp.wait(self.sy)
        return self.s_sp.dma(self.sy.dma_start(out=dst, in_=s4))

    def phase1b(self):
        sy, te, ve, se, po = self.sy, self.te, self.ve, self.se, self.po
        se.e.preload_act_table(AF.Sigmoid)
        gS = self.ARE[:, 16384:32768]
        ogS = self.ARE[:, 0:16384]
        ld_g = self.load_weight(self.WgT, 0)
        ld_k = self.load_weight(self.WkT, 1)
        psums = [self.p_a, self.p_b, self.p_c, self.p_d]
        self.s_ve.wait(te)
        self.s_se.wait(te)

        marks = {}
        for i in range(HT * NG):
            m, tg = divmod(i, NG)
            p = psums[i % 4]
            if i >= 4:
                self.s_se.wait(te, marks[i - 4])
            if i == 0:
                self.s_wl.wait(te, ld_g)
            for a in range(DMT):
                ins = te.matmul(p[:, 0:TG], self.wtile(0, a, m),
                                self.XN[:, a * R + tg * TG:a * R + (tg + 1) * TG],
                                start=(a == 0), stop=(a == DMT - 1))
            tm = self.s_te.inc(ins)
            self.s_te.wait(se, tm)
            marks[i] = se.activation(gS[:, m * R + tg * TG: m * R + (tg + 1) * TG], p[:, 0:TG], AF.Sigmoid)
        te_g_done = self.s_te.n

        ld_q = self.load_weight(self.WqT, 0, after_te=te_g_done)
        kge = {}
        spills = {}
        for i in range(HT * NG):
            m, tg = divmod(i, NG)
            p = psums[i % 4]
            if i >= 4:
                self.s_ve.wait(te, kge[i - 4])
            elif i == 0:
                self.s_se.wait(te, marks[HT * NG - 1])
            if i == 0:
                self.s_wl.wait(te, ld_k)
            for a in range(DMT):
                ins = te.matmul(p[:, 0:TG], self.wtile(1, a, m),
                                self.XN[:, a * R + tg * TG:a * R + (tg + 1) * TG],
                                start=(a == 0), stop=(a == DMT - 1))
            tm = self.s_te.inc(ins)
            self.s_te.wait(ve, tm)
            if i >= 4:
                self.s_sp.wait(ve, spills[i - 4])
            dst = self.SCB[:, (i % 4) * TG:(i % 4) * TG + TG]
            kge[i] = ve.tensor_mul(dst, p[:, 0:TG], gS[:, m * R + tg * TG: m * R + (tg + 1) * TG])
            self.s_ve.wait(sy, kge[i])
            spills[i] = self.spill4(self.kgT_d, m, tg, dst)
        te_k_done = self.s_te.n

        ld_v = self.load_weight(self.WvT, 1, after_te=te_k_done)
        qe = {}
        for i in range(HT * NG):
            m, tg = divmod(i, NG)
            p = psums[i % 4]
            if i >= 4:
                self.s_ve.wait(te, qe[i - 4])
            elif i == 0:
                self.s_ve.wait(te, kge[HT * NG - 1])
            if i == 0:
                self.s_wl.wait(te, ld_q)
            for a in range(DMT):
                ins = te.matmul(p[:, 0:TG], self.wtile(0, a, m),
                                self.XN[:, a * R + tg * TG:a * R + (tg + 1) * TG],
                                start=(a == 0), stop=(a == DMT - 1))
            tm = self.s_te.inc(ins)
            self.s_te.wait(ve, tm)
            self.s_sp.wait(ve, spills[i - 4] if i >= 4 else spills[HT * NG - 4 + i])
            dst = self.SCB[:, (i % 4) * TG:(i % 4) * TG + TG]
            qe[i] = ve.tensor_copy(dst, p[:, 0:TG])
            self.s_ve.wait(sy, qe[i])
            spills[i] = self.spill4(self.qT_d, m, tg, dst)
        te_q_done = self.s_te.n

        ld_og = self.load_weight(self.WogT, 0, after_te=te_q_done)
        vev = {}
        for i in range(NCH * 2):
            c, vb = divmod(i, 2)
            p = psums[i % 4]
            if i >= 4:
                self.s_ve.wait(te, vev[i - 4])
            elif i == 0:
                self.s_ve.wait(te, qe[HT * NG - 1])
            if i == 0:
                self.s_wl.wait(te, ld_v)
            for a in range(DMT):
                ins = te.matmul(p[:, 0:TG],
                                self.XN[:, a * R + c * CH: a * R + (c + 1) * CH],
                                self.WBUF[:, 8192 + a * HID + vb * TG: 8192 + a * HID + (vb + 1) * TG],
                                start=(a == 0), stop=(a == DMT - 1))
            tm = self.s_te.inc(ins)
            self.s_te.wait(ve, tm)
            self.s_sp.wait(ve, spills[i - 4] if i >= 4 else spills[HT * NG - 4 + i])
            dst = self.SCB[:, (i % 4) * TG:(i % 4) * TG + TG]
            vev[i] = ve.tensor_copy(dst, p[:, 0:TG])
            self.s_ve.wait(sy, vev[i])
            self.s_sp.wait(sy)
            spills[i] = self.s_sp.dma(sy.dma_start(
                out=self.vN_d[c * 128:(c + 1) * 128, vb * TG:(vb + 1) * TG], in_=dst))
        self.te_v_done = self.s_te.n

        oge = {}
        for i in range(HT * NG):
            m, tg = divmod(i, NG)
            p = psums[i % 4]
            if i >= 4:
                self.s_se.wait(te, oge[i - 4])
            elif i == 0:
                self.s_ve.wait(te, vev[NCH * 2 - 1])
            if i == 0:
                self.s_wl.wait(te, ld_og)
            for a in range(DMT):
                ins = te.matmul(p[:, 0:TG], self.wtile(0, a, m),
                                self.XN[:, a * R + tg * TG:a * R + (tg + 1) * TG],
                                start=(a == 0), stop=(a == DMT - 1))
            tm = self.s_te.inc(ins)
            self.s_te.wait(se, tm)
            oge[i] = se.activation(ogS[:, m * R + tg * TG: m * R + (tg + 1) * TG], p[:, 0:TG], AF.Sigmoid)


    def phase2a(self):
        sy, te, ve, se, po = self.sy, self.te, self.ve, self.se, self.po
        self.preload_set("natural_log_exp_and_others")
        self._aspl = {}
        KGC = {0: self.SCB[:, 0:1024], 1: self.OT[:, 1024:2048]}
        VNC = {0: self.SCB[:, 1024:2048], 1: self.OT[:, 2048:3072]}
        KHN = self.OT[:, 0:1024]

        SC16A = self.smalls[0:16, 0:256]
        ve.tensor_tensor(SC16A.rearrange("k (c h) -> k c h", h=H),
                         self.i16f[:].rearrange("k (x h) -> k x h", x=1).broadcast_to([16, NCH, H]),
                         self.cCcol[:].rearrange("k (c x) -> k c x", x=1).broadcast_to([16, NCH, H]),
                         ALU.mult)
        vsc = self.s_ve.n
        self.s_ve.wait(te, vsc)
        tca = self.tem(self.p_e[:, 0:256], self.ones16f[:], SC16A, start=True, stop=True)
        self.s_te.wait(ve, tca)
        DIFA = self.SCR[:, 512:768]
        EXPD = self.SCR[:, 768:1024]
        vda = ve.tensor_tensor(DIFA, self.p_e[:, 0:256], self.negC_all[:], ALU.add)
        self.s_ve.wait(se, vda)
        sea = se.activation(EXPD, DIFA, AF.Exp)
        e16a = self.smalls[0:16, 488:504] if False else self.smalls[0:16, 100:116]
        sca = se.activation(e16a, self.cCcol[:], AF.Exp)
        self.s_se.wait(ve, sca)
        E16BA = self.smallsb[0:16, 0:256]
        veb = ve.tensor_tensor(
            E16BA.rearrange("k (c h) -> k c h", h=H),
            self.i16b[:].rearrange("k (x h) -> k x h", x=1).broadcast_to([16, NCH, H]),
            e16a.rearrange("k (c x) -> k c x", x=1).broadcast_to([16, NCH, H]), ALU.mult)

        lda = {}
        tcoh = {}

        def a1_loads(c):
            p = c % 2
            if c >= 2:
                self.s_te.wait(sy, tcoh[c - 2])
            l1 = self.qdma(sy, self.s_x,
                           KGC[p].rearrange("p (a x t) -> p a x t", x=1, t=CH),
                           self.kgT_d.rearrange("(a c p) t -> p a c t", p=128, c=NCH)[:, :, c:c + 1, :])
            l2 = self.qdma(sy, self.s_x, VNC[p], self.vN_d[c * 128:(c + 1) * 128, :])
            lda[c] = (l1, l2)

        self.s_te.wait(sy)
        self.s_sp.wait(sy)
        a1_loads(0)
        a1_loads(1)

        scm = 0
        for c in range(NCH):
            p = c % 2
            sl = E16BA[0:16, c * H:(c + 1) * H]
            rhs_e = sl.rearrange("k (a b) -> k a b", b=2)[:, :, 0:1].broadcast_to([16, 8, 64])
            rhs_o = sl.rearrange("k (a b) -> k a b", b=2)[:, :, 1:2].broadcast_to([16, 8, 64])
            self.s_ve.wait(te, max(veb, scm))
            self.tem(self.p_d[0:64, 0:512], self.ones16b[:, 0:64], rhs_e, start=True, stop=True)
            teb = self.tem(self.p_d[64:128, 0:512], self.ones16b[:, 64:128], rhs_o, start=True, stop=True,
                           tile_position=(0, 64))
            if c >= 2:
                self.s_sp.wait(ve, self._aspl[c - 2])
            vsv = ve.tensor_copy(self.STB2[:, (c % 2) * 512:(c % 2) * 512 + 512],
                                 self.S_master[:])
            self.s_ve.wait(sy, vsv)
            self._aspl[c] = self.qdma(sy, self.s_sp, self.std_d[c * 128:(c + 1) * 128, :],
                                      self.STB2[:, (c % 2) * 512:(c % 2) * 512 + 512])
            if c + 1 < NCH:
                a1_loads(c + 1)
            self.s_x.wait(te, lda[c][0])
            self.s_ve.wait(te, vsv)
            for i in range(8):
                ins = te.transpose(self.p_t[:, i * 128:(i + 1) * 128],
                                   KGC[p][:, i * 128:(i + 1) * 128], self.ID_B)
            ttr = self.s_te.inc(ins)
            self.s_te.wait(ve, ttr)
            self.s_se.wait(ve, sea)
            vkh = ve.tensor_tensor(
                KHN.rearrange("p (h t) -> p h t", t=64),
                self.p_t[:, 0:1024].rearrange("p (h t) -> p h t", t=64),
                EXPD[:, c * H:(c + 1) * H].rearrange("p (h x) -> p h x", x=1).broadcast_to([128, H, 64]),
                ALU.mult)
            self.s_ve.wait(te, vkh)
            self.s_x.wait(te, lda[c][1])
            for h in range(H):
                pb = self.p_b if h % 2 == 0 else self.p_c
                ins = te.matmul(
                    pb[(h % 2) * 64:(h % 2) * 64 + 64,
                       (h // 2) * 64:(h // 2) * 64 + 64],
                    KHN[:, h * 64:(h + 1) * 64], VNC[p][:, h * 64:(h + 1) * 64],
                    start=(h < 2), stop=(h >= H - 2),
                    tile_position=(0, (h % 2) * 64))
            tcoh[c] = tco = self.s_te.inc(ins)
            self.s_te.wait(ve, max(tco, teb))
            scm = ve.tensor_tensor(self.SCR[:, 0:512], self.S_master[:], self.p_d[:, 0:512], ALU.mult)
            ve.tensor_tensor(self.S_master[0:64, :], self.SCR[0:64, 0:512], self.p_b[0:64, 0:512], ALU.add)
            ve.tensor_tensor(self.S_master[64:128, :], self.SCR[64:128, 0:512], self.p_c[64:128, 0:512], ALU.add)
        vbt = ve.tensor_copy(self.smalls[0:16, 224:225], self.Ballc[:, NCH:NCH + 1])
        self.s_ve.wait(te, vbt)
        tbt = self.tet(self.p_e[0:1, 0:16], self.smalls[0:16, 224:225], self.ID_F[0:16, 0:16])
        self.s_te.wait(ve, tbt)
        ve.memset(self.rows[0:1, 0:512], 0.0)
        vrow = ve.tensor_copy(self.rows[0:1, 0:16], self.p_e[0:1, 0:16])
        self.s_ve.wait(sy, vrow)
        self.qdma(sy, self.s_sp, self.cc_in[0:128, :], self.S_master[:])
        self.qdma(sy, self.s_sp, self.cc_in[128:129, :], self.rows[0:1, 0:512])
        self.s_sp.wait(po)
        if SIM1:
            for b in range(8):
                self.s_cc.inc(po.dma_start(out=self.cc_out[b * 129:(b + 1) * 129, :],
                                           in_=self.cc_in[:, :]), 16)
        else:
            ins = po.collective_compute(
                "AllGather", ALU.bypass,
                ins=[self.cc_in.opt()], outs=[self.cc_out.opt()],
                replica_groups=[list(range(N_CORES))])
            self.s_cc.inc(ins, 1)


    def phase2b(self):
        sy, te, ve, se, po = self.sy, self.te, self.ve, self.se, self.po
        A2S = self.A2S
        KG = {0: A2S[:, 0:1024], 1: A2S[:, 2048:3072]}
        QC = {0: A2S[:, 1024:2048], 1: A2S[:, 3072:4096]}
        SEF = {0: A2S[:, 4096:4608], 1: A2S[:, 4608:5120]}
        VN = {0: self.SCB[:, 0:1024], 1: self.SCB[:, 1024:2048]}
        QT = {0: self.ARE[:, 16384:16896], 1: self.ARE[:, 16896:17408]}
        PB = {0: self.ARE[:, 17408:17920], 1: self.ARE[:, 17920:18432]}
        ABUF = self.ARE[:, 18432:18944]
        EXPC = self.ARE[:, 18944:19456]
        ogS = self.ARE[:, 0:16384]
        SQA = self.ARE[:, 16384:16384 + 4096]
        YB = self.ARE[:, 16384 + 4096:16384 + 8192]
        XB2 = self.ARE[:, 16384 + 8192:16384 + 12288]
        SQ2 = self.ARE[:, 16384 + 12288:16384 + 16384]
        Dbuf = self.SCR[:, 512:1024]
        ld_o = self.load_weight(self.WoT, 0, after_te=self.s_te.n)
        self.f0_init()

        ld = {}
        self._tgo = {}; self._vql = {}; self._ogo = {}; self._evac = {}
        self._post_c_ve = 0
        self._post_c_te = 0
        self._post_c_xg = 0
        self._spx = 0

        def emit_loads(c):
            p = c % 2
            if c >= 2:
                self.s_te.wait(sy, self._tgo[c - 2])
                self.s_ve.wait(sy, self._vql[c - 2])
            l1 = self.qdma(sy, self.s_x, KG[p].rearrange("p (a x t) -> p a x t", x=1, t=CH),
                           self.kgT_d.rearrange("(a c p) t -> p a c t", p=128, c=NCH)[:, :, c:c + 1, :])
            l2 = self.qdma(sy, self.s_x, QC[p].rearrange("p (a x t) -> p a x t", x=1, t=CH),
                           self.qT_d.rearrange("(a c p) t -> p a c t", p=128, c=NCH)[:, :, c:c + 1, :])
            l3 = self.qdma(sy, self.s_x, VN[p], self.vN_d[c * 128:(c + 1) * 128, :])
            l4 = self.qdma(sy, self.s_x, self.LST[:, p * 512:p * 512 + 512],
                           self.std_d[c * 128:(c + 1) * 128, :])
            ld[c] = (l1, l2, l3, l4)

        self.s_sp.wait(sy)
        self.s_te.wait(sy)
        self.s_ve.wait(sy)
        emit_loads(0)
        emit_loads(1)
        next_ld = 2

        self.s_ve.wait(te)
        te.transpose(self.p_d[:, 0:128], self.C_all[:, 0:128], self.ID_F)
        tt2 = self.tet(self.p_d[:, 128:256], self.C_all[:, 128:256], self.ID_F)
        self.s_te.wait(ve, tt2)
        CT = self.SCR[:, 0:256]
        vct = ve.tensor_copy(CT, self.p_d[:, 0:256])
        self.s_ve.wait(sy, vct)
        self.s_sp.wait(sy)
        self.qdma(sy, self.s_sp, self.CHIA.rearrange("c (h t) -> c h t", t=128)[0:8],
                  CT[:, 0:128])
        rch = self.qdma(sy, self.s_sp, self.CHIA.rearrange("c (h t) -> c h t", t=128)[8:16],
                        CT[:, 128:256])
        self.s_sp.wait(se, rch)
        eBa = self.smalls[0:16, 144:160]
        sba = se.activation(eBa, self.Ballc[0:16, 0:16], AF.Exp)
        self.s_se.wait(ve, sba)
        E16A = self.smallsb[0:16, 0:256]
        vE16 = ve.tensor_tensor(
            E16A.rearrange("k (c h) -> k c h", h=16),
            self.i16b[:].rearrange("k (x h) -> k x h", x=1).broadcast_to([16, 16, 16]),
            eBa.rearrange("k (c x) -> k c x", x=1).broadcast_to([16, 16, 16]), ALU.mult)

        self.s_cc.wait(sy)
        B8 = self.smalls[0:8, 208:224]
        self.qdma(sy, self.s_x, B8,
                  self.cc_out.rearrange("(b r) f -> b r f", r=129)[:, 128:129, 0:16])
        EBF = self.SCR[:, 0:512]
        for b in range(8):
            slot = self.XG[:, (1 + (b % 2)) * 512:(2 + (b % 2)) * 512]
            lb = self.qdma(sy, self.s_x, slot,
                           self.cc_out[b * 129:(b * 129) + 128, :])
            wb = self.wind[:, b:b + 1].broadcast_to([8, 64])
            be = B8.rearrange("k (a b2) -> k a b2", b2=2)[:, :, 0:1].broadcast_to([8, 8, 64])
            bo = B8.rearrange("k (a b2) -> k a b2", b2=2)[:, :, 1:2].broadcast_to([8, 8, 64])
            if b == 0:
                self.s_x.wait(te)
                self.s_ve.wait(te)
            self.tem(self.p_a[0:64, 0:512], wb, be, start=True, stop=True)
            tsb = self.tem(self.p_a[64:128, 0:512], wb, bo, start=True, stop=True,
                           tile_position=(0, 64))
            self.s_te.wait(se, tsb)
            seb = se.activation(EBF, self.p_a[:, 0:512], AF.Exp, bias=self.alog[:, b:b + 1])
            self.s_x.wait(ve, lb)
            self.s_se.wait(ve, seb)
            if b == 0:
                vup = ve.tensor_tensor(self.S_master[:], EBF, slot, ALU.mult)
            else:
                ve.tensor_tensor(self.SCR[:, 512:1024], EBF, slot, ALU.mult)
                vup = ve.tensor_tensor(self.S_master[:], self.S_master[:],
                                       self.SCR[:, 512:1024], ALU.add)
            self.s_ve.wait(te, vup)
            self.s_ve.wait(sy, vup)
        comb_seb = seb
        self._post_c_ve = self.s_ve.n
        self._post_c_xg = self.s_ve.n

        vse_c = {}; scrm = {}

        def emit_sef(c):
            p = c % 2
            sl = E16A[0:16, c * 16:(c + 1) * 16]
            rhs_e = sl.rearrange("k (a b) -> k a b", b=2)[:, :, 0:1].broadcast_to([16, 8, 64])
            rhs_o = sl.rearrange("k (a b) -> k a b", b=2)[:, :, 1:2].broadcast_to([16, 8, 64])
            if c == 0:
                self.s_ve.wait(te, vE16)
            else:
                self.s_ve.wait(te, max(scrm[c - 1], self._post_c_ve))
            self.tem(self.p_d[0:64, 0:512], self.ones16b[:, 0:64], rhs_e, start=True, stop=True)
            tse = self.tem(self.p_d[64:128, 0:512], self.ones16b[:, 64:128], rhs_o,
                           start=True, stop=True, tile_position=(0, 64))
            self.s_te.wait(ve, tse)
            if c >= 2:
                self.s_te.wait(ve, self._tgo[c - 2])
            if c == 0:
                self.s_se.wait(ve, comb_seb)
            scrm[c] = ve.tensor_tensor(self.SCR[:, 0:512], self.S_master[:],
                                       self.p_d[:, 0:512], ALU.mult)
            self.s_x.wait(ve, ld[c][3])
            vse_c[c] = ve.tensor_tensor(SEF[p], self.SCR[:, 0:512],
                                        self.LST[:, p * 512:p * 512 + 512], ALU.add)

        emit_sef(0)

        gh = {}

        def emit_front(c, g):
            ga = c * 4 + g; gp = ga % 2; p = c % 2
            if ga == 0:
                self.s_se.wait(te, comb_seb)
                self.s_sp.wait(te, rch)
                self.s_ve.wait(te, vse_c[0])
            else:
                self.s_ve.wait(te, max(gh[ga - 1]["vd"], self._post_c_ve))
            selc = self.i16f[:, c:c + 1].broadcast_to([16, 128])
            te.matmul(self.p_a[:, 0:512], selc,
                      self.CHIA[0:16, g * 512:(g + 1) * 512], start=True, stop=False)
            tma = self.tem(self.p_a[:, 0:512], self.negdiag[:],
                           self.slrep[:].rearrange("p (x t) -> p x t", x=1).broadcast_to([128, 4, CH]),
                           start=False, stop=True)
            if ga > 0:
                self.s_se.wait(te, gh[ga - 1]["sx"])
            tme = self.tem(self.p_e[:, 0:512], selc,
                           self.CHIA[0:16, g * 512:(g + 1) * 512], start=True, stop=True)
            if g == 0:
                self.s_x.wait(te, ld[c][1])
            for hh in range(4):
                h = 4 * g + hh
                pb = self.p_b if hh % 2 == 0 else self.p_c
                ins = te.matmul(pb[:, gp * 256 + (hh // 2) * 128:gp * 256 + (hh // 2) * 128 + 128],
                                KG[p][(h % 2) * 64:(h % 2) * 64 + 64,
                                      (h // 2) * 128:(h // 2) * 128 + 128],
                                QC[p][(h % 2) * 64:(h % 2) * 64 + 64,
                                      (h // 2) * 128:(h // 2) * 128 + 128],
                                start=True, stop=True,
                                tile_position=((h % 2) * 64, 0))
            gh[ga] = {"tma": tma, "tme": tme, "tsp": self.s_te.inc(ins)}

        def emit_chain(c, g):
            ga = c * 4 + g; gp = ga % 2; p = c % 2
            G = gh[ga]
            negb = self.negC_all[:, c * H + 4 * g:c * H + 4 * g + 4].rearrange(
                "p (h x) -> p h x", x=1).broadcast_to([128, 4, 128])
            self.s_te.wait(ve, G["tma"])
            vd = ve.tensor_tensor(Dbuf.rearrange("p (h t) -> p h t", t=128),
                                  self.p_a[:, 0:512].rearrange("p (h t) -> p h t", t=128),
                                  negb, ALU.add)
            self.s_ve.wait(se, vd)
            sa = se.activation(ABUF, Dbuf, AF.Exp)
            self.s_te.wait(se, G["tme"])
            sx = se.activation(EXPC, self.p_e[:, 0:512], AF.Exp)
            self.s_se.wait(ve, sa)
            self.s_te.wait(ve, max(G["tsp"], self._post_c_te))
            if ga >= 2:
                self.s_te.wait(ve, self._ogo[ga - 2])
            a4 = ABUF.rearrange("p (a b t) -> p a b t", a=2, b=2, t=128)
            o4 = PB[gp].rearrange("p (a b t) -> p a b t", a=2, b=2, t=128)
            ve.tensor_tensor(o4[:, :, 0:1, :], a4[:, :, 0:1, :],
                             self.p_b[:, gp * 256:gp * 256 + 256].rearrange(
                                 "p (a x t) -> p a x t", x=1, t=128), ALU.mult)
            vp = ve.tensor_tensor(o4[:, :, 1:2, :], a4[:, :, 1:2, :],
                                  self.p_c[:, gp * 256:gp * 256 + 256].rearrange(
                                      "p (a x t) -> p a x t", x=1, t=128), ALU.mult)
            self.s_se.wait(ve, sx)
            if g == 0:
                self.s_x.wait(ve, ld[c][1])
            qt4 = QT[gp].rearrange("p (a b t) -> p a b t", a=2, b=2, t=128)
            ex4 = EXPC.rearrange("p (a b t) -> p a b t", a=2, b=2, t=128)
            qsrc = QC[p][:, 2 * g * 128:2 * g * 128 + 256].rearrange(
                "p (a x t) -> p a x t", x=1, t=128)
            ve.tensor_tensor(qt4[0:64, :, 0:1, :], qsrc[0:64], ex4[0:64, :, 0:1, :], ALU.mult)
            vq = ve.tensor_tensor(qt4[64:128, :, 1:2, :], qsrc[64:128], ex4[64:128, :, 1:2, :],
                                  ALU.mult)
            G["vd"] = vd; G["sx"] = sx; G["vp"] = vp; G["vq"] = vq

        def emit_O(c, g):
            ga = c * 4 + g; gp = ga % 2; p = c % 2
            G = gh[ga]
            self.s_ve.wait(te, max(G["vp"], G["vq"], vse_c[c]))
            if g == 0:
                self.s_x.wait(te, ld[c][2])
                if c >= 1:
                    self.s_se.wait(te, self._evac[c - 1])
            for hh in range(4):
                h = 4 * g + hh
                te.matmul(self.p_o[(h % 2) * 64:(h % 2) * 64 + 64,
                                   (h // 2) * 128:(h // 2) * 128 + 128],
                          VN[p][:, h * 64:(h + 1) * 64],
                          PB[gp][:, hh * 128:(hh + 1) * 128],
                          start=True, stop=False,
                          tile_position=(0, (h % 2) * 64))
                ins = te.matmul(self.p_o[(h % 2) * 64:(h % 2) * 64 + 64,
                                         (h // 2) * 128:(h // 2) * 128 + 128],
                                SEF[p][(h % 2) * 64:(h % 2) * 64 + 64,
                                       (h // 2) * 64:(h // 2) * 64 + 64],
                                QT[gp][(h % 2) * 64:(h % 2) * 64 + 64,
                                       hh * 128:(hh + 1) * 128],
                                start=False, stop=True,
                                tile_position=((h % 2) * 64, (h % 2) * 64))
            self._ogo[ga] = self.s_te.inc(ins)
            if g == 3:
                self._tgo[c] = self._ogo[ga]
                self._vql[c] = G["vq"]

        def emit_evac(c):
            cc = c % 4
            self.s_te.wait(se, self._tgo[c])
            if c % 4 == 0 and c > 0:
                self.s_ve.wait(se, self._post_c_ve)
            self._evac[c] = se.activation(
                self.OT[:].rearrange("p (m t) -> p m t", t=512)[:, :, cc * 128:(cc + 1) * 128],
                self.p_o[:, 0:1024].rearrange("p (m t) -> p m t", t=128), AF.Copy)

        pend = None
        for gi in range(NG):
            for cc in range(NG):
                c = gi * NG + cc
                if cc == 0:
                    self.s_ve.wait(sy, self._post_c_xg)
                    if self._spx:
                        self.s_sp.wait(sy, self._spx)
                    self._lx = self.qdma(sy, self.s_x,
                                         self.XG[:].rearrange("p (a f) -> p a f", f=TG),
                                         self.xT.rearrange("(a p) t -> p a t", p=128)[:, :, gi * TG:(gi + 1) * TG])
                for g in range(4):
                    emit_front(c, g)
                    emit_chain(c, g)
                    if pend is not None:
                        pc, pg = pend
                        emit_O(pc, pg)
                        if pg == 3:
                            emit_evac(pc)
                            while next_ld < NCH and next_ld <= pc + 2:
                                emit_loads(next_ld)
                                next_ld += 1
                    pend = (c, g)
                if c + 1 < NCH:
                    emit_sef(c + 1)
                self.f0_step(2)
            pc, pg = pend
            emit_O(pc, pg)
            emit_evac(pc)
            pend = None
            while next_ld < NCH and next_ld <= pc + 2:
                emit_loads(next_ld)
                next_ld += 1
            self._c_group(gi, ld_o, ogS, SQA, YB, XB2, SQ2)
            self._post_c_ve = self._c_vqc
            self._post_c_xg = self.s_ve.n
            self._post_c_te = self.s_te.n

    def _c_group(self, gi, ld_o, ogS, SQA, YB, XB2, SQ2):
        sy, te, ve, se, po = self.sy, self.te, self.ve, self.se, self.po
        self.f0_step(4)
        self.s_ve.wait(se)
        for a in range(DMT):
            sq = se.activation(SQA[:, a * 512:(a + 1) * 512],
                               self.OT[:, a * 512:(a + 1) * 512], AF.Square)
        self.s_se.wait(te, sq)
        self.s_ve.wait(te)
        for a in range(DMT):
            ins = te.matmul(self.p_a[0:1, 0:512], self.ONES_COL_B,
                            SQA[:, a * 512:(a + 1) * 512],
                            start=(a == 0), stop=(a == DMT - 1))
        tst = self.s_te.inc(ins)
        self.s_te.wait(se, tst)
        rrow = self.rows[0:1, 0:512]
        se.activation(rrow, self.p_a[0:1, 0:512], AF.Ln,
                      scale=1.0 / HID, bias=self.B_RMSEPS[0:1, :])
        srs = se.activation(rrow, rrow, AF.Exp, scale=-0.5)
        self.s_se.wait(te, srs)
        trb = self.tem(self.p_c[:, 0:512], self.ONES_ROW_F, rrow, start=True, stop=True)
        self.s_te.wait(ve, trb)
        for a in range(DMT):
            ve.scalar_tensor_tensor(YB[:, a * 512:(a + 1) * 512],
                                    self.OT[:, a * 512:(a + 1) * 512],
                                    self.normw_sb[:, a:a + 1],
                                    self.p_c[:, 0:512], ALU.mult, ALU.mult)
            vy = ve.tensor_tensor(YB[:, a * 512:(a + 1) * 512],
                                  YB[:, a * 512:(a + 1) * 512],
                                  ogS[:, a * R + gi * 512:a * R + gi * 512 + 512], ALU.mult)
        self.s_ve.wait(te, vy)
        self.s_wl.wait(te, ld_o)
        self.s_x.wait(ve, self._lx)
        prev_m = {}
        for m in range(DMT):
            pp = self.p_o[:, 0:512] if m % 2 == 0 else self.p_o[:, 512:1024]
            if m >= 2:
                self.s_ve.wait(te, prev_m[m - 2])
            for k in range(DMT):
                ins = te.matmul(pp, self.wtile(0, k, m), YB[:, k * 512:(k + 1) * 512],
                                start=(k == 0), stop=(k == DMT - 1))
            tm = self.s_te.inc(ins)
            self.s_te.wait(ve, tm)
            prev_m[m] = ve.tensor_tensor(self.XG[:, m * 512:(m + 1) * 512],
                                         self.XG[:, m * 512:(m + 1) * 512], pp, ALU.add)
        self.s_ve.wait(sy, prev_m[DMT - 1])
        spx = self.qdma(sy, self.s_sp,
                        self.x2_d.rearrange("(a p) t -> p a t", p=128)[:, :, gi * TG:(gi + 1) * TG],
                        self.XG[:].rearrange("p (a f) -> p a f", f=TG))
        self._spx = spx
        self.s_x.wait(se, self._lx)
        self.s_ve.wait(se, prev_m[DMT - 1])
        for a in range(DMT):
            se.activation(XB2[:, a * 512:(a + 1) * 512], self.XG[:, a * 512:(a + 1) * 512],
                          AF.Copy)
            sml = se.activation(SQ2[:, a * 512:(a + 1) * 512], self.XG[:, a * 512:(a + 1) * 512],
                                AF.Square)
        self.s_se.wait(te, sml)
        for a in range(DMT):
            ins = te.matmul(self.p_a[0:1, 0:512], self.ONES_COL_B,
                            XB2[:, a * 512:(a + 1) * 512], start=(a == 0), stop=(a == DMT - 1))
        self.s_te.inc(ins)
        for a in range(DMT):
            ins = te.matmul(self.p_b[0:1, 0:512], self.ONES_COL_B,
                            SQ2[:, a * 512:(a + 1) * 512], start=(a == 0), stop=(a == DMT - 1))
        tml = self.s_te.inc(ins)
        self.s_te.wait(ve, tml)
        rm = self.rows[0:1, 0:512]; rvar = self.rows[0:1, 512:1024]
        rr = self.rows[0:1, 1024:1536]; rmr = self.rows[0:1, 1536:2048]
        ve.tensor_scalar(rm, self.p_a[0:1, 0:512], 1.0 / DM, None, ALU.mult)
        ve.tensor_scalar(rvar, self.p_b[0:1, 0:512], 1.0 / DM, None, ALU.mult)
        ve.tensor_mul(rr, rm, rm)
        vst2 = ve.tensor_sub(rvar, rvar, rr)
        self.s_ve.wait(se, vst2)
        se.activation(rvar, rvar, AF.Ln, bias=self.B_LNEPS[0:1, :])
        sst2 = se.activation(rr, rvar, AF.Exp, scale=-0.5)
        self.s_se.wait(ve, sst2)
        vrb2 = ve.tensor_mul(rmr, rm, rr)
        self.s_ve.wait(te, vrb2)
        self.tem(self.p_c[:, 0:512], self.ONES_ROW_F, rr, start=True, stop=True)
        tbc2 = self.tem(self.p_d[:, 0:512], self.ONES_ROW_F, rmr, start=True, stop=True)
        self.s_te.wait(ve, tbc2)
        self.s_sp.wait(ve, spx)
        for a in range(DMT):
            ve.tensor_mul(self.SCR[:, 0:512], self.XG[:, a * 512:(a + 1) * 512], self.p_c[:, 0:512])
            vln2 = ve.tensor_sub(self.XG[:, a * 512:(a + 1) * 512], self.SCR[:, 0:512],
                                 self.p_d[:, 0:512])
        rmx = self.SCR[:, 0:512]; rmn = self.SCR[:, 512:1024]
        ve.tensor_copy(rmx, self.XG[:, 0:512])
        ve.tensor_copy(rmn, self.XG[:, 0:512])
        for a in range(1, DMT):
            ve.tensor_tensor(rmx, rmx, self.XG[:, a * 512:(a + 1) * 512], ALU.max)
            ve.tensor_tensor(rmn, rmn, self.XG[:, a * 512:(a + 1) * 512], ALU.min)
        ve.tensor_scalar(rmn, rmn, -1.0, None, ALU.mult)
        vtr = ve.tensor_tensor(rmx, rmx, rmn, ALU.max)
        self.s_ve.wait(te, vtr)
        for ct in range(4):
            ins = te.transpose(self.p_e[:, ct * 128:(ct + 1) * 128],
                               rmx[:, ct * 128:(ct + 1) * 128], self.ID_F)
        ttd = self.s_te.inc(ins)
        self.s_te.wait(ve, ttd)
        red = self.smalls[:, 0:4]
        ve.tensor_reduce(red, self.p_e[:, 0:512].rearrange("p (a b) -> p a b", b=128),
                         AX.X, ALU.max)
        vrd = ve.tensor_scalar(red, red, Q_EPS, None, ALU.max)
        self.s_ve.wait(te, vrd)
        tred = self.tet(self.p_e[0:4, 256:384], red, self.ID_F)
        self.s_te.wait(ve, tred)
        vev = ve.tensor_copy(self.st16[0:4, 0:128], self.p_e[0:4, 256:384])
        self.s_ve.wait(sy, vev)
        self.qdma(sy, self.s_sp, self.mx_d[gi * 4:(gi + 1) * 4, :], self.st16[0:4, 0:128])
        mq = self.qdma(sy, self.s_sp, self.SCR[0:1, 512:1024], self.st16[0:4, 0:128])
        self.s_sp.wait(se, mq)
        srow = self.rows[0:1, 0:512]
        se.activation(srow, self.SCR[0:1, 512:1024], AF.Ln)
        sq2 = se.activation(srow, srow, AF.Exp, scale=-1.0, bias=self.B_LN127[0:1, :])
        self.s_se.wait(te, sq2)
        tqb = self.tem(self.p_c[:, 0:512], self.ONES_ROW_F, srow, start=True, stop=True)
        self.s_te.wait(ve, tqb)
        self._c_vqc = ve.tensor_copy(self.SCR[:, 0:512], self.p_c[:, 0:512])
        i32v = self.SCR[:, 512:1024].bitcast(I32)
        for a in range(DMT):
            ve.tensor_tensor(i32v, self.XG[:, a * 512:(a + 1) * 512],
                             self.SCR[:, 0:512], ALU.mult)
            vq2 = ve.tensor_copy(self.XN[:, a * R + gi * 512:a * R + gi * 512 + 512], i32v)


    def f0_init(self):
        self._f0_steps = []
        for wi, (wd, na, fw) in enumerate([(self.WTg, 8, 4096), (self.WTu, 8, 4096),
                                           (self.WTd, 32, 1024)]):
            for a in range(na):
                if fw > 2048:
                    self._f0_steps.append((wi, wd, a, 0, 2048))
                    self._f0_steps.append((wi, wd, a, 2048, 2048))
                else:
                    self._f0_steps.append((wi, wd, a, 0, fw))
        self._f0_i = 0
        self._f0_l = 0
        self._f0_ld = {}
        self._f0_sab = {}
        self._f0_pm = None
        self.ve.memset(self.smalls[:, 230:233], 0.0)
        self.f0_load()

    def f0_load(self):
        sy, se, po = self.sy, self.se, self.po
        slot1f = self.WBUF[:, 8192:16384].bitcast(F32)
        if self._f0_l >= len(self._f0_steps):
            return
        i = self._f0_l
        self._f0_l += 1
        wi, wd, a, co, fw = self._f0_steps[i]
        half = i % 2
        dst = slot1f[:, half * 2048:half * 2048 + fw].rearrange("p (x f) -> p x f", x=1)
        src = wd.rearrange("(a p) f -> p a f", p=128)[:, a:a + 1, co:co + fw]
        if i < 2:
            self.s_te.wait(po, self.te_v_done)
        else:
            self.s_se.wait(po, self._f0_sab[i - 2])
        self._f0_ld[i] = self.qdma(po, self.s_wl, dst, src)

    def f0_step(self, n=1):
        sy, te, ve, se, po = self.sy, self.te, self.ve, self.se, self.po
        slot1f = self.WBUF[:, 8192:16384].bitcast(F32)
        red = self.smalls[:, 233:234]
        for _ in range(n):
            if self._f0_i >= len(self._f0_steps):
                return
            self.f0_load()
            i = self._f0_i
            self._f0_i += 1
            wi, wd, a, co, fw = self._f0_steps[i]
            half = i % 2
            sl = slot1f[:, half * 2048:half * 2048 + fw]
            self.s_wl.wait(se, self._f0_ld[i])
            if i > 0:
                self.s_ve.wait(se, self._f0_pm)
            self._f0_sab[i] = se.activation(sl, sl, AF.Abs, accum_out=red)
            self.s_se.wait(ve, self._f0_sab[i])
            self._f0_pm = ve.tensor_tensor(self.smalls[:, 230 + wi:231 + wi],
                                           self.smalls[:, 230 + wi:231 + wi], red, ALU.add)

    def phase3_f0(self):
        sy, te, ve, se, po = self.sy, self.te, self.ve, self.se, self.po
        self.f0_step(len(self._f0_steps))
        acc = self.smalls[:, 230:233]
        pm = self._f0_pm
        self.s_ve.wait(te, pm)
        for wi in range(3):
            self.tem(self.p_e[0:1, 100 + wi:101 + wi], self.ONES_COL_F, acc[:, wi:wi + 1],
                     start=True, stop=True)
        tf0 = self.s_te.n
        self.s_te.wait(ve, tf0)
        mrow = self.smalls[0:1, 236:239]
        ve.tensor_scalar(mrow, self.p_e[0:1, 100:103], 1.0 / (DM * FF), None, ALU.mult)
        vm0 = ve.tensor_scalar(mrow, mrow, Q_EPS, None, ALU.max)
        thr = self.smalls[0:1, 240:243]; nthr = self.smalls[0:1, 243:246]
        ve.tensor_scalar(thr, mrow, 0.5, None, ALU.mult)
        vth = ve.tensor_scalar(nthr, thr, -1.0, None, ALU.mult)
        sgm = self.smalls[0:1, 246:247]; md127 = self.smalls[0:1, 247:248]
        ve.tensor_tensor(sgm, mrow[0:1, 0:1], mrow[0:1, 0:1], ALU.mult)
        ve.tensor_tensor(sgm, sgm, mrow[0:1, 1:2], ALU.mult)
        ve.tensor_scalar(sgm, sgm, 1.0 / (127.0 ** 3), None, ALU.mult)
        vsc = ve.tensor_scalar(md127, mrow[0:1, 2:3], 1.0 / 127.0, None, ALU.mult)
        self.s_ve.wait(te, max(vth, vsc))
        for wi in range(3):
            te.matmul(self.p_e[:, 104 + wi:105 + wi], self.ONES_ROW_F, thr[0:1, wi:wi + 1],
                      start=True, stop=True)
            ins = te.matmul(self.p_e[:, 108 + wi:109 + wi], self.ONES_ROW_F, nthr[0:1, wi:wi + 1],
                            start=True, stop=True)
        tthr = self.s_te.inc(ins)
        self.s_te.wait(ve, tthr)
        self.thrc = self.smalls[:, 240:243]
        self.nthrc = self.smalls[:, 243:246]
        ve.tensor_copy(self.thrc, self.p_e[:, 104:107])
        ve.tensor_copy(self.nthrc, self.p_e[:, 108:111])
        self.sgm = sgm; self.md127 = md127

    def ternarize(self, eng, wsrc_f32, tern_dst, tmp, wi, width):
        eng.tensor_scalar(tmp[:, 0:width], wsrc_f32[:, 0:width], self.thrc[:, wi:wi + 1],
                          None, ALU.is_gt)
        eng.tensor_scalar(tern_dst[:, 0:width], wsrc_f32[:, 0:width], self.nthrc[:, wi:wi + 1],
                          None, ALU.is_lt)
        return eng.tensor_tensor(tern_dst[:, 0:width], tmp[:, 0:width], tern_dst[:, 0:width],
                                 ALU.subtract)

    def ternarize4(self, eng, wsrc, dst, tmp, wi):
        eng.tensor_scalar(tmp, wsrc, self.thrc[:, wi:wi + 1], None, ALU.is_gt)
        eng.tensor_scalar(dst, wsrc, self.nthrc[:, wi:wi + 1], None, ALU.is_lt)
        return eng.tensor_tensor(dst, tmp, dst, ALU.subtract)

    def fd_step(self, ternd, tstg, ttmp):
        sy, ve = self.sy, self.ve
        k, half = divmod(self._fd_i, 2)
        if k >= FFT:
            return
        self._fd_i += 1
        if self._fd_t is not None:
            self.s_ve.wait(sy, self._fd_t)
        lw = self.qdma(sy, self.s_x, tstg.rearrange("p (x y f) -> p x y f", x=1, y=1),
                       self.WTd.rearrange("(a p) (h f) -> p a h f", p=128, h=2)[:, k:k + 1, half:half + 1, :])
        self.s_x.wait(ve, lw)
        base = k * 1024 + half * 512
        self._fd_t = self.ternarize(ve, tstg, ternd[:, base:base + 512], ttmp, 2, 512)

    def phase3_f1(self):
        sy, te, ve, se, po = self.sy, self.te, self.ve, self.se, self.po
        slot1f = self.WBUF[:, 8192:16384].bitcast(F32)
        slot0f = self.WBUF[:, 0:8192].bitcast(F32)
        terng = self.A2S[:, 0:4096]
        ternu = self.XG[:].bitcast(BF16)[:, 4096:8192]
        hst = self.SCB[:, 1536:2048]
        ttmp = self.OT[:, 0:4096]
        ternd = self.ARE[:, 0:32768]
        tstg = self.SCB[:, 0:1024].bitcast(F32)
        RMX = self.XG[:, 0:2048]
        self.s_te.wait(ve)
        self.s_se.wait(ve)
        vmn = ve.memset(RMX, 0.0)
        self._fd_i = 0
        self._fd_t = None
        prev_h = None
        prev_ev = {0: None, 1: None}
        def ld_gu(ig):
            lg = self.qdma(sy, self.s_x, slot1f.rearrange("p (a f) -> p a f", f=512),
                           self.WTg.rearrange("(a p) f -> p a f", p=128)[:, :, ig * 512:(ig + 1) * 512])
            lu = self.qdma(self.se.e, self.s_sc, slot0f.rearrange("p (a f) -> p a f", f=512),
                           self.WTu.rearrange("(a p) f -> p a f", p=128)[:, :, ig * 512:(ig + 1) * 512])
            return lg, lu

        self.s_po.wait(sy)
        self.s_ve.wait(sy)
        self.s_te.wait(sy)
        self.s_se.wait(te)
        self.s_ve.wait(se)
        lds = {0: ld_gu(0)}
        prev_se = {0: None, 1: None}
        prev_vh = None

        def sl4(buf, it):
            return buf.rearrange("p (a f) -> p a f", f=512)[:, :, it * 128:(it + 1) * 128]

        for ig in range(DMT):
            lg, lu = lds[ig]
            self.s_x.wait(ve, lg)
            self.ternarize(ve, slot1f, terng, ttmp, 0, 4096)
            self.s_sc.wait(ve, lu)
            vt = self.ternarize(ve, slot0f, ternu, ttmp, 1, 4096)
            if ig + 1 < DMT:
                self.s_ve.wait(sy, vt)
                self.s_ve.wait(self.se.e, vt)
                lds[ig + 1] = ld_gu(ig + 1)
            self.s_ve.wait(te, vt)
            for tg in range(NG):
                for it in range(4):
                    self.fd_step(ternd, tstg, ttmp)
                    par = it % 2
                    pg = self.p_a if par == 0 else self.p_c
                    pu = self.p_b if par == 0 else self.p_d
                    if prev_ev[par] is not None:
                        self.s_ve.wait(te, prev_ev[par])
                        self.s_se.wait(te, prev_se[par])
                    for a2 in range(DMT):
                        te.matmul(pg[:, 0:512], terng[:, a2 * 512 + it * 128:a2 * 512 + it * 128 + 128],
                                  self.XN[:, a2 * R + tg * TG:a2 * R + (tg + 1) * TG],
                                  start=(a2 == 0), stop=(a2 == DMT - 1))
                    for a2 in range(DMT):
                        ins = te.matmul(pu[:, 0:512], ternu[:, a2 * 512 + it * 128:a2 * 512 + it * 128 + 128],
                                        self.XN[:, a2 * R + tg * TG:a2 * R + (tg + 1) * TG],
                                        start=(a2 == 0), stop=(a2 == DMT - 1))
                    tgu = self.s_te.inc(ins)
                    self.s_te.wait(ve, tgu)
                    self.s_te.wait(se, tgu)
                    if prev_h is not None:
                        self.s_sc.wait(ve, prev_h)
                    if prev_vh is not None:
                        self.s_ve.wait(se, prev_vh)
                    sr = se.activation(self.SCR[:, 0:512], pg[:, 0:512], AF.Relu)
                    sqm = se.activation(self.SCR[:, 512:1024], self.SCR[:, 0:512], AF.Square)
                    self.s_se.wait(ve, sqm)
                    vh = ve.tensor_tensor(hst, self.SCR[:, 512:1024], pu[:, 0:512], ALU.mult)
                    self.s_ve.wait(se, vh)
                    sab = se.activation(self.SCR[:, 0:512], hst, AF.Abs)
                    self.s_se.wait(ve, sab)
                    vhm = ve.tensor_tensor(RMX[:, tg * 512:(tg + 1) * 512], RMX[:, tg * 512:(tg + 1) * 512],
                                           self.SCR[:, 0:512], ALU.max)
                    prev_ev[par] = vh
                    prev_se[par] = sr
                    prev_vh = vhm
                    self.s_ve.wait(self.se.e, vh)
                    prev_h = self.qdma(self.se.e, self.s_sc,
                                       self.h_d[(ig * 4 + it) * 128:(ig * 4 + it + 1) * 128,
                                                tg * TG:(tg + 1) * TG], hst)
        self.s_ve.wait(sy)
        self.s_sc.wait(sy)
        self._f2lh0 = self.qdma(sy, self.s_x,
                                self.WBUF[:, 0:16384].rearrange("p (k t) -> p k t", t=512),
                                self.h_d.rearrange("(k p) t -> p k t", p=128)[:, :, 0:TG])
        for tg in range(NG):
            rx = RMX[:, tg * 512:(tg + 1) * 512]
            self.s_ve.wait(te)
            for ct in range(4):
                ins = te.transpose(self.p_e[:, ct * 128:(ct + 1) * 128],
                                   rx[:, ct * 128:(ct + 1) * 128], self.ID_F)
            ttd = self.s_te.inc(ins)
            self.s_te.wait(ve, ttd)
            red = self.smalls[:, 0:4]
            vrd = ve.tensor_reduce(red, self.p_e[:, 0:512].rearrange("p (a b) -> p a b", b=128),
                                   AX.X, ALU.max)
            self.s_ve.wait(te, vrd)
            tred = self.tet(self.p_e[0:4, 256:384], red, self.ID_F)
            self.s_te.wait(ve, tred)
            self.s_sp.wait(ve)
            vev = ve.tensor_copy(self.st16[0:4, 0:128], self.p_e[0:4, 256:384])
            self.s_ve.wait(sy, vev)
            self.qdma(sy, self.s_sp, self.mx_d[16 + tg * 4:16 + (tg + 1) * 4, :],
                      self.st16[0:4, 0:128])

    def phase3_f2(self):
        sy, te, ve, se, po = self.sy, self.te, self.ve, self.se, self.po
        ternd = self.ARE[:, 0:32768]
        HB = {0: self.WBUF[:, 0:16384], 1: self.XN[:]}
        OTF = self.OT[:].bitcast(F32)
        SCBF = self.SCB[:].bitcast(F32)
        S1B = {0: SCBF[:, 0:512], 1: SCBF[:, 512:1024]}
        DSC = {0: self.SCR[:, 0:512], 1: self.SCR[:, 512:1024]}
        i32v = OTF[:, 512:1024].bitcast(I32)
        OSTG = OTF[:, 0:512]
        accs1 = [self.p_a[:, 0:512], self.p_b[:, 0:512], self.p_c[:, 0:512],
                 self.p_e[:, 0:512], self.p_o[:, 0:512], self.p_o[:, 512:1024]]
        self.s_sc.wait(sy)
        self.s_te.wait(sy)
        self.s_ve.wait(sy)
        ldh = {}; ldx = {}; rmc = {}; smath = {}; vmath = {}
        self._f2t8 = {}; self._f2vq = {}; self._f2vo = {}

        def emit_lh(tg):
            p = tg % 2
            if tg >= 2:
                self.s_te.wait(sy, self._f2t8[tg - 2])
                self.s_ve.wait(sy, self._f2vq[tg - 2])
            ldh[tg] = self.qdma(sy, self.s_x, HB[p].rearrange("p (k t) -> p k t", t=512),
                                self.h_d.rearrange("(k p) t -> p k t", p=128)[:, :, tg * TG:(tg + 1) * TG])

        def emit_lx2(tg):
            if tg >= 1:
                self.s_ve.wait(sy, self._f2vo[tg - 1])
            ldx[tg] = self.qdma(sy, self.s_x, self.XG[:].rearrange("p (a f) -> p a f", f=TG),
                                self.x2_d.rearrange("(a p) t -> p a t", p=128)[:, :, tg * TG:(tg + 1) * TG])

        def emit_rows(tg):
            p = tg % 2
            self.s_sp.wait(sy)
            if tg >= 1:
                self.s_ve.wait(sy, vmath[tg - 1])
                self.s_se.wait(sy, smath[tg - 1])
            lr1 = self.qdma(sy, self.s_x, self.st16[0:4, 0:128], self.mx_d[tg * 4:(tg + 1) * 4, :])
            lr2 = self.qdma(sy, self.s_x, self.st16[0:4, 128:256], self.mx_d[16 + tg * 4:16 + (tg + 1) * 4, :])
            self.s_x.wait(sy, lr2)
            r1 = self.qdma(sy, self.s_sp, self.rows[0:1, 0:512], self.st16[0:4, 0:128])
            r2 = self.qdma(sy, self.s_sp, self.rows[0:1, 512:1024], self.st16[0:4, 128:256])
            self.s_sp.wait(ve, r2)
            mxcr = self.rows[0:1, 0:512]; M1r = self.rows[0:1, 512:1024]
            Fr = self.rows[0:1, 1024:1536]; Dr = self.rows[0:1, 1536:2048]
            ve.tensor_tensor(Fr, mxcr, mxcr, ALU.mult)
            ve.tensor_tensor(Fr, Fr, mxcr, ALU.mult)
            vF = ve.tensor_scalar(Fr, Fr, self.sgm, None, ALU.mult)
            self.s_ve.wait(se, vF)
            sF = se.activation(Dr, Fr, AF.Ln)
            self.s_se.wait(ve, sF)
            veF = ve.tensor_scalar(Dr, Dr, -1.0, None, ALU.mult)
            self.s_ve.wait(se, veF)
            sE = se.activation(Dr, Dr, AF.Exp, bias=self.B_LNQEPS[0:1, :])
            self.s_se.wait(ve, sE)
            vD = ve.tensor_tensor(Dr, M1r, Dr, ALU.max)
            self.s_ve.wait(se, vD)
            s1 = se.activation(mxcr, Dr, AF.Ln)
            smath[tg] = se.activation(mxcr, mxcr, AF.Exp, scale=-1.0,
                                      bias=self.B_LN127[0:1, :])
            self.s_se.wait(ve, smath[tg])
            ve.tensor_tensor(M1r, Fr, Dr, ALU.mult)
            vdsc = ve.tensor_scalar(M1r, M1r, self.md127, None, ALU.mult)
            self.s_ve.wait(te, vdsc)
            if tg >= 1:
                self.s_ve.wait(te, rmc[tg - 1])
            tb1 = self.tem(self.p_d[:, 0:512], self.ONES_ROW_F, mxcr, start=True, stop=True)
            self.s_te.wait(ve, tb1)
            vc1 = ve.tensor_copy(S1B[p], self.p_d[:, 0:512])
            self.s_ve.wait(te, vc1)
            tb2 = self.tem(self.p_d[:, 0:512], self.ONES_ROW_F, M1r, start=True, stop=True)
            self.s_te.wait(ve, tb2)
            rmc[tg] = ve.tensor_copy(DSC[p], self.p_d[:, 0:512])
            vmath[tg] = rmc[tg]

        ldh[0] = self._f2lh0
        emit_lx2(0)
        emit_rows(0)
        outr = self.outT.rearrange("(a p) t -> p a t", p=128)
        for tg in range(NG):
            p = tg % 2
            self.s_x.wait(ve, ldh[tg])
            vqs = {}
            for k in range(FFT):
                ve.tensor_tensor(i32v, HB[p][:, k * 512:(k + 1) * 512], S1B[p], ALU.mult)
                vqs[k] = ve.tensor_copy(HB[p][:, k * 512:(k + 1) * 512], i32v)
            self._f2vq[tg] = vqs[FFT - 1]
            if tg + 1 < NG:
                emit_lh(tg + 1)
            for k in range(FFT):
                self.s_ve.wait(te, vqs[k])
                for m2 in range(6):
                    ins = te.matmul(accs1[m2],
                                    ternd[:, k * 1024 + m2 * 128:k * 1024 + m2 * 128 + 128],
                                    HB[p][:, k * 512:(k + 1) * 512],
                                    start=(k == 0), stop=(k == FFT - 1))
            t6 = self.s_te.inc(ins)
            if tg + 1 < NG:
                emit_rows(tg + 1)
            self.s_x.wait(ve, ldx[tg])
            vos = {}

            def evac(m2, pp):
                self.s_out.wait(ve)
                ve.tensor_tensor(OSTG, pp, DSC[p], ALU.mult)
                vo = ve.tensor_tensor(OSTG, OSTG, self.XG[:, m2 * 512:(m2 + 1) * 512], ALU.add)
                self.s_ve.wait(sy, vo)
                self.s_out.wait(sy)
                self.s_out.dma(sy.dma_start(
                    out=outr[:, m2:m2 + 1, tg * TG:(tg + 1) * TG],
                    in_=OSTG.rearrange("p (x t) -> p x t", x=1)))
                return vo

            self.s_te.wait(ve, t6)
            vos[0] = evac(0, accs1[0])
            vos[1] = evac(1, accs1[1])
            self.s_ve.wait(te, vos[1])
            for k in range(FFT):
                for m2 in (6, 7):
                    ins = te.matmul(accs1[m2 - 6],
                                    ternd[:, k * 1024 + m2 * 128:k * 1024 + m2 * 128 + 128],
                                    HB[p][:, k * 512:(k + 1) * 512],
                                    start=(k == 0), stop=(k == FFT - 1))
            t8 = self.s_te.inc(ins)
            self._f2t8[tg] = t8
            for m2 in range(2, 6):
                vos[m2] = evac(m2, accs1[m2])
            self.s_te.wait(ve, t8)
            vos[6] = evac(6, accs1[0])
            vos[7] = evac(7, accs1[1])
            self._f2vo[tg] = vos[7]
            if tg + 1 < NG:
                emit_lx2(tg + 1)

    def emit_debug(self):
        sy = self.sy
        self.s_se.wait(sy)
        self.s_ve.wait(sy)
        self.s_te.wait(sy)
        self.s_sp.wait(sy)
        self.s_wl.wait(sy)
        def dbg(name, src, shape, dt):
            o = self.out(name, shape, dt)
            self.s_out.wait(sy)
            self.s_out.dma(sy.dma_start(out=o, in_=src))
        dbg("d_xn", self.XN[:], [128, DMT * R], BF16)
        dbg("d_rows", self.rows[:], [1, 4 * TG], F32)
        dbg("d_xg", self.XG[:], [128, DMT * TG], F32)
        dbg("d_sq", self.ARE[:, 16384 + DMT * TG:16384 + DMT * TG + 512], [128, 512], BF16)
        dbg("d_scrow", self.SCR[0:1, :], [1, 1024], F32)
        dbg("d_rowsb", self.rowsb, [1, 2 * TG], BF16)
        dbg("d_Call", self.C_all[:], [128, NCH * H], F32)
        if PHASES >= 2:
            dbg("d_gS", self.ARE[:, 16384:32768], [128, DMT * R], BF16)
            dbg("d_ogS", self.ARE[:, 0:16384], [128, DMT * R], BF16)
        dbg("d_Ball", self.Ballc[:], [16, NCH + 1], F32)
        dbg("d_cC", self.cCcol[:], [16, NCH], F32)
        dbg("d_states", self.std_d, [NCH * 128, 512], BF16)
        if PHASES >= 4:
            dbg("d_xq", self.XN[:], [128, DMT * R], BF16)
            dbg("d_mx", self.mx_d, [2 * NG * 4, 128], F32)
            dbg("d_x2", self.x2_d, [DM, R], F32)
        if PHASES >= 5:
            dbg("d_hd", self.h_d, [FFT * 128, R], BF16)
            dbg("d_smalls", self.smalls[0:1, 236:239], [1, 3], F32)
        dbg("d_Smaster", self.S_master[:], [128, 512], F32)
        if PHASES >= 2:
            dbg("d_kg", self.kgT_d, [HT * NCH * 128, CH], BF16)
            dbg("d_q", self.qT_d, [HT * NCH * 128, CH], BF16)
            dbg("d_v", self.vN_d, [NCH * 128, HID], BF16)

    def finish(self):
        sy = self.sy
        self.s_se.wait(sy)
        self.s_ve.wait(sy)
        self.s_te.wait(sy)
        self.s_sp.wait(sy)
        self.s_wl.wait(sy)
        self.s_po.wait(sy)
        self.s_sp.wait(sy)
        self.s_x.wait(sy)
        self.s_out.wait(sy)


def _consts(rank):
    import ml_dtypes
    j = np.arange(128)
    negUT = -np.triu(np.ones((128, 128), dtype=np.float32))
    wind = np.zeros((8, 8), dtype=np.float32)
    for b in range(8):
        for k in range(8):
            if b + 1 <= k <= rank - 1:
                wind[k, b] = 1.0
    alog = np.array([0.0 if b < rank else NEG for b in range(8)], dtype=np.float32)
    return {
        "c_id_bf": np.eye(128, dtype=ml_dtypes.bfloat16),
        "c_id_f": np.eye(128, dtype=np.float32),
        "c_negUT": negUT,
        "c_onescb": np.ones((128, 1), dtype=ml_dtypes.bfloat16),
        "c_onescf": np.ones((128, 1), dtype=np.float32),
        "c_onesrb": np.ones((1, 128), dtype=ml_dtypes.bfloat16),
        "c_onesrf": np.ones((1, 128), dtype=np.float32),
                "c_i16f": np.eye(16, dtype=np.float32),
        "c_i16b": np.eye(16, dtype=ml_dtypes.bfloat16),
        "c_ones16f": np.ones((16, 128), dtype=np.float32),
        "c_ones16b": np.ones((16, 128), dtype=ml_dtypes.bfloat16),
        "c_actc": np.tile(np.array([[LN_EPS, 1.0, RMS_EPS, LN127, float(np.log(Q_EPS))]], dtype=np.float32), (128, 1)),
        "c_negdiag": (np.eye(128) * NEG).astype(ml_dtypes.bfloat16),
        "c_slrep": np.where(j[:, None] > j[None, :], 1.0, 0.0).astype(ml_dtypes.bfloat16),
        "c_wind": wind,
        "c_alog": np.tile(alog[None, :], (128, 1)).astype(np.float32),
    }


def kernel(L0_attn_x, L0_attn_Wq, L0_attn_Wk, L0_attn_Wv, L0_attn_Wo, L0_attn_Wg,
           L0_attn_Wog, L0_attn_Wd, L0_attn_bd, L0_attn_norm_w,
           L0_ffn_w_gate, L0_ffn_w_up, L0_ffn_w_down):
    x = np.asarray(L0_attn_x)[0]
    T = lambda w: np.ascontiguousarray(np.asarray(w).T.astype(np.float32))
    WqT, WkT, WvT, WgT, WogT = T(L0_attn_Wq), T(L0_attn_Wk), T(L0_attn_Wv), T(L0_attn_Wg), T(L0_attn_Wog)
    WoT = T(L0_attn_Wo)
    WdT = T(L0_attn_Wd)
    WdTp = np.ascontiguousarray(WdT.reshape(DMT, 128, H).transpose(1, 0, 2).reshape(128, DMT * H))
    WTg, WTu = T(L0_ffn_w_gate), T(L0_ffn_w_up)
    WTd = T(L0_ffn_w_down)
    bd = np.asarray(L0_attn_bd, dtype=np.float32).reshape(1, H)
    normw = np.ascontiguousarray(np.asarray(L0_attn_norm_w, dtype=np.float32).reshape(HT, 128).T)

    mk = MK()
    nc = mk.build()
    in_maps = []
    for c in range(N_CORES):
        xT_c = np.ascontiguousarray(x[c * R:(c + 1) * R, :].T)
        m = {"xT": xT_c, "WqT": WqT, "WkT": WkT, "WvT": WvT, "WgT": WgT, "WogT": WogT,
             "WoT": WoT, "WdTp": WdTp, "bd": bd, "normw": normw,
             "WTg": WTg, "WTu": WTu, "WTd": WTd}
        m.update(_consts(c))
        in_maps.append(m)
    res = run_bass_kernel_spmd(nc, in_maps, core_ids=list(range(N_CORES)))
    outs = [np.asarray(r["outT"]).T for r in res.results]
    out = np.concatenate(outs, axis=0)[None]
    if DEBUG:
        kernel._results = res
    return out.astype(np.float32)


# version 21
# speedup vs baseline: 1.1182x; 1.0059x over previous
import numpy as np
import concourse.bass as bass
import concourse.mybir as mybir
from concourse.bass_utils import run_bass_kernel_spmd
from contextlib import ExitStack

F32 = mybir.dt.float32
BF16 = mybir.dt.bfloat16
I32 = mybir.dt.int32
AF = mybir.ActivationFunctionType
ALU = mybir.AluOpType
AX = mybir.AxisListType

DEBUG = False
SIM1 = False
PHASES = 99
N_CORES = 8
R = 2048
TG = 512
NG = 4
CH = 128
NCH = 16
H = 16; DH = 64; DM = 1024; DMT = 8; HID = 1024; HT = 8
FF = 4096; FFT = 32
LN_EPS = 1e-5
RMS_EPS = float(np.finfo(np.float32).eps)
Q_EPS = 1e-5
NEG = -1.0e30
LN127 = float(np.log(127.0))


class EW:
    OPS = {"tensor_copy", "tensor_tensor", "tensor_add", "tensor_sub", "tensor_mul",
           "tensor_max", "tensor_scalar", "tensor_single_scalar", "tensor_scalar_add",
           "tensor_scalar_sub", "tensor_scalar_mul", "tensor_scalar_max",
           "tensor_scalar_min", "tensor_relu", "tensor_reduce", "reduce_max",
           "reduce_sum", "scalar_tensor_tensor", "tensor_tensor_scan", "memset",
           "memzero", "select", "copy_predicated", "reciprocal", "transpose",
           "activation", "sign", "square", "sqrt", "mul", "copy", "add",
           "bn_stats", "bn_aggr", "max", "pool", "pool_avg", "pool_max",
           "stream_shuffle", "iota"}

    def __init__(self, eng, sem):
        self.e = eng
        self.s = sem

    def __getattr__(self, k):
        f = getattr(self.e, k)
        if k not in EW.OPS:
            return f
        def g(*a, **kw):
            if self.s.n > 0:
                self.e.wait_ge(self.s.h, self.s.n)
            ins = f(*a, **kw)
            return self.s.inc(ins)
        return g


class Sem:
    def __init__(self, nc, es, name):
        self.h = es.enter_context(nc.semaphore(name))
        self.n = 0

    def inc(self, instr, k=1):
        instr.then_inc(self.h, k)
        self.n += k
        return self.n

    def dma(self, instr):
        return self.inc(instr, 16)

    def wait(self, eng, n=None):
        e = eng.e if isinstance(eng, EW) else eng
        e.wait_ge(self.h, self.n if n is None else n)


class MK:

    def __init__(self):
        self.nc = bass.Bass("TRN2", target_bir_lowering=False)
        self.es = ExitStack()

    def sb(self, name, shape, dt):
        return self.es.enter_context(self.nc.sbuf_tensor(name, shape, dt))

    def ps(self, name, shape, dt=F32):
        return self.es.enter_context(self.nc.psum_tensor(name, shape, dt))

    def inp(self, name, shape, dt=F32):
        return self.nc.dram_tensor(name, shape, dt, kind="ExternalInput").ap()

    def out(self, name, shape, dt=F32):
        return self.nc.dram_tensor(name, shape, dt, kind="ExternalOutput").ap()

    def dram(self, name, shape, dt):
        return self.nc.dram_tensor(name, shape, dt, kind="Internal").ap()

    def qdma(self, queue, sem, out, in_, **kw):
        sem.wait(queue)
        return sem.dma(queue.dma_start(out=out, in_=in_, **kw))

    def build(self):
        nc = self.nc
        self.xT = self.inp("xT", [DM, R])
        self.WqT = self.inp("WqT", [DM, HID]); self.WkT = self.inp("WkT", [DM, HID])
        self.WvT = self.inp("WvT", [DM, HID]); self.WgT = self.inp("WgT", [DM, HID])
        self.WogT = self.inp("WogT", [DM, HID]); self.WoT = self.inp("WoT", [HID, DM])
        self.WdTp = self.inp("WdTp", [128, DMT * H])
        self.bd = self.inp("bd", [1, H])
        self.normw = self.inp("normw", [128, HT])
        self.WTg = self.inp("WTg", [DM, FF]); self.WTu = self.inp("WTu", [DM, FF])
        self.WTd = self.inp("WTd", [FF, DM])
        self.c_id_bf = self.inp("c_id_bf", [128, 128], BF16)
        self.c_id_f = self.inp("c_id_f", [128, 128], F32)
        self.c_negUT = self.inp("c_negUT", [128, 128], F32)
        self.c_onescb = self.inp("c_onescb", [128, 1], BF16)
        self.c_onescf = self.inp("c_onescf", [128, 1], F32)
        self.c_onesrb = self.inp("c_onesrb", [1, 128], BF16)
        self.c_onesrf = self.inp("c_onesrf", [1, 128], F32)
        self.c_i16f = self.inp("c_i16f", [16, 16], F32)
        self.c_i16b = self.inp("c_i16b", [16, 16], BF16)
        self.c_ones16f = self.inp("c_ones16f", [16, 128], F32)
        self.c_ones16b = self.inp("c_ones16b", [16, 128], BF16)
        self.c_actc = self.inp("c_actc", [128, 5], F32)
        self.c_negdiag = self.inp("c_negdiag", [128, 128], BF16)
        self.c_slrep = self.inp("c_slrep", [128, CH], BF16)
        self.c_wind = self.inp("c_wind", [8, 8], F32)
        self.c_alog = self.inp("c_alog", [128, 8], F32)
        self.outT = self.out("outT", [DM, R], F32)
        self.qT_d = self.dram("qT_d", [HT * NCH * 128, CH], BF16)
        self.kgT_d = self.dram("kgT_d", [HT * NCH * 128, CH], BF16)
        self.vN_d = self.dram("vN_d", [NCH * 128, HID], BF16)
        self.x2_d = self.dram("x2_d", [DM, R], F32)
        self.h_d = self.dram("h_d", [FFT * 128, R], BF16)
        self.mx_d = self.dram("mx_d", [2 * NG * 4, 128], F32)
        self.cc_in = self.dram("cc_in", [129, 512], F32)
        self.cc_out = self.dram("cc_out", [8 * 129, 512], F32)
        self.std_d = self.dram("std_d", [NCH * 128, 512], BF16)

        self.XN = self.sb("XN", [128, DMT * R], BF16)
        self.ARE = self.sb("ARE", [128, 32768], BF16)
        self.WBUF = self.sb("WBUF", [128, 2 * 8192], BF16)
        self.XG = self.sb("XG", [128, DMT * TG], F32)
        self.SCR = self.sb("SCR", [128, 1024], F32)
        self.SCB = self.sb("SCB", [128, 2048], BF16)
        self.OT = self.sb("OT", [128, HT * TG], BF16)
        self.C_all = self.sb("C_all", [128, NCH * H], F32)
        self.negC_all = self.sb("negC_all", [128, NCH * H], F32)
        self.A2S = self.sb("A2S", [128, 5120], BF16)
        self.CHIA = self.sb("CHIA", [16, 2048], F32)
        self.STB2 = self.sb("STB2", [128, 1024], BF16)
        self.LST = self.sb("LST", [128, 1024], BF16)
        self.st16 = self.sb("st16", [16, 256], F32)
        self.rows = self.sb("rows", [1, 4 * TG], F32)
        self.cCcol = self.sb("cCcol", [16, NCH], F32)
        self.Ballc = self.sb("Ballc", [16, NCH + 1], F32)
        self.S_master = self.sb("S_master", [128, 512], F32)
        self.smalls = self.sb("smalls", [128, 256], F32)
        self.smallsb = self.sb("smallsb", [128, 256], BF16)
        self.wd_sb = self.sb("wd_sb", [128, DMT * H], F32)
        self.wdbar = self.sb("wdbar", [128, H], F32)
        self.bdb = self.sb("bdb", [128, H], F32)
        self.normw_sb = self.sb("normw_sb", [128, HT], F32)
        self.consts = self.sb("consts", [128, 400], F32)
        self.constsb = self.sb("constsb", [128, 264], BF16)
        self.rowsb = self.A2S[0:1, 0:1024]
        self.i16f = self.sb("i16f", [16, 16], F32)
        self.i16b = self.sb("i16b", [16, 16], BF16)
        self.ones16f = self.sb("ones16f", [16, 128], F32)
        self.ones16b = self.sb("ones16b", [16, 128], BF16)
        self.negdiag = self.sb("negdiag", [128, 128], BF16)
        self.slrep = self.sb("slrep", [128, CH], BF16)
        self.wind = self.sb("wind", [8, 8], F32)
        self.alog = self.sb("alog", [128, 8], F32)
        self.p_a = self.ps("p_a", [128, 512]); self.p_b = self.ps("p_b", [128, 512])
        self.p_c = self.ps("p_c", [128, 512]); self.p_d = self.ps("p_d", [128, 512])
        self.p_e = self.ps("p_e", [128, 512])
        self.p_o = self.ps("p_o", [128, 1024])
        self.p_t = self.ps("p_t", [128, 1024], BF16)

        es = self.es
        self.s_wl = Sem(nc, es, "s_wl")
        self.s_x = Sem(nc, es, "s_x")
        self.s_sp = Sem(nc, es, "s_sp")
        self.s_te = Sem(nc, es, "s_te")
        self.s_ve = Sem(nc, es, "s_ve")
        self.s_se = Sem(nc, es, "s_se")
        self.s_po = Sem(nc, es, "s_po")
        self.s_wq = Sem(nc, es, "s_wq")
        self.s_sc = Sem(nc, es, "s_sc")
        self.s_cc = Sem(nc, es, "s_cc")
        self.s_out = Sem(nc, es, "s_out")

        with self.es:
            with nc.Block() as block:
                @block.sync
                def _(sync):
                    self.emit()
        return nc

    def emit(self):
        nc = self.nc
        self.sy = nc.sync
        self.te = nc.tensor
        self.ve = EW(nc.vector, self.s_ve)
        self.se = EW(nc.scalar, self.s_se)
        self.po = EW(nc.gpsimd, self.s_po)
        self.phase0()
        self.phase1a()
        if PHASES >= 2:
            self.phase1b()
        if PHASES >= 3:
            self.phase2a()
        if PHASES >= 4:
            self.phase2b()
        if PHASES >= 5:
            self.phase3_f0()
            self.phase3_f1()
            self.phase3_f2()
        if DEBUG:
            self.emit_debug()
        self.finish()

    def preload_set(self, name):
        from concourse.hw_specs import get_activation_tables
        import concourse.mybir as mb
        tabs = get_activation_tables(self.nc.m.arch)
        idx = list(tabs.keys()).index(name)
        return self.se.e.add_instruction(mb.InstLoadActFuncSet(
            name=self.nc.get_next_instruction_name(),
            act_func_set_id=idx, ins=[], outs=[]))

    def tem(self, *a, **kw):
        ins = self.te.matmul(*a, **kw)
        return self.s_te.inc(ins)

    def tet(self, *a, **kw):
        ins = self.te.transpose(*a, **kw)
        return self.s_te.inc(ins)

    def phase0(self):
        sy, te, ve, se, po = self.sy, self.te, self.ve, self.se, self.po
        cs, cb = self.consts, self.constsb
        self.ID_F = cs[:, 0:128]
        self.NEGUT = cs[:, 128:256]
        self.ONES_COL_F = cs[:, 256:257]
        self.ONES_ROW_F = cs[0:1, 257:385]
        self.B_LNEPS = cs[:, 385:386]
        self.B_ONE = cs[:, 386:387]
        self.B_RMSEPS = cs[:, 387:388]
        self.B_LN127 = cs[:, 388:389]
        self.B_LNQEPS = cs[:, 389:390]
        self.ID_B = cb[:, 0:128]
        self.ONES_COL_B = cb[:, 128:129]
        self.ONES_ROW_B = cb[0:1, 129:257]
        for dst, src in [
            (self.ID_F, self.c_id_f), (self.NEGUT, self.c_negUT),
            (self.ONES_COL_F, self.c_onescf), (self.ONES_ROW_F, self.c_onesrf),
            (self.ID_B, self.c_id_bf), (self.ONES_COL_B, self.c_onescb),
            (self.ONES_ROW_B, self.c_onesrb),             (self.normw_sb[:], self.normw),
            (self.i16f[:], self.c_i16f), (self.i16b[:], self.c_i16b),
            (self.ones16f[:], self.c_ones16f), (self.ones16b[:], self.c_ones16b),
            (cs[:, 385:390], self.c_actc), (self.wd_sb[:], self.WdTp),
            (self.st16[0:1, 240:256], self.bd),
            (self.negdiag[:], self.c_negdiag), (self.slrep[:], self.c_slrep),
            (self.wind[:], self.c_wind), (self.alog[:], self.c_alog),
        ]:
            self.qdma(sy, self.s_x, dst, src)
        self.s_x.wait(te)
        self.tem(self.p_a[:, 0:H], self.ONES_ROW_F, self.st16[0:1, 240:256], start=True, stop=True)
        for a in range(DMT):
            ins = te.matmul(self.p_b[0:1, 0:H], self.ONES_COL_F, self.wd_sb[:, a * H:(a + 1) * H],
                            start=(a == 0), stop=(a == DMT - 1))
        t0 = self.s_te.inc(ins)
        self.s_te.wait(ve, t0)
        ve.tensor_copy(self.bdb[:], self.p_a[:, 0:H])
        v0 = ve.tensor_copy(self.st16[0:1, 224:240], self.p_b[0:1, 0:H])
        self.s_ve.wait(te, v0)
        t1 = self.tem(self.p_a[:, H:2 * H], self.ONES_ROW_F, self.st16[0:1, 224:240],
                      start=True, stop=True)
        self.s_te.wait(ve, t1)
        ve.tensor_copy(self.wdbar[:], self.p_a[:, H:2 * H])
        ve.memset(self.S_master[:], 0.0)
        ve.memset(self.Ballc[:], 0.0)

    def phase1a(self):
        sy, te, ve, se, po = self.sy, self.te, self.ve, self.se, self.po
        self.preload_set("natural_log_exp_and_others")
        ROW = self.rows
        AREF = self.ARE[:].bitcast(F32)
        XGB1 = {0: self.XG[:], 1: AREF[:, 12288:16384]}
        ldx = {}
        lastz = {}
        self._p1ve = {}

        def p1_load(tg):
            if tg >= 2:
                self.s_te.wait(sy, lastz[tg - 2])
                self.s_ve.wait(sy, self._p1ve[tg - 2])
            ldx[tg] = self.qdma(sy, self.s_x,
                                XGB1[tg % 2].rearrange("p (a f) -> p a f", f=TG),
                                self.xT.rearrange("(a p) t -> p a t", p=128)[:, :, tg * TG:(tg + 1) * TG])

        self.s_te.wait(sy)
        p1_load(0)
        p1_load(1)
        for tg in range(NG):
            XGg = XGB1[tg % 2]
            ld = ldx[tg]
            self.s_x.wait(ve, ld)
            self.s_x.wait(se, ld)
            if tg > 0:
                self.s_te.wait(ve)
                self.s_te.wait(se)
            xb = self.ARE[:, 16384:16384 + DMT * TG]
            sqt = self.ARE[:, 16384 + DMT * TG:16384 + 2 * DMT * TG]
            for a in range(DMT):
                vml = ve.tensor_copy(xb[:, a * TG:(a + 1) * TG], XGg[:, a * TG:(a + 1) * TG])
            for a in range(DMT):
                sml = se.activation(sqt[:, a * TG:(a + 1) * TG], XGg[:, a * TG:(a + 1) * TG], AF.Square)
            self.s_ve.wait(te, vml)
            self.s_se.wait(te, sml)
            for a in range(DMT):
                ins = te.matmul(self.p_a[0:1, 0:TG], self.ONES_COL_B, xb[:, a * TG:(a + 1) * TG],
                                start=(a == 0), stop=(a == DMT - 1))
            self.s_te.inc(ins)
            for a in range(DMT):
                ins = te.matmul(self.p_b[0:1, 0:TG], self.ONES_COL_B, sqt[:, a * TG:(a + 1) * TG],
                                start=(a == 0), stop=(a == DMT - 1))
            tml = self.s_te.inc(ins)
            self.s_te.wait(ve, tml)
            if DEBUG and tg == 3:
                ve.tensor_copy(self.SCR[0:1, 512:1024], self.p_b[0:1, 0:TG])
                ve.tensor_copy(self.SCR[0:1, 0:512], self.p_a[0:1, 0:TG])
            rm = ROW[0:1, 0:TG]; rvar = ROW[0:1, TG:2 * TG]
            rr = ROW[0:1, 2 * TG:3 * TG]; rmr = ROW[0:1, 3 * TG:4 * TG]
            ve.tensor_scalar(rm, self.p_a[0:1, 0:TG], 1.0 / DM, None, ALU.mult)
            ve.tensor_scalar(rvar, self.p_b[0:1, 0:TG], 1.0 / DM, None, ALU.mult)
            ve.tensor_mul(rr, rm, rm)
            vst = ve.tensor_sub(rvar, rvar, rr)
            if DEBUG and tg == 3:
                vst = ve.tensor_copy(self.rowsb[0:1, 0:512], rvar)
            self.s_ve.wait(se, vst)
            se.activation(rvar, rvar, AF.Ln, bias=self.B_LNEPS[0:1, :])
            sst = se.activation(rr, rvar, AF.Exp, scale=-0.5)
            if DEBUG and tg == 3:
                sst = se.activation(self.rowsb[0:1, 512:1024], rvar, AF.Copy)
            self.s_se.wait(ve, sst)
            vrb = ve.tensor_mul(rmr, rm, rr)
            self.s_ve.wait(te, vrb)
            self.tem(self.p_c[:, 0:TG], self.ONES_ROW_F, rr, start=True, stop=True)
            tbc = self.tem(self.p_d[:, 0:TG], self.ONES_ROW_F, rmr, start=True, stop=True)
            self.s_te.wait(ve, tbc)
            for a in range(DMT):
                t1 = self.SCR[:, 0:TG]
                ve.tensor_mul(t1, XGg[:, a * TG:(a + 1) * TG], self.p_c[:, 0:TG])
                ve.tensor_sub(self.XN[:, a * R + tg * TG: a * R + (tg + 1) * TG], t1, self.p_d[:, 0:TG])

            self.s_ve.wait(te)
            for cc in range(NG):
                t0 = cc * CH
                te.transpose(self.p_e[0:CH, cc:cc + 1], rr[0:1, t0:t0 + CH], self.ID_F[0:1, 0:1])
                ins = te.transpose(self.p_e[0:CH, 4 + cc:5 + cc], rmr[0:1, t0:t0 + CH],
                                   self.ID_F[0:1, 0:1])
            self.s_te.inc(ins)
            for cc in range(NG):
                t0 = cc * CH
                for a in range(DMT):
                    ins = te.matmul(self.p_o[:, cc * H:(cc + 1) * H],
                                    XGg[:, a * TG + t0:a * TG + t0 + CH],
                                    self.wd_sb[:, a * H:(a + 1) * H],
                                    start=(a == 0), stop=(a == DMT - 1))
            tz = self.s_te.inc(ins)
            self.s_te.wait(ve, tz)
            rc8 = self.smalls[:, 0:8]
            ve.tensor_copy(rc8, self.p_e[:, 0:8])
            z1 = self.smalls[:, 16:80]; zz = self.smalls[:, 96:160]
            ve.tensor_tensor(z1.rearrange("p (c h) -> p c h", h=H),
                             self.p_o[:, 0:64].rearrange("p (c h) -> p c h", h=H),
                             rc8[:, 0:4].rearrange("p (c x) -> p c x", x=1).broadcast_to([128, 4, H]),
                             ALU.mult)
            ve.tensor_tensor(zz.rearrange("p (c h) -> p c h", h=H),
                             self.wdbar[:].rearrange("p (x h) -> p x h", x=1).broadcast_to([128, 4, H]),
                             rc8[:, 4:8].rearrange("p (c x) -> p c x", x=1).broadcast_to([128, 4, H]),
                             ALU.mult)
            ve.tensor_sub(z1, z1, zz)
            vz = ve.tensor_tensor(zz.rearrange("p (c h) -> p c h", h=H),
                                  z1.rearrange("p (c h) -> p c h", h=H),
                                  self.bdb[:].rearrange("p (x h) -> p x h", x=1).broadcast_to([128, 4, H]),
                                  ALU.add)
            self.s_ve.wait(se, vz)
            az = self.smalls[:, 160:224]
            se.activation(az, zz, AF.Abs)
            se.activation(az, az, AF.Exp, scale=-1.0)
            sz = se.activation(az, az, AF.Ln, bias=self.B_ONE)
            self.s_se.wait(ve, sz)
            sp = z1
            ve.tensor_scalar(sp, zz, 0.0, None, ALU.max)
            vsp = ve.tensor_add(sp, sp, az)
            self.s_ve.wait(te, vsp)
            for cc in range(NG):
                ins = self.tem(self.p_o[:, 512 + cc * H:512 + (cc + 1) * H], self.NEGUT,
                               sp[:, cc * H:(cc + 1) * H], start=True, stop=True)
            tc = ins
            self.s_te.wait(ve, tc)
            Cg = self.C_all[:, tg * 64:(tg + 1) * 64]
            ve.tensor_copy(Cg, self.p_o[:, 512:576])
            vch = ve.tensor_scalar(self.negC_all[:, tg * 64:(tg + 1) * 64],
                                   self.p_o[:, 512:576], -1.0, None, ALU.mult)
            self.s_ve.wait(te, vch)
            for cc in range(NG):
                ins = self.tet(self.p_e[0:H, cc * CH:(cc + 1) * CH],
                               Cg[:, cc * H:(cc + 1) * H], self.ID_F)
            tt = ins
            self.s_te.wait(ve, tt)
            tot = self.Ballc[:, NCH:NCH + 1]
            for cc in range(NG):
                c = tg * NG + cc
                ve.tensor_copy(self.Ballc[:, c:c + 1], tot)
                ve.tensor_copy(self.cCcol[:, c:c + 1],
                               self.p_e[0:H, cc * CH + CH - 1:cc * CH + CH])
                vbk = ve.tensor_add(tot, tot, self.cCcol[:, c:c + 1])
            lastz[tg] = tz
            self._p1ve[tg] = vbk
            if tg + 2 < NG:
                p1_load(tg + 2)

    def load_weight(self, w_dram, slot, cols=HID, after_te=None):
        po = self.po
        if after_te is not None:
            self.s_te.wait(po, after_te)
        self.s_wl.wait(po)
        dst = self.WBUF[:, slot * 8192: slot * 8192 + DMT * cols]
        ins = po.dma_start(
            out=dst.rearrange("p (a f) -> p a f", f=cols),
            in_=w_dram.rearrange("(a p) f -> p a f", p=128))
        return self.s_wl.dma(ins)

    def wtile(self, slot, a, m, w=128, cols=HID):
        base = slot * 8192 + a * cols + m * w
        return self.WBUF[:, base: base + w]

    def spill4(self, dram, m, tg, src):
        dst = dram.rearrange("(a c p) t -> p a c t", p=128, c=NCH)[
            :, m:m + 1, tg * NG:(tg + 1) * NG, :]
        s4 = src.rearrange("p (x c t) -> p x c t", x=1, t=CH)
        self.s_sp.wait(self.sy)
        return self.s_sp.dma(self.sy.dma_start(out=dst, in_=s4))

    def phase1b(self):
        sy, te, ve, se, po = self.sy, self.te, self.ve, self.se, self.po
        se.e.preload_act_table(AF.Sigmoid)
        gS = self.ARE[:, 16384:32768]
        ogS = self.ARE[:, 0:16384]
        ld_g = self.load_weight(self.WgT, 0)
        ld_k = self.load_weight(self.WkT, 1)
        psums = [self.p_a, self.p_b, self.p_c, self.p_d]
        self.s_ve.wait(te)
        self.s_se.wait(te)

        marks = {}
        for i in range(HT * NG):
            m, tg = divmod(i, NG)
            p = psums[i % 4]
            if i >= 4:
                self.s_se.wait(te, marks[i - 4])
            if i == 0:
                self.s_wl.wait(te, ld_g)
            for a in range(DMT):
                ins = te.matmul(p[:, 0:TG], self.wtile(0, a, m),
                                self.XN[:, a * R + tg * TG:a * R + (tg + 1) * TG],
                                start=(a == 0), stop=(a == DMT - 1))
            tm = self.s_te.inc(ins)
            self.s_te.wait(se, tm)
            marks[i] = se.activation(gS[:, m * R + tg * TG: m * R + (tg + 1) * TG], p[:, 0:TG], AF.Sigmoid)
        te_g_done = self.s_te.n

        ld_q = self.load_weight(self.WqT, 0, after_te=te_g_done)
        kge = {}
        spills = {}
        for i in range(HT * NG):
            m, tg = divmod(i, NG)
            p = psums[i % 4]
            if i >= 4:
                self.s_ve.wait(te, kge[i - 4])
            elif i == 0:
                self.s_se.wait(te, marks[HT * NG - 1])
            if i == 0:
                self.s_wl.wait(te, ld_k)
            for a in range(DMT):
                ins = te.matmul(p[:, 0:TG], self.wtile(1, a, m),
                                self.XN[:, a * R + tg * TG:a * R + (tg + 1) * TG],
                                start=(a == 0), stop=(a == DMT - 1))
            tm = self.s_te.inc(ins)
            self.s_te.wait(ve, tm)
            if i >= 4:
                self.s_sp.wait(ve, spills[i - 4])
            dst = self.SCB[:, (i % 4) * TG:(i % 4) * TG + TG]
            kge[i] = ve.tensor_mul(dst, p[:, 0:TG], gS[:, m * R + tg * TG: m * R + (tg + 1) * TG])
            self.s_ve.wait(sy, kge[i])
            spills[i] = self.spill4(self.kgT_d, m, tg, dst)
        te_k_done = self.s_te.n

        ld_v = self.load_weight(self.WvT, 1, after_te=te_k_done)
        qe = {}
        for i in range(HT * NG):
            m, tg = divmod(i, NG)
            p = psums[i % 4]
            if i >= 4:
                self.s_ve.wait(te, qe[i - 4])
            elif i == 0:
                self.s_ve.wait(te, kge[HT * NG - 1])
            if i == 0:
                self.s_wl.wait(te, ld_q)
            for a in range(DMT):
                ins = te.matmul(p[:, 0:TG], self.wtile(0, a, m),
                                self.XN[:, a * R + tg * TG:a * R + (tg + 1) * TG],
                                start=(a == 0), stop=(a == DMT - 1))
            tm = self.s_te.inc(ins)
            self.s_te.wait(ve, tm)
            self.s_sp.wait(ve, spills[i - 4] if i >= 4 else spills[HT * NG - 4 + i])
            dst = self.SCB[:, (i % 4) * TG:(i % 4) * TG + TG]
            qe[i] = ve.tensor_copy(dst, p[:, 0:TG])
            self.s_ve.wait(sy, qe[i])
            spills[i] = self.spill4(self.qT_d, m, tg, dst)
        te_q_done = self.s_te.n

        ld_og = self.load_weight(self.WogT, 0, after_te=te_q_done)
        vev = {}
        for i in range(NCH * 2):
            c, vb = divmod(i, 2)
            p = psums[i % 4]
            if i >= 4:
                self.s_ve.wait(te, vev[i - 4])
            elif i == 0:
                self.s_ve.wait(te, qe[HT * NG - 1])
            if i == 0:
                self.s_wl.wait(te, ld_v)
            for a in range(DMT):
                ins = te.matmul(p[:, 0:TG],
                                self.XN[:, a * R + c * CH: a * R + (c + 1) * CH],
                                self.WBUF[:, 8192 + a * HID + vb * TG: 8192 + a * HID + (vb + 1) * TG],
                                start=(a == 0), stop=(a == DMT - 1))
            tm = self.s_te.inc(ins)
            self.s_te.wait(ve, tm)
            self.s_sp.wait(ve, spills[i - 4] if i >= 4 else spills[HT * NG - 4 + i])
            dst = self.SCB[:, (i % 4) * TG:(i % 4) * TG + TG]
            vev[i] = ve.tensor_copy(dst, p[:, 0:TG])
            self.s_ve.wait(sy, vev[i])
            self.s_sp.wait(sy)
            spills[i] = self.s_sp.dma(sy.dma_start(
                out=self.vN_d[c * 128:(c + 1) * 128, vb * TG:(vb + 1) * TG], in_=dst))
        self.te_v_done = self.s_te.n

        oge = {}
        for i in range(HT * NG):
            m, tg = divmod(i, NG)
            p = psums[i % 4]
            if i >= 4:
                self.s_se.wait(te, oge[i - 4])
            elif i == 0:
                self.s_ve.wait(te, vev[NCH * 2 - 1])
            if i == 0:
                self.s_wl.wait(te, ld_og)
            for a in range(DMT):
                ins = te.matmul(p[:, 0:TG], self.wtile(0, a, m),
                                self.XN[:, a * R + tg * TG:a * R + (tg + 1) * TG],
                                start=(a == 0), stop=(a == DMT - 1))
            tm = self.s_te.inc(ins)
            self.s_te.wait(se, tm)
            oge[i] = se.activation(ogS[:, m * R + tg * TG: m * R + (tg + 1) * TG], p[:, 0:TG], AF.Sigmoid)


    def phase2a(self):
        sy, te, ve, se, po = self.sy, self.te, self.ve, self.se, self.po
        self.preload_set("natural_log_exp_and_others")
        self._aspl = {}
        KGC = {0: self.SCB[:, 0:1024], 1: self.OT[:, 1024:2048]}
        VNC = {0: self.SCB[:, 1024:2048], 1: self.OT[:, 2048:3072]}
        KHN = self.OT[:, 0:1024]

        SC16A = self.smalls[0:16, 0:256]
        ve.tensor_tensor(SC16A.rearrange("k (c h) -> k c h", h=H),
                         self.i16f[:].rearrange("k (x h) -> k x h", x=1).broadcast_to([16, NCH, H]),
                         self.cCcol[:].rearrange("k (c x) -> k c x", x=1).broadcast_to([16, NCH, H]),
                         ALU.mult)
        vsc = self.s_ve.n
        self.s_ve.wait(te, vsc)
        tca = self.tem(self.p_e[:, 0:256], self.ones16f[:], SC16A, start=True, stop=True)
        self.s_te.wait(ve, tca)
        DIFA = self.SCR[:, 512:768]
        EXPD = self.SCR[:, 768:1024]
        vda = ve.tensor_tensor(DIFA, self.p_e[:, 0:256], self.negC_all[:], ALU.add)
        self.s_ve.wait(se, vda)
        sea = se.activation(EXPD, DIFA, AF.Exp)
        e16a = self.smalls[0:16, 488:504] if False else self.smalls[0:16, 100:116]
        sca = se.activation(e16a, self.cCcol[:], AF.Exp)
        self.s_se.wait(ve, sca)
        E16BA = self.smallsb[0:16, 0:256]
        veb = ve.tensor_tensor(
            E16BA.rearrange("k (c h) -> k c h", h=H),
            self.i16b[:].rearrange("k (x h) -> k x h", x=1).broadcast_to([16, NCH, H]),
            e16a.rearrange("k (c x) -> k c x", x=1).broadcast_to([16, NCH, H]), ALU.mult)

        lda = {}
        tcoh = {}

        def a1_loads(c):
            p = c % 2
            if c >= 2:
                self.s_te.wait(sy, tcoh[c - 2])
            l1 = self.qdma(sy, self.s_x,
                           KGC[p].rearrange("p (a x t) -> p a x t", x=1, t=CH),
                           self.kgT_d.rearrange("(a c p) t -> p a c t", p=128, c=NCH)[:, :, c:c + 1, :])
            l2 = self.qdma(sy, self.s_x, VNC[p], self.vN_d[c * 128:(c + 1) * 128, :])
            lda[c] = (l1, l2)

        self.s_te.wait(sy)
        self.s_sp.wait(sy)
        a1_loads(0)
        a1_loads(1)

        scm = 0
        for c in range(NCH):
            p = c % 2
            sl = E16BA[0:16, c * H:(c + 1) * H]
            rhs_e = sl.rearrange("k (a b) -> k a b", b=2)[:, :, 0:1].broadcast_to([16, 8, 64])
            rhs_o = sl.rearrange("k (a b) -> k a b", b=2)[:, :, 1:2].broadcast_to([16, 8, 64])
            self.s_ve.wait(te, max(veb, scm))
            self.tem(self.p_d[0:64, 0:512], self.ones16b[:, 0:64], rhs_e, start=True, stop=True)
            teb = self.tem(self.p_d[64:128, 0:512], self.ones16b[:, 64:128], rhs_o, start=True, stop=True,
                           tile_position=(0, 64))
            if c >= 2:
                self.s_sp.wait(ve, self._aspl[c - 2])
            vsv = ve.tensor_copy(self.STB2[:, (c % 2) * 512:(c % 2) * 512 + 512],
                                 self.S_master[:])
            self.s_ve.wait(sy, vsv)
            self._aspl[c] = self.qdma(sy, self.s_sp, self.std_d[c * 128:(c + 1) * 128, :],
                                      self.STB2[:, (c % 2) * 512:(c % 2) * 512 + 512])
            if c + 1 < NCH:
                a1_loads(c + 1)
            self.s_x.wait(te, lda[c][0])
            self.s_ve.wait(te, vsv)
            for i in range(8):
                ins = te.transpose(self.p_t[:, i * 128:(i + 1) * 128],
                                   KGC[p][:, i * 128:(i + 1) * 128], self.ID_B)
            ttr = self.s_te.inc(ins)
            self.s_te.wait(ve, ttr)
            self.s_se.wait(ve, sea)
            vkh = ve.tensor_tensor(
                KHN.rearrange("p (h t) -> p h t", t=64),
                self.p_t[:, 0:1024].rearrange("p (h t) -> p h t", t=64),
                EXPD[:, c * H:(c + 1) * H].rearrange("p (h x) -> p h x", x=1).broadcast_to([128, H, 64]),
                ALU.mult)
            self.s_ve.wait(te, vkh)
            self.s_x.wait(te, lda[c][1])
            for h in range(H):
                pb = self.p_b if h % 2 == 0 else self.p_c
                ins = te.matmul(
                    pb[(h % 2) * 64:(h % 2) * 64 + 64,
                       (h // 2) * 64:(h // 2) * 64 + 64],
                    KHN[:, h * 64:(h + 1) * 64], VNC[p][:, h * 64:(h + 1) * 64],
                    start=(h < 2), stop=(h >= H - 2),
                    tile_position=(0, (h % 2) * 64))
            tcoh[c] = tco = self.s_te.inc(ins)
            self.s_te.wait(ve, max(tco, teb))
            scm = ve.tensor_tensor(self.SCR[:, 0:512], self.S_master[:], self.p_d[:, 0:512], ALU.mult)
            ve.tensor_tensor(self.S_master[0:64, :], self.SCR[0:64, 0:512], self.p_b[0:64, 0:512], ALU.add)
            ve.tensor_tensor(self.S_master[64:128, :], self.SCR[64:128, 0:512], self.p_c[64:128, 0:512], ALU.add)
        vbt = ve.tensor_copy(self.smalls[0:16, 224:225], self.Ballc[:, NCH:NCH + 1])
        self.s_ve.wait(te, vbt)
        tbt = self.tet(self.p_e[0:1, 0:16], self.smalls[0:16, 224:225], self.ID_F[0:16, 0:16])
        self.s_te.wait(ve, tbt)
        ve.memset(self.rows[0:1, 0:512], 0.0)
        vrow = ve.tensor_copy(self.rows[0:1, 0:16], self.p_e[0:1, 0:16])
        self.s_ve.wait(sy, vrow)
        self.qdma(sy, self.s_sp, self.cc_in[0:128, :], self.S_master[:])
        self.qdma(sy, self.s_sp, self.cc_in[128:129, :], self.rows[0:1, 0:512])
        self.s_sp.wait(po)
        if SIM1:
            for b in range(8):
                self.s_cc.inc(po.dma_start(out=self.cc_out[b * 129:(b + 1) * 129, :],
                                           in_=self.cc_in[:, :]), 16)
        else:
            ins = po.collective_compute(
                "AllGather", ALU.bypass,
                ins=[self.cc_in.opt()], outs=[self.cc_out.opt()],
                replica_groups=[list(range(N_CORES))])
            self.s_cc.inc(ins, 1)


    def phase2b(self):
        sy, te, ve, se, po = self.sy, self.te, self.ve, self.se, self.po
        A2S = self.A2S
        KG = {0: A2S[:, 0:1024], 1: A2S[:, 2048:3072]}
        QC = {0: A2S[:, 1024:2048], 1: A2S[:, 3072:4096]}
        SEF = {0: A2S[:, 4096:4608], 1: A2S[:, 4608:5120]}
        VN = {0: self.SCB[:, 0:1024], 1: self.SCB[:, 1024:2048]}
        QT = {0: self.ARE[:, 16384:16896], 1: self.ARE[:, 16896:17408]}
        PB = {0: self.ARE[:, 17408:17920], 1: self.ARE[:, 17920:18432]}
        ABUF = self.ARE[:, 18432:18944]
        EXPC = self.ARE[:, 18944:19456]
        ogS = self.ARE[:, 0:16384]
        SQA = self.ARE[:, 16384:16384 + 4096]
        YB = self.ARE[:, 16384 + 4096:16384 + 8192]
        XB2 = self.ARE[:, 16384 + 8192:16384 + 12288]
        SQ2 = self.ARE[:, 16384 + 12288:16384 + 16384]
        Dbuf = self.SCR[:, 512:1024]
        ld_o = self.load_weight(self.WoT, 0, after_te=self.s_te.n)
        self.f0_init()

        ld = {}
        self._tgo = {}; self._vql = {}; self._ogo = {}; self._evac = {}
        self._post_c_ve = 0
        self._post_c_te = 0
        self._post_c_xg = 0
        self._spx = 0

        def emit_loads(c):
            p = c % 2
            if c >= 2:
                self.s_te.wait(sy, self._tgo[c - 2])
                self.s_ve.wait(sy, self._vql[c - 2])
            l1 = self.qdma(sy, self.s_x, KG[p].rearrange("p (a x t) -> p a x t", x=1, t=CH),
                           self.kgT_d.rearrange("(a c p) t -> p a c t", p=128, c=NCH)[:, :, c:c + 1, :])
            l2 = self.qdma(sy, self.s_x, QC[p].rearrange("p (a x t) -> p a x t", x=1, t=CH),
                           self.qT_d.rearrange("(a c p) t -> p a c t", p=128, c=NCH)[:, :, c:c + 1, :])
            l3 = self.qdma(sy, self.s_x, VN[p], self.vN_d[c * 128:(c + 1) * 128, :])
            l4 = self.qdma(sy, self.s_x, self.LST[:, p * 512:p * 512 + 512],
                           self.std_d[c * 128:(c + 1) * 128, :])
            ld[c] = (l1, l2, l3, l4)

        self.s_sp.wait(sy)
        self.s_te.wait(sy)
        self.s_ve.wait(sy)
        emit_loads(0)
        emit_loads(1)
        next_ld = 2

        self.s_ve.wait(te)
        te.transpose(self.p_d[:, 0:128], self.C_all[:, 0:128], self.ID_F)
        tt2 = self.tet(self.p_d[:, 128:256], self.C_all[:, 128:256], self.ID_F)
        self.s_te.wait(ve, tt2)
        CT = self.SCR[:, 0:256]
        vct = ve.tensor_copy(CT, self.p_d[:, 0:256])
        self.s_ve.wait(sy, vct)
        self.s_sp.wait(sy)
        self.qdma(sy, self.s_sp, self.CHIA.rearrange("c (h t) -> c h t", t=128)[0:8],
                  CT[:, 0:128])
        rch = self.qdma(sy, self.s_sp, self.CHIA.rearrange("c (h t) -> c h t", t=128)[8:16],
                        CT[:, 128:256])
        self.s_sp.wait(se, rch)
        eBa = self.smalls[0:16, 144:160]
        sba = se.activation(eBa, self.Ballc[0:16, 0:16], AF.Exp)
        self.s_se.wait(ve, sba)
        E16A = self.smallsb[0:16, 0:256]
        vE16 = ve.tensor_tensor(
            E16A.rearrange("k (c h) -> k c h", h=16),
            self.i16b[:].rearrange("k (x h) -> k x h", x=1).broadcast_to([16, 16, 16]),
            eBa.rearrange("k (c x) -> k c x", x=1).broadcast_to([16, 16, 16]), ALU.mult)

        self.s_cc.wait(sy)
        B8 = self.smalls[0:8, 208:224]
        self.qdma(sy, self.s_x, B8,
                  self.cc_out.rearrange("(b r) f -> b r f", r=129)[:, 128:129, 0:16])
        EBF = self.SCR[:, 0:512]
        for b in range(8):
            slot = self.XG[:, (1 + (b % 2)) * 512:(2 + (b % 2)) * 512]
            lb = self.qdma(sy, self.s_x, slot,
                           self.cc_out[b * 129:(b * 129) + 128, :])
            wb = self.wind[:, b:b + 1].broadcast_to([8, 64])
            be = B8.rearrange("k (a b2) -> k a b2", b2=2)[:, :, 0:1].broadcast_to([8, 8, 64])
            bo = B8.rearrange("k (a b2) -> k a b2", b2=2)[:, :, 1:2].broadcast_to([8, 8, 64])
            if b == 0:
                self.s_x.wait(te)
                self.s_ve.wait(te)
            self.tem(self.p_a[0:64, 0:512], wb, be, start=True, stop=True)
            tsb = self.tem(self.p_a[64:128, 0:512], wb, bo, start=True, stop=True,
                           tile_position=(0, 64))
            self.s_te.wait(se, tsb)
            seb = se.activation(EBF, self.p_a[:, 0:512], AF.Exp, bias=self.alog[:, b:b + 1])
            self.s_x.wait(ve, lb)
            self.s_se.wait(ve, seb)
            if b == 0:
                vup = ve.tensor_tensor(self.S_master[:], EBF, slot, ALU.mult)
            else:
                ve.tensor_tensor(self.SCR[:, 512:1024], EBF, slot, ALU.mult)
                vup = ve.tensor_tensor(self.S_master[:], self.S_master[:],
                                       self.SCR[:, 512:1024], ALU.add)
            self.s_ve.wait(te, vup)
            self.s_ve.wait(sy, vup)
        comb_seb = seb
        self._post_c_ve = self.s_ve.n
        self._post_c_xg = self.s_ve.n

        vse_c = {}; scrm = {}

        def emit_sef(c):
            p = c % 2
            sl = E16A[0:16, c * 16:(c + 1) * 16]
            rhs_e = sl.rearrange("k (a b) -> k a b", b=2)[:, :, 0:1].broadcast_to([16, 8, 64])
            rhs_o = sl.rearrange("k (a b) -> k a b", b=2)[:, :, 1:2].broadcast_to([16, 8, 64])
            if c == 0:
                self.s_ve.wait(te, vE16)
            else:
                self.s_ve.wait(te, max(scrm[c - 1], self._post_c_ve))
            self.tem(self.p_d[0:64, 0:512], self.ones16b[:, 0:64], rhs_e, start=True, stop=True)
            tse = self.tem(self.p_d[64:128, 0:512], self.ones16b[:, 64:128], rhs_o,
                           start=True, stop=True, tile_position=(0, 64))
            self.s_te.wait(ve, tse)
            if c >= 2:
                self.s_te.wait(ve, self._tgo[c - 2])
            if c == 0:
                self.s_se.wait(ve, comb_seb)
            scrm[c] = ve.tensor_tensor(self.SCR[:, 0:512], self.S_master[:],
                                       self.p_d[:, 0:512], ALU.mult)
            self.s_x.wait(ve, ld[c][3])
            vse_c[c] = ve.tensor_tensor(SEF[p], self.SCR[:, 0:512],
                                        self.LST[:, p * 512:p * 512 + 512], ALU.add)

        emit_sef(0)

        gh = {}

        def emit_front(c, g):
            ga = c * 4 + g; gp = ga % 2; p = c % 2
            if ga == 0:
                self.s_se.wait(te, comb_seb)
                self.s_sp.wait(te, rch)
                self.s_ve.wait(te, vse_c[0])
            else:
                self.s_ve.wait(te, max(gh[ga - 1]["vd"], self._post_c_ve))
            selc = self.i16f[:, c:c + 1].broadcast_to([16, 128])
            te.matmul(self.p_a[:, 0:512], selc,
                      self.CHIA[0:16, g * 512:(g + 1) * 512], start=True, stop=False)
            tma = self.tem(self.p_a[:, 0:512], self.negdiag[:],
                           self.slrep[:].rearrange("p (x t) -> p x t", x=1).broadcast_to([128, 4, CH]),
                           start=False, stop=True)
            if ga > 0:
                self.s_se.wait(te, gh[ga - 1]["sx"])
            tme = self.tem(self.p_e[:, 0:512], selc,
                           self.CHIA[0:16, g * 512:(g + 1) * 512], start=True, stop=True)
            if g == 0:
                self.s_x.wait(te, ld[c][1])
            for hh in range(4):
                h = 4 * g + hh
                pb = self.p_b if hh % 2 == 0 else self.p_c
                ins = te.matmul(pb[:, gp * 256 + (hh // 2) * 128:gp * 256 + (hh // 2) * 128 + 128],
                                KG[p][(h % 2) * 64:(h % 2) * 64 + 64,
                                      (h // 2) * 128:(h // 2) * 128 + 128],
                                QC[p][(h % 2) * 64:(h % 2) * 64 + 64,
                                      (h // 2) * 128:(h // 2) * 128 + 128],
                                start=True, stop=True,
                                tile_position=((h % 2) * 64, 0))
            gh[ga] = {"tma": tma, "tme": tme, "tsp": self.s_te.inc(ins)}

        def emit_chain(c, g):
            ga = c * 4 + g; gp = ga % 2; p = c % 2
            G = gh[ga]
            negb = self.negC_all[:, c * H + 4 * g:c * H + 4 * g + 4].rearrange(
                "p (h x) -> p h x", x=1).broadcast_to([128, 4, 128])
            self.s_te.wait(ve, G["tma"])
            vd = ve.tensor_tensor(Dbuf.rearrange("p (h t) -> p h t", t=128),
                                  self.p_a[:, 0:512].rearrange("p (h t) -> p h t", t=128),
                                  negb, ALU.add)
            self.s_ve.wait(se, vd)
            sa = se.activation(ABUF, Dbuf, AF.Exp)
            self.s_te.wait(se, G["tme"])
            sx = se.activation(EXPC, self.p_e[:, 0:512], AF.Exp)
            self.s_se.wait(ve, sa)
            self.s_te.wait(ve, max(G["tsp"], self._post_c_te))
            if ga >= 2:
                self.s_te.wait(ve, self._ogo[ga - 2])
            a4 = ABUF.rearrange("p (a b t) -> p a b t", a=2, b=2, t=128)
            o4 = PB[gp].rearrange("p (a b t) -> p a b t", a=2, b=2, t=128)
            ve.tensor_tensor(o4[:, :, 0:1, :], a4[:, :, 0:1, :],
                             self.p_b[:, gp * 256:gp * 256 + 256].rearrange(
                                 "p (a x t) -> p a x t", x=1, t=128), ALU.mult)
            vp = ve.tensor_tensor(o4[:, :, 1:2, :], a4[:, :, 1:2, :],
                                  self.p_c[:, gp * 256:gp * 256 + 256].rearrange(
                                      "p (a x t) -> p a x t", x=1, t=128), ALU.mult)
            self.s_se.wait(ve, sx)
            if g == 0:
                self.s_x.wait(ve, ld[c][1])
            qt4 = QT[gp].rearrange("p (a b t) -> p a b t", a=2, b=2, t=128)
            ex4 = EXPC.rearrange("p (a b t) -> p a b t", a=2, b=2, t=128)
            qsrc = QC[p][:, 2 * g * 128:2 * g * 128 + 256].rearrange(
                "p (a x t) -> p a x t", x=1, t=128)
            ve.tensor_tensor(qt4[0:64, :, 0:1, :], qsrc[0:64], ex4[0:64, :, 0:1, :], ALU.mult)
            vq = ve.tensor_tensor(qt4[64:128, :, 1:2, :], qsrc[64:128], ex4[64:128, :, 1:2, :],
                                  ALU.mult)
            G["vd"] = vd; G["sx"] = sx; G["vp"] = vp; G["vq"] = vq

        def emit_O(c, g):
            ga = c * 4 + g; gp = ga % 2; p = c % 2
            G = gh[ga]
            self.s_ve.wait(te, max(G["vp"], G["vq"], vse_c[c]))
            if g == 0:
                self.s_x.wait(te, ld[c][2])
                if c >= 1:
                    self.s_se.wait(te, self._evac[c - 1])
            for hh in range(4):
                h = 4 * g + hh
                te.matmul(self.p_o[(h % 2) * 64:(h % 2) * 64 + 64,
                                   (h // 2) * 128:(h // 2) * 128 + 128],
                          VN[p][:, h * 64:(h + 1) * 64],
                          PB[gp][:, hh * 128:(hh + 1) * 128],
                          start=True, stop=False,
                          tile_position=(0, (h % 2) * 64))
                ins = te.matmul(self.p_o[(h % 2) * 64:(h % 2) * 64 + 64,
                                         (h // 2) * 128:(h // 2) * 128 + 128],
                                SEF[p][(h % 2) * 64:(h % 2) * 64 + 64,
                                       (h // 2) * 64:(h // 2) * 64 + 64],
                                QT[gp][(h % 2) * 64:(h % 2) * 64 + 64,
                                       hh * 128:(hh + 1) * 128],
                                start=False, stop=True,
                                tile_position=((h % 2) * 64, (h % 2) * 64))
            self._ogo[ga] = self.s_te.inc(ins)
            if g == 3:
                self._tgo[c] = self._ogo[ga]
                self._vql[c] = G["vq"]

        def emit_evac(c):
            cc = c % 4
            self.s_te.wait(se, self._tgo[c])
            if c % 4 == 0 and c > 0:
                self.s_ve.wait(se, self._post_c_ve)
            self._evac[c] = se.activation(
                self.OT[:].rearrange("p (m t) -> p m t", t=512)[:, :, cc * 128:(cc + 1) * 128],
                self.p_o[:, 0:1024].rearrange("p (m t) -> p m t", t=128), AF.Copy)

        pend = None
        for gi in range(NG):
            for cc in range(NG):
                c = gi * NG + cc
                if cc == 0:
                    self.s_ve.wait(sy, self._post_c_xg)
                    if self._spx:
                        self.s_sp.wait(sy, self._spx)
                    self._lx = self.qdma(sy, self.s_x,
                                         self.XG[:].rearrange("p (a f) -> p a f", f=TG),
                                         self.xT.rearrange("(a p) t -> p a t", p=128)[:, :, gi * TG:(gi + 1) * TG])
                for g in range(4):
                    emit_front(c, g)
                    emit_chain(c, g)
                    if pend is not None:
                        pc, pg = pend
                        emit_O(pc, pg)
                        if pg == 3:
                            emit_evac(pc)
                            while next_ld < NCH and next_ld <= pc + 2:
                                emit_loads(next_ld)
                                next_ld += 1
                    pend = (c, g)
                if c + 1 < NCH:
                    emit_sef(c + 1)
                self.f0_step(2)
            pc, pg = pend
            emit_O(pc, pg)
            emit_evac(pc)
            pend = None
            while next_ld < NCH and next_ld <= pc + 2:
                emit_loads(next_ld)
                next_ld += 1
            self._c_group(gi, ld_o, ogS, SQA, YB, XB2, SQ2)
            self._post_c_ve = self._c_vqc
            self._post_c_xg = self.s_ve.n
            self._post_c_te = self.s_te.n

    def _c_group(self, gi, ld_o, ogS, SQA, YB, XB2, SQ2):
        sy, te, ve, se, po = self.sy, self.te, self.ve, self.se, self.po
        self.f0_step(4)
        self.s_ve.wait(se)
        for a in range(DMT):
            sq = se.activation(SQA[:, a * 512:(a + 1) * 512],
                               self.OT[:, a * 512:(a + 1) * 512], AF.Square)
        self.s_se.wait(te, sq)
        self.s_ve.wait(te)
        for a in range(DMT):
            ins = te.matmul(self.p_a[0:1, 0:512], self.ONES_COL_B,
                            SQA[:, a * 512:(a + 1) * 512],
                            start=(a == 0), stop=(a == DMT - 1))
        tst = self.s_te.inc(ins)
        self.s_te.wait(se, tst)
        rrow = self.rows[0:1, 0:512]
        se.activation(rrow, self.p_a[0:1, 0:512], AF.Ln,
                      scale=1.0 / HID, bias=self.B_RMSEPS[0:1, :])
        srs = se.activation(rrow, rrow, AF.Exp, scale=-0.5)
        self.s_se.wait(te, srs)
        trb = self.tem(self.p_c[:, 0:512], self.ONES_ROW_F, rrow, start=True, stop=True)
        self.s_te.wait(ve, trb)
        vys = {}
        for a in range(DMT):
            ve.scalar_tensor_tensor(YB[:, a * 512:(a + 1) * 512],
                                    self.OT[:, a * 512:(a + 1) * 512],
                                    self.normw_sb[:, a:a + 1],
                                    self.p_c[:, 0:512], ALU.mult, ALU.mult)
            vys[a] = ve.tensor_tensor(YB[:, a * 512:(a + 1) * 512],
                                      YB[:, a * 512:(a + 1) * 512],
                                      ogS[:, a * R + gi * 512:a * R + gi * 512 + 512], ALU.mult)
        self.s_wl.wait(te, ld_o)
        self.s_x.wait(ve, self._lx)
        accs = [self.p_a[:, 0:512], self.p_b[:, 0:512], self.p_d[:, 0:512],
                self.p_e[:, 0:512], self.p_o[:, 0:512], self.p_o[:, 512:1024]]
        for k in range(DMT):
            self.s_ve.wait(te, vys[k])
            for m in range(6):
                ins = te.matmul(accs[m], self.wtile(0, k, m), YB[:, k * 512:(k + 1) * 512],
                                start=(k == 0), stop=(k == DMT - 1))
        t6 = self.s_te.inc(ins)
        self.s_te.wait(ve, t6)
        prev_m = {}
        for m in range(2):
            prev_m[m] = ve.tensor_tensor(self.XG[:, m * 512:(m + 1) * 512],
                                         self.XG[:, m * 512:(m + 1) * 512], accs[m], ALU.add)
        self.s_ve.wait(te, prev_m[1])
        for k in range(DMT):
            for m in (6, 7):
                ins = te.matmul(accs[m - 6], self.wtile(0, k, m), YB[:, k * 512:(k + 1) * 512],
                                start=(k == 0), stop=(k == DMT - 1))
        t8 = self.s_te.inc(ins)
        for m in range(2, 6):
            prev_m[m] = ve.tensor_tensor(self.XG[:, m * 512:(m + 1) * 512],
                                         self.XG[:, m * 512:(m + 1) * 512], accs[m], ALU.add)
        self.s_te.wait(ve, t8)
        for m in (6, 7):
            prev_m[m] = ve.tensor_tensor(self.XG[:, m * 512:(m + 1) * 512],
                                         self.XG[:, m * 512:(m + 1) * 512], accs[m - 6], ALU.add)
        self.s_ve.wait(sy, prev_m[DMT - 1])
        spx = self.qdma(sy, self.s_sp,
                        self.x2_d.rearrange("(a p) t -> p a t", p=128)[:, :, gi * TG:(gi + 1) * TG],
                        self.XG[:].rearrange("p (a f) -> p a f", f=TG))
        self._spx = spx
        self.s_x.wait(se, self._lx)
        self.s_ve.wait(se, prev_m[DMT - 1])
        for a in range(DMT):
            se.activation(XB2[:, a * 512:(a + 1) * 512], self.XG[:, a * 512:(a + 1) * 512],
                          AF.Copy)
            sml = se.activation(SQ2[:, a * 512:(a + 1) * 512], self.XG[:, a * 512:(a + 1) * 512],
                                AF.Square)
        self.s_se.wait(te, sml)
        self.s_ve.wait(te, prev_m[7])
        for a in range(DMT):
            ins = te.matmul(self.p_a[0:1, 0:512], self.ONES_COL_B,
                            XB2[:, a * 512:(a + 1) * 512], start=(a == 0), stop=(a == DMT - 1))
        self.s_te.inc(ins)
        for a in range(DMT):
            ins = te.matmul(self.p_b[0:1, 0:512], self.ONES_COL_B,
                            SQ2[:, a * 512:(a + 1) * 512], start=(a == 0), stop=(a == DMT - 1))
        tml = self.s_te.inc(ins)
        self.s_te.wait(ve, tml)
        rm = self.rows[0:1, 0:512]; rvar = self.rows[0:1, 512:1024]
        rr = self.rows[0:1, 1024:1536]; rmr = self.rows[0:1, 1536:2048]
        ve.tensor_scalar(rm, self.p_a[0:1, 0:512], 1.0 / DM, None, ALU.mult)
        ve.tensor_scalar(rvar, self.p_b[0:1, 0:512], 1.0 / DM, None, ALU.mult)
        ve.tensor_mul(rr, rm, rm)
        vst2 = ve.tensor_sub(rvar, rvar, rr)
        self.s_ve.wait(se, vst2)
        se.activation(rvar, rvar, AF.Ln, bias=self.B_LNEPS[0:1, :])
        sst2 = se.activation(rr, rvar, AF.Exp, scale=-0.5)
        self.s_se.wait(ve, sst2)
        vrb2 = ve.tensor_mul(rmr, rm, rr)
        self.s_ve.wait(te, vrb2)
        self.tem(self.p_c[:, 0:512], self.ONES_ROW_F, rr, start=True, stop=True)
        tbc2 = self.tem(self.p_d[:, 0:512], self.ONES_ROW_F, rmr, start=True, stop=True)
        self.s_te.wait(ve, tbc2)
        self.s_sp.wait(ve, spx)
        for a in range(DMT):
            ve.tensor_mul(self.SCR[:, 0:512], self.XG[:, a * 512:(a + 1) * 512], self.p_c[:, 0:512])
            vln2 = ve.tensor_sub(self.XG[:, a * 512:(a + 1) * 512], self.SCR[:, 0:512],
                                 self.p_d[:, 0:512])
        rmx = self.SCR[:, 0:512]; rmn = self.SCR[:, 512:1024]
        ve.tensor_copy(rmx, self.XG[:, 0:512])
        ve.tensor_copy(rmn, self.XG[:, 0:512])
        for a in range(1, DMT):
            ve.tensor_tensor(rmx, rmx, self.XG[:, a * 512:(a + 1) * 512], ALU.max)
            ve.tensor_tensor(rmn, rmn, self.XG[:, a * 512:(a + 1) * 512], ALU.min)
        ve.tensor_scalar(rmn, rmn, -1.0, None, ALU.mult)
        vtr = ve.tensor_tensor(rmx, rmx, rmn, ALU.max)
        self.s_ve.wait(te, vtr)
        for ct in range(4):
            ins = te.transpose(self.p_e[:, ct * 128:(ct + 1) * 128],
                               rmx[:, ct * 128:(ct + 1) * 128], self.ID_F)
        ttd = self.s_te.inc(ins)
        self.s_te.wait(ve, ttd)
        red = self.smalls[:, 0:4]
        ve.tensor_reduce(red, self.p_e[:, 0:512].rearrange("p (a b) -> p a b", b=128),
                         AX.X, ALU.max)
        vrd = ve.tensor_scalar(red, red, Q_EPS, None, ALU.max)
        self.s_ve.wait(te, vrd)
        tred = self.tet(self.p_e[0:4, 256:384], red, self.ID_F)
        self.s_te.wait(ve, tred)
        vev = ve.tensor_copy(self.st16[0:4, 0:128], self.p_e[0:4, 256:384])
        self.s_ve.wait(sy, vev)
        self.qdma(sy, self.s_sp, self.mx_d[gi * 4:(gi + 1) * 4, :], self.st16[0:4, 0:128])
        mq = self.qdma(sy, self.s_sp, self.SCR[0:1, 512:1024], self.st16[0:4, 0:128])
        self.s_sp.wait(se, mq)
        srow = self.rows[0:1, 0:512]
        se.activation(srow, self.SCR[0:1, 512:1024], AF.Ln)
        sq2 = se.activation(srow, srow, AF.Exp, scale=-1.0, bias=self.B_LN127[0:1, :])
        self.s_se.wait(te, sq2)
        tqb = self.tem(self.p_c[:, 0:512], self.ONES_ROW_F, srow, start=True, stop=True)
        self.s_te.wait(ve, tqb)
        self._c_vqc = ve.tensor_copy(self.SCR[:, 0:512], self.p_c[:, 0:512])
        i32v = self.SCR[:, 512:1024].bitcast(I32)
        for a in range(DMT):
            ve.tensor_tensor(i32v, self.XG[:, a * 512:(a + 1) * 512],
                             self.SCR[:, 0:512], ALU.mult)
            vq2 = ve.tensor_copy(self.XN[:, a * R + gi * 512:a * R + gi * 512 + 512], i32v)


    def f0_init(self):
        self._f0_steps = []
        for wi, (wd, na, fw) in enumerate([(self.WTg, 8, 4096), (self.WTu, 8, 4096),
                                           (self.WTd, 32, 1024)]):
            for a in range(na):
                if fw > 2048:
                    self._f0_steps.append((wi, wd, a, 0, 2048))
                    self._f0_steps.append((wi, wd, a, 2048, 2048))
                else:
                    self._f0_steps.append((wi, wd, a, 0, fw))
        self._f0_i = 0
        self._f0_l = 0
        self._f0_ld = {}
        self._f0_sab = {}
        self._f0_pm = None
        self.ve.memset(self.smalls[:, 230:233], 0.0)
        self.f0_load()

    def f0_load(self):
        sy, se, po = self.sy, self.se, self.po
        slot1f = self.WBUF[:, 8192:16384].bitcast(F32)
        if self._f0_l >= len(self._f0_steps):
            return
        i = self._f0_l
        self._f0_l += 1
        wi, wd, a, co, fw = self._f0_steps[i]
        half = i % 2
        dst = slot1f[:, half * 2048:half * 2048 + fw].rearrange("p (x f) -> p x f", x=1)
        src = wd.rearrange("(a p) f -> p a f", p=128)[:, a:a + 1, co:co + fw]
        if i < 2:
            self.s_te.wait(po, self.te_v_done)
        else:
            self.s_se.wait(po, self._f0_sab[i - 2])
        self._f0_ld[i] = self.qdma(po, self.s_wl, dst, src)

    def f0_step(self, n=1):
        sy, te, ve, se, po = self.sy, self.te, self.ve, self.se, self.po
        slot1f = self.WBUF[:, 8192:16384].bitcast(F32)
        red = self.smalls[:, 233:234]
        for _ in range(n):
            if self._f0_i >= len(self._f0_steps):
                return
            self.f0_load()
            i = self._f0_i
            self._f0_i += 1
            wi, wd, a, co, fw = self._f0_steps[i]
            half = i % 2
            sl = slot1f[:, half * 2048:half * 2048 + fw]
            self.s_wl.wait(se, self._f0_ld[i])
            if i > 0:
                self.s_ve.wait(se, self._f0_pm)
            self._f0_sab[i] = se.activation(sl, sl, AF.Abs, accum_out=red)
            self.s_se.wait(ve, self._f0_sab[i])
            self._f0_pm = ve.tensor_tensor(self.smalls[:, 230 + wi:231 + wi],
                                           self.smalls[:, 230 + wi:231 + wi], red, ALU.add)

    def phase3_f0(self):
        sy, te, ve, se, po = self.sy, self.te, self.ve, self.se, self.po
        self.f0_step(len(self._f0_steps))
        acc = self.smalls[:, 230:233]
        pm = self._f0_pm
        self.s_ve.wait(te, pm)
        for wi in range(3):
            self.tem(self.p_e[0:1, 100 + wi:101 + wi], self.ONES_COL_F, acc[:, wi:wi + 1],
                     start=True, stop=True)
        tf0 = self.s_te.n
        self.s_te.wait(ve, tf0)
        mrow = self.smalls[0:1, 236:239]
        ve.tensor_scalar(mrow, self.p_e[0:1, 100:103], 1.0 / (DM * FF), None, ALU.mult)
        vm0 = ve.tensor_scalar(mrow, mrow, Q_EPS, None, ALU.max)
        thr = self.smalls[0:1, 240:243]; nthr = self.smalls[0:1, 243:246]
        ve.tensor_scalar(thr, mrow, 0.5, None, ALU.mult)
        vth = ve.tensor_scalar(nthr, thr, -1.0, None, ALU.mult)
        sgm = self.smalls[0:1, 246:247]; md127 = self.smalls[0:1, 247:248]
        ve.tensor_tensor(sgm, mrow[0:1, 0:1], mrow[0:1, 0:1], ALU.mult)
        ve.tensor_tensor(sgm, sgm, mrow[0:1, 1:2], ALU.mult)
        ve.tensor_scalar(sgm, sgm, 1.0 / (127.0 ** 3), None, ALU.mult)
        vsc = ve.tensor_scalar(md127, mrow[0:1, 2:3], 1.0 / 127.0, None, ALU.mult)
        self.s_ve.wait(te, max(vth, vsc))
        for wi in range(3):
            te.matmul(self.p_e[:, 104 + wi:105 + wi], self.ONES_ROW_F, thr[0:1, wi:wi + 1],
                      start=True, stop=True)
            ins = te.matmul(self.p_e[:, 108 + wi:109 + wi], self.ONES_ROW_F, nthr[0:1, wi:wi + 1],
                            start=True, stop=True)
        tthr = self.s_te.inc(ins)
        self.s_te.wait(ve, tthr)
        self.thrc = self.smalls[:, 240:243]
        self.nthrc = self.smalls[:, 243:246]
        ve.tensor_copy(self.thrc, self.p_e[:, 104:107])
        ve.tensor_copy(self.nthrc, self.p_e[:, 108:111])
        self.sgm = sgm; self.md127 = md127

    def ternarize(self, eng, wsrc_f32, tern_dst, tmp, wi, width):
        eng.tensor_scalar(tmp[:, 0:width], wsrc_f32[:, 0:width], self.thrc[:, wi:wi + 1],
                          None, ALU.is_gt)
        eng.tensor_scalar(tern_dst[:, 0:width], wsrc_f32[:, 0:width], self.nthrc[:, wi:wi + 1],
                          None, ALU.is_lt)
        return eng.tensor_tensor(tern_dst[:, 0:width], tmp[:, 0:width], tern_dst[:, 0:width],
                                 ALU.subtract)

    def ternarize4(self, eng, wsrc, dst, tmp, wi):
        eng.tensor_scalar(tmp, wsrc, self.thrc[:, wi:wi + 1], None, ALU.is_gt)
        eng.tensor_scalar(dst, wsrc, self.nthrc[:, wi:wi + 1], None, ALU.is_lt)
        return eng.tensor_tensor(dst, tmp, dst, ALU.subtract)

    def fd_step(self, ternd, tstg, ttmp):
        sy, ve = self.sy, self.ve
        k, half = divmod(self._fd_i, 2)
        if k >= FFT:
            return
        self._fd_i += 1
        if self._fd_t is not None:
            self.s_ve.wait(sy, self._fd_t)
        lw = self.qdma(sy, self.s_x, tstg.rearrange("p (x y f) -> p x y f", x=1, y=1),
                       self.WTd.rearrange("(a p) (h f) -> p a h f", p=128, h=2)[:, k:k + 1, half:half + 1, :])
        self.s_x.wait(ve, lw)
        base = k * 1024 + half * 512
        self._fd_t = self.ternarize(ve, tstg, ternd[:, base:base + 512], ttmp, 2, 512)

    def phase3_f1(self):
        sy, te, ve, se, po = self.sy, self.te, self.ve, self.se, self.po
        slot1f = self.WBUF[:, 8192:16384].bitcast(F32)
        slot0f = self.WBUF[:, 0:8192].bitcast(F32)
        terng = self.A2S[:, 0:4096]
        ternu = self.XG[:].bitcast(BF16)[:, 4096:8192]
        hst = self.SCB[:, 1536:2048]
        ttmp = self.OT[:, 0:4096]
        ternd = self.ARE[:, 0:32768]
        tstg = self.SCB[:, 0:1024].bitcast(F32)
        RMX = self.XG[:, 0:2048]
        self.s_te.wait(ve)
        self.s_se.wait(ve)
        vmn = ve.memset(RMX, 0.0)
        self._fd_i = 0
        self._fd_t = None
        prev_h = None
        prev_ev = {0: None, 1: None}
        def ld_gu(ig):
            lg = self.qdma(sy, self.s_x, slot1f.rearrange("p (a f) -> p a f", f=512),
                           self.WTg.rearrange("(a p) f -> p a f", p=128)[:, :, ig * 512:(ig + 1) * 512])
            lu = self.qdma(self.se.e, self.s_sc, slot0f.rearrange("p (a f) -> p a f", f=512),
                           self.WTu.rearrange("(a p) f -> p a f", p=128)[:, :, ig * 512:(ig + 1) * 512])
            return lg, lu

        self.s_po.wait(sy)
        self.s_ve.wait(sy)
        self.s_te.wait(sy)
        self.s_se.wait(te)
        self.s_ve.wait(se)
        lds = {0: ld_gu(0)}
        prev_se = {0: None, 1: None}
        prev_vh = None

        def sl4(buf, it):
            return buf.rearrange("p (a f) -> p a f", f=512)[:, :, it * 128:(it + 1) * 128]

        for ig in range(DMT):
            lg, lu = lds[ig]
            self.s_x.wait(ve, lg)
            self.ternarize(ve, slot1f, terng, ttmp, 0, 4096)
            self.s_sc.wait(ve, lu)
            vt = self.ternarize(ve, slot0f, ternu, ttmp, 1, 4096)
            if ig + 1 < DMT:
                self.s_ve.wait(sy, vt)
                self.s_ve.wait(self.se.e, vt)
                lds[ig + 1] = ld_gu(ig + 1)
            self.s_ve.wait(te, vt)
            for tg in range(NG):
                for it in range(4):
                    self.fd_step(ternd, tstg, ttmp)
                    par = it % 2
                    pg = self.p_a if par == 0 else self.p_c
                    pu = self.p_b if par == 0 else self.p_d
                    if prev_ev[par] is not None:
                        self.s_ve.wait(te, prev_ev[par])
                        self.s_se.wait(te, prev_se[par])
                    for a2 in range(DMT):
                        te.matmul(pg[:, 0:512], terng[:, a2 * 512 + it * 128:a2 * 512 + it * 128 + 128],
                                  self.XN[:, a2 * R + tg * TG:a2 * R + (tg + 1) * TG],
                                  start=(a2 == 0), stop=(a2 == DMT - 1))
                    for a2 in range(DMT):
                        ins = te.matmul(pu[:, 0:512], ternu[:, a2 * 512 + it * 128:a2 * 512 + it * 128 + 128],
                                        self.XN[:, a2 * R + tg * TG:a2 * R + (tg + 1) * TG],
                                        start=(a2 == 0), stop=(a2 == DMT - 1))
                    tgu = self.s_te.inc(ins)
                    self.s_te.wait(ve, tgu)
                    self.s_te.wait(se, tgu)
                    if prev_h is not None:
                        self.s_sc.wait(ve, prev_h)
                    if prev_vh is not None:
                        self.s_ve.wait(se, prev_vh)
                    sr = se.activation(self.SCR[:, 0:512], pg[:, 0:512], AF.Relu)
                    sqm = se.activation(self.SCR[:, 512:1024], self.SCR[:, 0:512], AF.Square)
                    self.s_se.wait(ve, sqm)
                    vh = ve.tensor_tensor(hst, self.SCR[:, 512:1024], pu[:, 0:512], ALU.mult)
                    self.s_ve.wait(se, vh)
                    sab = se.activation(self.SCR[:, 0:512], hst, AF.Abs)
                    self.s_se.wait(ve, sab)
                    vhm = ve.tensor_tensor(RMX[:, tg * 512:(tg + 1) * 512], RMX[:, tg * 512:(tg + 1) * 512],
                                           self.SCR[:, 0:512], ALU.max)
                    prev_ev[par] = vh
                    prev_se[par] = sr
                    prev_vh = vhm
                    self.s_ve.wait(self.se.e, vh)
                    prev_h = self.qdma(self.se.e, self.s_sc,
                                       self.h_d[(ig * 4 + it) * 128:(ig * 4 + it + 1) * 128,
                                                tg * TG:(tg + 1) * TG], hst)
        self.s_ve.wait(sy)
        self.s_sc.wait(sy)
        self._f2lh0 = self.qdma(sy, self.s_x,
                                self.WBUF[:, 0:16384].rearrange("p (k t) -> p k t", t=512),
                                self.h_d.rearrange("(k p) t -> p k t", p=128)[:, :, 0:TG])
        for tg in range(NG):
            rx = RMX[:, tg * 512:(tg + 1) * 512]
            self.s_ve.wait(te)
            for ct in range(4):
                ins = te.transpose(self.p_e[:, ct * 128:(ct + 1) * 128],
                                   rx[:, ct * 128:(ct + 1) * 128], self.ID_F)
            ttd = self.s_te.inc(ins)
            self.s_te.wait(ve, ttd)
            red = self.smalls[:, 0:4]
            vrd = ve.tensor_reduce(red, self.p_e[:, 0:512].rearrange("p (a b) -> p a b", b=128),
                                   AX.X, ALU.max)
            self.s_ve.wait(te, vrd)
            tred = self.tet(self.p_e[0:4, 256:384], red, self.ID_F)
            self.s_te.wait(ve, tred)
            self.s_sp.wait(ve)
            vev = ve.tensor_copy(self.st16[0:4, 0:128], self.p_e[0:4, 256:384])
            self.s_ve.wait(sy, vev)
            self.qdma(sy, self.s_sp, self.mx_d[16 + tg * 4:16 + (tg + 1) * 4, :],
                      self.st16[0:4, 0:128])

    def phase3_f2(self):
        sy, te, ve, se, po = self.sy, self.te, self.ve, self.se, self.po
        ternd = self.ARE[:, 0:32768]
        HB = {0: self.WBUF[:, 0:16384], 1: self.XN[:]}
        OTF = self.OT[:].bitcast(F32)
        SCBF = self.SCB[:].bitcast(F32)
        S1B = {0: SCBF[:, 0:512], 1: SCBF[:, 512:1024]}
        DSC = {0: self.SCR[:, 0:512], 1: self.SCR[:, 512:1024]}
        i32v = OTF[:, 512:1024].bitcast(I32)
        OSTG = OTF[:, 0:512]
        accs1 = [self.p_a[:, 0:512], self.p_b[:, 0:512], self.p_c[:, 0:512],
                 self.p_e[:, 0:512], self.p_o[:, 0:512], self.p_o[:, 512:1024]]
        self.s_sc.wait(sy)
        self.s_te.wait(sy)
        self.s_ve.wait(sy)
        ldh = {}; ldx = {}; rmc = {}; smath = {}; vmath = {}
        self._f2t8 = {}; self._f2vq = {}; self._f2vo = {}

        def emit_lh(tg):
            p = tg % 2
            if tg >= 2:
                self.s_te.wait(sy, self._f2t8[tg - 2])
                self.s_ve.wait(sy, self._f2vq[tg - 2])
            ldh[tg] = self.qdma(sy, self.s_x, HB[p].rearrange("p (k t) -> p k t", t=512),
                                self.h_d.rearrange("(k p) t -> p k t", p=128)[:, :, tg * TG:(tg + 1) * TG])

        def emit_lx2(tg):
            if tg >= 1:
                self.s_ve.wait(sy, self._f2vo[tg - 1])
            ldx[tg] = self.qdma(sy, self.s_x, self.XG[:].rearrange("p (a f) -> p a f", f=TG),
                                self.x2_d.rearrange("(a p) t -> p a t", p=128)[:, :, tg * TG:(tg + 1) * TG])

        def emit_rows(tg):
            p = tg % 2
            self.s_sp.wait(sy)
            if tg >= 1:
                self.s_ve.wait(sy, vmath[tg - 1])
                self.s_se.wait(sy, smath[tg - 1])
            lr1 = self.qdma(sy, self.s_x, self.st16[0:4, 0:128], self.mx_d[tg * 4:(tg + 1) * 4, :])
            lr2 = self.qdma(sy, self.s_x, self.st16[0:4, 128:256], self.mx_d[16 + tg * 4:16 + (tg + 1) * 4, :])
            self.s_x.wait(sy, lr2)
            r1 = self.qdma(sy, self.s_sp, self.rows[0:1, 0:512], self.st16[0:4, 0:128])
            r2 = self.qdma(sy, self.s_sp, self.rows[0:1, 512:1024], self.st16[0:4, 128:256])
            self.s_sp.wait(ve, r2)
            mxcr = self.rows[0:1, 0:512]; M1r = self.rows[0:1, 512:1024]
            Fr = self.rows[0:1, 1024:1536]; Dr = self.rows[0:1, 1536:2048]
            ve.tensor_tensor(Fr, mxcr, mxcr, ALU.mult)
            ve.tensor_tensor(Fr, Fr, mxcr, ALU.mult)
            vF = ve.tensor_scalar(Fr, Fr, self.sgm, None, ALU.mult)
            self.s_ve.wait(se, vF)
            sF = se.activation(Dr, Fr, AF.Ln)
            self.s_se.wait(ve, sF)
            veF = ve.tensor_scalar(Dr, Dr, -1.0, None, ALU.mult)
            self.s_ve.wait(se, veF)
            sE = se.activation(Dr, Dr, AF.Exp, bias=self.B_LNQEPS[0:1, :])
            self.s_se.wait(ve, sE)
            vD = ve.tensor_tensor(Dr, M1r, Dr, ALU.max)
            self.s_ve.wait(se, vD)
            s1 = se.activation(mxcr, Dr, AF.Ln)
            smath[tg] = se.activation(mxcr, mxcr, AF.Exp, scale=-1.0,
                                      bias=self.B_LN127[0:1, :])
            self.s_se.wait(ve, smath[tg])
            ve.tensor_tensor(M1r, Fr, Dr, ALU.mult)
            vdsc = ve.tensor_scalar(M1r, M1r, self.md127, None, ALU.mult)
            self.s_ve.wait(te, vdsc)
            if tg >= 1:
                self.s_ve.wait(te, rmc[tg - 1])
            tb1 = self.tem(self.p_d[:, 0:512], self.ONES_ROW_F, mxcr, start=True, stop=True)
            self.s_te.wait(ve, tb1)
            vc1 = ve.tensor_copy(S1B[p], self.p_d[:, 0:512])
            self.s_ve.wait(te, vc1)
            tb2 = self.tem(self.p_d[:, 0:512], self.ONES_ROW_F, M1r, start=True, stop=True)
            self.s_te.wait(ve, tb2)
            rmc[tg] = ve.tensor_copy(DSC[p], self.p_d[:, 0:512])
            vmath[tg] = rmc[tg]

        ldh[0] = self._f2lh0
        emit_lx2(0)
        emit_rows(0)
        outr = self.outT.rearrange("(a p) t -> p a t", p=128)
        for tg in range(NG):
            p = tg % 2
            self.s_x.wait(ve, ldh[tg])
            vqs = {}
            for k in range(FFT):
                ve.tensor_tensor(i32v, HB[p][:, k * 512:(k + 1) * 512], S1B[p], ALU.mult)
                vqs[k] = ve.tensor_copy(HB[p][:, k * 512:(k + 1) * 512], i32v)
            self._f2vq[tg] = vqs[FFT - 1]
            if tg + 1 < NG:
                emit_lh(tg + 1)
            for k in range(FFT):
                self.s_ve.wait(te, vqs[k])
                for m2 in range(6):
                    ins = te.matmul(accs1[m2],
                                    ternd[:, k * 1024 + m2 * 128:k * 1024 + m2 * 128 + 128],
                                    HB[p][:, k * 512:(k + 1) * 512],
                                    start=(k == 0), stop=(k == FFT - 1))
            t6 = self.s_te.inc(ins)
            if tg + 1 < NG:
                emit_rows(tg + 1)
            self.s_x.wait(ve, ldx[tg])
            vos = {}

            def evac(m2, pp):
                self.s_out.wait(ve)
                ve.tensor_tensor(OSTG, pp, DSC[p], ALU.mult)
                vo = ve.tensor_tensor(OSTG, OSTG, self.XG[:, m2 * 512:(m2 + 1) * 512], ALU.add)
                self.s_ve.wait(sy, vo)
                self.s_out.wait(sy)
                self.s_out.dma(sy.dma_start(
                    out=outr[:, m2:m2 + 1, tg * TG:(tg + 1) * TG],
                    in_=OSTG.rearrange("p (x t) -> p x t", x=1)))
                return vo

            self.s_te.wait(ve, t6)
            vos[0] = evac(0, accs1[0])
            vos[1] = evac(1, accs1[1])
            self.s_ve.wait(te, vos[1])
            for k in range(FFT):
                for m2 in (6, 7):
                    ins = te.matmul(accs1[m2 - 6],
                                    ternd[:, k * 1024 + m2 * 128:k * 1024 + m2 * 128 + 128],
                                    HB[p][:, k * 512:(k + 1) * 512],
                                    start=(k == 0), stop=(k == FFT - 1))
            t8 = self.s_te.inc(ins)
            self._f2t8[tg] = t8
            for m2 in range(2, 6):
                vos[m2] = evac(m2, accs1[m2])
            self.s_te.wait(ve, t8)
            vos[6] = evac(6, accs1[0])
            vos[7] = evac(7, accs1[1])
            self._f2vo[tg] = vos[7]
            if tg + 1 < NG:
                emit_lx2(tg + 1)

    def emit_debug(self):
        sy = self.sy
        self.s_se.wait(sy)
        self.s_ve.wait(sy)
        self.s_te.wait(sy)
        self.s_sp.wait(sy)
        self.s_wl.wait(sy)
        def dbg(name, src, shape, dt):
            o = self.out(name, shape, dt)
            self.s_out.wait(sy)
            self.s_out.dma(sy.dma_start(out=o, in_=src))
        dbg("d_xn", self.XN[:], [128, DMT * R], BF16)
        dbg("d_rows", self.rows[:], [1, 4 * TG], F32)
        dbg("d_xg", self.XG[:], [128, DMT * TG], F32)
        dbg("d_sq", self.ARE[:, 16384 + DMT * TG:16384 + DMT * TG + 512], [128, 512], BF16)
        dbg("d_scrow", self.SCR[0:1, :], [1, 1024], F32)
        dbg("d_rowsb", self.rowsb, [1, 2 * TG], BF16)
        dbg("d_Call", self.C_all[:], [128, NCH * H], F32)
        if PHASES >= 2:
            dbg("d_gS", self.ARE[:, 16384:32768], [128, DMT * R], BF16)
            dbg("d_ogS", self.ARE[:, 0:16384], [128, DMT * R], BF16)
        dbg("d_Ball", self.Ballc[:], [16, NCH + 1], F32)
        dbg("d_cC", self.cCcol[:], [16, NCH], F32)
        dbg("d_states", self.std_d, [NCH * 128, 512], BF16)
        if PHASES >= 4:
            dbg("d_xq", self.XN[:], [128, DMT * R], BF16)
            dbg("d_mx", self.mx_d, [2 * NG * 4, 128], F32)
            dbg("d_x2", self.x2_d, [DM, R], F32)
        if PHASES >= 5:
            dbg("d_hd", self.h_d, [FFT * 128, R], BF16)
            dbg("d_smalls", self.smalls[0:1, 236:239], [1, 3], F32)
        dbg("d_Smaster", self.S_master[:], [128, 512], F32)
        if PHASES >= 2:
            dbg("d_kg", self.kgT_d, [HT * NCH * 128, CH], BF16)
            dbg("d_q", self.qT_d, [HT * NCH * 128, CH], BF16)
            dbg("d_v", self.vN_d, [NCH * 128, HID], BF16)

    def finish(self):
        sy = self.sy
        self.s_se.wait(sy)
        self.s_ve.wait(sy)
        self.s_te.wait(sy)
        self.s_sp.wait(sy)
        self.s_wl.wait(sy)
        self.s_po.wait(sy)
        self.s_sp.wait(sy)
        self.s_x.wait(sy)
        self.s_out.wait(sy)


def _consts(rank):
    import ml_dtypes
    j = np.arange(128)
    negUT = -np.triu(np.ones((128, 128), dtype=np.float32))
    wind = np.zeros((8, 8), dtype=np.float32)
    for b in range(8):
        for k in range(8):
            if b + 1 <= k <= rank - 1:
                wind[k, b] = 1.0
    alog = np.array([0.0 if b < rank else NEG for b in range(8)], dtype=np.float32)
    return {
        "c_id_bf": np.eye(128, dtype=ml_dtypes.bfloat16),
        "c_id_f": np.eye(128, dtype=np.float32),
        "c_negUT": negUT,
        "c_onescb": np.ones((128, 1), dtype=ml_dtypes.bfloat16),
        "c_onescf": np.ones((128, 1), dtype=np.float32),
        "c_onesrb": np.ones((1, 128), dtype=ml_dtypes.bfloat16),
        "c_onesrf": np.ones((1, 128), dtype=np.float32),
                "c_i16f": np.eye(16, dtype=np.float32),
        "c_i16b": np.eye(16, dtype=ml_dtypes.bfloat16),
        "c_ones16f": np.ones((16, 128), dtype=np.float32),
        "c_ones16b": np.ones((16, 128), dtype=ml_dtypes.bfloat16),
        "c_actc": np.tile(np.array([[LN_EPS, 1.0, RMS_EPS, LN127, float(np.log(Q_EPS))]], dtype=np.float32), (128, 1)),
        "c_negdiag": (np.eye(128) * NEG).astype(ml_dtypes.bfloat16),
        "c_slrep": np.where(j[:, None] > j[None, :], 1.0, 0.0).astype(ml_dtypes.bfloat16),
        "c_wind": wind,
        "c_alog": np.tile(alog[None, :], (128, 1)).astype(np.float32),
    }


def kernel(L0_attn_x, L0_attn_Wq, L0_attn_Wk, L0_attn_Wv, L0_attn_Wo, L0_attn_Wg,
           L0_attn_Wog, L0_attn_Wd, L0_attn_bd, L0_attn_norm_w,
           L0_ffn_w_gate, L0_ffn_w_up, L0_ffn_w_down):
    x = np.asarray(L0_attn_x)[0]
    T = lambda w: np.ascontiguousarray(np.asarray(w).T.astype(np.float32))
    WqT, WkT, WvT, WgT, WogT = T(L0_attn_Wq), T(L0_attn_Wk), T(L0_attn_Wv), T(L0_attn_Wg), T(L0_attn_Wog)
    WoT = T(L0_attn_Wo)
    WdT = T(L0_attn_Wd)
    WdTp = np.ascontiguousarray(WdT.reshape(DMT, 128, H).transpose(1, 0, 2).reshape(128, DMT * H))
    WTg, WTu = T(L0_ffn_w_gate), T(L0_ffn_w_up)
    WTd = T(L0_ffn_w_down)
    bd = np.asarray(L0_attn_bd, dtype=np.float32).reshape(1, H)
    normw = np.ascontiguousarray(np.asarray(L0_attn_norm_w, dtype=np.float32).reshape(HT, 128).T)

    mk = MK()
    nc = mk.build()
    in_maps = []
    for c in range(N_CORES):
        xT_c = np.ascontiguousarray(x[c * R:(c + 1) * R, :].T)
        m = {"xT": xT_c, "WqT": WqT, "WkT": WkT, "WvT": WvT, "WgT": WgT, "WogT": WogT,
             "WoT": WoT, "WdTp": WdTp, "bd": bd, "normw": normw,
             "WTg": WTg, "WTu": WTu, "WTd": WTd}
        m.update(_consts(c))
        in_maps.append(m)
    res = run_bass_kernel_spmd(nc, in_maps, core_ids=list(range(N_CORES)))
    outs = [np.asarray(r["outT"]).T for r in res.results]
    out = np.concatenate(outs, axis=0)[None]
    if DEBUG:
        kernel._results = res
    return out.astype(np.float32)


# version 22
# speedup vs baseline: 1.1204x; 1.0020x over previous
import numpy as np
import concourse.bass as bass
import concourse.mybir as mybir
from concourse.bass_utils import run_bass_kernel_spmd
from contextlib import ExitStack

F32 = mybir.dt.float32
BF16 = mybir.dt.bfloat16
I32 = mybir.dt.int32
AF = mybir.ActivationFunctionType
ALU = mybir.AluOpType
AX = mybir.AxisListType

DEBUG = False
SIM1 = False
PHASES = 99
N_CORES = 8
R = 2048
TG = 512
NG = 4
CH = 128
NCH = 16
H = 16; DH = 64; DM = 1024; DMT = 8; HID = 1024; HT = 8
FF = 4096; FFT = 32
LN_EPS = 1e-5
RMS_EPS = float(np.finfo(np.float32).eps)
Q_EPS = 1e-5
NEG = -1.0e30
LN127 = float(np.log(127.0))


class EW:
    OPS = {"tensor_copy", "tensor_tensor", "tensor_add", "tensor_sub", "tensor_mul",
           "tensor_max", "tensor_scalar", "tensor_single_scalar", "tensor_scalar_add",
           "tensor_scalar_sub", "tensor_scalar_mul", "tensor_scalar_max",
           "tensor_scalar_min", "tensor_relu", "tensor_reduce", "reduce_max",
           "reduce_sum", "scalar_tensor_tensor", "tensor_tensor_scan", "memset",
           "memzero", "select", "copy_predicated", "reciprocal", "transpose",
           "activation", "sign", "square", "sqrt", "mul", "copy", "add",
           "bn_stats", "bn_aggr", "max", "pool", "pool_avg", "pool_max",
           "stream_shuffle", "iota"}

    def __init__(self, eng, sem):
        self.e = eng
        self.s = sem

    def __getattr__(self, k):
        f = getattr(self.e, k)
        if k not in EW.OPS:
            return f
        def g(*a, **kw):
            if self.s.n > 0:
                self.e.wait_ge(self.s.h, self.s.n)
            ins = f(*a, **kw)
            return self.s.inc(ins)
        return g


class Sem:
    def __init__(self, nc, es, name):
        self.h = es.enter_context(nc.semaphore(name))
        self.n = 0

    def inc(self, instr, k=1):
        instr.then_inc(self.h, k)
        self.n += k
        return self.n

    def dma(self, instr):
        return self.inc(instr, 16)

    def wait(self, eng, n=None):
        e = eng.e if isinstance(eng, EW) else eng
        e.wait_ge(self.h, self.n if n is None else n)


class MK:

    def __init__(self):
        self.nc = bass.Bass("TRN2", target_bir_lowering=False)
        self.es = ExitStack()

    def sb(self, name, shape, dt):
        return self.es.enter_context(self.nc.sbuf_tensor(name, shape, dt))

    def ps(self, name, shape, dt=F32):
        return self.es.enter_context(self.nc.psum_tensor(name, shape, dt))

    def inp(self, name, shape, dt=F32):
        return self.nc.dram_tensor(name, shape, dt, kind="ExternalInput").ap()

    def out(self, name, shape, dt=F32):
        return self.nc.dram_tensor(name, shape, dt, kind="ExternalOutput").ap()

    def dram(self, name, shape, dt):
        return self.nc.dram_tensor(name, shape, dt, kind="Internal").ap()

    def qdma(self, queue, sem, out, in_, **kw):
        sem.wait(queue)
        return sem.dma(queue.dma_start(out=out, in_=in_, **kw))

    def build(self):
        nc = self.nc
        self.xT = self.inp("xT", [DM, R])
        self.WqT = self.inp("WqT", [DM, HID]); self.WkT = self.inp("WkT", [DM, HID])
        self.WvT = self.inp("WvT", [DM, HID]); self.WgT = self.inp("WgT", [DM, HID])
        self.WogT = self.inp("WogT", [DM, HID]); self.WoT = self.inp("WoT", [HID, DM])
        self.WdTp = self.inp("WdTp", [128, DMT * H])
        self.bd = self.inp("bd", [1, H])
        self.normw = self.inp("normw", [128, HT])
        self.WTg = self.inp("WTg", [DM, FF]); self.WTu = self.inp("WTu", [DM, FF])
        self.WTd = self.inp("WTd", [FF, DM])
        self.c_id_bf = self.inp("c_id_bf", [128, 128], BF16)
        self.c_id_f = self.inp("c_id_f", [128, 128], F32)
        self.c_negUT = self.inp("c_negUT", [128, 128], F32)
        self.c_onescb = self.inp("c_onescb", [128, 1], BF16)
        self.c_onescf = self.inp("c_onescf", [128, 1], F32)
        self.c_onesrb = self.inp("c_onesrb", [1, 128], BF16)
        self.c_onesrf = self.inp("c_onesrf", [1, 128], F32)
        self.c_i16f = self.inp("c_i16f", [16, 16], F32)
        self.c_i16b = self.inp("c_i16b", [16, 16], BF16)
        self.c_ones16f = self.inp("c_ones16f", [16, 128], F32)
        self.c_ones16b = self.inp("c_ones16b", [16, 128], BF16)
        self.c_actc = self.inp("c_actc", [128, 5], F32)
        self.c_negdiag = self.inp("c_negdiag", [128, 128], BF16)
        self.c_slrep = self.inp("c_slrep", [128, CH], BF16)
        self.c_wind = self.inp("c_wind", [8, 8], F32)
        self.c_alog = self.inp("c_alog", [128, 8], F32)
        self.outT = self.out("outT", [DM, R], F32)
        self.qT_d = self.dram("qT_d", [HT * NCH * 128, CH], BF16)
        self.kgT_d = self.dram("kgT_d", [HT * NCH * 128, CH], BF16)
        self.vN_d = self.dram("vN_d", [NCH * 128, HID], BF16)
        self.x2_d = self.dram("x2_d", [DM, R], F32)
        self.h_d = self.dram("h_d", [FFT * 128, R], BF16)
        self.mx_d = self.dram("mx_d", [2 * NG * 4, 128], F32)
        self.cc_in = self.dram("cc_in", [129, 512], F32)
        self.cc_out = self.dram("cc_out", [8 * 129, 512], F32)
        self.std_d = self.dram("std_d", [NCH * 128, 512], BF16)

        self.XN = self.sb("XN", [128, DMT * R], BF16)
        self.ARE = self.sb("ARE", [128, 32768], BF16)
        self.WBUF = self.sb("WBUF", [128, 2 * 8192], BF16)
        self.XG = self.sb("XG", [128, DMT * TG], F32)
        self.SCR = self.sb("SCR", [128, 1024], F32)
        self.SCB = self.sb("SCB", [128, 2048], BF16)
        self.OT = self.sb("OT", [128, HT * TG], BF16)
        self.C_all = self.sb("C_all", [128, NCH * H], F32)
        self.negC_all = self.sb("negC_all", [128, NCH * H], F32)
        self.A2S = self.sb("A2S", [128, 5120], BF16)
        self.CHIA = self.sb("CHIA", [16, 2048], F32)
        self.STB2 = self.sb("STB2", [128, 1024], BF16)
        self.LST = self.sb("LST", [128, 1024], BF16)
        self.st16 = self.sb("st16", [16, 256], F32)
        self.rows = self.sb("rows", [1, 4 * TG], F32)
        self.cCcol = self.sb("cCcol", [16, NCH], F32)
        self.Ballc = self.sb("Ballc", [16, NCH + 1], F32)
        self.S_master = self.sb("S_master", [128, 512], F32)
        self.smalls = self.sb("smalls", [128, 256], F32)
        self.smallsb = self.sb("smallsb", [128, 256], BF16)
        self.wd_sb = self.sb("wd_sb", [128, DMT * H], F32)
        self.wdbar = self.sb("wdbar", [128, H], F32)
        self.bdb = self.sb("bdb", [128, H], F32)
        self.normw_sb = self.sb("normw_sb", [128, HT], F32)
        self.consts = self.sb("consts", [128, 400], F32)
        self.constsb = self.sb("constsb", [128, 264], BF16)
        self.rowsb = self.A2S[0:1, 0:1024]
        self.i16f = self.sb("i16f", [16, 16], F32)
        self.i16b = self.sb("i16b", [16, 16], BF16)
        self.ones16f = self.sb("ones16f", [16, 128], F32)
        self.ones16b = self.sb("ones16b", [16, 128], BF16)
        self.negdiag = self.sb("negdiag", [128, 128], BF16)
        self.slrep = self.sb("slrep", [128, CH], BF16)
        self.wind = self.sb("wind", [8, 8], F32)
        self.alog = self.sb("alog", [128, 8], F32)
        self.p_a = self.ps("p_a", [128, 512]); self.p_b = self.ps("p_b", [128, 512])
        self.p_c = self.ps("p_c", [128, 512]); self.p_d = self.ps("p_d", [128, 512])
        self.p_e = self.ps("p_e", [128, 512])
        self.p_o = self.ps("p_o", [128, 1024])
        self.p_t = self.ps("p_t", [128, 1024], BF16)

        es = self.es
        self.s_wl = Sem(nc, es, "s_wl")
        self.s_x = Sem(nc, es, "s_x")
        self.s_sp = Sem(nc, es, "s_sp")
        self.s_te = Sem(nc, es, "s_te")
        self.s_ve = Sem(nc, es, "s_ve")
        self.s_se = Sem(nc, es, "s_se")
        self.s_po = Sem(nc, es, "s_po")
        self.s_wq = Sem(nc, es, "s_wq")
        self.s_sc = Sem(nc, es, "s_sc")
        self.s_cc = Sem(nc, es, "s_cc")
        self.s_out = Sem(nc, es, "s_out")

        with self.es:
            with nc.Block() as block:
                @block.sync
                def _(sync):
                    self.emit()
        return nc

    def emit(self):
        nc = self.nc
        self.sy = nc.sync
        self.te = nc.tensor
        self.ve = EW(nc.vector, self.s_ve)
        self.se = EW(nc.scalar, self.s_se)
        self.po = EW(nc.gpsimd, self.s_po)
        self.phase0()
        self.phase1a()
        if PHASES >= 2:
            self.phase1b()
        if PHASES >= 3:
            self.phase2a()
        if PHASES >= 4:
            self.phase2b()
        if PHASES >= 5:
            self.phase3_f0()
            self.phase3_f1()
            self.phase3_f2()
        if DEBUG:
            self.emit_debug()
        self.finish()

    def preload_set(self, name):
        from concourse.hw_specs import get_activation_tables
        import concourse.mybir as mb
        tabs = get_activation_tables(self.nc.m.arch)
        idx = list(tabs.keys()).index(name)
        return self.se.e.add_instruction(mb.InstLoadActFuncSet(
            name=self.nc.get_next_instruction_name(),
            act_func_set_id=idx, ins=[], outs=[]))

    def tem(self, *a, **kw):
        ins = self.te.matmul(*a, **kw)
        return self.s_te.inc(ins)

    def tet(self, *a, **kw):
        ins = self.te.transpose(*a, **kw)
        return self.s_te.inc(ins)

    def phase0(self):
        sy, te, ve, se, po = self.sy, self.te, self.ve, self.se, self.po
        cs, cb = self.consts, self.constsb
        self.ID_F = cs[:, 0:128]
        self.NEGUT = cs[:, 128:256]
        self.ONES_COL_F = cs[:, 256:257]
        self.ONES_ROW_F = cs[0:1, 257:385]
        self.B_LNEPS = cs[:, 385:386]
        self.B_ONE = cs[:, 386:387]
        self.B_RMSEPS = cs[:, 387:388]
        self.B_LN127 = cs[:, 388:389]
        self.B_LNQEPS = cs[:, 389:390]
        self.ID_B = cb[:, 0:128]
        self.ONES_COL_B = cb[:, 128:129]
        self.ONES_ROW_B = cb[0:1, 129:257]
        for dst, src in [
            (self.ID_F, self.c_id_f), (self.NEGUT, self.c_negUT),
            (self.ONES_COL_F, self.c_onescf), (self.ONES_ROW_F, self.c_onesrf),
            (self.ID_B, self.c_id_bf), (self.ONES_COL_B, self.c_onescb),
            (self.ONES_ROW_B, self.c_onesrb),             (self.normw_sb[:], self.normw),
            (self.i16f[:], self.c_i16f), (self.i16b[:], self.c_i16b),
            (self.ones16f[:], self.c_ones16f), (self.ones16b[:], self.c_ones16b),
            (cs[:, 385:390], self.c_actc), (self.wd_sb[:], self.WdTp),
            (self.st16[0:1, 240:256], self.bd),
            (self.negdiag[:], self.c_negdiag), (self.slrep[:], self.c_slrep),
            (self.wind[:], self.c_wind), (self.alog[:], self.c_alog),
        ]:
            self.qdma(sy, self.s_x, dst, src)
        self.s_x.wait(te)
        self.tem(self.p_a[:, 0:H], self.ONES_ROW_F, self.st16[0:1, 240:256], start=True, stop=True)
        for a in range(DMT):
            ins = te.matmul(self.p_b[0:1, 0:H], self.ONES_COL_F, self.wd_sb[:, a * H:(a + 1) * H],
                            start=(a == 0), stop=(a == DMT - 1))
        t0 = self.s_te.inc(ins)
        self.s_te.wait(ve, t0)
        ve.tensor_copy(self.bdb[:], self.p_a[:, 0:H])
        v0 = ve.tensor_copy(self.st16[0:1, 224:240], self.p_b[0:1, 0:H])
        self.s_ve.wait(te, v0)
        t1 = self.tem(self.p_a[:, H:2 * H], self.ONES_ROW_F, self.st16[0:1, 224:240],
                      start=True, stop=True)
        self.s_te.wait(ve, t1)
        ve.tensor_copy(self.wdbar[:], self.p_a[:, H:2 * H])
        ve.memset(self.S_master[:], 0.0)
        ve.memset(self.Ballc[:], 0.0)

    def phase1a(self):
        sy, te, ve, se, po = self.sy, self.te, self.ve, self.se, self.po
        self.preload_set("natural_log_exp_and_others")
        ROW = self.rows
        AREF = self.ARE[:].bitcast(F32)
        XGB1 = {0: self.XG[:], 1: AREF[:, 12288:16384]}
        ldx = {}
        lastz = {}
        self._p1ve = {}

        def p1_load(tg):
            if tg >= 2:
                self.s_te.wait(sy, lastz[tg - 2])
                self.s_ve.wait(sy, self._p1ve[tg - 2])
            ldx[tg] = self.qdma(sy, self.s_x,
                                XGB1[tg % 2].rearrange("p (a f) -> p a f", f=TG),
                                self.xT.rearrange("(a p) t -> p a t", p=128)[:, :, tg * TG:(tg + 1) * TG])

        self.s_te.wait(sy)
        p1_load(0)
        p1_load(1)
        for tg in range(NG):
            XGg = XGB1[tg % 2]
            ld = ldx[tg]
            self.s_x.wait(ve, ld)
            self.s_x.wait(se, ld)
            if tg > 0:
                self.s_te.wait(ve)
                self.s_te.wait(se)
            xb = self.ARE[:, 16384:16384 + DMT * TG]
            sqt = self.ARE[:, 16384 + DMT * TG:16384 + 2 * DMT * TG]
            for a in range(DMT):
                vml = ve.tensor_copy(xb[:, a * TG:(a + 1) * TG], XGg[:, a * TG:(a + 1) * TG])
            for a in range(DMT):
                sml = se.activation(sqt[:, a * TG:(a + 1) * TG], XGg[:, a * TG:(a + 1) * TG], AF.Square)
            self.s_ve.wait(te, vml)
            self.s_se.wait(te, sml)
            for a in range(DMT):
                ins = te.matmul(self.p_a[0:1, 0:TG], self.ONES_COL_B, xb[:, a * TG:(a + 1) * TG],
                                start=(a == 0), stop=(a == DMT - 1))
            self.s_te.inc(ins)
            for a in range(DMT):
                ins = te.matmul(self.p_b[0:1, 0:TG], self.ONES_COL_B, sqt[:, a * TG:(a + 1) * TG],
                                start=(a == 0), stop=(a == DMT - 1))
            tml = self.s_te.inc(ins)
            self.s_te.wait(ve, tml)
            if DEBUG and tg == 3:
                ve.tensor_copy(self.SCR[0:1, 512:1024], self.p_b[0:1, 0:TG])
                ve.tensor_copy(self.SCR[0:1, 0:512], self.p_a[0:1, 0:TG])
            rm = ROW[0:1, 0:TG]; rvar = ROW[0:1, TG:2 * TG]
            rr = ROW[0:1, 2 * TG:3 * TG]; rmr = ROW[0:1, 3 * TG:4 * TG]
            ve.tensor_scalar(rm, self.p_a[0:1, 0:TG], 1.0 / DM, None, ALU.mult)
            ve.tensor_scalar(rvar, self.p_b[0:1, 0:TG], 1.0 / DM, None, ALU.mult)
            ve.tensor_mul(rr, rm, rm)
            vst = ve.tensor_sub(rvar, rvar, rr)
            if DEBUG and tg == 3:
                vst = ve.tensor_copy(self.rowsb[0:1, 0:512], rvar)
            self.s_ve.wait(se, vst)
            se.activation(rvar, rvar, AF.Ln, bias=self.B_LNEPS[0:1, :])
            sst = se.activation(rr, rvar, AF.Exp, scale=-0.5)
            if DEBUG and tg == 3:
                sst = se.activation(self.rowsb[0:1, 512:1024], rvar, AF.Copy)
            self.s_se.wait(ve, sst)
            vrb = ve.tensor_mul(rmr, rm, rr)
            self.s_ve.wait(te, vrb)
            self.tem(self.p_c[:, 0:TG], self.ONES_ROW_F, rr, start=True, stop=True)
            tbc = self.tem(self.p_d[:, 0:TG], self.ONES_ROW_F, rmr, start=True, stop=True)
            self.s_te.wait(ve, tbc)
            for a in range(DMT):
                t1 = self.SCR[:, 0:TG]
                ve.tensor_mul(t1, XGg[:, a * TG:(a + 1) * TG], self.p_c[:, 0:TG])
                ve.tensor_sub(self.XN[:, a * R + tg * TG: a * R + (tg + 1) * TG], t1, self.p_d[:, 0:TG])

            self.s_ve.wait(te)
            for cc in range(NG):
                t0 = cc * CH
                te.transpose(self.p_e[0:CH, cc:cc + 1], rr[0:1, t0:t0 + CH], self.ID_F[0:1, 0:1])
                ins = te.transpose(self.p_e[0:CH, 4 + cc:5 + cc], rmr[0:1, t0:t0 + CH],
                                   self.ID_F[0:1, 0:1])
            self.s_te.inc(ins)
            for cc in range(NG):
                t0 = cc * CH
                for a in range(DMT):
                    ins = te.matmul(self.p_o[:, cc * H:(cc + 1) * H],
                                    XGg[:, a * TG + t0:a * TG + t0 + CH],
                                    self.wd_sb[:, a * H:(a + 1) * H],
                                    start=(a == 0), stop=(a == DMT - 1))
            tz = self.s_te.inc(ins)
            self.s_te.wait(ve, tz)
            rc8 = self.smalls[:, 0:8]
            ve.tensor_copy(rc8, self.p_e[:, 0:8])
            z1 = self.smalls[:, 16:80]; zz = self.smalls[:, 96:160]
            ve.tensor_tensor(z1.rearrange("p (c h) -> p c h", h=H),
                             self.p_o[:, 0:64].rearrange("p (c h) -> p c h", h=H),
                             rc8[:, 0:4].rearrange("p (c x) -> p c x", x=1).broadcast_to([128, 4, H]),
                             ALU.mult)
            ve.tensor_tensor(zz.rearrange("p (c h) -> p c h", h=H),
                             self.wdbar[:].rearrange("p (x h) -> p x h", x=1).broadcast_to([128, 4, H]),
                             rc8[:, 4:8].rearrange("p (c x) -> p c x", x=1).broadcast_to([128, 4, H]),
                             ALU.mult)
            ve.tensor_sub(z1, z1, zz)
            vz = ve.tensor_tensor(zz.rearrange("p (c h) -> p c h", h=H),
                                  z1.rearrange("p (c h) -> p c h", h=H),
                                  self.bdb[:].rearrange("p (x h) -> p x h", x=1).broadcast_to([128, 4, H]),
                                  ALU.add)
            self.s_ve.wait(se, vz)
            az = self.smalls[:, 160:224]
            se.activation(az, zz, AF.Abs)
            se.activation(az, az, AF.Exp, scale=-1.0)
            sz = se.activation(az, az, AF.Ln, bias=self.B_ONE)
            self.s_se.wait(ve, sz)
            sp = z1
            ve.tensor_scalar(sp, zz, 0.0, None, ALU.max)
            vsp = ve.tensor_add(sp, sp, az)
            self.s_ve.wait(te, vsp)
            for cc in range(NG):
                ins = self.tem(self.p_o[:, 512 + cc * H:512 + (cc + 1) * H], self.NEGUT,
                               sp[:, cc * H:(cc + 1) * H], start=True, stop=True)
            tc = ins
            self.s_te.wait(ve, tc)
            Cg = self.C_all[:, tg * 64:(tg + 1) * 64]
            ve.tensor_copy(Cg, self.p_o[:, 512:576])
            vch = ve.tensor_scalar(self.negC_all[:, tg * 64:(tg + 1) * 64],
                                   self.p_o[:, 512:576], -1.0, None, ALU.mult)
            self.s_ve.wait(te, vch)
            for cc in range(NG):
                ins = self.tet(self.p_e[0:H, cc * CH:(cc + 1) * CH],
                               Cg[:, cc * H:(cc + 1) * H], self.ID_F)
            tt = ins
            self.s_te.wait(ve, tt)
            tot = self.Ballc[:, NCH:NCH + 1]
            for cc in range(NG):
                c = tg * NG + cc
                ve.tensor_copy(self.Ballc[:, c:c + 1], tot)
                ve.tensor_copy(self.cCcol[:, c:c + 1],
                               self.p_e[0:H, cc * CH + CH - 1:cc * CH + CH])
                vbk = ve.tensor_add(tot, tot, self.cCcol[:, c:c + 1])
            lastz[tg] = tz
            self._p1ve[tg] = vbk
            if tg + 2 < NG:
                p1_load(tg + 2)

    def load_weight(self, w_dram, slot, cols=HID, after_te=None):
        po = self.po
        if after_te is not None:
            self.s_te.wait(po, after_te)
        self.s_wl.wait(po)
        dst = self.WBUF[:, slot * 8192: slot * 8192 + DMT * cols]
        ins = po.dma_start(
            out=dst.rearrange("p (a f) -> p a f", f=cols),
            in_=w_dram.rearrange("(a p) f -> p a f", p=128))
        return self.s_wl.dma(ins)

    def wtile(self, slot, a, m, w=128, cols=HID):
        base = slot * 8192 + a * cols + m * w
        return self.WBUF[:, base: base + w]

    def spill4(self, dram, m, tg, src):
        dst = dram.rearrange("(a c p) t -> p a c t", p=128, c=NCH)[
            :, m:m + 1, tg * NG:(tg + 1) * NG, :]
        s4 = src.rearrange("p (x c t) -> p x c t", x=1, t=CH)
        self.s_sp.wait(self.sy)
        return self.s_sp.dma(self.sy.dma_start(out=dst, in_=s4))

    def phase1b(self):
        sy, te, ve, se, po = self.sy, self.te, self.ve, self.se, self.po
        se.e.preload_act_table(AF.Sigmoid)
        gS = self.ARE[:, 16384:32768]
        ogS = self.ARE[:, 0:16384]
        ld_g = self.load_weight(self.WgT, 0)
        ld_k = self.load_weight(self.WkT, 1)
        psums = [self.p_a, self.p_b, self.p_c, self.p_d]
        self.s_ve.wait(te)
        self.s_se.wait(te)

        marks = {}
        for i in range(HT * NG):
            m, tg = divmod(i, NG)
            p = psums[i % 4]
            if i >= 4:
                self.s_se.wait(te, marks[i - 4])
            if i == 0:
                self.s_wl.wait(te, ld_g)
            for a in range(DMT):
                ins = te.matmul(p[:, 0:TG], self.wtile(0, a, m),
                                self.XN[:, a * R + tg * TG:a * R + (tg + 1) * TG],
                                start=(a == 0), stop=(a == DMT - 1))
            tm = self.s_te.inc(ins)
            self.s_te.wait(se, tm)
            marks[i] = se.activation(gS[:, m * R + tg * TG: m * R + (tg + 1) * TG], p[:, 0:TG], AF.Sigmoid)
        te_g_done = self.s_te.n

        ld_q = self.load_weight(self.WqT, 0, after_te=te_g_done)
        kge = {}
        spills = {}
        for i in range(HT * NG):
            m, tg = divmod(i, NG)
            p = psums[i % 4]
            if i >= 4:
                self.s_ve.wait(te, kge[i - 4])
            elif i == 0:
                self.s_se.wait(te, marks[HT * NG - 1])
            if i == 0:
                self.s_wl.wait(te, ld_k)
            for a in range(DMT):
                ins = te.matmul(p[:, 0:TG], self.wtile(1, a, m),
                                self.XN[:, a * R + tg * TG:a * R + (tg + 1) * TG],
                                start=(a == 0), stop=(a == DMT - 1))
            tm = self.s_te.inc(ins)
            self.s_te.wait(ve, tm)
            if i >= 4:
                self.s_sp.wait(ve, spills[i - 4])
            dst = self.SCB[:, (i % 4) * TG:(i % 4) * TG + TG]
            kge[i] = ve.tensor_mul(dst, p[:, 0:TG], gS[:, m * R + tg * TG: m * R + (tg + 1) * TG])
            self.s_ve.wait(sy, kge[i])
            spills[i] = self.spill4(self.kgT_d, m, tg, dst)
        te_k_done = self.s_te.n

        ld_v = self.load_weight(self.WvT, 1, after_te=te_k_done)
        qe = {}
        for i in range(HT * NG):
            m, tg = divmod(i, NG)
            p = psums[i % 4]
            if i >= 4:
                self.s_ve.wait(te, qe[i - 4])
            elif i == 0:
                self.s_ve.wait(te, kge[HT * NG - 1])
            if i == 0:
                self.s_wl.wait(te, ld_q)
            for a in range(DMT):
                ins = te.matmul(p[:, 0:TG], self.wtile(0, a, m),
                                self.XN[:, a * R + tg * TG:a * R + (tg + 1) * TG],
                                start=(a == 0), stop=(a == DMT - 1))
            tm = self.s_te.inc(ins)
            self.s_te.wait(ve, tm)
            self.s_sp.wait(ve, spills[i - 4] if i >= 4 else spills[HT * NG - 4 + i])
            dst = self.SCB[:, (i % 4) * TG:(i % 4) * TG + TG]
            qe[i] = ve.tensor_copy(dst, p[:, 0:TG])
            self.s_ve.wait(sy, qe[i])
            spills[i] = self.spill4(self.qT_d, m, tg, dst)
        te_q_done = self.s_te.n

        ld_og = self.load_weight(self.WogT, 0, after_te=te_q_done)
        vev = {}
        for i in range(NCH * 2):
            c, vb = divmod(i, 2)
            p = psums[i % 4]
            if i >= 4:
                self.s_ve.wait(te, vev[i - 4])
            elif i == 0:
                self.s_ve.wait(te, qe[HT * NG - 1])
            if i == 0:
                self.s_wl.wait(te, ld_v)
            for a in range(DMT):
                ins = te.matmul(p[:, 0:TG],
                                self.XN[:, a * R + c * CH: a * R + (c + 1) * CH],
                                self.WBUF[:, 8192 + a * HID + vb * TG: 8192 + a * HID + (vb + 1) * TG],
                                start=(a == 0), stop=(a == DMT - 1))
            tm = self.s_te.inc(ins)
            self.s_te.wait(ve, tm)
            self.s_sp.wait(ve, spills[i - 4] if i >= 4 else spills[HT * NG - 4 + i])
            dst = self.SCB[:, (i % 4) * TG:(i % 4) * TG + TG]
            vev[i] = ve.tensor_copy(dst, p[:, 0:TG])
            self.s_ve.wait(sy, vev[i])
            self.s_sp.wait(sy)
            spills[i] = self.s_sp.dma(sy.dma_start(
                out=self.vN_d[c * 128:(c + 1) * 128, vb * TG:(vb + 1) * TG], in_=dst))
        self.te_v_done = self.s_te.n

        oge = {}
        for i in range(HT * NG):
            m, tg = divmod(i, NG)
            p = psums[i % 4]
            if i >= 4:
                self.s_se.wait(te, oge[i - 4])
            elif i == 0:
                self.s_ve.wait(te, vev[NCH * 2 - 1])
            if i == 0:
                self.s_wl.wait(te, ld_og)
            for a in range(DMT):
                ins = te.matmul(p[:, 0:TG], self.wtile(0, a, m),
                                self.XN[:, a * R + tg * TG:a * R + (tg + 1) * TG],
                                start=(a == 0), stop=(a == DMT - 1))
            tm = self.s_te.inc(ins)
            self.s_te.wait(se, tm)
            oge[i] = se.activation(ogS[:, m * R + tg * TG: m * R + (tg + 1) * TG], p[:, 0:TG], AF.Sigmoid)


    def phase2a(self):
        sy, te, ve, se, po = self.sy, self.te, self.ve, self.se, self.po
        self.preload_set("natural_log_exp_and_others")
        self._aspl = {}
        KGC = {0: self.SCB[:, 0:1024], 1: self.OT[:, 1024:2048]}
        VNC = {0: self.SCB[:, 1024:2048], 1: self.OT[:, 2048:3072]}
        KHN = self.OT[:, 0:1024]

        SC16A = self.smalls[0:16, 0:256]
        ve.tensor_tensor(SC16A.rearrange("k (c h) -> k c h", h=H),
                         self.i16f[:].rearrange("k (x h) -> k x h", x=1).broadcast_to([16, NCH, H]),
                         self.cCcol[:].rearrange("k (c x) -> k c x", x=1).broadcast_to([16, NCH, H]),
                         ALU.mult)
        vsc = self.s_ve.n
        self.s_ve.wait(te, vsc)
        tca = self.tem(self.p_e[:, 0:256], self.ones16f[:], SC16A, start=True, stop=True)
        self.s_te.wait(ve, tca)
        DIFA = self.SCR[:, 512:768]
        EXPD = self.SCR[:, 768:1024]
        vda = ve.tensor_tensor(DIFA, self.p_e[:, 0:256], self.negC_all[:], ALU.add)
        self.s_ve.wait(se, vda)
        sea = se.activation(EXPD, DIFA, AF.Exp)
        e16a = self.smalls[0:16, 488:504] if False else self.smalls[0:16, 100:116]
        sca = se.activation(e16a, self.cCcol[:], AF.Exp)
        self.s_se.wait(ve, sca)
        E16BA = self.smallsb[0:16, 0:256]
        veb = ve.tensor_tensor(
            E16BA.rearrange("k (c h) -> k c h", h=H),
            self.i16b[:].rearrange("k (x h) -> k x h", x=1).broadcast_to([16, NCH, H]),
            e16a.rearrange("k (c x) -> k c x", x=1).broadcast_to([16, NCH, H]), ALU.mult)

        lda = {}
        tcoh = {}

        def a1_loads(c):
            p = c % 2
            if c >= 2:
                self.s_te.wait(sy, tcoh[c - 2])
            l1 = self.qdma(sy, self.s_x,
                           KGC[p].rearrange("p (a x t) -> p a x t", x=1, t=CH),
                           self.kgT_d.rearrange("(a c p) t -> p a c t", p=128, c=NCH)[:, :, c:c + 1, :])
            l2 = self.qdma(sy, self.s_x, VNC[p], self.vN_d[c * 128:(c + 1) * 128, :])
            lda[c] = (l1, l2)

        self.s_te.wait(sy)
        self.s_sp.wait(sy)
        a1_loads(0)
        a1_loads(1)

        scm = 0
        for c in range(NCH):
            p = c % 2
            sl = E16BA[0:16, c * H:(c + 1) * H]
            rhs_e = sl.rearrange("k (a b) -> k a b", b=2)[:, :, 0:1].broadcast_to([16, 8, 64])
            rhs_o = sl.rearrange("k (a b) -> k a b", b=2)[:, :, 1:2].broadcast_to([16, 8, 64])
            self.s_ve.wait(te, max(veb, scm))
            self.tem(self.p_d[0:64, 0:512], self.ones16b[:, 0:64], rhs_e, start=True, stop=True)
            teb = self.tem(self.p_d[64:128, 0:512], self.ones16b[:, 64:128], rhs_o, start=True, stop=True,
                           tile_position=(0, 64))
            if c >= 2:
                self.s_sp.wait(ve, self._aspl[c - 2])
            vsv = ve.tensor_copy(self.STB2[:, (c % 2) * 512:(c % 2) * 512 + 512],
                                 self.S_master[:])
            self.s_ve.wait(sy, vsv)
            self._aspl[c] = self.qdma(sy, self.s_sp, self.std_d[c * 128:(c + 1) * 128, :],
                                      self.STB2[:, (c % 2) * 512:(c % 2) * 512 + 512])
            if c + 1 < NCH:
                a1_loads(c + 1)
            self.s_x.wait(te, lda[c][0])
            self.s_ve.wait(te, vsv)
            for i in range(8):
                ins = te.transpose(self.p_t[:, i * 128:(i + 1) * 128],
                                   KGC[p][:, i * 128:(i + 1) * 128], self.ID_B)
            ttr = self.s_te.inc(ins)
            self.s_te.wait(ve, ttr)
            self.s_se.wait(ve, sea)
            vkh = ve.tensor_tensor(
                KHN.rearrange("p (h t) -> p h t", t=64),
                self.p_t[:, 0:1024].rearrange("p (h t) -> p h t", t=64),
                EXPD[:, c * H:(c + 1) * H].rearrange("p (h x) -> p h x", x=1).broadcast_to([128, H, 64]),
                ALU.mult)
            self.s_ve.wait(te, vkh)
            self.s_x.wait(te, lda[c][1])
            for h in range(H):
                pb = self.p_b if h % 2 == 0 else self.p_c
                ins = te.matmul(
                    pb[(h % 2) * 64:(h % 2) * 64 + 64,
                       (h // 2) * 64:(h // 2) * 64 + 64],
                    KHN[:, h * 64:(h + 1) * 64], VNC[p][:, h * 64:(h + 1) * 64],
                    start=(h < 2), stop=(h >= H - 2),
                    tile_position=(0, (h % 2) * 64))
            tcoh[c] = tco = self.s_te.inc(ins)
            self.s_te.wait(ve, max(tco, teb))
            scm = ve.tensor_tensor(self.SCR[:, 0:512], self.S_master[:], self.p_d[:, 0:512], ALU.mult)
            ve.tensor_tensor(self.S_master[0:64, :], self.SCR[0:64, 0:512], self.p_b[0:64, 0:512], ALU.add)
            ve.tensor_tensor(self.S_master[64:128, :], self.SCR[64:128, 0:512], self.p_c[64:128, 0:512], ALU.add)
        vbt = ve.tensor_copy(self.smalls[0:16, 224:225], self.Ballc[:, NCH:NCH + 1])
        self.s_ve.wait(te, vbt)
        tbt = self.tet(self.p_e[0:1, 0:16], self.smalls[0:16, 224:225], self.ID_F[0:16, 0:16])
        self.s_te.wait(ve, tbt)
        ve.memset(self.rows[0:1, 0:512], 0.0)
        vrow = ve.tensor_copy(self.rows[0:1, 0:16], self.p_e[0:1, 0:16])
        self.s_ve.wait(sy, vrow)
        self.qdma(sy, self.s_sp, self.cc_in[0:128, :], self.S_master[:])
        self.qdma(sy, self.s_sp, self.cc_in[128:129, :], self.rows[0:1, 0:512])
        self.s_sp.wait(po)
        if SIM1:
            for b in range(8):
                self.s_cc.inc(po.dma_start(out=self.cc_out[b * 129:(b + 1) * 129, :],
                                           in_=self.cc_in[:, :]), 16)
        else:
            ins = po.collective_compute(
                "AllGather", ALU.bypass,
                ins=[self.cc_in.opt()], outs=[self.cc_out.opt()],
                replica_groups=[list(range(N_CORES))])
            self.s_cc.inc(ins, 1)


    def phase2b(self):
        sy, te, ve, se, po = self.sy, self.te, self.ve, self.se, self.po
        A2S = self.A2S
        KG = {0: A2S[:, 0:1024], 1: A2S[:, 2048:3072]}
        QC = {0: A2S[:, 1024:2048], 1: A2S[:, 3072:4096]}
        SEF = {0: A2S[:, 4096:4608], 1: A2S[:, 4608:5120]}
        VN = {0: self.SCB[:, 0:1024], 1: self.SCB[:, 1024:2048]}
        QT = {0: self.ARE[:, 16384:16896], 1: self.ARE[:, 16896:17408]}
        PB = {0: self.ARE[:, 17408:17920], 1: self.ARE[:, 17920:18432]}
        ABUF = self.ARE[:, 18432:18944]
        EXPC = self.ARE[:, 18944:19456]
        ogS = self.ARE[:, 0:16384]
        SQA = self.ARE[:, 16384:16384 + 4096]
        YB = self.ARE[:, 16384 + 4096:16384 + 8192]
        XB2 = self.ARE[:, 16384 + 8192:16384 + 12288]
        SQ2 = self.ARE[:, 16384 + 12288:16384 + 16384]
        Dbuf = self.SCR[:, 512:1024]
        ld_o = self.load_weight(self.WoT, 0, after_te=self.s_te.n)
        self.f0_init()

        ld = {}
        self._tgo = {}; self._vql = {}; self._ogo = {}; self._evac = {}
        self._post_c_ve = 0
        self._post_c_te = 0
        self._post_c_xg = 0
        self._spx = 0

        def emit_loads(c):
            p = c % 2
            if c >= 2:
                self.s_te.wait(sy, self._tgo[c - 2])
                self.s_ve.wait(sy, self._vql[c - 2])
            l1 = self.qdma(sy, self.s_x, KG[p].rearrange("p (a x t) -> p a x t", x=1, t=CH),
                           self.kgT_d.rearrange("(a c p) t -> p a c t", p=128, c=NCH)[:, :, c:c + 1, :])
            l2 = self.qdma(sy, self.s_x, QC[p].rearrange("p (a x t) -> p a x t", x=1, t=CH),
                           self.qT_d.rearrange("(a c p) t -> p a c t", p=128, c=NCH)[:, :, c:c + 1, :])
            l3 = self.qdma(sy, self.s_x, VN[p], self.vN_d[c * 128:(c + 1) * 128, :])
            l4 = self.qdma(sy, self.s_x, self.LST[:, p * 512:p * 512 + 512],
                           self.std_d[c * 128:(c + 1) * 128, :])
            ld[c] = (l1, l2, l3, l4)

        self.s_sp.wait(sy)
        self.s_te.wait(sy)
        self.s_ve.wait(sy)
        emit_loads(0)
        emit_loads(1)
        next_ld = 2

        self.s_ve.wait(te)
        te.transpose(self.p_d[:, 0:128], self.C_all[:, 0:128], self.ID_F)
        tt2 = self.tet(self.p_d[:, 128:256], self.C_all[:, 128:256], self.ID_F)
        self.s_te.wait(ve, tt2)
        CT = self.SCR[:, 0:256]
        vct = ve.tensor_copy(CT, self.p_d[:, 0:256])
        self.s_ve.wait(sy, vct)
        self.s_sp.wait(sy)
        self.qdma(sy, self.s_sp, self.CHIA.rearrange("c (h t) -> c h t", t=128)[0:8],
                  CT[:, 0:128])
        rch = self.qdma(sy, self.s_sp, self.CHIA.rearrange("c (h t) -> c h t", t=128)[8:16],
                        CT[:, 128:256])
        self.s_sp.wait(se, rch)
        eBa = self.smalls[0:16, 144:160]
        sba = se.activation(eBa, self.Ballc[0:16, 0:16], AF.Exp)
        self.s_se.wait(ve, sba)
        E16A = self.smallsb[0:16, 0:256]
        vE16 = ve.tensor_tensor(
            E16A.rearrange("k (c h) -> k c h", h=16),
            self.i16b[:].rearrange("k (x h) -> k x h", x=1).broadcast_to([16, 16, 16]),
            eBa.rearrange("k (c x) -> k c x", x=1).broadcast_to([16, 16, 16]), ALU.mult)

        OTF2 = self.OT[:].bitcast(F32)
        self._comb_seb = None

        def emit_combine():
            self.s_cc.wait(sy)
            B8 = self.smalls[0:8, 208:224]
            self.qdma(sy, self.s_x, B8,
                      self.cc_out.rearrange("(b r) f -> b r f", r=129)[:, 128:129, 0:16])
            EBF = self.SCR[:, 0:512]
            seb = 0
            for b in range(8):
                slot = OTF2[:, (1 + (b % 2)) * 512:(2 + (b % 2)) * 512]
                lb = self.qdma(sy, self.s_x, slot,
                               self.cc_out[b * 129:(b * 129) + 128, :])
                wb = self.wind[:, b:b + 1].broadcast_to([8, 64])
                be = B8.rearrange("k (a b2) -> k a b2", b2=2)[:, :, 0:1].broadcast_to([8, 8, 64])
                bo = B8.rearrange("k (a b2) -> k a b2", b2=2)[:, :, 1:2].broadcast_to([8, 8, 64])
                if b == 0:
                    self.s_x.wait(te)
                    self.s_ve.wait(te)
                self.tem(self.p_o[0:64, 0:512], wb, be, start=True, stop=True)
                tsb = self.tem(self.p_o[64:128, 0:512], wb, bo, start=True, stop=True,
                               tile_position=(0, 64))
                self.s_te.wait(se, tsb)
                seb = se.activation(EBF, self.p_o[:, 0:512], AF.Exp, bias=self.alog[:, b:b + 1])
                self.s_x.wait(ve, lb)
                self.s_se.wait(ve, seb)
                if b == 0:
                    vup = ve.tensor_tensor(self.S_master[:], EBF, slot, ALU.mult)
                else:
                    ve.tensor_tensor(OTF2[:, 0:512], EBF, slot, ALU.mult)
                    vup = ve.tensor_tensor(self.S_master[:], self.S_master[:],
                                           OTF2[:, 0:512], ALU.add)
                self.s_ve.wait(te, vup)
                self.s_ve.wait(sy, vup)
            self._comb_seb = seb
            self._post_c_xg = self.s_ve.n

        vse_c = {}; scrm = {}

        def emit_sef(c):
            p = c % 2
            sl = E16A[0:16, c * 16:(c + 1) * 16]
            rhs_e = sl.rearrange("k (a b) -> k a b", b=2)[:, :, 0:1].broadcast_to([16, 8, 64])
            rhs_o = sl.rearrange("k (a b) -> k a b", b=2)[:, :, 1:2].broadcast_to([16, 8, 64])
            if c == 0:
                self.s_ve.wait(te, vE16)
            else:
                self.s_ve.wait(te, max(scrm[c - 1], self._post_c_ve))
            self.tem(self.p_d[0:64, 0:512], self.ones16b[:, 0:64], rhs_e, start=True, stop=True)
            tse = self.tem(self.p_d[64:128, 0:512], self.ones16b[:, 64:128], rhs_o,
                           start=True, stop=True, tile_position=(0, 64))
            self.s_te.wait(ve, tse)
            if c >= 2:
                self.s_te.wait(ve, self._tgo[c - 2])
            if c == 0:
                self.s_se.wait(ve, self._comb_seb)
            scrm[c] = ve.tensor_tensor(self.SCR[:, 0:512], self.S_master[:],
                                       self.p_d[:, 0:512], ALU.mult)
            self.s_x.wait(ve, ld[c][3])
            vse_c[c] = ve.tensor_tensor(SEF[p], self.SCR[:, 0:512],
                                        self.LST[:, p * 512:p * 512 + 512], ALU.add)

        gh = {}

        def emit_front(c, g):
            ga = c * 4 + g; gp = ga % 2; p = c % 2
            if ga == 0:
                self.s_sp.wait(te, rch)
                self.s_ve.wait(te, vE16)
            else:
                self.s_ve.wait(te, max(gh[ga - 1]["vd"], self._post_c_ve))
            selc = self.i16f[:, c:c + 1].broadcast_to([16, 128])
            te.matmul(self.p_a[:, 0:512], selc,
                      self.CHIA[0:16, g * 512:(g + 1) * 512], start=True, stop=False)
            tma = self.tem(self.p_a[:, 0:512], self.negdiag[:],
                           self.slrep[:].rearrange("p (x t) -> p x t", x=1).broadcast_to([128, 4, CH]),
                           start=False, stop=True)
            if ga > 0:
                self.s_se.wait(te, gh[ga - 1]["sx"])
            tme = self.tem(self.p_e[:, 0:512], selc,
                           self.CHIA[0:16, g * 512:(g + 1) * 512], start=True, stop=True)
            if g == 0:
                self.s_x.wait(te, ld[c][1])
            for hh in range(4):
                h = 4 * g + hh
                pb = self.p_b if hh % 2 == 0 else self.p_c
                ins = te.matmul(pb[:, gp * 256 + (hh // 2) * 128:gp * 256 + (hh // 2) * 128 + 128],
                                KG[p][(h % 2) * 64:(h % 2) * 64 + 64,
                                      (h // 2) * 128:(h // 2) * 128 + 128],
                                QC[p][(h % 2) * 64:(h % 2) * 64 + 64,
                                      (h // 2) * 128:(h // 2) * 128 + 128],
                                start=True, stop=True,
                                tile_position=((h % 2) * 64, 0))
            gh[ga] = {"tma": tma, "tme": tme, "tsp": self.s_te.inc(ins)}

        def emit_chain(c, g):
            ga = c * 4 + g; gp = ga % 2; p = c % 2
            G = gh[ga]
            negb = self.negC_all[:, c * H + 4 * g:c * H + 4 * g + 4].rearrange(
                "p (h x) -> p h x", x=1).broadcast_to([128, 4, 128])
            self.s_te.wait(ve, G["tma"])
            vd = ve.tensor_tensor(Dbuf.rearrange("p (h t) -> p h t", t=128),
                                  self.p_a[:, 0:512].rearrange("p (h t) -> p h t", t=128),
                                  negb, ALU.add)
            self.s_ve.wait(se, vd)
            sa = se.activation(ABUF, Dbuf, AF.Exp)
            self.s_te.wait(se, G["tme"])
            sx = se.activation(EXPC, self.p_e[:, 0:512], AF.Exp)
            self.s_se.wait(ve, sa)
            self.s_te.wait(ve, max(G["tsp"], self._post_c_te))
            if ga >= 2:
                self.s_te.wait(ve, self._ogo[ga - 2])
            a4 = ABUF.rearrange("p (a b t) -> p a b t", a=2, b=2, t=128)
            o4 = PB[gp].rearrange("p (a b t) -> p a b t", a=2, b=2, t=128)
            ve.tensor_tensor(o4[:, :, 0:1, :], a4[:, :, 0:1, :],
                             self.p_b[:, gp * 256:gp * 256 + 256].rearrange(
                                 "p (a x t) -> p a x t", x=1, t=128), ALU.mult)
            vp = ve.tensor_tensor(o4[:, :, 1:2, :], a4[:, :, 1:2, :],
                                  self.p_c[:, gp * 256:gp * 256 + 256].rearrange(
                                      "p (a x t) -> p a x t", x=1, t=128), ALU.mult)
            self.s_se.wait(ve, sx)
            if g == 0:
                self.s_x.wait(ve, ld[c][1])
            qt4 = QT[gp].rearrange("p (a b t) -> p a b t", a=2, b=2, t=128)
            ex4 = EXPC.rearrange("p (a b t) -> p a b t", a=2, b=2, t=128)
            qsrc = QC[p][:, 2 * g * 128:2 * g * 128 + 256].rearrange(
                "p (a x t) -> p a x t", x=1, t=128)
            ve.tensor_tensor(qt4[0:64, :, 0:1, :], qsrc[0:64], ex4[0:64, :, 0:1, :], ALU.mult)
            vq = ve.tensor_tensor(qt4[64:128, :, 1:2, :], qsrc[64:128], ex4[64:128, :, 1:2, :],
                                  ALU.mult)
            G["vd"] = vd; G["sx"] = sx; G["vp"] = vp; G["vq"] = vq

        def emit_O(c, g):
            ga = c * 4 + g; gp = ga % 2; p = c % 2
            G = gh[ga]
            self.s_ve.wait(te, max(G["vp"], G["vq"], vse_c[c]))
            if g == 0:
                self.s_x.wait(te, ld[c][2])
                if c == 0:
                    self.s_se.wait(te, self._comb_seb)
                if c >= 1:
                    self.s_se.wait(te, self._evac[c - 1])
            for hh in range(4):
                h = 4 * g + hh
                te.matmul(self.p_o[(h % 2) * 64:(h % 2) * 64 + 64,
                                   (h // 2) * 128:(h // 2) * 128 + 128],
                          VN[p][:, h * 64:(h + 1) * 64],
                          PB[gp][:, hh * 128:(hh + 1) * 128],
                          start=True, stop=False,
                          tile_position=(0, (h % 2) * 64))
                ins = te.matmul(self.p_o[(h % 2) * 64:(h % 2) * 64 + 64,
                                         (h // 2) * 128:(h // 2) * 128 + 128],
                                SEF[p][(h % 2) * 64:(h % 2) * 64 + 64,
                                       (h // 2) * 64:(h // 2) * 64 + 64],
                                QT[gp][(h % 2) * 64:(h % 2) * 64 + 64,
                                       hh * 128:(hh + 1) * 128],
                                start=False, stop=True,
                                tile_position=((h % 2) * 64, (h % 2) * 64))
            self._ogo[ga] = self.s_te.inc(ins)
            if g == 3:
                self._tgo[c] = self._ogo[ga]
                self._vql[c] = G["vq"]

        def emit_evac(c):
            cc = c % 4
            self.s_te.wait(se, self._tgo[c])
            if c % 4 == 0 and c > 0:
                self.s_ve.wait(se, self._post_c_ve)
            self._evac[c] = se.activation(
                self.OT[:].rearrange("p (m t) -> p m t", t=512)[:, :, cc * 128:(cc + 1) * 128],
                self.p_o[:, 0:1024].rearrange("p (m t) -> p m t", t=128), AF.Copy)

        pend = None
        for gi in range(NG):
            for cc in range(NG):
                c = gi * NG + cc
                if cc == 0:
                    self.s_ve.wait(sy, self._post_c_xg)
                    if self._spx:
                        self.s_sp.wait(sy, self._spx)
                    self._lx = self.qdma(sy, self.s_x,
                                         self.XG[:].rearrange("p (a f) -> p a f", f=TG),
                                         self.xT.rearrange("(a p) t -> p a t", p=128)[:, :, gi * TG:(gi + 1) * TG])
                for g in range(4):
                    emit_front(c, g)
                    emit_chain(c, g)
                    if pend is not None:
                        if self._comb_seb is None:
                            emit_combine()
                            emit_sef(0)
                        pc, pg = pend
                        emit_O(pc, pg)
                        if pg == 3:
                            emit_evac(pc)
                            while next_ld < NCH and next_ld <= pc + 2:
                                emit_loads(next_ld)
                                next_ld += 1
                    pend = (c, g)
                if c + 1 < NCH:
                    emit_sef(c + 1)
                self.f0_step(2)
            pc, pg = pend
            emit_O(pc, pg)
            emit_evac(pc)
            pend = None
            while next_ld < NCH and next_ld <= pc + 2:
                emit_loads(next_ld)
                next_ld += 1
            self._c_group(gi, ld_o, ogS, SQA, YB, XB2, SQ2)
            self._post_c_ve = self._c_vqc
            self._post_c_xg = self.s_ve.n
            self._post_c_te = self.s_te.n

    def _c_group(self, gi, ld_o, ogS, SQA, YB, XB2, SQ2):
        sy, te, ve, se, po = self.sy, self.te, self.ve, self.se, self.po
        self.f0_step(4)
        self.s_ve.wait(se)
        for a in range(DMT):
            sq = se.activation(SQA[:, a * 512:(a + 1) * 512],
                               self.OT[:, a * 512:(a + 1) * 512], AF.Square)
        self.s_se.wait(te, sq)
        self.s_ve.wait(te)
        for a in range(DMT):
            ins = te.matmul(self.p_a[0:1, 0:512], self.ONES_COL_B,
                            SQA[:, a * 512:(a + 1) * 512],
                            start=(a == 0), stop=(a == DMT - 1))
        tst = self.s_te.inc(ins)
        self.s_te.wait(se, tst)
        rrow = self.rows[0:1, 0:512]
        se.activation(rrow, self.p_a[0:1, 0:512], AF.Ln,
                      scale=1.0 / HID, bias=self.B_RMSEPS[0:1, :])
        srs = se.activation(rrow, rrow, AF.Exp, scale=-0.5)
        self.s_se.wait(te, srs)
        trb = self.tem(self.p_c[:, 0:512], self.ONES_ROW_F, rrow, start=True, stop=True)
        self.s_te.wait(ve, trb)
        vys = {}
        for a in range(DMT):
            ve.scalar_tensor_tensor(YB[:, a * 512:(a + 1) * 512],
                                    self.OT[:, a * 512:(a + 1) * 512],
                                    self.normw_sb[:, a:a + 1],
                                    self.p_c[:, 0:512], ALU.mult, ALU.mult)
            vys[a] = ve.tensor_tensor(YB[:, a * 512:(a + 1) * 512],
                                      YB[:, a * 512:(a + 1) * 512],
                                      ogS[:, a * R + gi * 512:a * R + gi * 512 + 512], ALU.mult)
        self.s_wl.wait(te, ld_o)
        self.s_x.wait(ve, self._lx)
        accs = [self.p_a[:, 0:512], self.p_b[:, 0:512], self.p_d[:, 0:512],
                self.p_e[:, 0:512], self.p_o[:, 0:512], self.p_o[:, 512:1024]]
        for k in range(DMT):
            self.s_ve.wait(te, vys[k])
            for m in range(6):
                ins = te.matmul(accs[m], self.wtile(0, k, m), YB[:, k * 512:(k + 1) * 512],
                                start=(k == 0), stop=(k == DMT - 1))
        t6 = self.s_te.inc(ins)
        self.s_te.wait(ve, t6)
        prev_m = {}
        for m in range(2):
            prev_m[m] = ve.tensor_tensor(self.XG[:, m * 512:(m + 1) * 512],
                                         self.XG[:, m * 512:(m + 1) * 512], accs[m], ALU.add)
        self.s_ve.wait(te, prev_m[1])
        for k in range(DMT):
            for m in (6, 7):
                ins = te.matmul(accs[m - 6], self.wtile(0, k, m), YB[:, k * 512:(k + 1) * 512],
                                start=(k == 0), stop=(k == DMT - 1))
        t8 = self.s_te.inc(ins)
        for m in range(2, 6):
            prev_m[m] = ve.tensor_tensor(self.XG[:, m * 512:(m + 1) * 512],
                                         self.XG[:, m * 512:(m + 1) * 512], accs[m], ALU.add)
        self.s_te.wait(ve, t8)
        for m in (6, 7):
            prev_m[m] = ve.tensor_tensor(self.XG[:, m * 512:(m + 1) * 512],
                                         self.XG[:, m * 512:(m + 1) * 512], accs[m - 6], ALU.add)
        self.s_ve.wait(sy, prev_m[DMT - 1])
        spx = self.qdma(sy, self.s_sp,
                        self.x2_d.rearrange("(a p) t -> p a t", p=128)[:, :, gi * TG:(gi + 1) * TG],
                        self.XG[:].rearrange("p (a f) -> p a f", f=TG))
        self._spx = spx
        self.s_x.wait(se, self._lx)
        self.s_ve.wait(se, prev_m[DMT - 1])
        for a in range(DMT):
            se.activation(XB2[:, a * 512:(a + 1) * 512], self.XG[:, a * 512:(a + 1) * 512],
                          AF.Copy)
            sml = se.activation(SQ2[:, a * 512:(a + 1) * 512], self.XG[:, a * 512:(a + 1) * 512],
                                AF.Square)
        self.s_se.wait(te, sml)
        self.s_ve.wait(te, prev_m[7])
        for a in range(DMT):
            ins = te.matmul(self.p_a[0:1, 0:512], self.ONES_COL_B,
                            XB2[:, a * 512:(a + 1) * 512], start=(a == 0), stop=(a == DMT - 1))
        self.s_te.inc(ins)
        for a in range(DMT):
            ins = te.matmul(self.p_b[0:1, 0:512], self.ONES_COL_B,
                            SQ2[:, a * 512:(a + 1) * 512], start=(a == 0), stop=(a == DMT - 1))
        tml = self.s_te.inc(ins)
        self.s_te.wait(ve, tml)
        rm = self.rows[0:1, 0:512]; rvar = self.rows[0:1, 512:1024]
        rr = self.rows[0:1, 1024:1536]; rmr = self.rows[0:1, 1536:2048]
        ve.tensor_scalar(rm, self.p_a[0:1, 0:512], 1.0 / DM, None, ALU.mult)
        ve.tensor_scalar(rvar, self.p_b[0:1, 0:512], 1.0 / DM, None, ALU.mult)
        ve.tensor_mul(rr, rm, rm)
        vst2 = ve.tensor_sub(rvar, rvar, rr)
        self.s_ve.wait(se, vst2)
        se.activation(rvar, rvar, AF.Ln, bias=self.B_LNEPS[0:1, :])
        sst2 = se.activation(rr, rvar, AF.Exp, scale=-0.5)
        self.s_se.wait(ve, sst2)
        vrb2 = ve.tensor_mul(rmr, rm, rr)
        self.s_ve.wait(te, vrb2)
        self.tem(self.p_c[:, 0:512], self.ONES_ROW_F, rr, start=True, stop=True)
        tbc2 = self.tem(self.p_d[:, 0:512], self.ONES_ROW_F, rmr, start=True, stop=True)
        self.s_te.wait(ve, tbc2)
        self.s_sp.wait(ve, spx)
        for a in range(DMT):
            ve.tensor_mul(self.SCR[:, 0:512], self.XG[:, a * 512:(a + 1) * 512], self.p_c[:, 0:512])
            vln2 = ve.tensor_sub(self.XG[:, a * 512:(a + 1) * 512], self.SCR[:, 0:512],
                                 self.p_d[:, 0:512])
        rmx = self.SCR[:, 0:512]; rmn = self.SCR[:, 512:1024]
        ve.tensor_copy(rmx, self.XG[:, 0:512])
        ve.tensor_copy(rmn, self.XG[:, 0:512])
        for a in range(1, DMT):
            ve.tensor_tensor(rmx, rmx, self.XG[:, a * 512:(a + 1) * 512], ALU.max)
            ve.tensor_tensor(rmn, rmn, self.XG[:, a * 512:(a + 1) * 512], ALU.min)
        ve.tensor_scalar(rmn, rmn, -1.0, None, ALU.mult)
        vtr = ve.tensor_tensor(rmx, rmx, rmn, ALU.max)
        self.s_ve.wait(te, vtr)
        for ct in range(4):
            ins = te.transpose(self.p_e[:, ct * 128:(ct + 1) * 128],
                               rmx[:, ct * 128:(ct + 1) * 128], self.ID_F)
        ttd = self.s_te.inc(ins)
        self.s_te.wait(ve, ttd)
        red = self.smalls[:, 0:4]
        ve.tensor_reduce(red, self.p_e[:, 0:512].rearrange("p (a b) -> p a b", b=128),
                         AX.X, ALU.max)
        vrd = ve.tensor_scalar(red, red, Q_EPS, None, ALU.max)
        self.s_ve.wait(te, vrd)
        tred = self.tet(self.p_e[0:4, 256:384], red, self.ID_F)
        self.s_te.wait(ve, tred)
        vev = ve.tensor_copy(self.st16[0:4, 0:128], self.p_e[0:4, 256:384])
        self.s_ve.wait(sy, vev)
        self.qdma(sy, self.s_sp, self.mx_d[gi * 4:(gi + 1) * 4, :], self.st16[0:4, 0:128])
        mq = self.qdma(sy, self.s_sp, self.SCR[0:1, 512:1024], self.st16[0:4, 0:128])
        self.s_sp.wait(se, mq)
        srow = self.rows[0:1, 0:512]
        se.activation(srow, self.SCR[0:1, 512:1024], AF.Ln)
        sq2 = se.activation(srow, srow, AF.Exp, scale=-1.0, bias=self.B_LN127[0:1, :])
        self.s_se.wait(te, sq2)
        tqb = self.tem(self.p_c[:, 0:512], self.ONES_ROW_F, srow, start=True, stop=True)
        self.s_te.wait(ve, tqb)
        self._c_vqc = ve.tensor_copy(self.SCR[:, 0:512], self.p_c[:, 0:512])
        i32v = self.SCR[:, 512:1024].bitcast(I32)
        for a in range(DMT):
            ve.tensor_tensor(i32v, self.XG[:, a * 512:(a + 1) * 512],
                             self.SCR[:, 0:512], ALU.mult)
            vq2 = ve.tensor_copy(self.XN[:, a * R + gi * 512:a * R + gi * 512 + 512], i32v)


    def f0_init(self):
        self._f0_steps = []
        for wi, (wd, na, fw) in enumerate([(self.WTg, 8, 4096), (self.WTu, 8, 4096),
                                           (self.WTd, 32, 1024)]):
            for a in range(na):
                if fw > 2048:
                    self._f0_steps.append((wi, wd, a, 0, 2048))
                    self._f0_steps.append((wi, wd, a, 2048, 2048))
                else:
                    self._f0_steps.append((wi, wd, a, 0, fw))
        self._f0_i = 0
        self._f0_l = 0
        self._f0_ld = {}
        self._f0_sab = {}
        self._f0_pm = None
        self.ve.memset(self.smalls[:, 230:233], 0.0)
        self.f0_load()

    def f0_load(self):
        sy, se, po = self.sy, self.se, self.po
        slot1f = self.WBUF[:, 8192:16384].bitcast(F32)
        if self._f0_l >= len(self._f0_steps):
            return
        i = self._f0_l
        self._f0_l += 1
        wi, wd, a, co, fw = self._f0_steps[i]
        half = i % 2
        dst = slot1f[:, half * 2048:half * 2048 + fw].rearrange("p (x f) -> p x f", x=1)
        src = wd.rearrange("(a p) f -> p a f", p=128)[:, a:a + 1, co:co + fw]
        if i < 2:
            self.s_te.wait(po, self.te_v_done)
        else:
            self.s_se.wait(po, self._f0_sab[i - 2])
        self._f0_ld[i] = self.qdma(po, self.s_wl, dst, src)

    def f0_step(self, n=1):
        sy, te, ve, se, po = self.sy, self.te, self.ve, self.se, self.po
        slot1f = self.WBUF[:, 8192:16384].bitcast(F32)
        red = self.smalls[:, 233:234]
        for _ in range(n):
            if self._f0_i >= len(self._f0_steps):
                return
            self.f0_load()
            i = self._f0_i
            self._f0_i += 1
            wi, wd, a, co, fw = self._f0_steps[i]
            half = i % 2
            sl = slot1f[:, half * 2048:half * 2048 + fw]
            self.s_wl.wait(se, self._f0_ld[i])
            if i > 0:
                self.s_ve.wait(se, self._f0_pm)
            self._f0_sab[i] = se.activation(sl, sl, AF.Abs, accum_out=red)
            self.s_se.wait(ve, self._f0_sab[i])
            self._f0_pm = ve.tensor_tensor(self.smalls[:, 230 + wi:231 + wi],
                                           self.smalls[:, 230 + wi:231 + wi], red, ALU.add)

    def phase3_f0(self):
        sy, te, ve, se, po = self.sy, self.te, self.ve, self.se, self.po
        self.f0_step(len(self._f0_steps))
        acc = self.smalls[:, 230:233]
        pm = self._f0_pm
        self.s_ve.wait(te, pm)
        for wi in range(3):
            self.tem(self.p_e[0:1, 100 + wi:101 + wi], self.ONES_COL_F, acc[:, wi:wi + 1],
                     start=True, stop=True)
        tf0 = self.s_te.n
        self.s_te.wait(ve, tf0)
        mrow = self.smalls[0:1, 236:239]
        ve.tensor_scalar(mrow, self.p_e[0:1, 100:103], 1.0 / (DM * FF), None, ALU.mult)
        vm0 = ve.tensor_scalar(mrow, mrow, Q_EPS, None, ALU.max)
        thr = self.smalls[0:1, 240:243]; nthr = self.smalls[0:1, 243:246]
        ve.tensor_scalar(thr, mrow, 0.5, None, ALU.mult)
        vth = ve.tensor_scalar(nthr, thr, -1.0, None, ALU.mult)
        sgm = self.smalls[0:1, 246:247]; md127 = self.smalls[0:1, 247:248]
        ve.tensor_tensor(sgm, mrow[0:1, 0:1], mrow[0:1, 0:1], ALU.mult)
        ve.tensor_tensor(sgm, sgm, mrow[0:1, 1:2], ALU.mult)
        ve.tensor_scalar(sgm, sgm, 1.0 / (127.0 ** 3), None, ALU.mult)
        vsc = ve.tensor_scalar(md127, mrow[0:1, 2:3], 1.0 / 127.0, None, ALU.mult)
        self.s_ve.wait(te, max(vth, vsc))
        for wi in range(3):
            te.matmul(self.p_e[:, 104 + wi:105 + wi], self.ONES_ROW_F, thr[0:1, wi:wi + 1],
                      start=True, stop=True)
            ins = te.matmul(self.p_e[:, 108 + wi:109 + wi], self.ONES_ROW_F, nthr[0:1, wi:wi + 1],
                            start=True, stop=True)
        tthr = self.s_te.inc(ins)
        self.s_te.wait(ve, tthr)
        self.thrc = self.smalls[:, 240:243]
        self.nthrc = self.smalls[:, 243:246]
        ve.tensor_copy(self.thrc, self.p_e[:, 104:107])
        ve.tensor_copy(self.nthrc, self.p_e[:, 108:111])
        self.sgm = sgm; self.md127 = md127

    def ternarize(self, eng, wsrc_f32, tern_dst, tmp, wi, width):
        eng.tensor_scalar(tmp[:, 0:width], wsrc_f32[:, 0:width], self.thrc[:, wi:wi + 1],
                          None, ALU.is_gt)
        eng.tensor_scalar(tern_dst[:, 0:width], wsrc_f32[:, 0:width], self.nthrc[:, wi:wi + 1],
                          None, ALU.is_lt)
        return eng.tensor_tensor(tern_dst[:, 0:width], tmp[:, 0:width], tern_dst[:, 0:width],
                                 ALU.subtract)

    def ternarize4(self, eng, wsrc, dst, tmp, wi):
        eng.tensor_scalar(tmp, wsrc, self.thrc[:, wi:wi + 1], None, ALU.is_gt)
        eng.tensor_scalar(dst, wsrc, self.nthrc[:, wi:wi + 1], None, ALU.is_lt)
        return eng.tensor_tensor(dst, tmp, dst, ALU.subtract)

    def fd_step(self, ternd, tstg, ttmp):
        sy, ve = self.sy, self.ve
        k, half = divmod(self._fd_i, 2)
        if k >= FFT:
            return
        self._fd_i += 1
        if self._fd_t is not None:
            self.s_ve.wait(sy, self._fd_t)
        lw = self.qdma(sy, self.s_x, tstg.rearrange("p (x y f) -> p x y f", x=1, y=1),
                       self.WTd.rearrange("(a p) (h f) -> p a h f", p=128, h=2)[:, k:k + 1, half:half + 1, :])
        self.s_x.wait(ve, lw)
        base = k * 1024 + half * 512
        self._fd_t = self.ternarize(ve, tstg, ternd[:, base:base + 512], ttmp, 2, 512)

    def phase3_f1(self):
        sy, te, ve, se, po = self.sy, self.te, self.ve, self.se, self.po
        slot1f = self.WBUF[:, 8192:16384].bitcast(F32)
        slot0f = self.WBUF[:, 0:8192].bitcast(F32)
        terng = self.A2S[:, 0:4096]
        ternu = self.XG[:].bitcast(BF16)[:, 4096:8192]
        hst = self.SCB[:, 1536:2048]
        ttmp = self.OT[:, 0:4096]
        ternd = self.ARE[:, 0:32768]
        tstg = self.SCB[:, 0:1024].bitcast(F32)
        RMX = self.XG[:, 0:2048]
        self.s_te.wait(ve)
        self.s_se.wait(ve)
        vmn = ve.memset(RMX, 0.0)
        self._fd_i = 0
        self._fd_t = None
        prev_h = None
        prev_ev = {0: None, 1: None}
        def ld_gu(ig):
            lg = self.qdma(sy, self.s_x, slot1f.rearrange("p (a f) -> p a f", f=512),
                           self.WTg.rearrange("(a p) f -> p a f", p=128)[:, :, ig * 512:(ig + 1) * 512])
            lu = self.qdma(self.se.e, self.s_sc, slot0f.rearrange("p (a f) -> p a f", f=512),
                           self.WTu.rearrange("(a p) f -> p a f", p=128)[:, :, ig * 512:(ig + 1) * 512])
            return lg, lu

        self.s_po.wait(sy)
        self.s_ve.wait(sy)
        self.s_te.wait(sy)
        self.s_se.wait(te)
        self.s_ve.wait(se)
        lds = {0: ld_gu(0)}
        prev_se = {0: None, 1: None}
        prev_vh = None

        def sl4(buf, it):
            return buf.rearrange("p (a f) -> p a f", f=512)[:, :, it * 128:(it + 1) * 128]

        for ig in range(DMT):
            lg, lu = lds[ig]
            self.s_x.wait(ve, lg)
            self.ternarize(ve, slot1f, terng, ttmp, 0, 4096)
            self.s_sc.wait(ve, lu)
            vt = self.ternarize(ve, slot0f, ternu, ttmp, 1, 4096)
            if ig + 1 < DMT:
                self.s_ve.wait(sy, vt)
                self.s_ve.wait(self.se.e, vt)
                lds[ig + 1] = ld_gu(ig + 1)
            self.s_ve.wait(te, vt)
            for tg in range(NG):
                for it in range(4):
                    self.fd_step(ternd, tstg, ttmp)
                    par = it % 2
                    pg = self.p_a if par == 0 else self.p_c
                    pu = self.p_b if par == 0 else self.p_d
                    if prev_ev[par] is not None:
                        self.s_ve.wait(te, prev_ev[par])
                        self.s_se.wait(te, prev_se[par])
                    for a2 in range(DMT):
                        te.matmul(pg[:, 0:512], terng[:, a2 * 512 + it * 128:a2 * 512 + it * 128 + 128],
                                  self.XN[:, a2 * R + tg * TG:a2 * R + (tg + 1) * TG],
                                  start=(a2 == 0), stop=(a2 == DMT - 1))
                    for a2 in range(DMT):
                        ins = te.matmul(pu[:, 0:512], ternu[:, a2 * 512 + it * 128:a2 * 512 + it * 128 + 128],
                                        self.XN[:, a2 * R + tg * TG:a2 * R + (tg + 1) * TG],
                                        start=(a2 == 0), stop=(a2 == DMT - 1))
                    tgu = self.s_te.inc(ins)
                    self.s_te.wait(ve, tgu)
                    self.s_te.wait(se, tgu)
                    if prev_h is not None:
                        self.s_sc.wait(ve, prev_h)
                    if prev_vh is not None:
                        self.s_ve.wait(se, prev_vh)
                    sr = se.activation(self.SCR[:, 0:512], pg[:, 0:512], AF.Relu)
                    sqm = se.activation(self.SCR[:, 512:1024], self.SCR[:, 0:512], AF.Square)
                    self.s_se.wait(ve, sqm)
                    vh = ve.tensor_tensor(hst, self.SCR[:, 512:1024], pu[:, 0:512], ALU.mult)
                    self.s_ve.wait(se, vh)
                    sab = se.activation(self.SCR[:, 0:512], hst, AF.Abs)
                    self.s_se.wait(ve, sab)
                    vhm = ve.tensor_tensor(RMX[:, tg * 512:(tg + 1) * 512], RMX[:, tg * 512:(tg + 1) * 512],
                                           self.SCR[:, 0:512], ALU.max)
                    prev_ev[par] = vh
                    prev_se[par] = sr
                    prev_vh = vhm
                    self.s_ve.wait(self.se.e, vh)
                    prev_h = self.qdma(self.se.e, self.s_sc,
                                       self.h_d[(ig * 4 + it) * 128:(ig * 4 + it + 1) * 128,
                                                tg * TG:(tg + 1) * TG], hst)
        self.s_ve.wait(sy)
        self.s_sc.wait(sy)
        self._f2lh0 = self.qdma(sy, self.s_x,
                                self.WBUF[:, 0:16384].rearrange("p (k t) -> p k t", t=512),
                                self.h_d.rearrange("(k p) t -> p k t", p=128)[:, :, 0:TG])
        for tg in range(NG):
            rx = RMX[:, tg * 512:(tg + 1) * 512]
            self.s_ve.wait(te)
            for ct in range(4):
                ins = te.transpose(self.p_e[:, ct * 128:(ct + 1) * 128],
                                   rx[:, ct * 128:(ct + 1) * 128], self.ID_F)
            ttd = self.s_te.inc(ins)
            self.s_te.wait(ve, ttd)
            red = self.smalls[:, 0:4]
            vrd = ve.tensor_reduce(red, self.p_e[:, 0:512].rearrange("p (a b) -> p a b", b=128),
                                   AX.X, ALU.max)
            self.s_ve.wait(te, vrd)
            tred = self.tet(self.p_e[0:4, 256:384], red, self.ID_F)
            self.s_te.wait(ve, tred)
            self.s_sp.wait(ve)
            vev = ve.tensor_copy(self.st16[0:4, 0:128], self.p_e[0:4, 256:384])
            self.s_ve.wait(sy, vev)
            self.qdma(sy, self.s_sp, self.mx_d[16 + tg * 4:16 + (tg + 1) * 4, :],
                      self.st16[0:4, 0:128])

    def phase3_f2(self):
        sy, te, ve, se, po = self.sy, self.te, self.ve, self.se, self.po
        ternd = self.ARE[:, 0:32768]
        HB = {0: self.WBUF[:, 0:16384], 1: self.XN[:]}
        OTF = self.OT[:].bitcast(F32)
        SCBF = self.SCB[:].bitcast(F32)
        S1B = {0: SCBF[:, 0:512], 1: SCBF[:, 512:1024]}
        DSC = {0: self.SCR[:, 0:512], 1: self.SCR[:, 512:1024]}
        i32v = OTF[:, 512:1024].bitcast(I32)
        OSTG = OTF[:, 0:512]
        accs1 = [self.p_a[:, 0:512], self.p_b[:, 0:512], self.p_c[:, 0:512],
                 self.p_e[:, 0:512], self.p_o[:, 0:512], self.p_o[:, 512:1024]]
        self.s_sc.wait(sy)
        self.s_te.wait(sy)
        self.s_ve.wait(sy)
        ldh = {}; ldx = {}; rmc = {}; smath = {}; vmath = {}
        self._f2t8 = {}; self._f2vq = {}; self._f2vo = {}

        def emit_lh(tg):
            p = tg % 2
            if tg >= 2:
                self.s_te.wait(sy, self._f2t8[tg - 2])
                self.s_ve.wait(sy, self._f2vq[tg - 2])
            ldh[tg] = self.qdma(sy, self.s_x, HB[p].rearrange("p (k t) -> p k t", t=512),
                                self.h_d.rearrange("(k p) t -> p k t", p=128)[:, :, tg * TG:(tg + 1) * TG])

        def emit_lx2(tg):
            if tg >= 1:
                self.s_ve.wait(sy, self._f2vo[tg - 1])
            ldx[tg] = self.qdma(sy, self.s_x, self.XG[:].rearrange("p (a f) -> p a f", f=TG),
                                self.x2_d.rearrange("(a p) t -> p a t", p=128)[:, :, tg * TG:(tg + 1) * TG])

        def emit_rows(tg):
            p = tg % 2
            self.s_sp.wait(sy)
            if tg >= 1:
                self.s_ve.wait(sy, vmath[tg - 1])
                self.s_se.wait(sy, smath[tg - 1])
            lr1 = self.qdma(sy, self.s_x, self.st16[0:4, 0:128], self.mx_d[tg * 4:(tg + 1) * 4, :])
            lr2 = self.qdma(sy, self.s_x, self.st16[0:4, 128:256], self.mx_d[16 + tg * 4:16 + (tg + 1) * 4, :])
            self.s_x.wait(sy, lr2)
            r1 = self.qdma(sy, self.s_sp, self.rows[0:1, 0:512], self.st16[0:4, 0:128])
            r2 = self.qdma(sy, self.s_sp, self.rows[0:1, 512:1024], self.st16[0:4, 128:256])
            self.s_sp.wait(ve, r2)
            mxcr = self.rows[0:1, 0:512]; M1r = self.rows[0:1, 512:1024]
            Fr = self.rows[0:1, 1024:1536]; Dr = self.rows[0:1, 1536:2048]
            ve.tensor_tensor(Fr, mxcr, mxcr, ALU.mult)
            ve.tensor_tensor(Fr, Fr, mxcr, ALU.mult)
            vF = ve.tensor_scalar(Fr, Fr, self.sgm, None, ALU.mult)
            self.s_ve.wait(se, vF)
            sF = se.activation(Dr, Fr, AF.Ln)
            self.s_se.wait(ve, sF)
            veF = ve.tensor_scalar(Dr, Dr, -1.0, None, ALU.mult)
            self.s_ve.wait(se, veF)
            sE = se.activation(Dr, Dr, AF.Exp, bias=self.B_LNQEPS[0:1, :])
            self.s_se.wait(ve, sE)
            vD = ve.tensor_tensor(Dr, M1r, Dr, ALU.max)
            self.s_ve.wait(se, vD)
            s1 = se.activation(mxcr, Dr, AF.Ln)
            smath[tg] = se.activation(mxcr, mxcr, AF.Exp, scale=-1.0,
                                      bias=self.B_LN127[0:1, :])
            self.s_se.wait(ve, smath[tg])
            ve.tensor_tensor(M1r, Fr, Dr, ALU.mult)
            vdsc = ve.tensor_scalar(M1r, M1r, self.md127, None, ALU.mult)
            self.s_ve.wait(te, vdsc)
            if tg >= 1:
                self.s_ve.wait(te, rmc[tg - 1])
            tb1 = self.tem(self.p_d[:, 0:512], self.ONES_ROW_F, mxcr, start=True, stop=True)
            self.s_te.wait(ve, tb1)
            vc1 = ve.tensor_copy(S1B[p], self.p_d[:, 0:512])
            self.s_ve.wait(te, vc1)
            tb2 = self.tem(self.p_d[:, 0:512], self.ONES_ROW_F, M1r, start=True, stop=True)
            self.s_te.wait(ve, tb2)
            rmc[tg] = ve.tensor_copy(DSC[p], self.p_d[:, 0:512])
            vmath[tg] = rmc[tg]

        ldh[0] = self._f2lh0
        emit_lx2(0)
        emit_rows(0)
        outr = self.outT.rearrange("(a p) t -> p a t", p=128)
        for tg in range(NG):
            p = tg % 2
            self.s_x.wait(ve, ldh[tg])
            vqs = {}
            for k in range(FFT):
                ve.tensor_tensor(i32v, HB[p][:, k * 512:(k + 1) * 512], S1B[p], ALU.mult)
                vqs[k] = ve.tensor_copy(HB[p][:, k * 512:(k + 1) * 512], i32v)
            self._f2vq[tg] = vqs[FFT - 1]
            if tg + 1 < NG:
                emit_lh(tg + 1)
            for k in range(FFT):
                self.s_ve.wait(te, vqs[k])
                for m2 in range(6):
                    ins = te.matmul(accs1[m2],
                                    ternd[:, k * 1024 + m2 * 128:k * 1024 + m2 * 128 + 128],
                                    HB[p][:, k * 512:(k + 1) * 512],
                                    start=(k == 0), stop=(k == FFT - 1))
            t6 = self.s_te.inc(ins)
            if tg + 1 < NG:
                emit_rows(tg + 1)
            self.s_x.wait(ve, ldx[tg])
            vos = {}

            def evac(m2, pp):
                self.s_out.wait(ve)
                ve.tensor_tensor(OSTG, pp, DSC[p], ALU.mult)
                vo = ve.tensor_tensor(OSTG, OSTG, self.XG[:, m2 * 512:(m2 + 1) * 512], ALU.add)
                self.s_ve.wait(sy, vo)
                self.s_out.wait(sy)
                self.s_out.dma(sy.dma_start(
                    out=outr[:, m2:m2 + 1, tg * TG:(tg + 1) * TG],
                    in_=OSTG.rearrange("p (x t) -> p x t", x=1)))
                return vo

            self.s_te.wait(ve, t6)
            vos[0] = evac(0, accs1[0])
            vos[1] = evac(1, accs1[1])
            self.s_ve.wait(te, vos[1])
            for k in range(FFT):
                for m2 in (6, 7):
                    ins = te.matmul(accs1[m2 - 6],
                                    ternd[:, k * 1024 + m2 * 128:k * 1024 + m2 * 128 + 128],
                                    HB[p][:, k * 512:(k + 1) * 512],
                                    start=(k == 0), stop=(k == FFT - 1))
            t8 = self.s_te.inc(ins)
            self._f2t8[tg] = t8
            for m2 in range(2, 6):
                vos[m2] = evac(m2, accs1[m2])
            self.s_te.wait(ve, t8)
            vos[6] = evac(6, accs1[0])
            vos[7] = evac(7, accs1[1])
            self._f2vo[tg] = vos[7]
            if tg + 1 < NG:
                emit_lx2(tg + 1)

    def emit_debug(self):
        sy = self.sy
        self.s_se.wait(sy)
        self.s_ve.wait(sy)
        self.s_te.wait(sy)
        self.s_sp.wait(sy)
        self.s_wl.wait(sy)
        def dbg(name, src, shape, dt):
            o = self.out(name, shape, dt)
            self.s_out.wait(sy)
            self.s_out.dma(sy.dma_start(out=o, in_=src))
        dbg("d_xn", self.XN[:], [128, DMT * R], BF16)
        dbg("d_rows", self.rows[:], [1, 4 * TG], F32)
        dbg("d_xg", self.XG[:], [128, DMT * TG], F32)
        dbg("d_sq", self.ARE[:, 16384 + DMT * TG:16384 + DMT * TG + 512], [128, 512], BF16)
        dbg("d_scrow", self.SCR[0:1, :], [1, 1024], F32)
        dbg("d_rowsb", self.rowsb, [1, 2 * TG], BF16)
        dbg("d_Call", self.C_all[:], [128, NCH * H], F32)
        if PHASES >= 2:
            dbg("d_gS", self.ARE[:, 16384:32768], [128, DMT * R], BF16)
            dbg("d_ogS", self.ARE[:, 0:16384], [128, DMT * R], BF16)
        dbg("d_Ball", self.Ballc[:], [16, NCH + 1], F32)
        dbg("d_cC", self.cCcol[:], [16, NCH], F32)
        dbg("d_states", self.std_d, [NCH * 128, 512], BF16)
        if PHASES >= 4:
            dbg("d_xq", self.XN[:], [128, DMT * R], BF16)
            dbg("d_mx", self.mx_d, [2 * NG * 4, 128], F32)
            dbg("d_x2", self.x2_d, [DM, R], F32)
        if PHASES >= 5:
            dbg("d_hd", self.h_d, [FFT * 128, R], BF16)
            dbg("d_smalls", self.smalls[0:1, 236:239], [1, 3], F32)
        dbg("d_Smaster", self.S_master[:], [128, 512], F32)
        if PHASES >= 2:
            dbg("d_kg", self.kgT_d, [HT * NCH * 128, CH], BF16)
            dbg("d_q", self.qT_d, [HT * NCH * 128, CH], BF16)
            dbg("d_v", self.vN_d, [NCH * 128, HID], BF16)

    def finish(self):
        sy = self.sy
        self.s_se.wait(sy)
        self.s_ve.wait(sy)
        self.s_te.wait(sy)
        self.s_sp.wait(sy)
        self.s_wl.wait(sy)
        self.s_po.wait(sy)
        self.s_sp.wait(sy)
        self.s_x.wait(sy)
        self.s_out.wait(sy)


def _consts(rank):
    import ml_dtypes
    j = np.arange(128)
    negUT = -np.triu(np.ones((128, 128), dtype=np.float32))
    wind = np.zeros((8, 8), dtype=np.float32)
    for b in range(8):
        for k in range(8):
            if b + 1 <= k <= rank - 1:
                wind[k, b] = 1.0
    alog = np.array([0.0 if b < rank else NEG for b in range(8)], dtype=np.float32)
    return {
        "c_id_bf": np.eye(128, dtype=ml_dtypes.bfloat16),
        "c_id_f": np.eye(128, dtype=np.float32),
        "c_negUT": negUT,
        "c_onescb": np.ones((128, 1), dtype=ml_dtypes.bfloat16),
        "c_onescf": np.ones((128, 1), dtype=np.float32),
        "c_onesrb": np.ones((1, 128), dtype=ml_dtypes.bfloat16),
        "c_onesrf": np.ones((1, 128), dtype=np.float32),
                "c_i16f": np.eye(16, dtype=np.float32),
        "c_i16b": np.eye(16, dtype=ml_dtypes.bfloat16),
        "c_ones16f": np.ones((16, 128), dtype=np.float32),
        "c_ones16b": np.ones((16, 128), dtype=ml_dtypes.bfloat16),
        "c_actc": np.tile(np.array([[LN_EPS, 1.0, RMS_EPS, LN127, float(np.log(Q_EPS))]], dtype=np.float32), (128, 1)),
        "c_negdiag": (np.eye(128) * NEG).astype(ml_dtypes.bfloat16),
        "c_slrep": np.where(j[:, None] > j[None, :], 1.0, 0.0).astype(ml_dtypes.bfloat16),
        "c_wind": wind,
        "c_alog": np.tile(alog[None, :], (128, 1)).astype(np.float32),
    }


def kernel(L0_attn_x, L0_attn_Wq, L0_attn_Wk, L0_attn_Wv, L0_attn_Wo, L0_attn_Wg,
           L0_attn_Wog, L0_attn_Wd, L0_attn_bd, L0_attn_norm_w,
           L0_ffn_w_gate, L0_ffn_w_up, L0_ffn_w_down):
    x = np.asarray(L0_attn_x)[0]
    T = lambda w: np.ascontiguousarray(np.asarray(w).T.astype(np.float32))
    WqT, WkT, WvT, WgT, WogT = T(L0_attn_Wq), T(L0_attn_Wk), T(L0_attn_Wv), T(L0_attn_Wg), T(L0_attn_Wog)
    WoT = T(L0_attn_Wo)
    WdT = T(L0_attn_Wd)
    WdTp = np.ascontiguousarray(WdT.reshape(DMT, 128, H).transpose(1, 0, 2).reshape(128, DMT * H))
    WTg, WTu = T(L0_ffn_w_gate), T(L0_ffn_w_up)
    WTd = T(L0_ffn_w_down)
    bd = np.asarray(L0_attn_bd, dtype=np.float32).reshape(1, H)
    normw = np.ascontiguousarray(np.asarray(L0_attn_norm_w, dtype=np.float32).reshape(HT, 128).T)

    mk = MK()
    nc = mk.build()
    in_maps = []
    for c in range(N_CORES):
        xT_c = np.ascontiguousarray(x[c * R:(c + 1) * R, :].T)
        m = {"xT": xT_c, "WqT": WqT, "WkT": WkT, "WvT": WvT, "WgT": WgT, "WogT": WogT,
             "WoT": WoT, "WdTp": WdTp, "bd": bd, "normw": normw,
             "WTg": WTg, "WTu": WTu, "WTd": WTd}
        m.update(_consts(c))
        in_maps.append(m)
    res = run_bass_kernel_spmd(nc, in_maps, core_ids=list(range(N_CORES)))
    outs = [np.asarray(r["outT"]).T for r in res.results]
    out = np.concatenate(outs, axis=0)[None]
    if DEBUG:
        kernel._results = res
    return out.astype(np.float32)
